# version 24
# baseline (speedup 1.0000x reference)
import dataclasses
from contextlib import ExitStack

import numpy as np

import concourse.bass as bass
import concourse.mybir as mybir
from concourse.bass_utils import run_bass_kernel_spmd

F32 = mybir.dt.float32
BF16 = mybir.dt.bfloat16
I32 = mybir.dt.int32
AF = mybir.ActivationFunctionType
ALU = mybir.AluOpType

B, L, D, DIN, H, NS = 2, 8192, 1024, 2048, 3072, 16
TC = 512
NCH = 16
NJ = 4
KD = 8
NMI = 4
HM = H // 128
EPS = 1e-5
GROUPS = [[0, 1, 2, 3], [4, 5, 6, 7]]
ENGINES = ("sync", "vector", "scalar", "tensor", "gpsimd")


def bc_mid(ap, rep):
    assert len(ap.ap) == 2
    return dataclasses.replace(ap, ap=[ap.ap[0], [0, rep], ap.ap[1]])


class Ctr:
    def __init__(self, h):
        self.h = h
        self.n = 0


class Bld:
    def __init__(self, nc, ctx):
        self.nc = nc
        self.ops = {e: [] for e in ENGINES}
        E = ctx.enter_context
        self.LD = Ctr(E(nc.semaphore("LD")))
        self.CC = Ctr(E(nc.semaphore("CC")))
        self.VC = Ctr(E(nc.semaphore("VC")))
        self.SC = Ctr(E(nc.semaphore("SC")))
        self.TM = Ctr(E(nc.semaphore("TM")))
        self.PC = Ctr(E(nc.semaphore("PC")))
        self.CD = Ctr(E(nc.semaphore("CD")))
        self.WQ = Ctr(E(nc.semaphore("WQ")))
        self.WO = Ctr(E(nc.semaphore("WO")))
        self.XN = Ctr(E(nc.semaphore("XN")))
        self.BAR = Ctr(E(nc.semaphore("BAR")))

    def op(self, eng, build, ctr=None):
        if ctr is None:
            self.ops[eng].append(build)
            return None
        ctr.n += 1
        tgt = ctr.n
        self.ops[eng].append(lambda e, b=build, c=ctr: b(e).then_inc(c.h, 1))
        return tgt

    def dma(self, eng, build, ctr=None):
        ctr = ctr or self.LD
        ctr.n += 16
        tgt = ctr.n
        self.ops[eng].append(lambda e, b=build, c=ctr: b(e).then_inc(c.h, 16))
        return tgt

    def wait(self, eng, ctr, val):
        if val is None:
            return
        self.ops[eng].append(lambda e, c=ctr, v=val: e.wait_ge(c.h, v))

    def barrier(self):
        self.wait("sync", self.LD, self.LD.n)
        self.wait("sync", self.WQ, self.WQ.n)
        self.wait("sync", self.WO, self.WO.n)
        self.wait("sync", self.XN, self.XN.n)
        tgt = self.BAR.n + len(ENGINES)
        for e in ENGINES:
            self.ops[e].append(lambda E: E.sem_inc(self.BAR.h, 1))
        self.BAR.n = tgt
        for e in ENGINES:
            self.ops[e].append(lambda E, v=tgt: E.wait_ge(self.BAR.h, v))

    def coll(self, kind, op_, ins, outs, wait_ld=None):
        if wait_ld is not None:
            self.wait("gpsimd", self.LD, wait_ld)
        self.CC.n += 1
        tgt = self.CC.n
        self.ops["gpsimd"].append(lambda e, k=kind, o=op_, i=ins, u=outs: e.collective_compute(
            k, o, GROUPS, i, u).then_inc(self.CC.h, 1))
        self.wait("gpsimd", self.CC, tgt)
        self.ops["gpsimd"].append(lambda e: e.drain(fusable=False))
        self.CD.n += 1
        cdt = self.CD.n
        self.ops["gpsimd"].append(lambda e, v=cdt: e.sem_inc(self.CD.h, 1))
        return cdt

    def emit(self, block):
        b = self
        @block.sync
        def _(eng):
            for f in b.ops["sync"]:
                f(eng)
        @block.vector
        def _(eng):
            for f in b.ops["vector"]:
                f(eng)
        @block.scalar
        def _(eng):
            for f in b.ops["scalar"]:
                f(eng)
        @block.tensor
        def _(eng):
            for f in b.ops["tensor"]:
                f(eng)
        @block.gpsimd
        def _(eng):
            for f in b.ops["gpsimd"]:
                f(eng)


class Arena:
    ESZ = {F32: 4, BF16: 2, I32: 4}

    def __init__(self, nc, ctx, name, words):
        self.h = ctx.enter_context(nc.sbuf_tensor(name, [128, words], F32))
        self.hb = self.h.bitcast(BF16)
        self.hi = self.h.bitcast(I32)
        self.cap = words * 4
        self.off = 0
        self.stack = []

    def push(self):
        self.stack.append(self.off)

    def pop(self):
        self.off = self.stack.pop()

    def alloc(self, shape, dt=F32):
        n = 1
        for s in shape[1:]:
            n *= s
        esz = self.ESZ[dt]
        self.off = (self.off + 63) & ~63
        start = self.off
        self.off += n * esz
        assert self.off <= self.cap, f"arena overflow: {self.off} > {self.cap}"
        if dt == F32:
            base = self.h[:, start // 4: start // 4 + n]
        elif dt == I32:
            base = self.hi[:, start // 4: start // 4 + n]
        else:
            base = self.hb[:, start // 2: start // 2 + n]
        if shape[0] != 128:
            base = base[0:shape[0], :]
        if len(shape) == 3:
            base = base.rearrange("p (a b) -> p a b", a=shape[1])
        return base


def build_program(debug=False, dbg_parts=("p1","scan","wop","rs","p3","dty")):
    nc = bass.Bass("TRN2", target_bir_lowering=False, detect_race_conditions=False)

    def din(name, shape, dt=F32):
        return nc.dram_tensor(name, shape, dt, kind="ExternalInput").ap()

    x_fm = din("x_fm", [D, NJ * TC])
    winc = din("winc", [D, 1024])
    woc = din("woc", [512, D])
    wxpc = din("wxpc", [512, 33])
    wgT = din("wgT", [D, H], I32)
    wuT = din("wuT", [D, H], I32)
    wdT = din("wdT", [H, D], I32)
    cwc = din("cwc", [128, 16])
    avals = din("avals", [128, 64])
    vecs = din("vecs", [128, 16])
    lngb = din("lngb", [128, 16])
    sel_i = din("sel", [16, 2048])
    ones128 = din("ones128", [128, 128])
    ones1 = din("ones1", [1, 128])

    o_y = nc.dram_tensor("o_y", [D, NJ * TC], F32, kind="ExternalOutput").ap()

    def dint(name, shape, dt=F32):
        return nc.dram_tensor(name, shape, dt, kind="Internal")

    d_wg = dint("d_wg", [D, H], BF16)
    d_wu = dint("d_wu", [D, H], BF16)
    d_wd = dint("d_wd", [H, D], BF16)
    d_winb = dint("d_winb", [D, 1024], BF16)
    d_wob = dint("d_wob", [512, D], BF16)
    d_x1 = dint("d_x1", [D, NJ * TC], F32)
    d_ln2 = dint("d_ln2", [D, NJ * TC], F32)
    d_xn = dint("d_xn", [NJ * D, TC], BF16)
    d_xng = dint("d_xng", [NJ * 4 * D, TC], BF16)
    d_xc = dint("d_xc", [512, L], F32)
    d_z = dint("d_z", [512, L], F32)
    d_xpj = dint("d_xpj", [33, L], F32)
    d_xpjr = dint("d_xpjr", [33, L], F32)
    d_wop_j = [dint(f"d_wop{j}", [4 * D, TC], F32) for j in range(NJ)]
    d_att_j = [dint(f"d_att{j}", [D, TC], F32) for j in range(NJ)]

    dbg = {}
    if debug:
        for name, shape, dt in [
            ("dbg_xn", [D, NJ * TC], BF16), ("dbg_x1", [D, NJ * TC], F32),
            ("dbg_dt", [128, NMI * TC], F32),
            ("dbg_y", [128, NMI * TC], F32), ("dbg_att", [NJ * D, TC], F32),
            ("dbg_u", [128, NMI * TC], F32), ("dbg_a0", [128, NMI * TC], F32),
            ("dbg_b0", [128, NMI * TC], F32), ("dbg_h0", [128, NMI * TC], F32),
            ("dbg_y0", [128, NMI * TC], F32), ("dbg_wop", [4 * D, TC], F32),
            ("dbg_wop2", [4 * D, TC], F32),
        ]:
            dbg[name] = nc.dram_tensor(name, shape, dt, kind="ExternalOutput").ap()

    x_fm_r = x_fm.rearrange("(k p) t -> p k t", p=128)
    d_x1_r = d_x1.ap().rearrange("(k p) t -> p k t", p=128)
    d_ln2_r = d_ln2.ap().rearrange("(k p) t -> p k t", p=128)
    d_xc_r = d_xc.ap().rearrange("(m p) t -> p m t", p=128)
    d_z_r = d_z.ap().rearrange("(m p) t -> p m t", p=128)
    o_y_r = o_y.rearrange("(k p) t -> p k t", p=128)

    with ExitStack() as ctx:
        E = ctx.enter_context
        P = Bld(nc, ctx)
        LD, CC, VC, SC, TM, PC, CD, WQ, WO, XN = P.LD, P.CC, P.VC, P.SC, P.TM, P.PC, P.CD, P.WQ, P.WO, P.XN

        def sb(name, shape, dt=F32):
            return E(nc.sbuf_tensor(name, shape, dt))

        mu_t = sb("mu_t", [128, TC])
        ms_t = sb("ms_t", [128, TC])
        var_t = sb("var_t", [128, TC])
        rstd_t = sb("rstd_t", [128, TC])
        f1buf = sb("f1buf", [128, TC])
        cwt = sb("cwt", [128, 16])
        Aneg = sb("Aneg", [128, 64])
        vect = sb("vect", [128, 16])
        lngbt = sb("lngbt", [128, 16])
        onesb = sb("onesb", [128, 128], BF16)
        ones1t = sb("ones1t", [1, 128])
        eps_t = sb("eps_t", [128, 1])
        one_t = sb("one_t", [128, 1])
        selb = sb("selb", [16, 2048], BF16)
        st = sb("st", [128, NMI, NS])

        PS = [E(nc.psum_tensor(f"ps{i}", [128, TC], F32)) for i in range(8)]
        AR = Arena(nc, ctx, "arena", 36 * 1024)

        if True:
            AR.push()
            deqi = [AR.alloc([128, H], I32) for _ in range(2)]
            deqb = [AR.alloc([128, H], BF16) for _ in range(2)]
            wfl = AR.alloc([128, 1024], F32)
            p0sel = AR.alloc([16, 2048], F32)

            P.op("vector", lambda e: e.memset(eps_t[:], float(EPS)))
            vcn = P.op("vector", lambda e: e.memset(one_t[:], 1.0), VC)
            P.wait("scalar", VC, vcn)
            P.dma("sync", lambda e: e.dma_start(out=cwt[:], in_=cwc))
            P.dma("sync", lambda e: e.dma_start(out=vect[:], in_=vecs))
            P.dma("sync", lambda e: e.dma_start(out=lngbt[:], in_=lngb))
            P.dma("sync", lambda e: e.dma_start(out=ones1t[:], in_=ones1))
            t5 = P.dma("sync", lambda e: e.dma_start(out=mu_t[:, 0:64], in_=avals))
            P.wait("scalar", LD, t5)
            s1 = P.op("scalar", lambda e: e.activation(ms_t[:, 0:64], mu_t[:, 0:64], AF.Exp), SC)
            P.wait("vector", SC, s1)
            P.op("vector", lambda e: e.tensor_scalar(Aneg[:], ms_t[:, 0:64], -1.0, None, ALU.mult), VC)
            t6 = P.dma("sync", lambda e: e.dma_start(out=rstd_t[:, 0:128], in_=ones128))
            P.wait("scalar", LD, t6)
            P.op("scalar", lambda e: e.activation(onesb[:], rstd_t[:, 0:128], AF.Copy), SC)
            t7 = P.dma("sync", lambda e: e.dma_start(out=p0sel[:], in_=sel_i))
            P.wait("scalar", LD, t7)
            P.op("scalar", lambda e: e.activation(selb[:], p0sel[:], AF.Copy), SC)

            def dequant(src_ap, dst_t, nblocks, width):
                vvs = [None] * nblocks
                dmas = [None] * nblocks
                for i in range(nblocks):
                    bb = i % 2
                    if i >= 2:
                        P.wait("sync", VC, vvs[i - 2])
                    tl = P.dma("sync", lambda e, i=i, bb=bb, s=src_ap, w=width: e.dma_start(
                        out=deqi[bb][:, 0:w], in_=s[128 * i:128 * (i + 1), 0:w]))
                    P.wait("vector", LD, tl)
                    if i >= 2:
                        P.wait("vector", LD, dmas[i - 2])
                    vvs[i] = P.op("vector", lambda e, bb=bb, w=width: e.tensor_scalar(
                        deqb[bb][:, 0:w], deqi[bb][:, 0:w], 1.0 / 127.0, None, ALU.mult), VC)
                    P.wait("sync", VC, vvs[i])
                    dmas[i] = P.dma("sync", lambda e, i=i, bb=bb, d=dst_t, w=width: e.dma_start(
                        out=d.ap()[128 * i:128 * (i + 1), 0:w], in_=deqb[bb][:, 0:w]))

            dequant(wgT, d_wg, KD, H)
            dequant(wuT, d_wu, KD, H)
            dequant(wdT, d_wd, HM, D)

            def stage(src, dst, nblocks):
                prev = {"sv": None, "dma": None}
                for i in range(nblocks):
                    if prev["sv"] is not None:
                        P.wait("sync", SC, prev["sv"])
                    tl = P.dma("sync", lambda e, i=i, s=src: e.dma_start(
                        out=wfl[:], in_=s[128 * i:128 * (i + 1), :]))
                    P.wait("scalar", LD, tl)
                    if prev["dma"] is not None:
                        P.wait("scalar", LD, prev["dma"])
                    sv = P.op("scalar", lambda e: e.activation(deqb[0][:, 0:1024], wfl[:], AF.Copy), SC)
                    prev["sv"] = sv
                    P.wait("sync", SC, sv)
                    prev["dma"] = P.dma("sync", lambda e, i=i, d=dst: e.dma_start(
                        out=d.ap()[128 * i:128 * (i + 1), :], in_=deqb[0][:, 0:1024]))
            stage(winc, d_winb, KD)
            stage(woc, d_wob, NMI)
            P.barrier()
            AR.pop()

        def ln_stats(T, src, wait_vc=None):
            if wait_vc is not None:
                P.wait("scalar", VC, wait_vc)
            P.op("scalar", lambda e: e.activation(T["xbb"][:], src[:], AF.Copy))
            c2 = P.op("scalar", lambda e: e.activation(T["xnb"][:], src[:], AF.Square), SC)
            P.wait("tensor", SC, c2)
            for kb in range(KD):
                P.op("tensor", lambda e, kb=kb: e.matmul(
                    PS[4][:], onesb[:], T["xbb"][:, kb, :], start=(kb == 0), stop=(kb == KD - 1)),
                    TM if kb == KD - 1 else None)
            tm2 = None
            for kb in range(KD):
                tm2 = P.op("tensor", lambda e, kb=kb: e.matmul(
                    PS[5][:], onesb[:], T["xnb"][:, kb, :], start=(kb == 0), stop=(kb == KD - 1)),
                    TM if kb == KD - 1 else None)
            P.wait("vector", TM, tm2)
            P.op("vector", lambda e: e.tensor_scalar(mu_t[:], PS[4][:], 1.0 / D, None, ALU.mult))
            P.op("vector", lambda e: e.tensor_scalar(ms_t[:], PS[5][:], 1.0 / D, None, ALU.mult))
            P.op("vector", lambda e: e.tensor_tensor(var_t[:], mu_t[:], mu_t[:], ALU.mult))
            v1 = P.op("vector", lambda e: e.tensor_tensor(var_t[:], ms_t[:], var_t[:], ALU.subtract), VC)
            P.wait("scalar", VC, v1)
            P.op("scalar", lambda e: e.activation(rstd_t[:], var_t[:], AF.Ln, bias=eps_t[:, 0:1]))
            s2 = P.op("scalar", lambda e: e.activation(rstd_t[:], rstd_t[:], AF.Exp, scale=-0.5), SC)
            return s2, c2

        def ffn(T, par, prev):
            xs, xnb, fbuf = T["xs"][par], T["xnb"], T["fbuf"]
            wgt_s, wut_s, wdt_s = T["wgt_s"], T["wut_s"], T["wdt_s"]
            if prev.get("tds7") is not None:
                P.wait("sync", TM, prev["tds7"])
            tus = [None] * HM
            fvs = [None] * HM
            f1s = [None] * HM
            for mh in range(HM):
                bw = mh % 2
                pg = PS[0 + 2 * bw]
                pu = PS[1 + 2 * bw]
                if mh >= 2:
                    P.wait("sync", TM, tus[mh - 2])
                P.dma("sync", lambda e, mh=mh, bw=bw: e.dma_start(
                    out=wgt_s[bw][:],
                    in_=d_wg.ap()[:, 128 * mh:128 * (mh + 1)].rearrange("(k p) m -> p k m", p=128)), ctr=WQ)
                lw2 = P.dma("sync", lambda e, mh=mh, bw=bw: e.dma_start(
                    out=wut_s[bw][:],
                    in_=d_wu.ap()[:, 128 * mh:128 * (mh + 1)].rearrange("(k p) m -> p k m", p=128)), ctr=WQ)
                P.wait("tensor", WQ, lw2)
                if mh >= 2:
                    P.wait("tensor", SC, f1s[mh - 2])
                    P.wait("tensor", VC, fvs[mh - 2])
                elif prev.get("f1s7") is not None:
                    P.wait("tensor", SC, prev["f1s7"])
                    P.wait("tensor", VC, prev["fvs7"])
                tg = None
                for kb in range(KD):
                    tg = P.op("tensor", lambda e, kb=kb, bw=bw, pg=pg: e.matmul(
                        pg[:], wgt_s[bw][:, kb, :], xnb[:, kb, :],
                        start=(kb == 0), stop=(kb == KD - 1)), TM if kb == KD - 1 else None)
                for kb in range(KD):
                    tus[mh] = P.op("tensor", lambda e, kb=kb, bw=bw, pu=pu: e.matmul(
                        pu[:], wut_s[bw][:, kb, :], xnb[:, kb, :],
                        start=(kb == 0), stop=(kb == KD - 1)), TM if kb == KD - 1 else None)
                P.wait("scalar", TM, tg)
                if mh >= 1:
                    P.wait("scalar", VC, fvs[mh - 1])
                elif prev.get("fvs7") is not None:
                    P.wait("scalar", VC, prev["fvs7"])
                f1s[mh] = P.op("scalar", lambda e, pg=pg: e.activation(f1buf[:], pg[:], AF.Silu), SC)
                P.wait("vector", SC, f1s[mh])
                P.wait("vector", TM, tus[mh])
                if mh == 0 and prev.get("tds7") is not None:
                    P.wait("vector", TM, prev["tds7"])
                fvs[mh] = P.op("vector", lambda e, mh=mh, pu=pu: e.tensor_tensor(
                    fbuf[:, mh, :], f1buf[:], pu[:], ALU.mult), VC)
            xv = None
            tds = [None] * KD
            for mo in range(KD):
                bw = mo % 2
                pd = PS[6 + bw]
                if mo >= 2:
                    P.wait("sync", TM, tds[mo - 2])
                lw = P.dma("sync", lambda e, mo=mo, bw=bw: e.dma_start(
                    out=wdt_s[bw][:],
                    in_=d_wd.ap()[:, 128 * mo:128 * (mo + 1)].rearrange("(k p) m -> p k m", p=128)), ctr=WQ)
                P.wait("tensor", WQ, lw)
                if mo == 0:
                    P.wait("tensor", VC, fvs[-1])
                if mo >= 2:
                    P.wait("tensor", VC, xv)
                elif prev.get("xv7") is not None:
                    P.wait("tensor", VC, prev["xv7"])
                for kh in range(HM):
                    tds[mo] = P.op("tensor", lambda e, kh=kh, bw=bw, pd=pd: e.matmul(
                        pd[:], wdt_s[bw][:, kh, :], fbuf[:, kh, :],
                        start=(kh == 0), stop=(kh == HM - 1)), TM if kh == HM - 1 else None)
                P.wait("vector", TM, tds[mo])
                xv = P.op("vector", lambda e, mo=mo, pd=pd, xs=xs: e.scalar_tensor_tensor(
                    xs[:, mo, :], pd[:], 0.5, xs[:, mo, :], ALU.mult, ALU.add), VC)
            return dict(xv=xv, tds7=tds[-1], f1s7=f1s[-1], fvs7=fvs[-1], xv7=xv)

        def ln_apply_plain(T, src, dst_bf):
            P.op("vector", lambda e, src=src: e.tensor_tensor(
                T["cen"][:], src[:], bc_mid(mu_t[:], KD), ALU.subtract))
            return P.op("vector", lambda e, dst_bf=dst_bf: e.tensor_tensor(
                dst_bf[:], T["cen"][:], bc_mid(rstd_t[:], KD), ALU.mult), VC)

        def alloc_ffn_scope():
            T = {}
            T["xs"] = [AR.alloc([128, KD, TC], F32) for _ in range(2)]
            T["ln2t"] = AR.alloc([128, KD, TC], F32)
            T["cen"] = AR.alloc([128, KD, TC], F32)
            T["xbb"] = AR.alloc([128, KD, TC], BF16)
            T["xnb"] = AR.alloc([128, KD, TC], BF16)
            T["fbuf"] = AR.alloc([128, HM, TC], BF16)
            T["wgt_s"] = [AR.alloc([128, KD, 128], BF16) for _ in range(2)]
            T["wut_s"] = [AR.alloc([128, KD, 128], BF16) for _ in range(2)]
            T["wdt_s"] = [AR.alloc([128, HM, 128], BF16) for _ in range(2)]
            return T

        if True:
            AR.push()
            T = alloc_ffn_scope()
            prev = {}
            x1sp = [None, None]
            xs_vc = [None, None]
            xs_sc = [None, None]
            for j in range(NJ):
                par = j % 2
                jt = slice(TC * j, TC * (j + 1))
                xs = T["xs"][par]
                if j >= 2:
                    P.wait("sync", VC, xs_vc[par])
                    P.wait("sync", SC, xs_sc[par])
                    P.wait("sync", LD, x1sp[par])
                lx = P.dma("sync", lambda e, jt=jt, xs=xs: e.dma_start(out=xs[:], in_=x_fm_r[:, :, jt]))
                P.wait("scalar", LD, lx)
                P.wait("vector", LD, lx)
                s_r, _ = ln_stats(T, xs)
                P.wait("vector", SC, s_r)
                vx = ln_apply_plain(T, xs, T["xnb"])
                P.wait("tensor", VC, vx)
                prev = ffn(T, par, prev)
                xv = prev["xv"]
                P.wait("sync", VC, xv)
                x1sp[par] = P.dma("sync", lambda e, jt=jt, xs=xs: e.dma_start(
                    out=d_x1_r[:, :, jt], in_=xs[:]))
                s_r, c2m = ln_stats(T, xs, wait_vc=xv)
                xs_sc[par] = c2m
                P.wait("vector", SC, s_r)
                xs_vc[par] = P.op("vector", lambda e, xs=xs: e.tensor_tensor(
                    T["ln2t"][:], xs[:], bc_mid(mu_t[:], KD), ALU.subtract), VC)
                v2 = P.op("vector", lambda e: e.tensor_tensor(
                    T["ln2t"][:], T["ln2t"][:], bc_mid(rstd_t[:], KD), ALU.mult), VC)
                P.wait("sync", VC, v2)
                P.dma("sync", lambda e, jt=jt: e.dma_start(out=d_ln2_r[:, :, jt], in_=T["ln2t"][:]))
                s_r, _ = ln_stats(T, T["ln2t"], wait_vc=v2)
                P.wait("vector", SC, s_r)
                P.op("vector", lambda e: e.tensor_tensor(
                    T["cen"][:], T["ln2t"][:], bc_mid(mu_t[:], KD), ALU.subtract))
                P.op("vector", lambda e: e.tensor_tensor(
                    T["cen"][:], T["cen"][:], bc_mid(rstd_t[:], KD), ALU.mult))
                v3 = None
                for kb in range(KD):
                    v3 = P.op("vector", lambda e, kb=kb: e.tensor_scalar(
                        T["xnb"][:, kb, :], T["cen"][:, kb, :],
                        lngbt[:, 2 * kb:2 * kb + 1], lngbt[:, 2 * kb + 1:2 * kb + 2],
                        ALU.mult, ALU.add), VC)
                P.wait("sync", VC, v3)
                P.dma("sync", lambda e, j=j: e.dma_start(
                    out=d_xn.ap()[D * j:D * (j + 1), :].rearrange("(k p) t -> p k t", p=128),
                    in_=T["xnb"][:]), ctr=XN)
                if debug and "p1" in dbg_parts:
                    P.dma("sync", lambda e, jt=jt: e.dma_start(
                        out=dbg["dbg_xn"].rearrange("(k p) t -> p k t", p=128)[:, :, jt], in_=T["xnb"][:]))
                    P.dma("sync", lambda e, jt=jt, xs=xs: e.dma_start(
                        out=dbg["dbg_x1"].rearrange("(k p) t -> p k t", p=128)[:, :, jt], in_=xs[:]))
            P.barrier()
            AR.pop()
        ag_done = [None] * NJ
        for j in range(NJ):
            ag_done[j] = P.coll(
                "AllGather", ALU.bypass,
                [d_xn.ap()[D * j:D * (j + 1), :].opt()],
                [d_xng.ap()[4 * D * j:4 * D * (j + 1), :].opt()])

        if True:
            AR.push()
            Winb = AR.alloc([128, KD, 1024], BF16)
            xng = [AR.alloc([128, KD, TC], BF16) for _ in range(2)]
            xhh = AR.alloc([128, NMI, TC + 8], F32)
            zst = AR.alloc([128, NMI, TC], F32)
            xct2 = AR.alloc([128, NMI, TC], F32)
            xcb2 = AR.alloc([128, NMI, TC], BF16)
            cvac = AR.alloc([128, NMI, TC], F32)
            Wxpb = AR.alloc([128, NMI, 33], BF16)
            xpjt = AR.alloc([33, TC], F32)
            lw = P.dma("sync", lambda e: e.dma_start(
                out=Winb[:], in_=d_winb.ap().rearrange("(k p) m -> p k m", p=128)))
            for mi in range(NMI):
                tw = P.dma("sync", lambda e, mi=mi: e.dma_start(
                    out=var_t[:, 0:33], in_=wxpc[128 * mi:128 * (mi + 1), :]))
                P.wait("scalar", LD, tw)
                sw = P.op("scalar", lambda e, mi=mi: e.activation(Wxpb[:, mi, :], var_t[:, 0:33], AF.Copy), SC)
                P.wait("sync", SC, sw)
            P.wait("tensor", LD, lw)
            P.op("gpsimd", lambda e: e.memset(xhh[:, :, 0:3], 0.0), PC)
            P.wait("scalar", PC, PC.n)
            conv_done = None
            sil_done = None
            cast_done = None
            tpj_prev = None
            spj_prev = None
            zs_spill = None
            xc_spill = None
            pj_spill = None
            bank_ev = {}
            mm_last = [None, None]
            for k in range(NCH):
                bk = k % 2
                r0 = 4 * D * (k // 4) + D * (k % 4)
                c0 = TC * k
                if mm_last[bk] is not None:
                    P.wait("sync", TM, mm_last[bk])
                P.wait("sync", CD, ag_done[k // 4])
                lx = P.dma("sync", lambda e, r0=r0, bk=bk: e.dma_start(
                    out=xng[bk][:],
                    in_=d_xng.ap()[r0:r0 + D, :].rearrange("(k p) t -> p k t", p=128)))
                if k > 0:
                    P.wait("scalar", VC, conv_done)
                    P.op("scalar", lambda e: e.activation(xhh[:, :, 0:3], xhh[:, :, TC:TC + 3], AF.Copy))
                P.wait("tensor", LD, lx)
                ev = []
                for mi in range(KD):
                    psn = mi % 4
                    if bank_ev.get(psn) is not None:
                        P.wait("tensor", SC, bank_ev[psn])
                    tmm = None
                    for kb in range(KD):
                        tmm = P.op("tensor", lambda e, kb=kb, mi=mi, bk=bk, psn=psn: e.matmul(
                            PS[psn][:], Winb[:, kb, 128 * mi:128 * (mi + 1)], xng[bk][:, kb, :],
                            start=(kb == 0), stop=(kb == KD - 1)), TM if kb == KD - 1 else None)
                    if mi == KD - 1:
                        mm_last[bk] = tmm
                    P.wait("scalar", TM, tmm)
                    if mi < NMI:
                        evv = P.op("scalar", lambda e, mi=mi, psn=psn: e.activation(
                            xhh[:, mi, 3:3 + TC], PS[psn][:], AF.Copy), SC)
                    else:
                        if zs_spill is not None and mi == NMI:
                            P.wait("scalar", LD, zs_spill)
                        evv = P.op("scalar", lambda e, mi=mi, psn=psn: e.activation(
                            zst[:, mi - NMI, :], PS[psn][:], AF.Silu), SC)
                    ev.append(evv)
                    bank_ev[psn] = evv
                P.wait("sync", SC, ev[-1])
                zs_spill = P.dma("sync", lambda e, c0=c0: e.dma_start(
                    out=d_z_r[:, :, c0:c0 + TC], in_=zst[:]))
                P.wait("vector", SC, ev[NMI - 1])
                if sil_done is not None:
                    P.wait("vector", SC, sil_done)
                cv = None
                for mi in range(NMI):
                    P.op("vector", lambda e, mi=mi: e.tensor_scalar(
                        cvac[:, mi, :], xhh[:, mi, 0:TC],
                        cwt[:, 4 * mi + 0:4 * mi + 1], vect[:, 4 * mi + 0:4 * mi + 1],
                        ALU.mult, ALU.add))
                    for kk in range(1, 4):
                        cv = P.op("vector", lambda e, mi=mi, kk=kk: e.scalar_tensor_tensor(
                            cvac[:, mi, :], xhh[:, mi, kk:kk + TC],
                            cwt[:, 4 * mi + kk:4 * mi + kk + 1], cvac[:, mi, :],
                            ALU.mult, ALU.add), VC if (mi == NMI - 1 and kk == 3) else None)
                conv_done = cv
                P.wait("scalar", VC, cv)
                if xc_spill is not None:
                    P.wait("scalar", LD, xc_spill)
                sil_done = P.op("scalar", lambda e: e.activation(
                    xct2[:].rearrange("p m t -> p (m t)"), cvac[:].rearrange("p m t -> p (m t)"),
                    AF.Silu), SC)
                if tpj_prev is not None:
                    P.wait("scalar", TM, tpj_prev)
                cast_done = P.op("scalar", lambda e: e.activation(
                    xcb2[:].rearrange("p m t -> p (m t)"), xct2[:].rearrange("p m t -> p (m t)"),
                    AF.Copy), SC)
                P.wait("sync", SC, sil_done)
                xc_spill = P.dma("sync", lambda e, c0=c0: e.dma_start(
                    out=d_xc_r[:, :, c0:c0 + TC], in_=xct2[:]))
                P.wait("tensor", SC, cast_done)
                if spj_prev is not None:
                    P.wait("tensor", SC, spj_prev)
                tpj = None
                for mi in range(NMI):
                    tpj = P.op("tensor", lambda e, mi=mi: e.matmul(
                        PS[4][0:33, :], Wxpb[:, mi, :], xcb2[:, mi, :],
                        start=(mi == 0), stop=(mi == NMI - 1)), TM if mi == NMI - 1 else None)
                tpj_prev = tpj
                P.wait("scalar", TM, tpj)
                if pj_spill is not None:
                    P.wait("scalar", LD, pj_spill)
                spj = P.op("scalar", lambda e: e.activation(xpjt[:], PS[4][0:33, :], AF.Copy), SC)
                spj_prev = spj
                P.wait("sync", SC, spj)
                pj_spill = P.dma("sync", lambda e, c0=c0: e.dma_start(
                    out=d_xpj.ap()[:, c0:c0 + TC], in_=xpjt[:]))
            P.barrier()
            AR.pop()

        art = P.coll("AllReduce", ALU.add, [d_xpj.ap().opt()], [d_xpjr.ap().opt()],
                     wait_ld=P.LD.n)
        P.barrier()

        rs_done = [None] * NJ
        if True:
            AR.push()
            Wob = AR.alloc([128, NMI, 1024], BF16)
            xct3 = [AR.alloc([128, NMI, TC], F32) for _ in range(2)]
            zsl = [AR.alloc([128, NMI, TC], F32) for _ in range(2)]
            dtt = AR.alloc([128, NMI, TC], F32)
            ut = AR.alloc([128, NMI, TC], BF16)
            Bxt = AR.alloc([16, TC], F32)
            Cxt = AR.alloc([16, TC], F32)
            dtr = AR.alloc([1, TC], F32)
            Bxb = AR.alloc([16, TC], BF16)
            Cxb = AR.alloc([16, TC], BF16)
            Cb = [AR.alloc([128, TC], BF16) for _ in range(2)]
            Bbb = [AR.alloc([128, TC], BF16) for _ in range(2)]
            ab = [AR.alloc([128, NMI * TC], F32) for _ in range(2)]
            bb3 = AR.alloc([128, NMI * TC], BF16)
            hta = AR.alloc([128, NMI, TC], F32)
            tmpb = [AR.alloc([128, NMI * TC], BF16) for _ in range(2)]
            yt = [AR.alloc([128, NMI, TC], BF16) for _ in range(2)]
            gt = [AR.alloc([128, NMI, TC], BF16) for _ in range(2)]
            eb2b = AR.alloc([128, NMI * TC], F32)
            rb2b = AR.alloc([128, NMI * TC], F32)
            wot = [AR.alloc([128, TC], F32) for _ in range(2)]

            lw = P.dma("sync", lambda e: e.dma_start(
                out=Wob[:], in_=d_wob.ap().rearrange("(k p) m -> p k m", p=128)))
            P.op("vector", lambda e: e.memset(st[:], 0.0), VC)
            P.wait("tensor", LD, lw)

            roll = {"a3": {}, "dve": {}, "psB_mm": {}, "psC_ev": {}, "tmp": {}, "tmp_rd": {}}
            vg_k = [None, None]
            wo_last_k = [None, None]
            wo_bank_ev = {0: None, 1: None}
            dt_vsp = None
            u_done = None
            u_done_k = [None, None]
            b3_last = None
            bxb_done = None
            dtb_tm = [None, None]
            prev_tdt = None
            gl_n = 0
            for k in range(NCH):
                c0 = TC * k
                kt = slice(c0, c0 + TC)
                pk = k % 2
                if vg_k[pk] is not None:
                    P.wait("sync", VC, vg_k[pk])
                    P.wait("sync", PC, u_done_k[pk])
                l1 = P.dma("sync", lambda e, kt=kt, pk=pk: e.dma_start(out=xct3[pk][:], in_=d_xc_r[:, :, kt]))
                l2 = P.dma("sync", lambda e, kt=kt, pk=pk: e.dma_start(out=zsl[pk][:], in_=d_z_r[:, :, kt]))
                if bxb_done is not None:
                    P.wait("sync", SC, bxb_done)
                if prev_tdt is not None:
                    P.wait("sync", TM, prev_tdt)
                lB = P.dma("sync", lambda e, kt=kt: e.dma_start(out=Bxt[:], in_=d_xpjr.ap()[0:16, kt]))
                lC = P.dma("sync", lambda e, kt=kt: e.dma_start(out=Cxt[:], in_=d_xpjr.ap()[16:32, kt]))
                lD = P.dma("sync", lambda e, kt=kt: e.dma_start(out=dtr[:], in_=d_xpjr.ap()[32:33, kt]))
                P.wait("tensor", LD, lD)
                if dtb_tm[pk] is not None:
                    P.wait("tensor", VC, dt_vsp)
                tdt = P.op("tensor", lambda e, pk=pk: e.matmul(
                    PS[6 + pk][:], ones1t[:], dtr[:], start=True, stop=True), TM)
                dtb_tm[pk] = tdt
                prev_tdt = tdt
                P.wait("vector", TM, tdt)
                for mi in range(NMI):
                    P.op("vector", lambda e, mi=mi, pk=pk: e.tensor_scalar(
                        dtt[:, mi, :], PS[6 + pk][:],
                        vect[:, 4 * mi + 2:4 * mi + 3], vect[:, 4 * mi + 1:4 * mi + 2],
                        ALU.mult, ALU.add))
                vrl = P.op("vector", lambda e: e.tensor_scalar(
                    rb2b[:], dtt[:].rearrange("p m t -> p (m t)"), 0.0, None, ALU.max), VC)
                P.wait("scalar", VC, vrl)
                P.op("scalar", lambda e: e.activation(eb2b[:], dtt[:].rearrange("p m t -> p (m t)"), AF.Abs))
                P.op("scalar", lambda e: e.activation(eb2b[:], eb2b[:], AF.Exp, scale=-1.0))
                ssp = P.op("scalar", lambda e: e.activation(eb2b[:], eb2b[:], AF.Ln, bias=one_t[:, 0:1]), SC)
                P.wait("vector", SC, ssp)
                dt_vsp = P.op("vector", lambda e: e.tensor_tensor(
                    dtt[:].rearrange("p m t -> p (m t)"), rb2b[:], eb2b[:], ALU.add), VC)
                P.wait("gpsimd", VC, dt_vsp)
                P.wait("gpsimd", LD, l1)
                if b3_last is not None:
                    P.wait("gpsimd", VC, b3_last)
                u_done = P.op("gpsimd", lambda e, pk=pk: e.tensor_tensor(
                    ut[:].rearrange("p m t -> p (m t)"), dtt[:].rearrange("p m t -> p (m t)"),
                    xct3[pk][:].rearrange("p m t -> p (m t)"), ALU.mult), PC)
                u_done_k[pk] = u_done
                P.wait("scalar", LD, lC)
                if gl_n > 0:
                    P.wait("scalar", TM, roll["psB_mm"][gl_n - 1])
                P.op("scalar", lambda e: e.activation(Bxb[:], Bxt[:], AF.Copy))
                bxb_done = P.op("scalar", lambda e: e.activation(Cxb[:], Cxt[:], AF.Copy), SC)
                if debug and "dty" in dbg_parts and k == 0:
                    P.wait("sync", VC, dt_vsp)
                    P.dma("sync", lambda e: e.dma_start(out=dbg["dbg_dt"], in_=dtt[:].rearrange("p m t -> p (m t)")))
                first_v = True
                for n in range(NS):
                    g = gl_n
                    gl_n += 1
                    pp = g % 2
                    psB = PS[0 + 2 * pp]
                    psC = PS[1 + 2 * pp]
                    P.wait("tensor", SC, bxb_done)
                    if roll["psC_ev"].get(g - 2) is not None:
                        P.wait("tensor", SC, roll["psC_ev"][g - 2])
                    P.op("tensor", lambda e, n=n, psB=psB: e.matmul(
                        psB[:], selb[:, 128 * n:128 * (n + 1)], Bxb[:], start=True, stop=True))
                    tbc = P.op("tensor", lambda e, n=n, psC=psC: e.matmul(
                        psC[:], selb[:, 128 * n:128 * (n + 1)], Cxb[:], start=True, stop=True), TM)
                    roll["psB_mm"][g] = tbc
                    P.wait("scalar", TM, tbc)
                    if roll["tmp"].get(g - 2) is not None:
                        P.wait("scalar", PC, roll["tmp"][g - 2])
                    if roll["dve"].get(g - 2) is not None:
                        P.wait("scalar", VC, roll["dve"][g - 2])
                    P.op("scalar", lambda e, pp=pp, psB=psB: e.activation(Bbb[pp][:], psB[:], AF.Copy))
                    cev = P.op("scalar", lambda e, pp=pp, psC=psC: e.activation(Cb[pp][:], psC[:], AF.Copy), SC)
                    roll["psC_ev"][g] = cev
                    if roll["dve"].get(g - 2) is not None:
                        P.wait("scalar", VC, roll["dve"][g - 2])
                    if n == 0:
                        P.wait("scalar", VC, dt_vsp)
                    sa = None
                    for mi in range(NMI):
                        sa = P.op("scalar", lambda e, mi=mi, n=n, pp=pp: e.activation(
                            ab[pp][:, TC * mi:TC * (mi + 1)], dtt[:, mi, :], AF.Exp,
                            scale=Aneg[:, 16 * mi + n:16 * mi + n + 1]),
                            SC if mi == NMI - 1 else None)
                    roll["a3"][g] = sa
                    if n == 0:
                        P.wait("vector", PC, u_done)
                    P.wait("vector", SC, cev)
                    if roll["tmp"].get(g - 1) is not None:
                        P.wait("vector", PC, roll["tmp"][g - 1])
                    b3m = P.op("vector", lambda e, pp=pp: e.tensor_tensor(
                        bb3[:].rearrange("p (m t) -> p m t", m=NMI),
                        ut[:], bc_mid(Bbb[pp][:], NMI), ALU.mult), VC)
                    if n == NS - 1:
                        b3_last = b3m
                    P.wait("vector", SC, roll["a3"][g])
                    for mi in range(NMI):
                        P.op("vector", lambda e, mi=mi, n=n, pp=pp: e.tensor_tensor_scan(
                            hta[:, mi, :],
                            ab[pp][:, TC * mi:TC * (mi + 1)],
                            bb3[:, TC * mi:TC * (mi + 1)],
                            st[:, mi, n:n + 1], ALU.mult, ALU.add))
                    vsc = P.op("vector", lambda e, n=n: e.tensor_copy(
                        st[:, :, n:n + 1], hta[:, :, TC - 1:TC]), VC)
                    roll["dve"][g] = vsc
                    P.wait("gpsimd", VC, vsc)
                    P.wait("gpsimd", SC, cev)
                    if roll["tmp_rd"].get(g - 2) is not None:
                        P.wait("gpsimd", VC, roll["tmp_rd"][g - 2])
                    tm_ = P.op("gpsimd", lambda e, pp=pp: e.tensor_tensor(
                        tmpb[pp][:].rearrange("p (m t) -> p m t", m=NMI),
                        hta[:], bc_mid(Cb[pp][:], NMI), ALU.mult), PC)
                    roll["tmp"][g] = tm_
                    P.wait("vector", PC, tm_)
                    if n == 0:
                        yadd = P.op("vector", lambda e, pp=pp, pk=pk: e.tensor_scalar(
                            yt[pk][:].rearrange("p m t -> p (m t)"),
                            tmpb[pp][:], 0.0, None, ALU.add), VC)
                    else:
                        yadd = P.op("vector", lambda e, pp=pp, pk=pk: e.tensor_tensor(
                            yt[pk][:].rearrange("p m t -> p (m t)"),
                            yt[pk][:].rearrange("p m t -> p (m t)"),
                            tmpb[pp][:], ALU.add), VC)
                    roll["tmp_rd"][g] = yadd
                    if debug and "scan" in dbg_parts and k == 0 and n == 0:
                        P.wait("sync", PC, u_done)
                        du = P.dma("sync", lambda e: e.dma_start(out=dbg["dbg_u"], in_=ut[:].rearrange("p m t -> p (m t)")))
                        P.wait("sync", SC, roll["a3"][g])
                        da = P.dma("sync", lambda e, pp=pp: e.dma_start(out=dbg["dbg_a0"], in_=ab[pp][:]))
                        P.wait("sync", VC, roll["dve"][g])
                        dbm = P.dma("sync", lambda e: e.dma_start(out=dbg["dbg_b0"], in_=bb3[:]))
                        dh = P.dma("sync", lambda e: e.dma_start(out=dbg["dbg_h0"], in_=hta[:].rearrange("p m t -> p (m t)")))
                        P.wait("sync", VC, yadd)
                        dy0 = P.dma("sync", lambda e, pk=pk: e.dma_start(out=dbg["dbg_y0"], in_=yt[pk][:].rearrange("p m t -> p (m t)")))
                        P.wait("sync", LD, dy0)
                        P.barrier()
                P.wait("vector", LD, l1)
                for mi in range(NMI):
                    P.op("vector", lambda e, mi=mi, pk=pk: e.scalar_tensor_tensor(
                        yt[pk][:, mi, :], xct3[pk][:, mi, :],
                        vect[:, 4 * mi + 3:4 * mi + 4], yt[pk][:, mi, :], ALU.mult, ALU.add))
                P.wait("vector", LD, l2)
                if wo_last_k[pk] is not None:
                    P.wait("vector", TM, wo_last_k[pk])
                vg = P.op("vector", lambda e, pk=pk: e.tensor_tensor(
                    gt[pk][:].rearrange("p m t -> p (m t)"), yt[pk][:].rearrange("p m t -> p (m t)"),
                    zsl[pk][:].rearrange("p m t -> p (m t)"), ALU.mult), VC)
                vg_k[pk] = vg
                if debug and "dty" in dbg_parts and k == 0:
                    P.wait("sync", VC, vg)
                    P.dma("sync", lambda e: e.dma_start(out=dbg["dbg_y"], in_=yt[0][:].rearrange("p m t -> p (m t)")))
                slot = k
                P.wait("tensor", VC, vg)
                wo_dma = {}
                for mo in range(KD):
                    pw = mo % 2
                    psW = PS[4 + pw]
                    if wo_bank_ev[pw] is not None:
                        P.wait("tensor", SC, wo_bank_ev[pw])
                    two = None
                    for kd in range(NMI):
                        two = P.op("tensor", lambda e, kd=kd, mo=mo, pk=pk, psW=psW: e.matmul(
                            psW[:], Wob[:, kd, 128 * mo:128 * (mo + 1)], gt[pk][:, kd, :],
                            start=(kd == 0), stop=(kd == NMI - 1)), TM if kd == NMI - 1 else None)
                    if mo == KD - 1:
                        wo_last_k[pk] = two
                    P.wait("scalar", TM, two)
                    if wo_dma.get(mo - 2) is not None:
                        P.wait("scalar", WO, wo_dma[mo - 2])
                    swo = P.op("scalar", lambda e, pw=pw, psW=psW: e.activation(wot[pw][:], psW[:], AF.Copy), SC)
                    wo_bank_ev[pw] = swo
                    P.wait("sync", SC, swo)
                    wo_dma[mo] = P.dma("sync", lambda e, k=k, mo=mo, pw=pw: e.dma_start(
                        out=d_wop_j[k // 4].ap()[1024 * (k % 4) + 128 * mo:1024 * (k % 4) + 128 * (mo + 1), :],
                        in_=wot[pw][:]), ctr=WO)
                if k % 4 == 3:
                    P.wait("gpsimd", WO, 16 * KD * (k + 1))
                    CC.n += 1
                    P.op("gpsimd", lambda e, j=k // 4: e.collective_compute(
                        "ReduceScatter", ALU.add, GROUPS,
                        [d_wop_j[j].ap().opt()], [d_att_j[j].ap().opt()]).then_inc(CC.h, 1))
                if debug and "wop" in dbg_parts and k == 3:
                    P.wait("sync", LD, P.LD.n)
                    P.dma("sync", lambda e: e.dma_start(out=dbg["dbg_wop"], in_=d_wop_j[0].ap()))
            P.wait("gpsimd", CC, CC.n)
            P.op("gpsimd", lambda e: e.drain(fusable=False))
            P.barrier()
            AR.pop()

        if True:
            AR.push()
            T = alloc_ffn_scope()
            prev = {}
            oysp = [None, None]
            xs_vc = [None, None]
            xs_sc = [None, None]
            ln2_rd = None
            cen_rd = None
            for j in range(NJ):
                par = j % 2
                jt = slice(TC * j, TC * (j + 1))
                xs = T["xs"][par]
                if j >= 2:
                    P.wait("sync", VC, xs_vc[par])
                    P.wait("sync", SC, xs_sc[par])
                    P.wait("sync", LD, oysp[par])
                l1 = P.dma("sync", lambda e, jt=jt, xs=xs: e.dma_start(out=xs[:], in_=d_x1_r[:, :, jt]))
                if ln2_rd is not None:
                    P.wait("sync", VC, ln2_rd)
                l2 = P.dma("sync", lambda e, jt=jt: e.dma_start(out=T["ln2t"][:], in_=d_ln2_r[:, :, jt]))
                if cen_rd is not None:
                    P.wait("sync", VC, cen_rd)
                l3 = P.dma("sync", lambda e, j=j: e.dma_start(
                    out=T["cen"][:],
                    in_=d_att_j[j].ap().rearrange("(k p) t -> p k t", p=128)))
                P.wait("vector", LD, l3)
                ln2_rd = P.op("vector", lambda e, xs=xs: e.tensor_tensor(
                    xs[:].rearrange("p k t -> p (k t)"), xs[:].rearrange("p k t -> p (k t)"),
                    T["ln2t"][:].rearrange("p k t -> p (k t)"), ALU.add), VC)
                vx2 = P.op("vector", lambda e, xs=xs: e.tensor_tensor(
                    xs[:].rearrange("p k t -> p (k t)"), xs[:].rearrange("p k t -> p (k t)"),
                    T["cen"][:].rearrange("p k t -> p (k t)"), ALU.add), VC)
                if debug and "p3" in dbg_parts:
                    P.wait("sync", LD, l3)
                    P.dma("sync", lambda e, j=j: e.dma_start(
                        out=dbg["dbg_att"][D * j:D * (j + 1), :].rearrange("(k p) t -> p k t", p=128),
                        in_=T["cen"][:]))
                s_r, c2m = ln_stats(T, xs, wait_vc=vx2)
                xs_sc[par] = c2m
                P.wait("vector", SC, s_r)
                vx = ln_apply_plain(T, xs, T["xnb"])
                cen_rd = vx
                xs_vc[par] = vx
                P.wait("tensor", VC, vx)
                prev = ffn(T, par, prev)
                xv = prev["xv"]
                P.wait("sync", VC, xv)
                oysp[par] = P.dma("sync", lambda e, jt=jt, xs=xs: e.dma_start(
                    out=o_y_r[:, :, jt], in_=xs[:]))

        P.wait("sync", LD, P.LD.n)

        block = E(nc.Block())
        P.emit(block)

    return nc


_DEBUG = False
_NC_CACHE = {}


_DBG_PARTS = ("p3",)


def _get_nc():
    key = (bool(_DEBUG), tuple(_DBG_PARTS))
    if key not in _NC_CACHE:
        _NC_CACHE[key] = build_program(debug=key[0], dbg_parts=key[1])
    return _NC_CACHE[key]


def kernel(L0_attn_x, L0_attn_ln_g, L0_attn_ln_b, L0_attn_W_in, L0_attn_conv_w,
           L0_attn_conv_b, L0_attn_W_xp, L0_attn_W_dt, L0_attn_b_dt, L0_attn_A_log,
           L0_attn_D, L0_attn_W_out, L0_ffn_w_gate, L0_ffn_w_up, L0_ffn_w_down):
    f32 = np.float32
    x = np.asarray(L0_attn_x, f32)
    W_in = np.asarray(L0_attn_W_in, f32)
    W_out = np.asarray(L0_attn_W_out, f32)
    W_xp = np.asarray(L0_attn_W_xp, f32)
    conv_w = np.asarray(L0_attn_conv_w, f32)
    conv_b = np.asarray(L0_attn_conv_b, f32)
    W_dt = np.asarray(L0_attn_W_dt, f32)
    b_dt = np.asarray(L0_attn_b_dt, f32)
    A_log = np.asarray(L0_attn_A_log, f32)
    Dvec = np.asarray(L0_attn_D, f32)
    ln_g = np.asarray(L0_attn_ln_g, f32)
    ln_b = np.asarray(L0_attn_ln_b, f32)
    wg = np.ascontiguousarray(np.asarray(L0_ffn_w_gate, np.int32).T)
    wu = np.ascontiguousarray(np.asarray(L0_ffn_w_up, np.int32).T)
    wd = np.ascontiguousarray(np.asarray(L0_ffn_w_down, np.int32).T)

    sel = np.zeros((16, 2048), f32)
    for n in range(16):
        sel[n, 128 * n:128 * (n + 1)] = 1.0
    ones128 = np.ones((128, 128), f32)
    ones1 = np.ones((1, 128), f32)
    lngb = np.zeros((128, 16), f32)
    for kb in range(KD):
        lngb[:, 2 * kb] = ln_g[128 * kb:128 * (kb + 1)]
        lngb[:, 2 * kb + 1] = ln_b[128 * kb:128 * (kb + 1)]

    in_maps = []
    for c in range(8):
        b, r = divmod(c, 4)
        sl = slice(512 * r, 512 * (r + 1))
        xc = np.concatenate(
            [x[b, TC * (4 * j + r):TC * (4 * j + r + 1), :].T for j in range(NJ)], axis=1)
        winc_c = np.concatenate([W_in[:, sl], W_in[:, 2048 + sl.start:2048 + sl.stop]], axis=1)
        cwc = conv_w[sl].reshape(4, 128, 4).transpose(1, 0, 2).reshape(128, 16)
        avals = A_log[sl].reshape(4, 128, 16).transpose(1, 0, 2).reshape(128, 64)
        vv = np.stack([conv_b[sl], b_dt[sl], W_dt[0, sl], Dvec[sl]], axis=-1)
        vv = vv.reshape(4, 128, 4).transpose(1, 0, 2).reshape(128, 16)
        in_maps.append(dict(
            x_fm=np.ascontiguousarray(xc),
            winc=np.ascontiguousarray(winc_c),
            woc=np.ascontiguousarray(W_out[sl]),
            wxpc=np.ascontiguousarray(W_xp[sl]),
            wgT=wg, wuT=wu, wdT=wd,
            cwc=np.ascontiguousarray(cwc),
            avals=np.ascontiguousarray(avals),
            vecs=np.ascontiguousarray(vv),
            lngb=lngb, sel=sel, ones128=ones128, ones1=ones1,
        ))

    nc = _get_nc()
    res = run_bass_kernel_spmd(nc, in_maps, core_ids=list(range(8)))

    out = np.empty((B, L, D), f32)
    for c in range(8):
        b, r = divmod(c, 4)
        oy = np.asarray(res.results[c]["o_y"], f32)
        for j in range(NJ):
            out[b, TC * (4 * j + r):TC * (4 * j + r + 1), :] = oy[:, TC * j:TC * (j + 1)].T
    if _DEBUG:
        kernel._dbg = res.results
    return out


# version 25
# speedup vs baseline: 1.0201x; 1.0201x over previous
import dataclasses
from contextlib import ExitStack

import numpy as np

import concourse.bass as bass
import concourse.mybir as mybir
from concourse.bass_utils import run_bass_kernel_spmd

F32 = mybir.dt.float32
BF16 = mybir.dt.bfloat16
I32 = mybir.dt.int32
AF = mybir.ActivationFunctionType
ALU = mybir.AluOpType

B, L, D, DIN, H, NS = 2, 8192, 1024, 2048, 3072, 16
TC = 512
NCH = 16
NJ = 4
KD = 8
NMI = 4
HM = H // 128
EPS = 1e-5
GROUPS = [[0, 1, 2, 3], [4, 5, 6, 7]]
ENGINES = ("sync", "vector", "scalar", "tensor", "gpsimd")


def bc_mid(ap, rep):
    assert len(ap.ap) == 2
    return dataclasses.replace(ap, ap=[ap.ap[0], [0, rep], ap.ap[1]])


class Ctr:
    def __init__(self, h):
        self.h = h
        self.n = 0


class Bld:
    def __init__(self, nc, ctx):
        self.nc = nc
        self.ops = {e: [] for e in ENGINES}
        E = ctx.enter_context
        self.LD = Ctr(E(nc.semaphore("LD")))
        self.CC = Ctr(E(nc.semaphore("CC")))
        self.VC = Ctr(E(nc.semaphore("VC")))
        self.SC = Ctr(E(nc.semaphore("SC")))
        self.TM = Ctr(E(nc.semaphore("TM")))
        self.PC = Ctr(E(nc.semaphore("PC")))
        self.CD = Ctr(E(nc.semaphore("CD")))
        self.WQ = Ctr(E(nc.semaphore("WQ")))
        self.WO = Ctr(E(nc.semaphore("WO")))
        self.XN = Ctr(E(nc.semaphore("XN")))
        self.BAR = Ctr(E(nc.semaphore("BAR")))

    def op(self, eng, build, ctr=None):
        if ctr is None:
            self.ops[eng].append(build)
            return None
        ctr.n += 1
        tgt = ctr.n
        self.ops[eng].append(lambda e, b=build, c=ctr: b(e).then_inc(c.h, 1))
        return tgt

    def dma(self, eng, build, ctr=None):
        ctr = ctr or self.LD
        ctr.n += 16
        tgt = ctr.n
        self.ops[eng].append(lambda e, b=build, c=ctr: b(e).then_inc(c.h, 16))
        return tgt

    def wait(self, eng, ctr, val):
        if val is None:
            return
        self.ops[eng].append(lambda e, c=ctr, v=val: e.wait_ge(c.h, v))

    def barrier(self):
        self.wait("sync", self.LD, self.LD.n)
        self.wait("sync", self.WQ, self.WQ.n)
        self.wait("sync", self.WO, self.WO.n)
        self.wait("sync", self.XN, self.XN.n)
        tgt = self.BAR.n + len(ENGINES)
        for e in ENGINES:
            self.ops[e].append(lambda E: E.sem_inc(self.BAR.h, 1))
        self.BAR.n = tgt
        for e in ENGINES:
            self.ops[e].append(lambda E, v=tgt: E.wait_ge(self.BAR.h, v))

    def coll(self, kind, op_, ins, outs, wait_ld=None):
        if wait_ld is not None:
            self.wait("gpsimd", self.LD, wait_ld)
        self.CC.n += 1
        tgt = self.CC.n
        self.ops["gpsimd"].append(lambda e, k=kind, o=op_, i=ins, u=outs: e.collective_compute(
            k, o, GROUPS, i, u).then_inc(self.CC.h, 1))
        self.wait("gpsimd", self.CC, tgt)
        self.ops["gpsimd"].append(lambda e: e.drain(fusable=False))
        self.CD.n += 1
        cdt = self.CD.n
        self.ops["gpsimd"].append(lambda e, v=cdt: e.sem_inc(self.CD.h, 1))
        return cdt

    def emit(self, block):
        b = self
        @block.sync
        def _(eng):
            for f in b.ops["sync"]:
                f(eng)
        @block.vector
        def _(eng):
            for f in b.ops["vector"]:
                f(eng)
        @block.scalar
        def _(eng):
            for f in b.ops["scalar"]:
                f(eng)
        @block.tensor
        def _(eng):
            for f in b.ops["tensor"]:
                f(eng)
        @block.gpsimd
        def _(eng):
            for f in b.ops["gpsimd"]:
                f(eng)


class Arena:
    ESZ = {F32: 4, BF16: 2, I32: 4}

    def __init__(self, nc, ctx, name, words):
        self.h = ctx.enter_context(nc.sbuf_tensor(name, [128, words], F32))
        self.hb = self.h.bitcast(BF16)
        self.hi = self.h.bitcast(I32)
        self.cap = words * 4
        self.off = 0
        self.stack = []

    def push(self):
        self.stack.append(self.off)

    def pop(self):
        self.off = self.stack.pop()

    def alloc(self, shape, dt=F32):
        n = 1
        for s in shape[1:]:
            n *= s
        esz = self.ESZ[dt]
        self.off = (self.off + 63) & ~63
        start = self.off
        self.off += n * esz
        assert self.off <= self.cap, f"arena overflow: {self.off} > {self.cap}"
        if dt == F32:
            base = self.h[:, start // 4: start // 4 + n]
        elif dt == I32:
            base = self.hi[:, start // 4: start // 4 + n]
        else:
            base = self.hb[:, start // 2: start // 2 + n]
        if shape[0] != 128:
            base = base[0:shape[0], :]
        if len(shape) == 3:
            base = base.rearrange("p (a b) -> p a b", a=shape[1])
        return base


def build_program(debug=False, dbg_parts=("p1","scan","wop","rs","p3","dty")):
    nc = bass.Bass("TRN2", target_bir_lowering=False, detect_race_conditions=False)

    def din(name, shape, dt=F32):
        return nc.dram_tensor(name, shape, dt, kind="ExternalInput").ap()

    x_fm = din("x_fm", [D, NJ * TC])
    winc = din("winc", [D, 1024])
    woc = din("woc", [512, D])
    wxpc = din("wxpc", [512, 33])
    wgT = din("wgT", [D, H], I32)
    wuT = din("wuT", [D, H], I32)
    wdT = din("wdT", [H, D], I32)
    cwc = din("cwc", [128, 16])
    avals = din("avals", [128, 64])
    vecs = din("vecs", [128, 16])
    lngb = din("lngb", [128, 16])
    sel_i = din("sel", [16, 2048])
    ones128 = din("ones128", [128, 128])
    ones1 = din("ones1", [1, 128])

    o_y = nc.dram_tensor("o_y", [D, NJ * TC], F32, kind="ExternalOutput").ap()

    def dint(name, shape, dt=F32):
        return nc.dram_tensor(name, shape, dt, kind="Internal")

    d_wg = dint("d_wg", [D, H], BF16)
    d_wu = dint("d_wu", [D, H], BF16)
    d_wd = dint("d_wd", [H, D], BF16)
    d_winb = dint("d_winb", [D, 1024], BF16)
    d_wob = dint("d_wob", [512, D], BF16)
    d_x1 = dint("d_x1", [D, NJ * TC], F32)
    d_ln2 = dint("d_ln2", [D, NJ * TC], F32)
    d_xn = dint("d_xn", [NJ * D, TC], BF16)
    d_xng = dint("d_xng", [NJ * 4 * D, TC], BF16)
    d_xc = dint("d_xc", [512, L], F32)
    d_z = dint("d_z", [512, L], F32)
    d_xpj = dint("d_xpj", [33, L], F32)
    d_xpjr = dint("d_xpjr", [33, L], F32)
    d_wop_j = [dint(f"d_wop{j}", [4 * D, TC], F32) for j in range(NJ)]
    d_att_j = [dint(f"d_att{j}", [D, TC], F32) for j in range(NJ)]

    dbg = {}
    if debug:
        for name, shape, dt in [
            ("dbg_xn", [D, NJ * TC], BF16), ("dbg_x1", [D, NJ * TC], F32),
            ("dbg_dt", [128, NMI * TC], F32),
            ("dbg_y", [128, NMI * TC], F32), ("dbg_att", [NJ * D, TC], F32),
            ("dbg_u", [128, NMI * TC], F32), ("dbg_a0", [128, NMI * TC], F32),
            ("dbg_b0", [128, NMI * TC], F32), ("dbg_h0", [128, NMI * TC], F32),
            ("dbg_y0", [128, NMI * TC], F32), ("dbg_wop", [4 * D, TC], F32),
            ("dbg_wop2", [4 * D, TC], F32),
        ]:
            dbg[name] = nc.dram_tensor(name, shape, dt, kind="ExternalOutput").ap()

    x_fm_r = x_fm.rearrange("(k p) t -> p k t", p=128)
    d_x1_r = d_x1.ap().rearrange("(k p) t -> p k t", p=128)
    d_ln2_r = d_ln2.ap().rearrange("(k p) t -> p k t", p=128)
    d_xc_r = d_xc.ap().rearrange("(m p) t -> p m t", p=128)
    d_z_r = d_z.ap().rearrange("(m p) t -> p m t", p=128)
    o_y_r = o_y.rearrange("(k p) t -> p k t", p=128)

    with ExitStack() as ctx:
        E = ctx.enter_context
        P = Bld(nc, ctx)
        LD, CC, VC, SC, TM, PC, CD, WQ, WO, XN = P.LD, P.CC, P.VC, P.SC, P.TM, P.PC, P.CD, P.WQ, P.WO, P.XN

        def sb(name, shape, dt=F32):
            return E(nc.sbuf_tensor(name, shape, dt))

        mu_t = sb("mu_t", [128, TC])
        ms_t = sb("ms_t", [128, TC])
        var_t = sb("var_t", [128, TC])
        rstd_t = sb("rstd_t", [128, TC])
        f1buf = sb("f1buf", [128, TC])
        cwt = sb("cwt", [128, 16])
        Aneg = sb("Aneg", [128, 64])
        vect = sb("vect", [128, 16])
        lngbt = sb("lngbt", [128, 16])
        onesb = sb("onesb", [128, 128], BF16)
        ones1t = sb("ones1t", [1, 128])
        eps_t = sb("eps_t", [128, 1])
        one_t = sb("one_t", [128, 1])
        selb = sb("selb", [16, 2048], BF16)
        st = sb("st", [128, NMI, NS])

        PS = [E(nc.psum_tensor(f"ps{i}", [128, TC], F32)) for i in range(8)]
        AR = Arena(nc, ctx, "arena", 43 * 1024)

        if True:
            AR.push()
            deqi = [AR.alloc([128, H], I32) for _ in range(2)]
            deqb = [AR.alloc([128, H], BF16) for _ in range(2)]
            wfl = AR.alloc([128, 1024], F32)
            p0sel = AR.alloc([16, 2048], F32)

            P.op("vector", lambda e: e.memset(eps_t[:], float(EPS)))
            vcn = P.op("vector", lambda e: e.memset(one_t[:], 1.0), VC)
            P.wait("scalar", VC, vcn)
            P.dma("sync", lambda e: e.dma_start(out=cwt[:], in_=cwc))
            P.dma("sync", lambda e: e.dma_start(out=vect[:], in_=vecs))
            P.dma("sync", lambda e: e.dma_start(out=lngbt[:], in_=lngb))
            P.dma("sync", lambda e: e.dma_start(out=ones1t[:], in_=ones1))
            t5 = P.dma("sync", lambda e: e.dma_start(out=mu_t[:, 0:64], in_=avals))
            P.wait("scalar", LD, t5)
            s1 = P.op("scalar", lambda e: e.activation(ms_t[:, 0:64], mu_t[:, 0:64], AF.Exp), SC)
            P.wait("vector", SC, s1)
            P.op("vector", lambda e: e.tensor_scalar(Aneg[:], ms_t[:, 0:64], -1.0, None, ALU.mult), VC)
            t6 = P.dma("sync", lambda e: e.dma_start(out=rstd_t[:, 0:128], in_=ones128))
            P.wait("scalar", LD, t6)
            P.op("scalar", lambda e: e.activation(onesb[:], rstd_t[:, 0:128], AF.Copy), SC)
            t7 = P.dma("sync", lambda e: e.dma_start(out=p0sel[:], in_=sel_i))
            P.wait("scalar", LD, t7)
            P.op("scalar", lambda e: e.activation(selb[:], p0sel[:], AF.Copy), SC)

            def dequant(src_ap, dst_t, nblocks, width):
                vvs = [None] * nblocks
                dmas = [None] * nblocks
                for i in range(nblocks):
                    bb = i % 2
                    if i >= 2:
                        P.wait("sync", VC, vvs[i - 2])
                    tl = P.dma("sync", lambda e, i=i, bb=bb, s=src_ap, w=width: e.dma_start(
                        out=deqi[bb][:, 0:w], in_=s[128 * i:128 * (i + 1), 0:w]))
                    P.wait("vector", LD, tl)
                    if i >= 2:
                        P.wait("vector", LD, dmas[i - 2])
                    vvs[i] = P.op("vector", lambda e, bb=bb, w=width: e.tensor_scalar(
                        deqb[bb][:, 0:w], deqi[bb][:, 0:w], 1.0 / 127.0, None, ALU.mult), VC)
                    P.wait("sync", VC, vvs[i])
                    dmas[i] = P.dma("sync", lambda e, i=i, bb=bb, d=dst_t, w=width: e.dma_start(
                        out=d.ap()[128 * i:128 * (i + 1), 0:w], in_=deqb[bb][:, 0:w]))

            dequant(wgT, d_wg, KD, H)
            dequant(wuT, d_wu, KD, H)
            dequant(wdT, d_wd, HM, D)

            def stage(src, dst, nblocks):
                prev = {"sv": None, "dma": None}
                for i in range(nblocks):
                    if prev["sv"] is not None:
                        P.wait("sync", SC, prev["sv"])
                    tl = P.dma("sync", lambda e, i=i, s=src: e.dma_start(
                        out=wfl[:], in_=s[128 * i:128 * (i + 1), :]))
                    P.wait("scalar", LD, tl)
                    if prev["dma"] is not None:
                        P.wait("scalar", LD, prev["dma"])
                    sv = P.op("scalar", lambda e: e.activation(deqb[0][:, 0:1024], wfl[:], AF.Copy), SC)
                    prev["sv"] = sv
                    P.wait("sync", SC, sv)
                    prev["dma"] = P.dma("sync", lambda e, i=i, d=dst: e.dma_start(
                        out=d.ap()[128 * i:128 * (i + 1), :], in_=deqb[0][:, 0:1024]))
            stage(winc, d_winb, KD)
            stage(woc, d_wob, NMI)
            P.barrier()
            AR.pop()

        def ln_stats(T, src, wait_vc=None):
            if wait_vc is not None:
                P.wait("scalar", VC, wait_vc)
            P.op("scalar", lambda e: e.activation(T["xbb"][:], src[:], AF.Copy))
            c2 = P.op("scalar", lambda e: e.activation(T["xnb"][:], src[:], AF.Square), SC)
            P.wait("tensor", SC, c2)
            for kb in range(KD):
                P.op("tensor", lambda e, kb=kb: e.matmul(
                    PS[4][:], onesb[:], T["xbb"][:, kb, :], start=(kb == 0), stop=(kb == KD - 1)),
                    TM if kb == KD - 1 else None)
            tm2 = None
            for kb in range(KD):
                tm2 = P.op("tensor", lambda e, kb=kb: e.matmul(
                    PS[5][:], onesb[:], T["xnb"][:, kb, :], start=(kb == 0), stop=(kb == KD - 1)),
                    TM if kb == KD - 1 else None)
            P.wait("vector", TM, tm2)
            P.op("vector", lambda e: e.tensor_scalar(mu_t[:], PS[4][:], 1.0 / D, None, ALU.mult))
            P.op("vector", lambda e: e.tensor_scalar(ms_t[:], PS[5][:], 1.0 / D, None, ALU.mult))
            P.op("vector", lambda e: e.tensor_tensor(var_t[:], mu_t[:], mu_t[:], ALU.mult))
            v1 = P.op("vector", lambda e: e.tensor_tensor(var_t[:], ms_t[:], var_t[:], ALU.subtract), VC)
            P.wait("scalar", VC, v1)
            P.op("scalar", lambda e: e.activation(rstd_t[:], var_t[:], AF.Ln, bias=eps_t[:, 0:1]))
            s2 = P.op("scalar", lambda e: e.activation(rstd_t[:], rstd_t[:], AF.Exp, scale=-0.5), SC)
            return s2, c2

        def ffn(T, par, prev):
            xs, xnb, fbuf = T["xs"][par], T["xnb"], T["fbuf"]
            wgt_s, wut_s, wdt_s = T["wgt_s"], T["wut_s"], T["wdt_s"]
            if prev.get("tds7") is not None:
                P.wait("sync", TM, prev["tds7"])
            tus = [None] * HM
            fvs = [None] * HM
            f1s = [None] * HM
            for mh in range(HM):
                bw = mh % 2
                pg = PS[0 + 2 * bw]
                pu = PS[1 + 2 * bw]
                if mh >= 2:
                    P.wait("sync", TM, tus[mh - 2])
                P.dma("sync", lambda e, mh=mh, bw=bw: e.dma_start(
                    out=wgt_s[bw][:],
                    in_=d_wg.ap()[:, 128 * mh:128 * (mh + 1)].rearrange("(k p) m -> p k m", p=128)), ctr=WQ)
                lw2 = P.dma("sync", lambda e, mh=mh, bw=bw: e.dma_start(
                    out=wut_s[bw][:],
                    in_=d_wu.ap()[:, 128 * mh:128 * (mh + 1)].rearrange("(k p) m -> p k m", p=128)), ctr=WQ)
                P.wait("tensor", WQ, lw2)
                if mh >= 2:
                    P.wait("tensor", SC, f1s[mh - 2])
                    P.wait("tensor", VC, fvs[mh - 2])
                elif prev.get("f1s7") is not None:
                    P.wait("tensor", SC, prev["f1s7"])
                    P.wait("tensor", VC, prev["fvs7"])
                tg = None
                for kb in range(KD):
                    tg = P.op("tensor", lambda e, kb=kb, bw=bw, pg=pg: e.matmul(
                        pg[:], wgt_s[bw][:, kb, :], xnb[:, kb, :],
                        start=(kb == 0), stop=(kb == KD - 1)), TM if kb == KD - 1 else None)
                for kb in range(KD):
                    tus[mh] = P.op("tensor", lambda e, kb=kb, bw=bw, pu=pu: e.matmul(
                        pu[:], wut_s[bw][:, kb, :], xnb[:, kb, :],
                        start=(kb == 0), stop=(kb == KD - 1)), TM if kb == KD - 1 else None)
                P.wait("scalar", TM, tg)
                if mh >= 1:
                    P.wait("scalar", VC, fvs[mh - 1])
                elif prev.get("fvs7") is not None:
                    P.wait("scalar", VC, prev["fvs7"])
                f1s[mh] = P.op("scalar", lambda e, pg=pg: e.activation(f1buf[:], pg[:], AF.Silu), SC)
                P.wait("vector", SC, f1s[mh])
                P.wait("vector", TM, tus[mh])
                if mh == 0 and prev.get("tds7") is not None:
                    P.wait("vector", TM, prev["tds7"])
                fvs[mh] = P.op("vector", lambda e, mh=mh, pu=pu: e.tensor_tensor(
                    fbuf[:, mh, :], f1buf[:], pu[:], ALU.mult), VC)
            xv = None
            tds = [None] * KD
            for mo in range(KD):
                bw = mo % 2
                pd = PS[6 + bw]
                if mo >= 2:
                    P.wait("sync", TM, tds[mo - 2])
                lw = P.dma("sync", lambda e, mo=mo, bw=bw: e.dma_start(
                    out=wdt_s[bw][:],
                    in_=d_wd.ap()[:, 128 * mo:128 * (mo + 1)].rearrange("(k p) m -> p k m", p=128)), ctr=WQ)
                P.wait("tensor", WQ, lw)
                if mo == 0:
                    P.wait("tensor", VC, fvs[-1])
                if mo >= 2:
                    P.wait("tensor", VC, xv)
                elif prev.get("xv7") is not None:
                    P.wait("tensor", VC, prev["xv7"])
                for kh in range(HM):
                    tds[mo] = P.op("tensor", lambda e, kh=kh, bw=bw, pd=pd: e.matmul(
                        pd[:], wdt_s[bw][:, kh, :], fbuf[:, kh, :],
                        start=(kh == 0), stop=(kh == HM - 1)), TM if kh == HM - 1 else None)
                P.wait("vector", TM, tds[mo])
                xv = P.op("vector", lambda e, mo=mo, pd=pd, xs=xs: e.scalar_tensor_tensor(
                    xs[:, mo, :], pd[:], 0.5, xs[:, mo, :], ALU.mult, ALU.add), VC)
            return dict(xv=xv, tds7=tds[-1], f1s7=f1s[-1], fvs7=fvs[-1], xv7=xv)

        def ln_apply_plain(T, src, dst_bf):
            P.op("vector", lambda e, src=src: e.tensor_tensor(
                T["cen"][:], src[:], bc_mid(mu_t[:], KD), ALU.subtract))
            return P.op("vector", lambda e, dst_bf=dst_bf: e.tensor_tensor(
                dst_bf[:], T["cen"][:], bc_mid(rstd_t[:], KD), ALU.mult), VC)

        def alloc_ffn_scope():
            T = {}
            T["xs"] = [AR.alloc([128, KD, TC], F32) for _ in range(2)]
            T["ln2t"] = AR.alloc([128, KD, TC], F32)
            T["cen"] = AR.alloc([128, KD, TC], F32)
            T["xbb"] = AR.alloc([128, KD, TC], BF16)
            T["xnb"] = AR.alloc([128, KD, TC], BF16)
            T["fbuf"] = AR.alloc([128, HM, TC], BF16)
            T["wgt_s"] = [AR.alloc([128, KD, 128], BF16) for _ in range(2)]
            T["wut_s"] = [AR.alloc([128, KD, 128], BF16) for _ in range(2)]
            T["wdt_s"] = [AR.alloc([128, HM, 128], BF16) for _ in range(2)]
            return T

        if True:
            AR.push()
            T = alloc_ffn_scope()
            prev = {}
            x1sp = [None, None]
            xs_vc = [None, None]
            xs_sc = [None, None]
            for j in range(NJ):
                par = j % 2
                jt = slice(TC * j, TC * (j + 1))
                xs = T["xs"][par]
                if j >= 2:
                    P.wait("sync", VC, xs_vc[par])
                    P.wait("sync", SC, xs_sc[par])
                    P.wait("sync", LD, x1sp[par])
                lx = P.dma("sync", lambda e, jt=jt, xs=xs: e.dma_start(out=xs[:], in_=x_fm_r[:, :, jt]))
                P.wait("scalar", LD, lx)
                P.wait("vector", LD, lx)
                s_r, _ = ln_stats(T, xs)
                P.wait("vector", SC, s_r)
                vx = ln_apply_plain(T, xs, T["xnb"])
                P.wait("tensor", VC, vx)
                prev = ffn(T, par, prev)
                xv = prev["xv"]
                P.wait("sync", VC, xv)
                x1sp[par] = P.dma("sync", lambda e, jt=jt, xs=xs: e.dma_start(
                    out=d_x1_r[:, :, jt], in_=xs[:]))
                s_r, c2m = ln_stats(T, xs, wait_vc=xv)
                xs_sc[par] = c2m
                P.wait("vector", SC, s_r)
                xs_vc[par] = P.op("vector", lambda e, xs=xs: e.tensor_tensor(
                    T["ln2t"][:], xs[:], bc_mid(mu_t[:], KD), ALU.subtract), VC)
                v2 = P.op("vector", lambda e: e.tensor_tensor(
                    T["ln2t"][:], T["ln2t"][:], bc_mid(rstd_t[:], KD), ALU.mult), VC)
                P.wait("sync", VC, v2)
                P.dma("sync", lambda e, jt=jt: e.dma_start(out=d_ln2_r[:, :, jt], in_=T["ln2t"][:]))
                s_r, _ = ln_stats(T, T["ln2t"], wait_vc=v2)
                P.wait("vector", SC, s_r)
                P.op("vector", lambda e: e.tensor_tensor(
                    T["cen"][:], T["ln2t"][:], bc_mid(mu_t[:], KD), ALU.subtract))
                P.op("vector", lambda e: e.tensor_tensor(
                    T["cen"][:], T["cen"][:], bc_mid(rstd_t[:], KD), ALU.mult))
                v3 = None
                for kb in range(KD):
                    v3 = P.op("vector", lambda e, kb=kb: e.tensor_scalar(
                        T["xnb"][:, kb, :], T["cen"][:, kb, :],
                        lngbt[:, 2 * kb:2 * kb + 1], lngbt[:, 2 * kb + 1:2 * kb + 2],
                        ALU.mult, ALU.add), VC)
                P.wait("sync", VC, v3)
                P.dma("sync", lambda e, j=j: e.dma_start(
                    out=d_xn.ap()[D * j:D * (j + 1), :].rearrange("(k p) t -> p k t", p=128),
                    in_=T["xnb"][:]), ctr=XN)
                if debug and "p1" in dbg_parts:
                    P.dma("sync", lambda e, jt=jt: e.dma_start(
                        out=dbg["dbg_xn"].rearrange("(k p) t -> p k t", p=128)[:, :, jt], in_=T["xnb"][:]))
                    P.dma("sync", lambda e, jt=jt, xs=xs: e.dma_start(
                        out=dbg["dbg_x1"].rearrange("(k p) t -> p k t", p=128)[:, :, jt], in_=xs[:]))
            P.barrier()
            AR.pop()
        ag_done = [None] * NJ
        for j in range(NJ):
            ag_done[j] = P.coll(
                "AllGather", ALU.bypass,
                [d_xn.ap()[D * j:D * (j + 1), :].opt()],
                [d_xng.ap()[4 * D * j:4 * D * (j + 1), :].opt()])

        if True:
            AR.push()
            Winb = AR.alloc([128, KD, 1024], BF16)
            xng = [AR.alloc([128, KD, TC], BF16) for _ in range(2)]
            xhh = AR.alloc([128, NMI, TC + 8], F32)
            zst = AR.alloc([128, NMI, TC], F32)
            xct2 = AR.alloc([128, NMI, TC], F32)
            xcb2 = AR.alloc([128, NMI, TC], BF16)
            cvac = AR.alloc([128, NMI, TC], F32)
            Wxpb = AR.alloc([128, NMI, 33], BF16)
            xpjt = AR.alloc([33, TC], F32)
            lw = P.dma("sync", lambda e: e.dma_start(
                out=Winb[:], in_=d_winb.ap().rearrange("(k p) m -> p k m", p=128)))
            for mi in range(NMI):
                tw = P.dma("sync", lambda e, mi=mi: e.dma_start(
                    out=var_t[:, 0:33], in_=wxpc[128 * mi:128 * (mi + 1), :]))
                P.wait("scalar", LD, tw)
                sw = P.op("scalar", lambda e, mi=mi: e.activation(Wxpb[:, mi, :], var_t[:, 0:33], AF.Copy), SC)
                P.wait("sync", SC, sw)
            P.wait("tensor", LD, lw)
            P.op("gpsimd", lambda e: e.memset(xhh[:, :, 0:3], 0.0), PC)
            P.wait("scalar", PC, PC.n)
            conv_done = None
            sil_done = None
            cast_done = None
            tpj_prev = None
            spj_prev = None
            zs_spill = None
            xc_spill = None
            pj_spill = None
            bank_ev = {}
            mm_last = [None, None]
            for k in range(NCH):
                bk = k % 2
                r0 = 4 * D * (k // 4) + D * (k % 4)
                c0 = TC * k
                if mm_last[bk] is not None:
                    P.wait("sync", TM, mm_last[bk])
                P.wait("sync", CD, ag_done[k // 4])
                lx = P.dma("sync", lambda e, r0=r0, bk=bk: e.dma_start(
                    out=xng[bk][:],
                    in_=d_xng.ap()[r0:r0 + D, :].rearrange("(k p) t -> p k t", p=128)))
                if k > 0:
                    P.wait("scalar", VC, conv_done)
                    P.op("scalar", lambda e: e.activation(xhh[:, :, 0:3], xhh[:, :, TC:TC + 3], AF.Copy))
                P.wait("tensor", LD, lx)
                ev = []
                for mi in range(KD):
                    psn = mi % 4
                    if bank_ev.get(psn) is not None:
                        P.wait("tensor", SC, bank_ev[psn])
                    tmm = None
                    for kb in range(KD):
                        tmm = P.op("tensor", lambda e, kb=kb, mi=mi, bk=bk, psn=psn: e.matmul(
                            PS[psn][:], Winb[:, kb, 128 * mi:128 * (mi + 1)], xng[bk][:, kb, :],
                            start=(kb == 0), stop=(kb == KD - 1)), TM if kb == KD - 1 else None)
                    if mi == KD - 1:
                        mm_last[bk] = tmm
                    P.wait("scalar", TM, tmm)
                    if mi < NMI:
                        evv = P.op("scalar", lambda e, mi=mi, psn=psn: e.activation(
                            xhh[:, mi, 3:3 + TC], PS[psn][:], AF.Copy), SC)
                    else:
                        if zs_spill is not None and mi == NMI:
                            P.wait("scalar", LD, zs_spill)
                        evv = P.op("scalar", lambda e, mi=mi, psn=psn: e.activation(
                            zst[:, mi - NMI, :], PS[psn][:], AF.Silu), SC)
                    ev.append(evv)
                    bank_ev[psn] = evv
                P.wait("sync", SC, ev[-1])
                zs_spill = P.dma("sync", lambda e, c0=c0: e.dma_start(
                    out=d_z_r[:, :, c0:c0 + TC], in_=zst[:]))
                P.wait("vector", SC, ev[NMI - 1])
                if sil_done is not None:
                    P.wait("vector", SC, sil_done)
                cv = None
                for mi in range(NMI):
                    P.op("vector", lambda e, mi=mi: e.tensor_scalar(
                        cvac[:, mi, :], xhh[:, mi, 0:TC],
                        cwt[:, 4 * mi + 0:4 * mi + 1], vect[:, 4 * mi + 0:4 * mi + 1],
                        ALU.mult, ALU.add))
                    for kk in range(1, 4):
                        cv = P.op("vector", lambda e, mi=mi, kk=kk: e.scalar_tensor_tensor(
                            cvac[:, mi, :], xhh[:, mi, kk:kk + TC],
                            cwt[:, 4 * mi + kk:4 * mi + kk + 1], cvac[:, mi, :],
                            ALU.mult, ALU.add), VC if (mi == NMI - 1 and kk == 3) else None)
                conv_done = cv
                P.wait("scalar", VC, cv)
                if xc_spill is not None:
                    P.wait("scalar", LD, xc_spill)
                sil_done = P.op("scalar", lambda e: e.activation(
                    xct2[:].rearrange("p m t -> p (m t)"), cvac[:].rearrange("p m t -> p (m t)"),
                    AF.Silu), SC)
                if tpj_prev is not None:
                    P.wait("scalar", TM, tpj_prev)
                cast_done = P.op("scalar", lambda e: e.activation(
                    xcb2[:].rearrange("p m t -> p (m t)"), xct2[:].rearrange("p m t -> p (m t)"),
                    AF.Copy), SC)
                P.wait("sync", SC, sil_done)
                xc_spill = P.dma("sync", lambda e, c0=c0: e.dma_start(
                    out=d_xc_r[:, :, c0:c0 + TC], in_=xct2[:]))
                P.wait("tensor", SC, cast_done)
                if spj_prev is not None:
                    P.wait("tensor", SC, spj_prev)
                tpj = None
                for mi in range(NMI):
                    tpj = P.op("tensor", lambda e, mi=mi: e.matmul(
                        PS[4][0:33, :], Wxpb[:, mi, :], xcb2[:, mi, :],
                        start=(mi == 0), stop=(mi == NMI - 1)), TM if mi == NMI - 1 else None)
                tpj_prev = tpj
                P.wait("scalar", TM, tpj)
                if pj_spill is not None:
                    P.wait("scalar", LD, pj_spill)
                spj = P.op("scalar", lambda e: e.activation(xpjt[:], PS[4][0:33, :], AF.Copy), SC)
                spj_prev = spj
                P.wait("sync", SC, spj)
                pj_spill = P.dma("sync", lambda e, c0=c0: e.dma_start(
                    out=d_xpj.ap()[:, c0:c0 + TC], in_=xpjt[:]))
            P.barrier()
            AR.pop()

        art = P.coll("AllReduce", ALU.add, [d_xpj.ap().opt()], [d_xpjr.ap().opt()],
                     wait_ld=P.LD.n)
        P.barrier()

        if True:
            AR.push()
            Wob = AR.alloc([128, NMI, 1024], BF16)
            xct3 = [AR.alloc([128, NMI, TC], F32) for _ in range(2)]
            zsl = [AR.alloc([128, NMI, TC], F32) for _ in range(2)]
            dtt = [AR.alloc([128, NMI, TC], F32) for _ in range(2)]
            ut = [AR.alloc([128, NMI, TC], BF16) for _ in range(2)]
            Bxt = [AR.alloc([16, TC], F32) for _ in range(2)]
            Cxt = [AR.alloc([16, TC], F32) for _ in range(2)]
            dtr = [AR.alloc([1, TC], F32) for _ in range(2)]
            Bxb = [AR.alloc([16, TC], BF16) for _ in range(2)]
            Cxb = [AR.alloc([16, TC], BF16) for _ in range(2)]
            Cb = [AR.alloc([128, TC], BF16) for _ in range(2)]
            Bbb = [AR.alloc([128, TC], BF16) for _ in range(2)]
            ab = [AR.alloc([128, NMI * TC], F32) for _ in range(2)]
            bb3 = AR.alloc([128, NMI * TC], BF16)
            hta = AR.alloc([128, NMI, TC], F32)
            tmpb = [AR.alloc([128, NMI * TC], BF16) for _ in range(2)]
            yt = [AR.alloc([128, NMI, TC], BF16) for _ in range(2)]
            gt = [AR.alloc([128, NMI, TC], BF16) for _ in range(2)]
            eb2b = AR.alloc([128, NMI * TC], F32)
            rb2b = AR.alloc([128, NMI * TC], F32)
            wot = [AR.alloc([128, TC], F32) for _ in range(2)]

            lw = P.dma("sync", lambda e: e.dma_start(
                out=Wob[:], in_=d_wob.ap().rearrange("(k p) m -> p k m", p=128)))
            P.op("vector", lambda e: e.memset(st[:], 0.0), VC)
            P.wait("tensor", LD, lw)

            roll = {"a3": {}, "dve": {}, "psB_mm": {}, "psC_ev": {}, "tmp": {}, "tmp_rd": {}}
            S = {"l1": {}, "l2": {}, "vsp": {}, "u": {}, "bxb": {}, "tdt": {},
                 "b3l": {}, "vg": {}, "wol": {}}
            wo_bank_ev = {0: None, 1: None}

            def head(k):
                pk = k % 2
                kt = slice(TC * k, TC * (k + 1))
                if S["vg"].get(k - 2) is not None:
                    P.wait("sync", VC, S["vg"][k - 2])
                    P.wait("sync", PC, S["u"][k - 2])
                S["l1"][k] = P.dma("sync", lambda e, kt=kt, pk=pk: e.dma_start(
                    out=xct3[pk][:], in_=d_xc_r[:, :, kt]))
                S["l2"][k] = P.dma("sync", lambda e, kt=kt, pk=pk: e.dma_start(
                    out=zsl[pk][:], in_=d_z_r[:, :, kt]))
                if S["bxb"].get(k - 2) is not None:
                    P.wait("sync", SC, S["bxb"][k - 2])
                if S["tdt"].get(k - 2) is not None:
                    P.wait("sync", TM, S["tdt"][k - 2])
                lB = P.dma("sync", lambda e, kt=kt, pk=pk: e.dma_start(
                    out=Bxt[pk][:], in_=d_xpjr.ap()[0:16, kt]))
                lC = P.dma("sync", lambda e, kt=kt, pk=pk: e.dma_start(
                    out=Cxt[pk][:], in_=d_xpjr.ap()[16:32, kt]))
                lD = P.dma("sync", lambda e, kt=kt, pk=pk: e.dma_start(
                    out=dtr[pk][:], in_=d_xpjr.ap()[32:33, kt]))
                P.wait("tensor", LD, lD)
                if S["vsp"].get(k - 1) is not None:
                    P.wait("tensor", VC, S["vsp"][k - 1])
                tdt = P.op("tensor", lambda e, pk=pk: e.matmul(
                    PS[6 + pk][:], ones1t[:], dtr[pk][:], start=True, stop=True), TM)
                S["tdt"][k] = tdt
                P.wait("vector", TM, tdt)
                if k >= 2:
                    P.wait("vector", SC, roll["a3"][16 * (k - 2) + 15])
                    P.wait("vector", PC, S["u"][k - 2])
                for mi in range(NMI):
                    P.op("vector", lambda e, mi=mi, pk=pk: e.tensor_scalar(
                        dtt[pk][:, mi, :], PS[6 + pk][:],
                        vect[:, 4 * mi + 2:4 * mi + 3], vect[:, 4 * mi + 1:4 * mi + 2],
                        ALU.mult, ALU.add))
                vrl = P.op("vector", lambda e, pk=pk: e.tensor_scalar(
                    rb2b[:], dtt[pk][:].rearrange("p m t -> p (m t)"), 0.0, None, ALU.max), VC)
                P.wait("scalar", VC, vrl)
                P.op("scalar", lambda e, pk=pk: e.activation(
                    eb2b[:], dtt[pk][:].rearrange("p m t -> p (m t)"), AF.Abs))
                P.op("scalar", lambda e: e.activation(eb2b[:], eb2b[:], AF.Exp, scale=-1.0))
                ssp = P.op("scalar", lambda e: e.activation(eb2b[:], eb2b[:], AF.Ln, bias=one_t[:, 0:1]), SC)
                P.wait("vector", SC, ssp)
                vsp = P.op("vector", lambda e, pk=pk: e.tensor_tensor(
                    dtt[pk][:].rearrange("p m t -> p (m t)"), rb2b[:], eb2b[:], ALU.add), VC)
                S["vsp"][k] = vsp
                P.wait("gpsimd", VC, vsp)
                P.wait("gpsimd", LD, S["l1"][k])
                if S["b3l"].get(k - 2) is not None:
                    P.wait("gpsimd", VC, S["b3l"][k - 2])
                S["u"][k] = P.op("gpsimd", lambda e, pk=pk: e.tensor_tensor(
                    ut[pk][:].rearrange("p m t -> p (m t)"), dtt[pk][:].rearrange("p m t -> p (m t)"),
                    xct3[pk][:].rearrange("p m t -> p (m t)"), ALU.mult), PC)
                P.wait("scalar", LD, lC)
                if k >= 2:
                    P.wait("scalar", TM, roll["psB_mm"][16 * (k - 2) + 15])
                P.op("scalar", lambda e, pk=pk: e.activation(Bxb[pk][:], Bxt[pk][:], AF.Copy))
                S["bxb"][k] = P.op("scalar", lambda e, pk=pk: e.activation(Cxb[pk][:], Cxt[pk][:], AF.Copy), SC)

            head(0)
            for k in range(NCH):
                pk = k % 2
                if debug and "dty" in dbg_parts and k == 0:
                    P.wait("sync", VC, S["vsp"][0])
                    P.dma("sync", lambda e: e.dma_start(out=dbg["dbg_dt"], in_=dtt[0][:].rearrange("p m t -> p (m t)")))
                for n in range(NS):
                    g = 16 * k + n
                    pp = g % 2
                    psB = PS[0 + 2 * pp]
                    psC = PS[1 + 2 * pp]
                    P.wait("tensor", SC, S["bxb"][k])
                    if roll["psC_ev"].get(g - 2) is not None:
                        P.wait("tensor", SC, roll["psC_ev"][g - 2])
                    P.op("tensor", lambda e, n=n, psB=psB, pk=pk: e.matmul(
                        psB[:], selb[:, 128 * n:128 * (n + 1)], Bxb[pk][:], start=True, stop=True))
                    tbc = P.op("tensor", lambda e, n=n, psC=psC, pk=pk: e.matmul(
                        psC[:], selb[:, 128 * n:128 * (n + 1)], Cxb[pk][:], start=True, stop=True), TM)
                    roll["psB_mm"][g] = tbc
                    P.wait("scalar", TM, tbc)
                    if roll["tmp"].get(g - 2) is not None:
                        P.wait("scalar", PC, roll["tmp"][g - 2])
                    if roll["dve"].get(g - 2) is not None:
                        P.wait("scalar", VC, roll["dve"][g - 2])
                    P.op("scalar", lambda e, pp=pp, psB=psB: e.activation(Bbb[pp][:], psB[:], AF.Copy))
                    cev = P.op("scalar", lambda e, pp=pp, psC=psC: e.activation(Cb[pp][:], psC[:], AF.Copy), SC)
                    roll["psC_ev"][g] = cev
                    if n == 0:
                        P.wait("scalar", VC, S["vsp"][k])
                    sa = None
                    for mi in range(NMI):
                        sa = P.op("scalar", lambda e, mi=mi, n=n, pp=pp, pk=pk: e.activation(
                            ab[pp][:, TC * mi:TC * (mi + 1)], dtt[pk][:, mi, :], AF.Exp,
                            scale=Aneg[:, 16 * mi + n:16 * mi + n + 1]),
                            SC if mi == NMI - 1 else None)
                    roll["a3"][g] = sa
                    if n == 0:
                        P.wait("vector", PC, S["u"][k])
                    P.wait("vector", SC, cev)
                    if roll["tmp"].get(g - 1) is not None:
                        P.wait("vector", PC, roll["tmp"][g - 1])
                    b3m = P.op("vector", lambda e, pp=pp, pk=pk: e.tensor_tensor(
                        bb3[:].rearrange("p (m t) -> p m t", m=NMI),
                        ut[pk][:], bc_mid(Bbb[pp][:], NMI), ALU.mult), VC)
                    if n == NS - 1:
                        S["b3l"][k] = b3m
                    P.wait("vector", SC, roll["a3"][g])
                    for mi in range(NMI):
                        P.op("vector", lambda e, mi=mi, n=n, pp=pp: e.tensor_tensor_scan(
                            hta[:, mi, :],
                            ab[pp][:, TC * mi:TC * (mi + 1)],
                            bb3[:, TC * mi:TC * (mi + 1)],
                            st[:, mi, n:n + 1], ALU.mult, ALU.add))
                    vsc = P.op("vector", lambda e, n=n: e.tensor_copy(
                        st[:, :, n:n + 1], hta[:, :, TC - 1:TC]), VC)
                    roll["dve"][g] = vsc
                    P.wait("gpsimd", VC, vsc)
                    P.wait("gpsimd", SC, cev)
                    if roll["tmp_rd"].get(g - 2) is not None:
                        P.wait("gpsimd", VC, roll["tmp_rd"][g - 2])
                    tm_ = P.op("gpsimd", lambda e, pp=pp: e.tensor_tensor(
                        tmpb[pp][:].rearrange("p (m t) -> p m t", m=NMI),
                        hta[:], bc_mid(Cb[pp][:], NMI), ALU.mult), PC)
                    roll["tmp"][g] = tm_
                    P.wait("vector", PC, tm_)
                    if n == 0:
                        yadd = P.op("vector", lambda e, pp=pp, pk=pk: e.tensor_scalar(
                            yt[pk][:].rearrange("p m t -> p (m t)"),
                            tmpb[pp][:], 0.0, None, ALU.add), VC)
                    else:
                        yadd = P.op("vector", lambda e, pp=pp, pk=pk: e.tensor_tensor(
                            yt[pk][:].rearrange("p m t -> p (m t)"),
                            yt[pk][:].rearrange("p m t -> p (m t)"),
                            tmpb[pp][:], ALU.add), VC)
                    roll["tmp_rd"][g] = yadd
                    if n == 8 and k + 1 < NCH:
                        head(k + 1)
                    if debug and "scan" in dbg_parts and k == 0 and n == 0:
                        P.wait("sync", PC, S["u"][0])
                        P.dma("sync", lambda e: e.dma_start(out=dbg["dbg_u"], in_=ut[0][:].rearrange("p m t -> p (m t)")))
                        P.wait("sync", SC, roll["a3"][g])
                        P.dma("sync", lambda e, pp=pp: e.dma_start(out=dbg["dbg_a0"], in_=ab[pp][:]))
                        P.wait("sync", VC, roll["dve"][g])
                        P.dma("sync", lambda e: e.dma_start(out=dbg["dbg_b0"], in_=bb3[:]))
                        P.dma("sync", lambda e: e.dma_start(out=dbg["dbg_h0"], in_=hta[:].rearrange("p m t -> p (m t)")))
                        P.wait("sync", VC, yadd)
                        dy0 = P.dma("sync", lambda e, pk=pk: e.dma_start(out=dbg["dbg_y0"], in_=yt[pk][:].rearrange("p m t -> p (m t)")))
                        P.wait("sync", LD, dy0)
                        P.barrier()
                P.wait("vector", LD, S["l1"][k])
                for mi in range(NMI):
                    P.op("vector", lambda e, mi=mi, pk=pk: e.scalar_tensor_tensor(
                        yt[pk][:, mi, :], xct3[pk][:, mi, :],
                        vect[:, 4 * mi + 3:4 * mi + 4], yt[pk][:, mi, :], ALU.mult, ALU.add))
                P.wait("vector", LD, S["l2"][k])
                if S["wol"].get(k - 2) is not None:
                    P.wait("vector", TM, S["wol"][k - 2])
                vg = P.op("vector", lambda e, pk=pk: e.tensor_tensor(
                    gt[pk][:].rearrange("p m t -> p (m t)"), yt[pk][:].rearrange("p m t -> p (m t)"),
                    zsl[pk][:].rearrange("p m t -> p (m t)"), ALU.mult), VC)
                S["vg"][k] = vg
                if debug and "dty" in dbg_parts and k == 0:
                    P.wait("sync", VC, vg)
                    P.dma("sync", lambda e: e.dma_start(out=dbg["dbg_y"], in_=yt[0][:].rearrange("p m t -> p (m t)")))
                P.wait("tensor", VC, vg)
                wo_dma = {}
                for mo in range(KD):
                    pw = mo % 2
                    psW = PS[4 + pw]
                    if wo_bank_ev[pw] is not None:
                        P.wait("tensor", SC, wo_bank_ev[pw])
                    two = None
                    for kd in range(NMI):
                        two = P.op("tensor", lambda e, kd=kd, mo=mo, pk=pk, psW=psW: e.matmul(
                            psW[:], Wob[:, kd, 128 * mo:128 * (mo + 1)], gt[pk][:, kd, :],
                            start=(kd == 0), stop=(kd == NMI - 1)), TM if kd == NMI - 1 else None)
                    if mo == KD - 1:
                        S["wol"][k] = two
                    P.wait("scalar", TM, two)
                    if wo_dma.get(mo - 2) is not None:
                        P.wait("scalar", WO, wo_dma[mo - 2])
                    swo = P.op("scalar", lambda e, pw=pw, psW=psW: e.activation(wot[pw][:], psW[:], AF.Copy), SC)
                    wo_bank_ev[pw] = swo
                    P.wait("sync", SC, swo)
                    wo_dma[mo] = P.dma("sync", lambda e, k=k, mo=mo, pw=pw: e.dma_start(
                        out=d_wop_j[k // 4].ap()[1024 * (k % 4) + 128 * mo:1024 * (k % 4) + 128 * (mo + 1), :],
                        in_=wot[pw][:]), ctr=WO)
                if k % 4 == 3:
                    P.wait("gpsimd", WO, 16 * KD * (k + 1))
                    CC.n += 1
                    P.op("gpsimd", lambda e, j=k // 4: e.collective_compute(
                        "ReduceScatter", ALU.add, GROUPS,
                        [d_wop_j[j].ap().opt()], [d_att_j[j].ap().opt()]).then_inc(CC.h, 1))
                if debug and "wop" in dbg_parts and k == 3:
                    P.wait("sync", LD, P.LD.n)
                    P.dma("sync", lambda e: e.dma_start(out=dbg["dbg_wop"], in_=d_wop_j[0].ap()))
            P.wait("gpsimd", CC, CC.n)
            P.op("gpsimd", lambda e: e.drain(fusable=False))
            P.barrier()
            AR.pop()

        if True:
            AR.push()
            T = alloc_ffn_scope()
            prev = {}
            oysp = [None, None]
            xs_vc = [None, None]
            xs_sc = [None, None]
            ln2_rd = None
            cen_rd = None
            for j in range(NJ):
                par = j % 2
                jt = slice(TC * j, TC * (j + 1))
                xs = T["xs"][par]
                if j >= 2:
                    P.wait("sync", VC, xs_vc[par])
                    P.wait("sync", SC, xs_sc[par])
                    P.wait("sync", LD, oysp[par])
                l1 = P.dma("sync", lambda e, jt=jt, xs=xs: e.dma_start(out=xs[:], in_=d_x1_r[:, :, jt]))
                if ln2_rd is not None:
                    P.wait("sync", VC, ln2_rd)
                l2 = P.dma("sync", lambda e, jt=jt: e.dma_start(out=T["ln2t"][:], in_=d_ln2_r[:, :, jt]))
                if cen_rd is not None:
                    P.wait("sync", VC, cen_rd)
                l3 = P.dma("sync", lambda e, j=j: e.dma_start(
                    out=T["cen"][:],
                    in_=d_att_j[j].ap().rearrange("(k p) t -> p k t", p=128)))
                P.wait("vector", LD, l3)
                ln2_rd = P.op("vector", lambda e, xs=xs: e.tensor_tensor(
                    xs[:].rearrange("p k t -> p (k t)"), xs[:].rearrange("p k t -> p (k t)"),
                    T["ln2t"][:].rearrange("p k t -> p (k t)"), ALU.add), VC)
                vx2 = P.op("vector", lambda e, xs=xs: e.tensor_tensor(
                    xs[:].rearrange("p k t -> p (k t)"), xs[:].rearrange("p k t -> p (k t)"),
                    T["cen"][:].rearrange("p k t -> p (k t)"), ALU.add), VC)
                if debug and "p3" in dbg_parts:
                    P.wait("sync", LD, l3)
                    P.dma("sync", lambda e, j=j: e.dma_start(
                        out=dbg["dbg_att"][D * j:D * (j + 1), :].rearrange("(k p) t -> p k t", p=128),
                        in_=T["cen"][:]))
                s_r, c2m = ln_stats(T, xs, wait_vc=vx2)
                xs_sc[par] = c2m
                P.wait("vector", SC, s_r)
                vx = ln_apply_plain(T, xs, T["xnb"])
                cen_rd = vx
                xs_vc[par] = vx
                P.wait("tensor", VC, vx)
                prev = ffn(T, par, prev)
                xv = prev["xv"]
                P.wait("sync", VC, xv)
                oysp[par] = P.dma("sync", lambda e, jt=jt, xs=xs: e.dma_start(
                    out=o_y_r[:, :, jt], in_=xs[:]))

        P.wait("sync", LD, P.LD.n)

        block = E(nc.Block())
        P.emit(block)

    return nc


_DEBUG = False
_NC_CACHE = {}


_DBG_PARTS = ("p3",)


def _get_nc():
    key = (bool(_DEBUG), tuple(_DBG_PARTS))
    if key not in _NC_CACHE:
        _NC_CACHE[key] = build_program(debug=key[0], dbg_parts=key[1])
    return _NC_CACHE[key]


def kernel(L0_attn_x, L0_attn_ln_g, L0_attn_ln_b, L0_attn_W_in, L0_attn_conv_w,
           L0_attn_conv_b, L0_attn_W_xp, L0_attn_W_dt, L0_attn_b_dt, L0_attn_A_log,
           L0_attn_D, L0_attn_W_out, L0_ffn_w_gate, L0_ffn_w_up, L0_ffn_w_down):
    f32 = np.float32
    x = np.asarray(L0_attn_x, f32)
    W_in = np.asarray(L0_attn_W_in, f32)
    W_out = np.asarray(L0_attn_W_out, f32)
    W_xp = np.asarray(L0_attn_W_xp, f32)
    conv_w = np.asarray(L0_attn_conv_w, f32)
    conv_b = np.asarray(L0_attn_conv_b, f32)
    W_dt = np.asarray(L0_attn_W_dt, f32)
    b_dt = np.asarray(L0_attn_b_dt, f32)
    A_log = np.asarray(L0_attn_A_log, f32)
    Dvec = np.asarray(L0_attn_D, f32)
    ln_g = np.asarray(L0_attn_ln_g, f32)
    ln_b = np.asarray(L0_attn_ln_b, f32)
    wg = np.ascontiguousarray(np.asarray(L0_ffn_w_gate, np.int32).T)
    wu = np.ascontiguousarray(np.asarray(L0_ffn_w_up, np.int32).T)
    wd = np.ascontiguousarray(np.asarray(L0_ffn_w_down, np.int32).T)

    sel = np.zeros((16, 2048), f32)
    for n in range(16):
        sel[n, 128 * n:128 * (n + 1)] = 1.0
    ones128 = np.ones((128, 128), f32)
    ones1 = np.ones((1, 128), f32)
    lngb = np.zeros((128, 16), f32)
    for kb in range(KD):
        lngb[:, 2 * kb] = ln_g[128 * kb:128 * (kb + 1)]
        lngb[:, 2 * kb + 1] = ln_b[128 * kb:128 * (kb + 1)]

    in_maps = []
    for c in range(8):
        b, r = divmod(c, 4)
        sl = slice(512 * r, 512 * (r + 1))
        xc = np.concatenate(
            [x[b, TC * (4 * j + r):TC * (4 * j + r + 1), :].T for j in range(NJ)], axis=1)
        winc_c = np.concatenate([W_in[:, sl], W_in[:, 2048 + sl.start:2048 + sl.stop]], axis=1)
        cwc = conv_w[sl].reshape(4, 128, 4).transpose(1, 0, 2).reshape(128, 16)
        avals = A_log[sl].reshape(4, 128, 16).transpose(1, 0, 2).reshape(128, 64)
        vv = np.stack([conv_b[sl], b_dt[sl], W_dt[0, sl], Dvec[sl]], axis=-1)
        vv = vv.reshape(4, 128, 4).transpose(1, 0, 2).reshape(128, 16)
        in_maps.append(dict(
            x_fm=np.ascontiguousarray(xc),
            winc=np.ascontiguousarray(winc_c),
            woc=np.ascontiguousarray(W_out[sl]),
            wxpc=np.ascontiguousarray(W_xp[sl]),
            wgT=wg, wuT=wu, wdT=wd,
            cwc=np.ascontiguousarray(cwc),
            avals=np.ascontiguousarray(avals),
            vecs=np.ascontiguousarray(vv),
            lngb=lngb, sel=sel, ones128=ones128, ones1=ones1,
        ))

    nc = _get_nc()
    res = run_bass_kernel_spmd(nc, in_maps, core_ids=list(range(8)))

    out = np.empty((B, L, D), f32)
    for c in range(8):
        b, r = divmod(c, 4)
        oy = np.asarray(res.results[c]["o_y"], f32)
        for j in range(NJ):
            out[b, TC * (4 * j + r):TC * (4 * j + r + 1), :] = oy[:, TC * j:TC * (j + 1)].T
    if _DEBUG:
        kernel._dbg = res.results
    return out


# version 26
# speedup vs baseline: 1.0678x; 1.0467x over previous
import dataclasses
from contextlib import ExitStack

import numpy as np

import concourse.bass as bass
import concourse.mybir as mybir
from concourse.bass_utils import run_bass_kernel_spmd

F32 = mybir.dt.float32
BF16 = mybir.dt.bfloat16
I32 = mybir.dt.int32
AF = mybir.ActivationFunctionType
ALU = mybir.AluOpType

B, L, D, DIN, H, NS = 2, 8192, 1024, 2048, 3072, 16
TC = 512
NCH = 16
NJ = 4
KD = 8
NMI = 4
HM = H // 128
EPS = 1e-5
GROUPS = [[0, 1, 2, 3], [4, 5, 6, 7]]
ENGINES = ("sync", "vector", "scalar", "tensor", "gpsimd")


def bc_mid(ap, rep):
    assert len(ap.ap) == 2
    return dataclasses.replace(ap, ap=[ap.ap[0], [0, rep], ap.ap[1]])


class Ctr:
    def __init__(self, h):
        self.h = h
        self.n = 0


class Bld:
    def __init__(self, nc, ctx):
        self.nc = nc
        self.ops = {e: [] for e in ENGINES}
        E = ctx.enter_context
        self.LD = Ctr(E(nc.semaphore("LD")))
        self.CC = Ctr(E(nc.semaphore("CC")))
        self.VC = Ctr(E(nc.semaphore("VC")))
        self.SC = Ctr(E(nc.semaphore("SC")))
        self.TM = Ctr(E(nc.semaphore("TM")))
        self.PC = Ctr(E(nc.semaphore("PC")))
        self.CD = Ctr(E(nc.semaphore("CD")))
        self.WQ = Ctr(E(nc.semaphore("WQ")))
        self.WO = Ctr(E(nc.semaphore("WO")))
        self.XN = Ctr(E(nc.semaphore("XN")))
        self.BAR = Ctr(E(nc.semaphore("BAR")))

    def op(self, eng, build, ctr=None):
        if ctr is None:
            self.ops[eng].append(build)
            return None
        ctr.n += 1
        tgt = ctr.n
        self.ops[eng].append(lambda e, b=build, c=ctr: b(e).then_inc(c.h, 1))
        return tgt

    def dma(self, eng, build, ctr=None):
        ctr = ctr or self.LD
        ctr.n += 16
        tgt = ctr.n
        self.ops[eng].append(lambda e, b=build, c=ctr: b(e).then_inc(c.h, 16))
        return tgt

    def wait(self, eng, ctr, val):
        if val is None:
            return
        self.ops[eng].append(lambda e, c=ctr, v=val: e.wait_ge(c.h, v))

    def barrier(self):
        self.wait("sync", self.LD, self.LD.n)
        self.wait("sync", self.WQ, self.WQ.n)
        self.wait("sync", self.WO, self.WO.n)
        self.wait("sync", self.XN, self.XN.n)
        tgt = self.BAR.n + len(ENGINES)
        for e in ENGINES:
            self.ops[e].append(lambda E: E.sem_inc(self.BAR.h, 1))
        self.BAR.n = tgt
        for e in ENGINES:
            self.ops[e].append(lambda E, v=tgt: E.wait_ge(self.BAR.h, v))

    def coll(self, kind, op_, ins, outs, wait_ld=None):
        if wait_ld is not None:
            self.wait("gpsimd", self.LD, wait_ld)
        self.CC.n += 1
        tgt = self.CC.n
        self.ops["gpsimd"].append(lambda e, k=kind, o=op_, i=ins, u=outs: e.collective_compute(
            k, o, GROUPS, i, u).then_inc(self.CC.h, 1))
        self.wait("gpsimd", self.CC, tgt)
        self.ops["gpsimd"].append(lambda e: e.drain(fusable=False))
        self.CD.n += 1
        cdt = self.CD.n
        self.ops["gpsimd"].append(lambda e, v=cdt: e.sem_inc(self.CD.h, 1))
        return cdt

    def emit(self, block):
        b = self
        @block.sync
        def _(eng):
            for f in b.ops["sync"]:
                f(eng)
        @block.vector
        def _(eng):
            for f in b.ops["vector"]:
                f(eng)
        @block.scalar
        def _(eng):
            for f in b.ops["scalar"]:
                f(eng)
        @block.tensor
        def _(eng):
            for f in b.ops["tensor"]:
                f(eng)
        @block.gpsimd
        def _(eng):
            for f in b.ops["gpsimd"]:
                f(eng)


class Arena:
    ESZ = {F32: 4, BF16: 2, I32: 4}

    def __init__(self, nc, ctx, name, words):
        self.h = ctx.enter_context(nc.sbuf_tensor(name, [128, words], F32))
        self.hb = self.h.bitcast(BF16)
        self.hi = self.h.bitcast(I32)
        self.cap = words * 4
        self.off = 0
        self.stack = []

    def push(self):
        self.stack.append(self.off)

    def pop(self):
        self.off = self.stack.pop()

    def alloc(self, shape, dt=F32):
        n = 1
        for s in shape[1:]:
            n *= s
        esz = self.ESZ[dt]
        self.off = (self.off + 63) & ~63
        start = self.off
        self.off += n * esz
        assert self.off <= self.cap, f"arena overflow: {self.off} > {self.cap}"
        if dt == F32:
            base = self.h[:, start // 4: start // 4 + n]
        elif dt == I32:
            base = self.hi[:, start // 4: start // 4 + n]
        else:
            base = self.hb[:, start // 2: start // 2 + n]
        if shape[0] != 128:
            base = base[0:shape[0], :]
        if len(shape) == 3:
            base = base.rearrange("p (a b) -> p a b", a=shape[1])
        return base


def build_program(debug=False, dbg_parts=("p1","scan","wop","rs","p3","dty")):
    nc = bass.Bass("TRN2", target_bir_lowering=False, detect_race_conditions=False)

    def din(name, shape, dt=F32):
        return nc.dram_tensor(name, shape, dt, kind="ExternalInput").ap()

    x_fm = din("x_fm", [D, NJ * TC])
    winc = din("winc", [D, 1024])
    woc = din("woc", [512, D])
    wxpc = din("wxpc", [512, 33])
    wgT = din("wgT", [D, H], I32)
    wuT = din("wuT", [D, H], I32)
    wdT = din("wdT", [H, D], I32)
    cwc = din("cwc", [128, 16])
    avals = din("avals", [128, 64])
    vecs = din("vecs", [128, 16])
    lngb = din("lngb", [128, 16])
    sel_i = din("sel", [16, 2048])
    ones128 = din("ones128", [128, 128])
    ones1 = din("ones1", [1, 128])

    o_y = nc.dram_tensor("o_y", [D, NJ * TC], F32, kind="ExternalOutput").ap()

    def dint(name, shape, dt=F32):
        return nc.dram_tensor(name, shape, dt, kind="Internal")

    d_wg = dint("d_wg", [D, H], BF16)
    d_wu = dint("d_wu", [D, H], BF16)
    d_wd = dint("d_wd", [H, D], BF16)
    d_winb = dint("d_winb", [D, 1024], BF16)
    d_wob = dint("d_wob", [512, D], BF16)
    d_x1 = dint("d_x1", [D, NJ * TC], F32)
    d_ln2 = dint("d_ln2", [D, NJ * TC], F32)
    d_xn = dint("d_xn", [NJ * D, TC], BF16)
    d_xng = dint("d_xng", [NJ * 4 * D, TC], BF16)
    d_xc = dint("d_xc", [512, L], F32)
    d_z = dint("d_z", [512, L], F32)
    d_xpj = dint("d_xpj", [33, L], F32)
    d_xpjr = dint("d_xpjr", [33, L], F32)
    d_wop_j = [dint(f"d_wop{j}", [4 * D, TC], F32) for j in range(NJ)]
    d_att_j = [dint(f"d_att{j}", [D, TC], F32) for j in range(NJ)]

    dbg = {}
    if debug:
        for name, shape, dt in [
            ("dbg_xn", [D, NJ * TC], BF16), ("dbg_x1", [D, NJ * TC], F32),
            ("dbg_dt", [128, NMI * TC], F32),
            ("dbg_y", [128, NMI * TC], F32), ("dbg_att", [NJ * D, TC], F32),
            ("dbg_u", [128, NMI * TC], F32), ("dbg_a0", [128, NMI * TC], F32),
            ("dbg_b0", [128, NMI * TC], F32), ("dbg_h0", [128, NMI * TC], F32),
            ("dbg_y0", [128, NMI * TC], F32), ("dbg_wop", [4 * D, TC], F32),
            ("dbg_wop2", [4 * D, TC], F32),
        ]:
            dbg[name] = nc.dram_tensor(name, shape, dt, kind="ExternalOutput").ap()

    x_fm_r = x_fm.rearrange("(k p) t -> p k t", p=128)
    d_x1_r = d_x1.ap().rearrange("(k p) t -> p k t", p=128)
    d_ln2_r = d_ln2.ap().rearrange("(k p) t -> p k t", p=128)
    d_xc_r = d_xc.ap().rearrange("(m p) t -> p m t", p=128)
    d_z_r = d_z.ap().rearrange("(m p) t -> p m t", p=128)
    o_y_r = o_y.rearrange("(k p) t -> p k t", p=128)

    with ExitStack() as ctx:
        E = ctx.enter_context
        P = Bld(nc, ctx)
        LD, CC, VC, SC, TM, PC, CD, WQ, WO, XN = P.LD, P.CC, P.VC, P.SC, P.TM, P.PC, P.CD, P.WQ, P.WO, P.XN

        def sb(name, shape, dt=F32):
            return E(nc.sbuf_tensor(name, shape, dt))

        mu_t = sb("mu_t", [128, TC])
        ms_t = sb("ms_t", [128, TC])
        var_t = sb("var_t", [128, TC])
        rstd_t = sb("rstd_t", [128, TC])
        f1buf = sb("f1buf", [128, TC])
        cwt = sb("cwt", [128, 16])
        Aneg = sb("Aneg", [128, 64])
        vect = sb("vect", [128, 16])
        lngbt = sb("lngbt", [128, 16])
        onesb = sb("onesb", [128, 128], BF16)
        ones1t = sb("ones1t", [1, 128])
        eps_t = sb("eps_t", [128, 1])
        one_t = sb("one_t", [128, 1])
        selb = sb("selb", [16, 2048], BF16)
        st = sb("st", [128, NMI, NS])

        PS = [E(nc.psum_tensor(f"ps{i}", [128, TC], F32)) for i in range(8)]
        AR = Arena(nc, ctx, "arena", 43 * 1024)

        if True:
            AR.push()
            deqi = [AR.alloc([128, H], I32) for _ in range(2)]
            deqb = [AR.alloc([128, H], BF16) for _ in range(2)]
            wfl = AR.alloc([128, 1024], F32)
            p0sel = AR.alloc([16, 2048], F32)

            P.op("vector", lambda e: e.memset(eps_t[:], float(EPS)))
            vcn = P.op("vector", lambda e: e.memset(one_t[:], 1.0), VC)
            P.wait("scalar", VC, vcn)
            P.dma("sync", lambda e: e.dma_start(out=cwt[:], in_=cwc))
            P.dma("sync", lambda e: e.dma_start(out=vect[:], in_=vecs))
            P.dma("sync", lambda e: e.dma_start(out=lngbt[:], in_=lngb))
            P.dma("sync", lambda e: e.dma_start(out=ones1t[:], in_=ones1))
            t5 = P.dma("sync", lambda e: e.dma_start(out=mu_t[:, 0:64], in_=avals))
            P.wait("scalar", LD, t5)
            s1 = P.op("scalar", lambda e: e.activation(ms_t[:, 0:64], mu_t[:, 0:64], AF.Exp), SC)
            P.wait("vector", SC, s1)
            P.op("vector", lambda e: e.tensor_scalar(Aneg[:], ms_t[:, 0:64], -1.0, None, ALU.mult), VC)
            t6 = P.dma("sync", lambda e: e.dma_start(out=rstd_t[:, 0:128], in_=ones128))
            P.wait("scalar", LD, t6)
            P.op("scalar", lambda e: e.activation(onesb[:], rstd_t[:, 0:128], AF.Copy), SC)
            t7 = P.dma("sync", lambda e: e.dma_start(out=p0sel[:], in_=sel_i))
            P.wait("scalar", LD, t7)
            P.op("scalar", lambda e: e.activation(selb[:], p0sel[:], AF.Copy), SC)

            def dequant(src_ap, dst_t, nblocks, width):
                vvs = [None] * nblocks
                dmas = [None] * nblocks
                for i in range(nblocks):
                    bb = i % 2
                    if i >= 2:
                        P.wait("sync", VC, vvs[i - 2])
                    tl = P.dma("sync", lambda e, i=i, bb=bb, s=src_ap, w=width: e.dma_start(
                        out=deqi[bb][:, 0:w], in_=s[128 * i:128 * (i + 1), 0:w]))
                    P.wait("vector", LD, tl)
                    if i >= 2:
                        P.wait("vector", LD, dmas[i - 2])
                    vvs[i] = P.op("vector", lambda e, bb=bb, w=width: e.tensor_scalar(
                        deqb[bb][:, 0:w], deqi[bb][:, 0:w], 1.0 / 127.0, None, ALU.mult), VC)
                    P.wait("sync", VC, vvs[i])
                    dmas[i] = P.dma("sync", lambda e, i=i, bb=bb, d=dst_t, w=width: e.dma_start(
                        out=d.ap()[128 * i:128 * (i + 1), 0:w], in_=deqb[bb][:, 0:w]))

            dequant(wgT, d_wg, KD, H)
            dequant(wuT, d_wu, KD, H)
            dequant(wdT, d_wd, HM, D)

            def stage(src, dst, nblocks):
                prev = {"sv": None, "dma": None}
                for i in range(nblocks):
                    if prev["sv"] is not None:
                        P.wait("sync", SC, prev["sv"])
                    tl = P.dma("sync", lambda e, i=i, s=src: e.dma_start(
                        out=wfl[:], in_=s[128 * i:128 * (i + 1), :]))
                    P.wait("scalar", LD, tl)
                    if prev["dma"] is not None:
                        P.wait("scalar", LD, prev["dma"])
                    sv = P.op("scalar", lambda e: e.activation(deqb[0][:, 0:1024], wfl[:], AF.Copy), SC)
                    prev["sv"] = sv
                    P.wait("sync", SC, sv)
                    prev["dma"] = P.dma("sync", lambda e, i=i, d=dst: e.dma_start(
                        out=d.ap()[128 * i:128 * (i + 1), :], in_=deqb[0][:, 0:1024]))
            stage(winc, d_winb, KD)
            stage(woc, d_wob, NMI)
            P.barrier()
            AR.pop()

        def ln_stats(T, src, wait_vc=None):
            if wait_vc is not None:
                P.wait("scalar", VC, wait_vc)
            P.op("scalar", lambda e: e.activation(T["xbb"][:], src[:], AF.Copy))
            c2 = P.op("scalar", lambda e: e.activation(T["xnb"][:], src[:], AF.Square), SC)
            P.wait("tensor", SC, c2)
            for kb in range(KD):
                P.op("tensor", lambda e, kb=kb: e.matmul(
                    PS[4][:], onesb[:], T["xbb"][:, kb, :], start=(kb == 0), stop=(kb == KD - 1)),
                    TM if kb == KD - 1 else None)
            tm2 = None
            for kb in range(KD):
                tm2 = P.op("tensor", lambda e, kb=kb: e.matmul(
                    PS[5][:], onesb[:], T["xnb"][:, kb, :], start=(kb == 0), stop=(kb == KD - 1)),
                    TM if kb == KD - 1 else None)
            P.wait("vector", TM, tm2)
            P.op("vector", lambda e: e.tensor_scalar(mu_t[:], PS[4][:], 1.0 / D, None, ALU.mult))
            P.op("vector", lambda e: e.tensor_scalar(ms_t[:], PS[5][:], 1.0 / D, None, ALU.mult))
            P.op("vector", lambda e: e.tensor_tensor(var_t[:], mu_t[:], mu_t[:], ALU.mult))
            v1 = P.op("vector", lambda e: e.tensor_tensor(var_t[:], ms_t[:], var_t[:], ALU.subtract), VC)
            P.wait("scalar", VC, v1)
            P.op("scalar", lambda e: e.activation(rstd_t[:], var_t[:], AF.Ln, bias=eps_t[:, 0:1]))
            s2 = P.op("scalar", lambda e: e.activation(rstd_t[:], rstd_t[:], AF.Exp, scale=-0.5), SC)
            return s2, c2

        def ffn(T, par, prev):
            xs, xnb, fbuf = T["xs"][par], T["xnb"], T["fbuf"]
            wgt_s, wut_s, wdt_s = T["wgt_s"], T["wut_s"], T["wdt_s"]
            if prev.get("tds7") is not None:
                P.wait("sync", TM, prev["tds7"])
            tus = [None] * HM
            fvs = [None] * HM
            f1s = [None] * HM
            for mh in range(HM):
                bw = mh % 2
                pg = PS[0 + 2 * bw]
                pu = PS[1 + 2 * bw]
                if mh >= 2:
                    P.wait("sync", TM, tus[mh - 2])
                P.dma("sync", lambda e, mh=mh, bw=bw: e.dma_start(
                    out=wgt_s[bw][:],
                    in_=d_wg.ap()[:, 128 * mh:128 * (mh + 1)].rearrange("(k p) m -> p k m", p=128)), ctr=WQ)
                lw2 = P.dma("sync", lambda e, mh=mh, bw=bw: e.dma_start(
                    out=wut_s[bw][:],
                    in_=d_wu.ap()[:, 128 * mh:128 * (mh + 1)].rearrange("(k p) m -> p k m", p=128)), ctr=WQ)
                P.wait("tensor", WQ, lw2)
                if mh >= 2:
                    P.wait("tensor", SC, f1s[mh - 2])
                    P.wait("tensor", VC, fvs[mh - 2])
                elif prev.get("f1s7") is not None:
                    P.wait("tensor", SC, prev["f1s7"])
                    P.wait("tensor", VC, prev["fvs7"])
                tg = None
                for kb in range(KD):
                    tg = P.op("tensor", lambda e, kb=kb, bw=bw, pg=pg: e.matmul(
                        pg[:], wgt_s[bw][:, kb, :], xnb[:, kb, :],
                        start=(kb == 0), stop=(kb == KD - 1)), TM if kb == KD - 1 else None)
                for kb in range(KD):
                    tus[mh] = P.op("tensor", lambda e, kb=kb, bw=bw, pu=pu: e.matmul(
                        pu[:], wut_s[bw][:, kb, :], xnb[:, kb, :],
                        start=(kb == 0), stop=(kb == KD - 1)), TM if kb == KD - 1 else None)
                P.wait("scalar", TM, tg)
                if mh >= 1:
                    P.wait("scalar", VC, fvs[mh - 1])
                elif prev.get("fvs7") is not None:
                    P.wait("scalar", VC, prev["fvs7"])
                f1s[mh] = P.op("scalar", lambda e, pg=pg: e.activation(f1buf[:], pg[:], AF.Silu), SC)
                P.wait("vector", SC, f1s[mh])
                P.wait("vector", TM, tus[mh])
                if mh == 0 and prev.get("tds7") is not None:
                    P.wait("vector", TM, prev["tds7"])
                fvs[mh] = P.op("vector", lambda e, mh=mh, pu=pu: e.tensor_tensor(
                    fbuf[:, mh, :], f1buf[:], pu[:], ALU.mult), VC)
            xv = None
            tds = [None] * KD
            for mo in range(KD):
                bw = mo % 2
                pd = PS[6 + bw]
                if mo >= 2:
                    P.wait("sync", TM, tds[mo - 2])
                lw = P.dma("sync", lambda e, mo=mo, bw=bw: e.dma_start(
                    out=wdt_s[bw][:],
                    in_=d_wd.ap()[:, 128 * mo:128 * (mo + 1)].rearrange("(k p) m -> p k m", p=128)), ctr=WQ)
                P.wait("tensor", WQ, lw)
                if mo == 0:
                    P.wait("tensor", VC, fvs[-1])
                if mo >= 2:
                    P.wait("tensor", VC, xv)
                elif prev.get("xv7") is not None:
                    P.wait("tensor", VC, prev["xv7"])
                for kh in range(HM):
                    tds[mo] = P.op("tensor", lambda e, kh=kh, bw=bw, pd=pd: e.matmul(
                        pd[:], wdt_s[bw][:, kh, :], fbuf[:, kh, :],
                        start=(kh == 0), stop=(kh == HM - 1)), TM if kh == HM - 1 else None)
                P.wait("vector", TM, tds[mo])
                xv = P.op("vector", lambda e, mo=mo, pd=pd, xs=xs: e.scalar_tensor_tensor(
                    xs[:, mo, :], pd[:], 0.5, xs[:, mo, :], ALU.mult, ALU.add), VC)
            return dict(xv=xv, tds7=tds[-1], f1s7=f1s[-1], fvs7=fvs[-1], xv7=xv)

        def ln_apply_plain(T, src, dst_bf):
            P.op("vector", lambda e, src=src: e.tensor_tensor(
                T["cen"][:], src[:], bc_mid(mu_t[:], KD), ALU.subtract))
            return P.op("vector", lambda e, dst_bf=dst_bf: e.tensor_tensor(
                dst_bf[:], T["cen"][:], bc_mid(rstd_t[:], KD), ALU.mult), VC)

        def alloc_ffn_scope():
            T = {}
            T["xs"] = [AR.alloc([128, KD, TC], F32) for _ in range(2)]
            T["ln2t"] = AR.alloc([128, KD, TC], F32)
            T["cen"] = AR.alloc([128, KD, TC], F32)
            T["xbb"] = AR.alloc([128, KD, TC], BF16)
            T["xnb"] = AR.alloc([128, KD, TC], BF16)
            T["fbuf"] = AR.alloc([128, HM, TC], BF16)
            T["wgt_s"] = [AR.alloc([128, KD, 128], BF16) for _ in range(2)]
            T["wut_s"] = [AR.alloc([128, KD, 128], BF16) for _ in range(2)]
            T["wdt_s"] = [AR.alloc([128, HM, 128], BF16) for _ in range(2)]
            return T

        if True:
            AR.push()
            T = alloc_ffn_scope()
            prev = {}
            x1sp = [None, None]
            xs_vc = [None, None]
            xs_sc = [None, None]
            for j in range(NJ):
                par = j % 2
                jt = slice(TC * j, TC * (j + 1))
                xs = T["xs"][par]
                if j >= 2:
                    P.wait("sync", VC, xs_vc[par])
                    P.wait("sync", SC, xs_sc[par])
                    P.wait("sync", LD, x1sp[par])
                lx = P.dma("sync", lambda e, jt=jt, xs=xs: e.dma_start(out=xs[:], in_=x_fm_r[:, :, jt]))
                P.wait("scalar", LD, lx)
                P.wait("vector", LD, lx)
                s_r, _ = ln_stats(T, xs)
                P.wait("vector", SC, s_r)
                vx = ln_apply_plain(T, xs, T["xnb"])
                P.wait("tensor", VC, vx)
                prev = ffn(T, par, prev)
                xv = prev["xv"]
                P.wait("sync", VC, xv)
                x1sp[par] = P.dma("sync", lambda e, jt=jt, xs=xs: e.dma_start(
                    out=d_x1_r[:, :, jt], in_=xs[:]))
                s_r, c2m = ln_stats(T, xs, wait_vc=xv)
                xs_sc[par] = c2m
                P.wait("vector", SC, s_r)
                xs_vc[par] = P.op("vector", lambda e, xs=xs: e.tensor_tensor(
                    T["ln2t"][:], xs[:], bc_mid(mu_t[:], KD), ALU.subtract), VC)
                v2 = P.op("vector", lambda e: e.tensor_tensor(
                    T["ln2t"][:], T["ln2t"][:], bc_mid(rstd_t[:], KD), ALU.mult), VC)
                P.wait("sync", VC, v2)
                P.dma("sync", lambda e, jt=jt: e.dma_start(out=d_ln2_r[:, :, jt], in_=T["ln2t"][:]))
                s_r, _ = ln_stats(T, T["ln2t"], wait_vc=v2)
                P.wait("vector", SC, s_r)
                P.op("vector", lambda e: e.tensor_tensor(
                    T["cen"][:], T["ln2t"][:], bc_mid(mu_t[:], KD), ALU.subtract))
                P.op("vector", lambda e: e.tensor_tensor(
                    T["cen"][:], T["cen"][:], bc_mid(rstd_t[:], KD), ALU.mult))
                v3 = None
                for kb in range(KD):
                    v3 = P.op("vector", lambda e, kb=kb: e.tensor_scalar(
                        T["xnb"][:, kb, :], T["cen"][:, kb, :],
                        lngbt[:, 2 * kb:2 * kb + 1], lngbt[:, 2 * kb + 1:2 * kb + 2],
                        ALU.mult, ALU.add), VC)
                P.wait("sync", VC, v3)
                P.dma("sync", lambda e, j=j: e.dma_start(
                    out=d_xn.ap()[D * j:D * (j + 1), :].rearrange("(k p) t -> p k t", p=128),
                    in_=T["xnb"][:]), ctr=XN)
                if debug and "p1" in dbg_parts:
                    P.dma("sync", lambda e, jt=jt: e.dma_start(
                        out=dbg["dbg_xn"].rearrange("(k p) t -> p k t", p=128)[:, :, jt], in_=T["xnb"][:]))
                    P.dma("sync", lambda e, jt=jt, xs=xs: e.dma_start(
                        out=dbg["dbg_x1"].rearrange("(k p) t -> p k t", p=128)[:, :, jt], in_=xs[:]))
            P.barrier()
            AR.pop()
        ag_done = [None] * NJ
        for j in range(NJ):
            ag_done[j] = P.coll(
                "AllGather", ALU.bypass,
                [d_xn.ap()[D * j:D * (j + 1), :].opt()],
                [d_xng.ap()[4 * D * j:4 * D * (j + 1), :].opt()])

        if True:
            AR.push()
            Winb = AR.alloc([128, KD, 1024], BF16)
            xng = [AR.alloc([128, KD, TC], BF16) for _ in range(2)]
            xhh = AR.alloc([128, NMI, TC + 8], F32)
            zst = AR.alloc([128, NMI, TC], F32)
            xct2 = AR.alloc([128, NMI, TC], F32)
            xcb2 = AR.alloc([128, NMI, TC], BF16)
            cvac = AR.alloc([128, NMI, TC], F32)
            Wxpb = AR.alloc([128, NMI, 33], BF16)
            xpjt = AR.alloc([33, TC], F32)
            lw = P.dma("sync", lambda e: e.dma_start(
                out=Winb[:], in_=d_winb.ap().rearrange("(k p) m -> p k m", p=128)))
            for mi in range(NMI):
                tw = P.dma("sync", lambda e, mi=mi: e.dma_start(
                    out=var_t[:, 0:33], in_=wxpc[128 * mi:128 * (mi + 1), :]))
                P.wait("scalar", LD, tw)
                sw = P.op("scalar", lambda e, mi=mi: e.activation(Wxpb[:, mi, :], var_t[:, 0:33], AF.Copy), SC)
                P.wait("sync", SC, sw)
            P.wait("tensor", LD, lw)
            P.op("gpsimd", lambda e: e.memset(xhh[:, :, 0:3], 0.0), PC)
            P.wait("scalar", PC, PC.n)
            conv_done = None
            sil_done = None
            cast_done = None
            tpj_prev = None
            spj_prev = None
            zs_spill = None
            xc_spill = None
            pj_spill = None
            bank_ev = {}
            mm_last = [None, None]
            for k in range(NCH):
                bk = k % 2
                r0 = 4 * D * (k // 4) + D * (k % 4)
                c0 = TC * k
                if mm_last[bk] is not None:
                    P.wait("sync", TM, mm_last[bk])
                P.wait("sync", CD, ag_done[k // 4])
                lx = P.dma("sync", lambda e, r0=r0, bk=bk: e.dma_start(
                    out=xng[bk][:],
                    in_=d_xng.ap()[r0:r0 + D, :].rearrange("(k p) t -> p k t", p=128)))
                if k > 0:
                    P.wait("scalar", VC, conv_done)
                    P.op("scalar", lambda e: e.activation(xhh[:, :, 0:3], xhh[:, :, TC:TC + 3], AF.Copy))
                P.wait("tensor", LD, lx)
                ev = []
                for mi in range(KD):
                    psn = mi % 4
                    if bank_ev.get(psn) is not None:
                        P.wait("tensor", SC, bank_ev[psn])
                    tmm = None
                    for kb in range(KD):
                        tmm = P.op("tensor", lambda e, kb=kb, mi=mi, bk=bk, psn=psn: e.matmul(
                            PS[psn][:], Winb[:, kb, 128 * mi:128 * (mi + 1)], xng[bk][:, kb, :],
                            start=(kb == 0), stop=(kb == KD - 1)), TM if kb == KD - 1 else None)
                    if mi == KD - 1:
                        mm_last[bk] = tmm
                    P.wait("scalar", TM, tmm)
                    if mi < NMI:
                        evv = P.op("scalar", lambda e, mi=mi, psn=psn: e.activation(
                            xhh[:, mi, 3:3 + TC], PS[psn][:], AF.Copy), SC)
                    else:
                        if zs_spill is not None and mi == NMI:
                            P.wait("scalar", LD, zs_spill)
                        evv = P.op("scalar", lambda e, mi=mi, psn=psn: e.activation(
                            zst[:, mi - NMI, :], PS[psn][:], AF.Silu), SC)
                    ev.append(evv)
                    bank_ev[psn] = evv
                P.wait("sync", SC, ev[-1])
                zs_spill = P.dma("sync", lambda e, c0=c0: e.dma_start(
                    out=d_z_r[:, :, c0:c0 + TC], in_=zst[:]))
                P.wait("vector", SC, ev[NMI - 1])
                if sil_done is not None:
                    P.wait("vector", SC, sil_done)
                cv = None
                for mi in range(NMI):
                    P.op("vector", lambda e, mi=mi: e.tensor_scalar(
                        cvac[:, mi, :], xhh[:, mi, 0:TC],
                        cwt[:, 4 * mi + 0:4 * mi + 1], vect[:, 4 * mi + 0:4 * mi + 1],
                        ALU.mult, ALU.add))
                    for kk in range(1, 4):
                        cv = P.op("vector", lambda e, mi=mi, kk=kk: e.scalar_tensor_tensor(
                            cvac[:, mi, :], xhh[:, mi, kk:kk + TC],
                            cwt[:, 4 * mi + kk:4 * mi + kk + 1], cvac[:, mi, :],
                            ALU.mult, ALU.add), VC if (mi == NMI - 1 and kk == 3) else None)
                conv_done = cv
                P.wait("scalar", VC, cv)
                if xc_spill is not None:
                    P.wait("scalar", LD, xc_spill)
                sil_done = P.op("scalar", lambda e: e.activation(
                    xct2[:].rearrange("p m t -> p (m t)"), cvac[:].rearrange("p m t -> p (m t)"),
                    AF.Silu), SC)
                if tpj_prev is not None:
                    P.wait("scalar", TM, tpj_prev)
                cast_done = P.op("scalar", lambda e: e.activation(
                    xcb2[:].rearrange("p m t -> p (m t)"), xct2[:].rearrange("p m t -> p (m t)"),
                    AF.Copy), SC)
                P.wait("sync", SC, sil_done)
                xc_spill = P.dma("sync", lambda e, c0=c0: e.dma_start(
                    out=d_xc_r[:, :, c0:c0 + TC], in_=xct2[:]))
                P.wait("tensor", SC, cast_done)
                if spj_prev is not None:
                    P.wait("tensor", SC, spj_prev)
                tpj = None
                for mi in range(NMI):
                    tpj = P.op("tensor", lambda e, mi=mi: e.matmul(
                        PS[4][0:33, :], Wxpb[:, mi, :], xcb2[:, mi, :],
                        start=(mi == 0), stop=(mi == NMI - 1)), TM if mi == NMI - 1 else None)
                tpj_prev = tpj
                P.wait("scalar", TM, tpj)
                if pj_spill is not None:
                    P.wait("scalar", LD, pj_spill)
                spj = P.op("scalar", lambda e: e.activation(xpjt[:], PS[4][0:33, :], AF.Copy), SC)
                spj_prev = spj
                P.wait("sync", SC, spj)
                pj_spill = P.dma("sync", lambda e, c0=c0: e.dma_start(
                    out=d_xpj.ap()[:, c0:c0 + TC], in_=xpjt[:]))
            P.barrier()
            AR.pop()

        art = P.coll("AllReduce", ALU.add, [d_xpj.ap().opt()], [d_xpjr.ap().opt()],
                     wait_ld=P.LD.n)
        P.barrier()

        if True:
            AR.push()
            Wob = AR.alloc([128, NMI, 1024], BF16)
            xct3 = [AR.alloc([128, NMI, TC], F32) for _ in range(2)]
            zsl = [AR.alloc([128, NMI, TC], F32) for _ in range(2)]
            dtt = [AR.alloc([128, NMI, TC], F32) for _ in range(2)]
            ut = [AR.alloc([128, NMI, TC], BF16) for _ in range(2)]
            Bxt = [AR.alloc([16, TC], F32) for _ in range(2)]
            Cxt = [AR.alloc([16, TC], F32) for _ in range(2)]
            dtr = [AR.alloc([1, TC], F32) for _ in range(2)]
            Bxb = [AR.alloc([16, TC], BF16) for _ in range(2)]
            Cxb = [AR.alloc([16, TC], BF16) for _ in range(2)]
            Cb = [AR.alloc([128, TC], BF16) for _ in range(2)]
            Bbb = [AR.alloc([128, TC], BF16) for _ in range(2)]
            ab = [AR.alloc([128, NMI * TC], F32) for _ in range(2)]
            bb3 = AR.alloc([128, NMI * TC], BF16)
            hta = [AR.alloc([128, NMI, TC], F32) for _ in range(2)]
            tmpb = [AR.alloc([128, NMI * TC], BF16) for _ in range(2)]
            yt = [AR.alloc([128, NMI, TC], BF16) for _ in range(2)]
            gt = [AR.alloc([128, NMI, TC], BF16) for _ in range(2)]
            eb2b = AR.alloc([128, NMI * TC], F32)
            rb2b = AR.alloc([128, NMI * TC], F32)
            wot = [AR.alloc([128, TC], F32) for _ in range(2)]

            lw = P.dma("sync", lambda e: e.dma_start(
                out=Wob[:], in_=d_wob.ap().rearrange("(k p) m -> p k m", p=128)))
            P.op("vector", lambda e: e.memset(st[:], 0.0), VC)
            P.wait("tensor", LD, lw)

            roll = {"a3": {}, "dve": {}, "psB_mm": {}, "psC_ev": {}, "tmp": {}, "tmp_rd": {}}
            S = {"l1": {}, "l2": {}, "vsp": {}, "u": {}, "bxb": {}, "tdt": {},
                 "b3l": {}, "vg": {}, "wol": {}}
            wo_bank_ev = {0: None, 1: None}

            def head(k):
                pk = k % 2
                kt = slice(TC * k, TC * (k + 1))
                if S["vg"].get(k - 2) is not None:
                    P.wait("sync", VC, S["vg"][k - 2])
                    P.wait("sync", PC, S["u"][k - 2])
                S["l1"][k] = P.dma("sync", lambda e, kt=kt, pk=pk: e.dma_start(
                    out=xct3[pk][:], in_=d_xc_r[:, :, kt]))
                S["l2"][k] = P.dma("sync", lambda e, kt=kt, pk=pk: e.dma_start(
                    out=zsl[pk][:], in_=d_z_r[:, :, kt]))
                if S["bxb"].get(k - 2) is not None:
                    P.wait("sync", SC, S["bxb"][k - 2])
                if S["tdt"].get(k - 2) is not None:
                    P.wait("sync", TM, S["tdt"][k - 2])
                lB = P.dma("sync", lambda e, kt=kt, pk=pk: e.dma_start(
                    out=Bxt[pk][:], in_=d_xpjr.ap()[0:16, kt]))
                lC = P.dma("sync", lambda e, kt=kt, pk=pk: e.dma_start(
                    out=Cxt[pk][:], in_=d_xpjr.ap()[16:32, kt]))
                lD = P.dma("sync", lambda e, kt=kt, pk=pk: e.dma_start(
                    out=dtr[pk][:], in_=d_xpjr.ap()[32:33, kt]))
                P.wait("tensor", LD, lD)
                if S["vsp"].get(k - 1) is not None:
                    P.wait("tensor", VC, S["vsp"][k - 1])
                tdt = P.op("tensor", lambda e, pk=pk: e.matmul(
                    PS[6 + pk][:], ones1t[:], dtr[pk][:], start=True, stop=True), TM)
                S["tdt"][k] = tdt
                P.wait("vector", TM, tdt)
                if k >= 2:
                    P.wait("vector", SC, roll["a3"][16 * (k - 2) + 15])
                    P.wait("vector", PC, S["u"][k - 2])
                for mi in range(NMI):
                    P.op("vector", lambda e, mi=mi, pk=pk: e.tensor_scalar(
                        dtt[pk][:, mi, :], PS[6 + pk][:],
                        vect[:, 4 * mi + 2:4 * mi + 3], vect[:, 4 * mi + 1:4 * mi + 2],
                        ALU.mult, ALU.add))
                vrl = P.op("vector", lambda e, pk=pk: e.tensor_scalar(
                    rb2b[:], dtt[pk][:].rearrange("p m t -> p (m t)"), 0.0, None, ALU.max), VC)
                P.wait("scalar", VC, vrl)
                P.op("scalar", lambda e, pk=pk: e.activation(
                    eb2b[:], dtt[pk][:].rearrange("p m t -> p (m t)"), AF.Abs))
                P.op("scalar", lambda e: e.activation(eb2b[:], eb2b[:], AF.Exp, scale=-1.0))
                ssp = P.op("scalar", lambda e: e.activation(eb2b[:], eb2b[:], AF.Ln, bias=one_t[:, 0:1]), SC)
                P.wait("vector", SC, ssp)
                vsp = P.op("vector", lambda e, pk=pk: e.tensor_tensor(
                    dtt[pk][:].rearrange("p m t -> p (m t)"), rb2b[:], eb2b[:], ALU.add), VC)
                S["vsp"][k] = vsp
                P.wait("gpsimd", VC, vsp)
                P.wait("gpsimd", LD, S["l1"][k])
                if S["b3l"].get(k - 2) is not None:
                    P.wait("gpsimd", VC, S["b3l"][k - 2])
                S["u"][k] = P.op("gpsimd", lambda e, pk=pk: e.tensor_tensor(
                    ut[pk][:].rearrange("p m t -> p (m t)"), dtt[pk][:].rearrange("p m t -> p (m t)"),
                    xct3[pk][:].rearrange("p m t -> p (m t)"), ALU.mult), PC)
                P.wait("scalar", LD, lC)
                if k >= 2:
                    P.wait("scalar", TM, roll["psB_mm"][16 * (k - 2) + 15])
                P.op("scalar", lambda e, pk=pk: e.activation(Bxb[pk][:], Bxt[pk][:], AF.Copy))
                S["bxb"][k] = P.op("scalar", lambda e, pk=pk: e.activation(Cxb[pk][:], Cxt[pk][:], AF.Copy), SC)

            head(0)
            for k in range(NCH):
                pk = k % 2
                if debug and "dty" in dbg_parts and k == 0:
                    P.wait("sync", VC, S["vsp"][0])
                    P.dma("sync", lambda e: e.dma_start(out=dbg["dbg_dt"], in_=dtt[0][:].rearrange("p m t -> p (m t)")))
                def ensure_bcast(g2, kk):
                    if g2 in roll["psB_mm"] or g2 >= 16 * NCH:
                        return
                    n2 = g2 % 16
                    pp2 = g2 % 2
                    psB2 = PS[0 + 2 * pp2]
                    psC2 = PS[1 + 2 * pp2]
                    P.wait("tensor", SC, S["bxb"][kk])
                    if roll["psC_ev"].get(g2 - 2) is not None:
                        P.wait("tensor", SC, roll["psC_ev"][g2 - 2])
                    P.op("tensor", lambda e, n2=n2, psB2=psB2, kk=kk: e.matmul(
                        psB2[:], selb[:, 128 * n2:128 * (n2 + 1)], Bxb[kk % 2][:], start=True, stop=True))
                    tbc = P.op("tensor", lambda e, n2=n2, psC2=psC2, kk=kk: e.matmul(
                        psC2[:], selb[:, 128 * n2:128 * (n2 + 1)], Cxb[kk % 2][:], start=True, stop=True), TM)
                    roll["psB_mm"][g2] = tbc
                    P.wait("scalar", TM, tbc)
                    if roll["tmp"].get(g2 - 2) is not None:
                        P.wait("scalar", PC, roll["tmp"][g2 - 2])
                    if roll["dve"].get(g2 - 2) is not None:
                        P.wait("scalar", VC, roll["dve"][g2 - 2])
                    P.op("scalar", lambda e, pp2=pp2, psB2=psB2: e.activation(Bbb[pp2][:], psB2[:], AF.Copy))
                    cev = P.op("scalar", lambda e, pp2=pp2, psC2=psC2: e.activation(Cb[pp2][:], psC2[:], AF.Copy), SC)
                    roll["psC_ev"][g2] = cev

                def emit_yadd(g2):
                    pp2 = g2 % 2
                    pk2 = (g2 // 16) % 2
                    P.wait("vector", PC, roll["tmp"][g2])
                    if g2 % 16 == 0:
                        v = P.op("vector", lambda e, pp2=pp2, pk2=pk2: e.tensor_scalar(
                            yt[pk2][:].rearrange("p m t -> p (m t)"),
                            tmpb[pp2][:], 0.0, None, ALU.add), VC)
                    else:
                        v = P.op("vector", lambda e, pp2=pp2, pk2=pk2: e.tensor_tensor(
                            yt[pk2][:].rearrange("p m t -> p (m t)"),
                            yt[pk2][:].rearrange("p m t -> p (m t)"),
                            tmpb[pp2][:], ALU.add), VC)
                    roll["tmp_rd"][g2] = v

                for n in range(NS):
                    g = 16 * k + n
                    pp = g % 2
                    ensure_bcast(g, k)
                    cev = roll["psC_ev"][g]
                    if n == 0:
                        P.wait("scalar", VC, S["vsp"][k])
                    sa = None
                    for mi in range(NMI):
                        sa = P.op("scalar", lambda e, mi=mi, n=n, pp=pp, pk=pk: e.activation(
                            ab[pp][:, TC * mi:TC * (mi + 1)], dtt[pk][:, mi, :], AF.Exp,
                            scale=Aneg[:, 16 * mi + n:16 * mi + n + 1]),
                            SC if mi == NMI - 1 else None)
                    roll["a3"][g] = sa
                    if n == 0:
                        P.wait("vector", PC, S["u"][k])
                    P.wait("vector", SC, cev)
                    if roll["tmp"].get(g - 2) is not None:
                        P.wait("vector", PC, roll["tmp"][g - 2])
                    b3m = P.op("vector", lambda e, pp=pp, pk=pk: e.tensor_tensor(
                        bb3[:].rearrange("p (m t) -> p m t", m=NMI),
                        ut[pk][:], bc_mid(Bbb[pp][:], NMI), ALU.mult), VC)
                    if n == NS - 1:
                        S["b3l"][k] = b3m
                    P.wait("vector", SC, roll["a3"][g])
                    for mi in range(NMI):
                        P.op("vector", lambda e, mi=mi, n=n, pp=pp: e.tensor_tensor_scan(
                            hta[pp][:, mi, :],
                            ab[pp][:, TC * mi:TC * (mi + 1)],
                            bb3[:, TC * mi:TC * (mi + 1)],
                            st[:, mi, n:n + 1], ALU.mult, ALU.add))
                    vsc = P.op("vector", lambda e, n=n, pp=pp: e.tensor_copy(
                        st[:, :, n:n + 1], hta[pp][:, :, TC - 1:TC]), VC)
                    roll["dve"][g] = vsc
                    P.wait("gpsimd", VC, vsc)
                    P.wait("gpsimd", SC, cev)
                    if roll["tmp_rd"].get(g - 2) is not None:
                        P.wait("gpsimd", VC, roll["tmp_rd"][g - 2])
                    tm_ = P.op("gpsimd", lambda e, pp=pp: e.tensor_tensor(
                        tmpb[pp][:].rearrange("p (m t) -> p m t", m=NMI),
                        hta[pp][:], bc_mid(Cb[pp][:], NMI), ALU.mult), PC)
                    roll["tmp"][g] = tm_
                    if g > 0 and (g - 1) not in roll["tmp_rd"]:
                        emit_yadd(g - 1)
                    ensure_bcast(g + 1, k if n + 1 < NS else k + 1)
                    if n == 8 and k + 1 < NCH:
                        head(k + 1)
                emit_yadd(16 * k + 15)
                P.wait("vector", LD, S["l1"][k])
                for mi in range(NMI):
                    P.op("vector", lambda e, mi=mi, pk=pk: e.scalar_tensor_tensor(
                        yt[pk][:, mi, :], xct3[pk][:, mi, :],
                        vect[:, 4 * mi + 3:4 * mi + 4], yt[pk][:, mi, :], ALU.mult, ALU.add))
                P.wait("vector", LD, S["l2"][k])
                if S["wol"].get(k - 2) is not None:
                    P.wait("vector", TM, S["wol"][k - 2])
                vg = P.op("vector", lambda e, pk=pk: e.tensor_tensor(
                    gt[pk][:].rearrange("p m t -> p (m t)"), yt[pk][:].rearrange("p m t -> p (m t)"),
                    zsl[pk][:].rearrange("p m t -> p (m t)"), ALU.mult), VC)
                S["vg"][k] = vg
                if debug and "dty" in dbg_parts and k == 0:
                    P.wait("sync", VC, vg)
                    P.dma("sync", lambda e: e.dma_start(out=dbg["dbg_y"], in_=yt[0][:].rearrange("p m t -> p (m t)")))
                P.wait("tensor", VC, vg)
                wo_dma = {}
                for mo in range(KD):
                    pw = mo % 2
                    psW = PS[4 + pw]
                    if wo_bank_ev[pw] is not None:
                        P.wait("tensor", SC, wo_bank_ev[pw])
                    two = None
                    for kd in range(NMI):
                        two = P.op("tensor", lambda e, kd=kd, mo=mo, pk=pk, psW=psW: e.matmul(
                            psW[:], Wob[:, kd, 128 * mo:128 * (mo + 1)], gt[pk][:, kd, :],
                            start=(kd == 0), stop=(kd == NMI - 1)), TM if kd == NMI - 1 else None)
                    if mo == KD - 1:
                        S["wol"][k] = two
                    P.wait("scalar", TM, two)
                    if wo_dma.get(mo - 2) is not None:
                        P.wait("scalar", WO, wo_dma[mo - 2])
                    swo = P.op("scalar", lambda e, pw=pw, psW=psW: e.activation(wot[pw][:], psW[:], AF.Copy), SC)
                    wo_bank_ev[pw] = swo
                    P.wait("sync", SC, swo)
                    wo_dma[mo] = P.dma("sync", lambda e, k=k, mo=mo, pw=pw: e.dma_start(
                        out=d_wop_j[k // 4].ap()[1024 * (k % 4) + 128 * mo:1024 * (k % 4) + 128 * (mo + 1), :],
                        in_=wot[pw][:]), ctr=WO)
                if k % 4 == 3:
                    P.wait("gpsimd", WO, 16 * KD * (k + 1))
                    CC.n += 1
                    P.op("gpsimd", lambda e, j=k // 4: e.collective_compute(
                        "ReduceScatter", ALU.add, GROUPS,
                        [d_wop_j[j].ap().opt()], [d_att_j[j].ap().opt()]).then_inc(CC.h, 1))
                if debug and "wop" in dbg_parts and k == 3:
                    P.wait("sync", LD, P.LD.n)
                    P.dma("sync", lambda e: e.dma_start(out=dbg["dbg_wop"], in_=d_wop_j[0].ap()))
            P.wait("gpsimd", CC, CC.n)
            P.op("gpsimd", lambda e: e.drain(fusable=False))
            P.barrier()
            AR.pop()

        if True:
            AR.push()
            T = alloc_ffn_scope()
            prev = {}
            oysp = [None, None]
            xs_vc = [None, None]
            xs_sc = [None, None]
            ln2_rd = None
            cen_rd = None
            for j in range(NJ):
                par = j % 2
                jt = slice(TC * j, TC * (j + 1))
                xs = T["xs"][par]
                if j >= 2:
                    P.wait("sync", VC, xs_vc[par])
                    P.wait("sync", SC, xs_sc[par])
                    P.wait("sync", LD, oysp[par])
                l1 = P.dma("sync", lambda e, jt=jt, xs=xs: e.dma_start(out=xs[:], in_=d_x1_r[:, :, jt]))
                if ln2_rd is not None:
                    P.wait("sync", VC, ln2_rd)
                l2 = P.dma("sync", lambda e, jt=jt: e.dma_start(out=T["ln2t"][:], in_=d_ln2_r[:, :, jt]))
                if cen_rd is not None:
                    P.wait("sync", VC, cen_rd)
                l3 = P.dma("sync", lambda e, j=j: e.dma_start(
                    out=T["cen"][:],
                    in_=d_att_j[j].ap().rearrange("(k p) t -> p k t", p=128)))
                P.wait("vector", LD, l3)
                ln2_rd = P.op("vector", lambda e, xs=xs: e.tensor_tensor(
                    xs[:].rearrange("p k t -> p (k t)"), xs[:].rearrange("p k t -> p (k t)"),
                    T["ln2t"][:].rearrange("p k t -> p (k t)"), ALU.add), VC)
                vx2 = P.op("vector", lambda e, xs=xs: e.tensor_tensor(
                    xs[:].rearrange("p k t -> p (k t)"), xs[:].rearrange("p k t -> p (k t)"),
                    T["cen"][:].rearrange("p k t -> p (k t)"), ALU.add), VC)
                if debug and "p3" in dbg_parts:
                    P.wait("sync", LD, l3)
                    P.dma("sync", lambda e, j=j: e.dma_start(
                        out=dbg["dbg_att"][D * j:D * (j + 1), :].rearrange("(k p) t -> p k t", p=128),
                        in_=T["cen"][:]))
                s_r, c2m = ln_stats(T, xs, wait_vc=vx2)
                xs_sc[par] = c2m
                P.wait("vector", SC, s_r)
                vx = ln_apply_plain(T, xs, T["xnb"])
                cen_rd = vx
                xs_vc[par] = vx
                P.wait("tensor", VC, vx)
                prev = ffn(T, par, prev)
                xv = prev["xv"]
                P.wait("sync", VC, xv)
                oysp[par] = P.dma("sync", lambda e, jt=jt, xs=xs: e.dma_start(
                    out=o_y_r[:, :, jt], in_=xs[:]))

        P.wait("sync", LD, P.LD.n)

        block = E(nc.Block())
        P.emit(block)

    return nc


_DEBUG = False
_NC_CACHE = {}


_DBG_PARTS = ("p3",)


def _get_nc():
    key = (bool(_DEBUG), tuple(_DBG_PARTS))
    if key not in _NC_CACHE:
        _NC_CACHE[key] = build_program(debug=key[0], dbg_parts=key[1])
    return _NC_CACHE[key]


def kernel(L0_attn_x, L0_attn_ln_g, L0_attn_ln_b, L0_attn_W_in, L0_attn_conv_w,
           L0_attn_conv_b, L0_attn_W_xp, L0_attn_W_dt, L0_attn_b_dt, L0_attn_A_log,
           L0_attn_D, L0_attn_W_out, L0_ffn_w_gate, L0_ffn_w_up, L0_ffn_w_down):
    f32 = np.float32
    x = np.asarray(L0_attn_x, f32)
    W_in = np.asarray(L0_attn_W_in, f32)
    W_out = np.asarray(L0_attn_W_out, f32)
    W_xp = np.asarray(L0_attn_W_xp, f32)
    conv_w = np.asarray(L0_attn_conv_w, f32)
    conv_b = np.asarray(L0_attn_conv_b, f32)
    W_dt = np.asarray(L0_attn_W_dt, f32)
    b_dt = np.asarray(L0_attn_b_dt, f32)
    A_log = np.asarray(L0_attn_A_log, f32)
    Dvec = np.asarray(L0_attn_D, f32)
    ln_g = np.asarray(L0_attn_ln_g, f32)
    ln_b = np.asarray(L0_attn_ln_b, f32)
    wg = np.ascontiguousarray(np.asarray(L0_ffn_w_gate, np.int32).T)
    wu = np.ascontiguousarray(np.asarray(L0_ffn_w_up, np.int32).T)
    wd = np.ascontiguousarray(np.asarray(L0_ffn_w_down, np.int32).T)

    sel = np.zeros((16, 2048), f32)
    for n in range(16):
        sel[n, 128 * n:128 * (n + 1)] = 1.0
    ones128 = np.ones((128, 128), f32)
    ones1 = np.ones((1, 128), f32)
    lngb = np.zeros((128, 16), f32)
    for kb in range(KD):
        lngb[:, 2 * kb] = ln_g[128 * kb:128 * (kb + 1)]
        lngb[:, 2 * kb + 1] = ln_b[128 * kb:128 * (kb + 1)]

    in_maps = []
    for c in range(8):
        b, r = divmod(c, 4)
        sl = slice(512 * r, 512 * (r + 1))
        xc = np.concatenate(
            [x[b, TC * (4 * j + r):TC * (4 * j + r + 1), :].T for j in range(NJ)], axis=1)
        winc_c = np.concatenate([W_in[:, sl], W_in[:, 2048 + sl.start:2048 + sl.stop]], axis=1)
        cwc = conv_w[sl].reshape(4, 128, 4).transpose(1, 0, 2).reshape(128, 16)
        avals = A_log[sl].reshape(4, 128, 16).transpose(1, 0, 2).reshape(128, 64)
        vv = np.stack([conv_b[sl], b_dt[sl], W_dt[0, sl], Dvec[sl]], axis=-1)
        vv = vv.reshape(4, 128, 4).transpose(1, 0, 2).reshape(128, 16)
        in_maps.append(dict(
            x_fm=np.ascontiguousarray(xc),
            winc=np.ascontiguousarray(winc_c),
            woc=np.ascontiguousarray(W_out[sl]),
            wxpc=np.ascontiguousarray(W_xp[sl]),
            wgT=wg, wuT=wu, wdT=wd,
            cwc=np.ascontiguousarray(cwc),
            avals=np.ascontiguousarray(avals),
            vecs=np.ascontiguousarray(vv),
            lngb=lngb, sel=sel, ones128=ones128, ones1=ones1,
        ))

    nc = _get_nc()
    res = run_bass_kernel_spmd(nc, in_maps, core_ids=list(range(8)))

    out = np.empty((B, L, D), f32)
    for c in range(8):
        b, r = divmod(c, 4)
        oy = np.asarray(res.results[c]["o_y"], f32)
        for j in range(NJ):
            out[b, TC * (4 * j + r):TC * (4 * j + r + 1), :] = oy[:, TC * j:TC * (j + 1)].T
    if _DEBUG:
        kernel._dbg = res.results
    return out


# version 27
# speedup vs baseline: 1.1275x; 1.0560x over previous
import dataclasses
from contextlib import ExitStack

import numpy as np

import concourse.bass as bass
import concourse.mybir as mybir
from concourse.bass_utils import run_bass_kernel_spmd

F32 = mybir.dt.float32
BF16 = mybir.dt.bfloat16
I32 = mybir.dt.int32
AF = mybir.ActivationFunctionType
ALU = mybir.AluOpType

B, L, D, DIN, H, NS = 2, 8192, 1024, 2048, 3072, 16
TC = 512
NCH = 16
NJ = 4
KD = 8
NMI = 4
HM = H // 128
EPS = 1e-5
GROUPS = [[0, 1, 2, 3], [4, 5, 6, 7]]
ENGINES = ("sync", "vector", "scalar", "tensor", "gpsimd")


def bc_mid(ap, rep):
    assert len(ap.ap) == 2
    return dataclasses.replace(ap, ap=[ap.ap[0], [0, rep], ap.ap[1]])


class Ctr:
    def __init__(self, h):
        self.h = h
        self.n = 0


class Bld:
    def __init__(self, nc, ctx):
        self.nc = nc
        self.ops = {e: [] for e in ENGINES}
        E = ctx.enter_context
        self.LD = Ctr(E(nc.semaphore("LD")))
        self.CC = Ctr(E(nc.semaphore("CC")))
        self.VC = Ctr(E(nc.semaphore("VC")))
        self.SC = Ctr(E(nc.semaphore("SC")))
        self.TM = Ctr(E(nc.semaphore("TM")))
        self.PC = Ctr(E(nc.semaphore("PC")))
        self.CD = Ctr(E(nc.semaphore("CD")))
        self.WQ = Ctr(E(nc.semaphore("WQ")))
        self.WO = Ctr(E(nc.semaphore("WO")))
        self.XN = Ctr(E(nc.semaphore("XN")))
        self.BAR = Ctr(E(nc.semaphore("BAR")))

    def op(self, eng, build, ctr=None):
        if ctr is None:
            self.ops[eng].append(build)
            return None
        ctr.n += 1
        tgt = ctr.n
        self.ops[eng].append(lambda e, b=build, c=ctr: b(e).then_inc(c.h, 1))
        return tgt

    def dma(self, eng, build, ctr=None):
        ctr = ctr or self.LD
        ctr.n += 16
        tgt = ctr.n
        self.ops[eng].append(lambda e, b=build, c=ctr: b(e).then_inc(c.h, 16))
        return tgt

    def wait(self, eng, ctr, val):
        if val is None:
            return
        self.ops[eng].append(lambda e, c=ctr, v=val: e.wait_ge(c.h, v))

    def barrier(self):
        self.wait("sync", self.LD, self.LD.n)
        self.wait("sync", self.WQ, self.WQ.n)
        self.wait("sync", self.WO, self.WO.n)
        self.wait("sync", self.XN, self.XN.n)
        tgt = self.BAR.n + len(ENGINES)
        for e in ENGINES:
            self.ops[e].append(lambda E: E.sem_inc(self.BAR.h, 1))
        self.BAR.n = tgt
        for e in ENGINES:
            self.ops[e].append(lambda E, v=tgt: E.wait_ge(self.BAR.h, v))

    def coll(self, kind, op_, ins, outs, wait_ld=None):
        if wait_ld is not None:
            self.wait("gpsimd", self.LD, wait_ld)
        self.CC.n += 1
        tgt = self.CC.n
        self.ops["gpsimd"].append(lambda e, k=kind, o=op_, i=ins, u=outs: e.collective_compute(
            k, o, GROUPS, i, u).then_inc(self.CC.h, 1))
        self.wait("gpsimd", self.CC, tgt)
        self.ops["gpsimd"].append(lambda e: e.drain(fusable=False))
        self.CD.n += 1
        cdt = self.CD.n
        self.ops["gpsimd"].append(lambda e, v=cdt: e.sem_inc(self.CD.h, 1))
        return cdt

    def emit(self, block):
        b = self
        @block.sync
        def _(eng):
            for f in b.ops["sync"]:
                f(eng)
        @block.vector
        def _(eng):
            for f in b.ops["vector"]:
                f(eng)
        @block.scalar
        def _(eng):
            for f in b.ops["scalar"]:
                f(eng)
        @block.tensor
        def _(eng):
            for f in b.ops["tensor"]:
                f(eng)
        @block.gpsimd
        def _(eng):
            for f in b.ops["gpsimd"]:
                f(eng)


class Arena:
    ESZ = {F32: 4, BF16: 2, I32: 4}

    def __init__(self, nc, ctx, name, words):
        self.h = ctx.enter_context(nc.sbuf_tensor(name, [128, words], F32))
        self.hb = self.h.bitcast(BF16)
        self.hi = self.h.bitcast(I32)
        self.cap = words * 4
        self.off = 0
        self.stack = []

    def push(self):
        self.stack.append(self.off)

    def pop(self):
        self.off = self.stack.pop()

    def alloc(self, shape, dt=F32):
        n = 1
        for s in shape[1:]:
            n *= s
        esz = self.ESZ[dt]
        self.off = (self.off + 63) & ~63
        start = self.off
        self.off += n * esz
        assert self.off <= self.cap, f"arena overflow: {self.off} > {self.cap}"
        if dt == F32:
            base = self.h[:, start // 4: start // 4 + n]
        elif dt == I32:
            base = self.hi[:, start // 4: start // 4 + n]
        else:
            base = self.hb[:, start // 2: start // 2 + n]
        if shape[0] != 128:
            base = base[0:shape[0], :]
        if len(shape) == 3:
            base = base.rearrange("p (a b) -> p a b", a=shape[1])
        return base


def build_program(debug=False, dbg_parts=("p1","scan","wop","rs","p3","dty")):
    nc = bass.Bass("TRN2", target_bir_lowering=False, detect_race_conditions=False)

    def din(name, shape, dt=F32):
        return nc.dram_tensor(name, shape, dt, kind="ExternalInput").ap()

    x_fm = din("x_fm", [D, NJ * TC])
    winc = din("winc", [D, 1024])
    woc = din("woc", [512, D])
    wxpc = din("wxpc", [512, 33])
    wgT = din("wgT", [D, H], I32)
    wuT = din("wuT", [D, H], I32)
    wdT = din("wdT", [H, D], I32)
    cwc = din("cwc", [128, 16])
    avals = din("avals", [128, 64])
    vecs = din("vecs", [128, 16])
    lngb = din("lngb", [128, 16])
    sel_i = din("sel", [16, 2048])
    ones128 = din("ones128", [128, 128])
    ones1 = din("ones1", [1, 128])

    o_y = nc.dram_tensor("o_y", [D, NJ * TC], F32, kind="ExternalOutput").ap()

    def dint(name, shape, dt=F32):
        return nc.dram_tensor(name, shape, dt, kind="Internal")

    d_wg = dint("d_wg", [D, H], BF16)
    d_wu = dint("d_wu", [D, H], BF16)
    d_wd = dint("d_wd", [H, D], BF16)
    d_winb = dint("d_winb", [D, 1024], BF16)
    d_wob = dint("d_wob", [512, D], BF16)
    d_x1 = dint("d_x1", [D, NJ * TC], F32)
    d_ln2 = dint("d_ln2", [D, NJ * TC], F32)
    d_xn = dint("d_xn", [NJ * D, TC], BF16)
    d_xng = dint("d_xng", [NJ * 4 * D, TC], BF16)
    d_xc = dint("d_xc", [512, L], F32)
    d_z = dint("d_z", [512, L], F32)
    d_xpj = dint("d_xpj", [33, L], F32)
    d_xpjr = dint("d_xpjr", [33, L], F32)
    d_wop_j = [dint(f"d_wop{j}", [4 * D, TC], F32) for j in range(NJ)]
    d_att_j = [dint(f"d_att{j}", [D, TC], F32) for j in range(NJ)]

    dbg = {}
    if debug:
        for name, shape, dt in [
            ("dbg_xn", [D, NJ * TC], BF16), ("dbg_x1", [D, NJ * TC], F32),
            ("dbg_dt", [128, NMI * TC], F32),
            ("dbg_y", [128, NMI * TC], F32), ("dbg_att", [NJ * D, TC], F32),
            ("dbg_u", [128, NMI * TC], F32), ("dbg_a0", [128, NMI * TC], F32),
            ("dbg_b0", [128, NMI * TC], F32), ("dbg_h0", [128, NMI * TC], F32),
            ("dbg_y0", [128, NMI * TC], F32), ("dbg_wop", [4 * D, TC], F32),
            ("dbg_wop2", [4 * D, TC], F32),
        ]:
            dbg[name] = nc.dram_tensor(name, shape, dt, kind="ExternalOutput").ap()

    x_fm_r = x_fm.rearrange("(k p) t -> p k t", p=128)
    d_x1_r = d_x1.ap().rearrange("(k p) t -> p k t", p=128)
    d_ln2_r = d_ln2.ap().rearrange("(k p) t -> p k t", p=128)
    d_xc_r = d_xc.ap().rearrange("(m p) t -> p m t", p=128)
    d_z_r = d_z.ap().rearrange("(m p) t -> p m t", p=128)
    o_y_r = o_y.rearrange("(k p) t -> p k t", p=128)

    with ExitStack() as ctx:
        E = ctx.enter_context
        P = Bld(nc, ctx)
        LD, CC, VC, SC, TM, PC, CD, WQ, WO, XN = P.LD, P.CC, P.VC, P.SC, P.TM, P.PC, P.CD, P.WQ, P.WO, P.XN

        def sb(name, shape, dt=F32):
            return E(nc.sbuf_tensor(name, shape, dt))

        mu_t = sb("mu_t", [128, TC])
        ms_t = sb("ms_t", [128, TC])
        var_t = sb("var_t", [128, TC])
        rstd_t = sb("rstd_t", [128, TC])
        f1buf = sb("f1buf", [128, TC])
        cwt = sb("cwt", [128, 16])
        Aneg = sb("Aneg", [128, 64])
        vect = sb("vect", [128, 16])
        lngbt = sb("lngbt", [128, 16])
        onesb = sb("onesb", [128, 128], BF16)
        ones1t = sb("ones1t", [1, 128])
        eps_t = sb("eps_t", [128, 1])
        one_t = sb("one_t", [128, 1])
        selb = sb("selb", [16, 2048], BF16)
        st = sb("st", [128, NMI, NS])

        PS = [E(nc.psum_tensor(f"ps{i}", [128, TC], F32)) for i in range(8)]
        AR = Arena(nc, ctx, "arena", 43 * 1024)

        if True:
            AR.push()
            deqi = [AR.alloc([128, H], I32) for _ in range(2)]
            deqb = [AR.alloc([128, H], BF16) for _ in range(2)]
            wfl = AR.alloc([128, 1024], F32)
            p0sel = AR.alloc([16, 2048], F32)

            P.op("vector", lambda e: e.memset(eps_t[:], float(EPS)))
            vcn = P.op("vector", lambda e: e.memset(one_t[:], 1.0), VC)
            P.wait("scalar", VC, vcn)
            P.dma("sync", lambda e: e.dma_start(out=cwt[:], in_=cwc))
            P.dma("sync", lambda e: e.dma_start(out=vect[:], in_=vecs))
            P.dma("sync", lambda e: e.dma_start(out=lngbt[:], in_=lngb))
            P.dma("sync", lambda e: e.dma_start(out=ones1t[:], in_=ones1))
            t5 = P.dma("sync", lambda e: e.dma_start(out=mu_t[:, 0:64], in_=avals))
            P.wait("scalar", LD, t5)
            s1 = P.op("scalar", lambda e: e.activation(ms_t[:, 0:64], mu_t[:, 0:64], AF.Exp), SC)
            P.wait("vector", SC, s1)
            P.op("vector", lambda e: e.tensor_scalar(Aneg[:], ms_t[:, 0:64], -1.0, None, ALU.mult), VC)
            t6 = P.dma("sync", lambda e: e.dma_start(out=rstd_t[:, 0:128], in_=ones128))
            P.wait("scalar", LD, t6)
            P.op("scalar", lambda e: e.activation(onesb[:], rstd_t[:, 0:128], AF.Copy), SC)
            t7 = P.dma("sync", lambda e: e.dma_start(out=p0sel[:], in_=sel_i))
            P.wait("scalar", LD, t7)
            P.op("scalar", lambda e: e.activation(selb[:], p0sel[:], AF.Copy), SC)

            def dequant(src_ap, dst_t, nblocks, width):
                vvs = [None] * nblocks
                dmas = [None] * nblocks
                for i in range(nblocks):
                    bb = i % 2
                    if i >= 2:
                        P.wait("sync", VC, vvs[i - 2])
                    tl = P.dma("sync", lambda e, i=i, bb=bb, s=src_ap, w=width: e.dma_start(
                        out=deqi[bb][:, 0:w], in_=s[128 * i:128 * (i + 1), 0:w]))
                    P.wait("vector", LD, tl)
                    if i >= 2:
                        P.wait("vector", LD, dmas[i - 2])
                    vvs[i] = P.op("vector", lambda e, bb=bb, w=width: e.tensor_scalar(
                        deqb[bb][:, 0:w], deqi[bb][:, 0:w], 1.0 / 127.0, None, ALU.mult), VC)
                    P.wait("sync", VC, vvs[i])
                    dmas[i] = P.dma("sync", lambda e, i=i, bb=bb, d=dst_t, w=width: e.dma_start(
                        out=d.ap()[128 * i:128 * (i + 1), 0:w], in_=deqb[bb][:, 0:w]))

            dequant(wgT, d_wg, KD, H)
            dequant(wuT, d_wu, KD, H)
            dequant(wdT, d_wd, HM, D)

            def stage(src, dst, nblocks):
                prev = {"sv": None, "dma": None}
                for i in range(nblocks):
                    if prev["sv"] is not None:
                        P.wait("sync", SC, prev["sv"])
                    tl = P.dma("sync", lambda e, i=i, s=src: e.dma_start(
                        out=wfl[:], in_=s[128 * i:128 * (i + 1), :]))
                    P.wait("scalar", LD, tl)
                    if prev["dma"] is not None:
                        P.wait("scalar", LD, prev["dma"])
                    sv = P.op("scalar", lambda e: e.activation(deqb[0][:, 0:1024], wfl[:], AF.Copy), SC)
                    prev["sv"] = sv
                    P.wait("sync", SC, sv)
                    prev["dma"] = P.dma("sync", lambda e, i=i, d=dst: e.dma_start(
                        out=d.ap()[128 * i:128 * (i + 1), :], in_=deqb[0][:, 0:1024]))
            stage(winc, d_winb, KD)
            stage(woc, d_wob, NMI)
            P.barrier()
            AR.pop()

        def ln_stats(T, src, wait_vc=None):
            if wait_vc is not None:
                P.wait("scalar", VC, wait_vc)
            P.op("scalar", lambda e: e.activation(T["xbb"][:], src[:], AF.Copy))
            c2 = P.op("scalar", lambda e: e.activation(T["xnb"][:], src[:], AF.Square), SC)
            P.wait("tensor", SC, c2)
            for kb in range(KD):
                P.op("tensor", lambda e, kb=kb: e.matmul(
                    PS[4][:], onesb[:], T["xbb"][:, kb, :], start=(kb == 0), stop=(kb == KD - 1)),
                    TM if kb == KD - 1 else None)
            tm2 = None
            for kb in range(KD):
                tm2 = P.op("tensor", lambda e, kb=kb: e.matmul(
                    PS[5][:], onesb[:], T["xnb"][:, kb, :], start=(kb == 0), stop=(kb == KD - 1)),
                    TM if kb == KD - 1 else None)
            P.wait("vector", TM, tm2)
            P.op("vector", lambda e: e.tensor_scalar(mu_t[:], PS[4][:], 1.0 / D, None, ALU.mult))
            P.op("vector", lambda e: e.tensor_scalar(ms_t[:], PS[5][:], 1.0 / D, None, ALU.mult))
            P.op("vector", lambda e: e.tensor_tensor(var_t[:], mu_t[:], mu_t[:], ALU.mult))
            v1 = P.op("vector", lambda e: e.tensor_tensor(var_t[:], ms_t[:], var_t[:], ALU.subtract), VC)
            P.wait("scalar", VC, v1)
            P.op("scalar", lambda e: e.activation(rstd_t[:], var_t[:], AF.Ln, bias=eps_t[:, 0:1]))
            s2 = P.op("scalar", lambda e: e.activation(rstd_t[:], rstd_t[:], AF.Exp, scale=-0.5), SC)
            return s2, c2

        def ffn(T, par, prev):
            xs, xnb, fbuf = T["xs"][par], T["xnb"], T["fbuf"]
            wgt_s, wut_s, wdt_s = T["wgt_s"], T["wut_s"], T["wdt_s"]
            if prev.get("tds7") is not None:
                P.wait("sync", TM, prev["tds7"])
            tus = [None] * HM
            fvs = [None] * HM
            f1s = [None] * HM
            for mh in range(HM):
                bw = mh % 2
                pg = PS[0 + 2 * bw]
                pu = PS[1 + 2 * bw]
                if mh >= 2:
                    P.wait("sync", TM, tus[mh - 2])
                P.dma("sync", lambda e, mh=mh, bw=bw: e.dma_start(
                    out=wgt_s[bw][:],
                    in_=d_wg.ap()[:, 128 * mh:128 * (mh + 1)].rearrange("(k p) m -> p k m", p=128)), ctr=WQ)
                lw2 = P.dma("sync", lambda e, mh=mh, bw=bw: e.dma_start(
                    out=wut_s[bw][:],
                    in_=d_wu.ap()[:, 128 * mh:128 * (mh + 1)].rearrange("(k p) m -> p k m", p=128)), ctr=WQ)
                P.wait("tensor", WQ, lw2)
                if mh >= 2:
                    P.wait("tensor", SC, f1s[mh - 2])
                    P.wait("tensor", VC, fvs[mh - 2])
                elif prev.get("f1s7") is not None:
                    P.wait("tensor", SC, prev["f1s7"])
                    P.wait("tensor", VC, prev["fvs7"])
                tg = None
                for kb in range(KD):
                    tg = P.op("tensor", lambda e, kb=kb, bw=bw, pg=pg: e.matmul(
                        pg[:], wgt_s[bw][:, kb, :], xnb[:, kb, :],
                        start=(kb == 0), stop=(kb == KD - 1)), TM if kb == KD - 1 else None)
                for kb in range(KD):
                    tus[mh] = P.op("tensor", lambda e, kb=kb, bw=bw, pu=pu: e.matmul(
                        pu[:], wut_s[bw][:, kb, :], xnb[:, kb, :],
                        start=(kb == 0), stop=(kb == KD - 1)), TM if kb == KD - 1 else None)
                P.wait("scalar", TM, tg)
                if mh >= 1:
                    P.wait("scalar", VC, fvs[mh - 1])
                elif prev.get("fvs7") is not None:
                    P.wait("scalar", VC, prev["fvs7"])
                f1s[mh] = P.op("scalar", lambda e, pg=pg: e.activation(f1buf[:], pg[:], AF.Silu), SC)
                P.wait("vector", SC, f1s[mh])
                P.wait("vector", TM, tus[mh])
                if mh == 0 and prev.get("tds7") is not None:
                    P.wait("vector", TM, prev["tds7"])
                fvs[mh] = P.op("vector", lambda e, mh=mh, pu=pu: e.tensor_tensor(
                    fbuf[:, mh, :], f1buf[:], pu[:], ALU.mult), VC)
            xv = None
            tds = [None] * KD
            for mo in range(KD):
                bw = mo % 2
                pd = PS[6 + bw]
                if mo >= 2:
                    P.wait("sync", TM, tds[mo - 2])
                lw = P.dma("sync", lambda e, mo=mo, bw=bw: e.dma_start(
                    out=wdt_s[bw][:],
                    in_=d_wd.ap()[:, 128 * mo:128 * (mo + 1)].rearrange("(k p) m -> p k m", p=128)), ctr=WQ)
                P.wait("tensor", WQ, lw)
                if mo == 0:
                    P.wait("tensor", VC, fvs[-1])
                if mo >= 2:
                    P.wait("tensor", VC, xv)
                elif prev.get("xv7") is not None:
                    P.wait("tensor", VC, prev["xv7"])
                for kh in range(HM):
                    tds[mo] = P.op("tensor", lambda e, kh=kh, bw=bw, pd=pd: e.matmul(
                        pd[:], wdt_s[bw][:, kh, :], fbuf[:, kh, :],
                        start=(kh == 0), stop=(kh == HM - 1)), TM if kh == HM - 1 else None)
                P.wait("vector", TM, tds[mo])
                xv = P.op("vector", lambda e, mo=mo, pd=pd, xs=xs: e.scalar_tensor_tensor(
                    xs[:, mo, :], pd[:], 0.5, xs[:, mo, :], ALU.mult, ALU.add), VC)
            return dict(xv=xv, tds7=tds[-1], f1s7=f1s[-1], fvs7=fvs[-1], xv7=xv)

        def ln_apply_plain(T, src, dst_bf):
            P.op("vector", lambda e, src=src: e.tensor_tensor(
                T["cen"][:], src[:], bc_mid(mu_t[:], KD), ALU.subtract))
            return P.op("vector", lambda e, dst_bf=dst_bf: e.tensor_tensor(
                dst_bf[:], T["cen"][:], bc_mid(rstd_t[:], KD), ALU.mult), VC)

        def alloc_ffn_scope():
            T = {}
            T["xs"] = [AR.alloc([128, KD, TC], F32) for _ in range(2)]
            T["ln2t"] = AR.alloc([128, KD, TC], F32)
            T["cen"] = AR.alloc([128, KD, TC], F32)
            T["xbb"] = AR.alloc([128, KD, TC], BF16)
            T["xnb"] = AR.alloc([128, KD, TC], BF16)
            T["fbuf"] = AR.alloc([128, HM, TC], BF16)
            T["wgt_s"] = [AR.alloc([128, KD, 128], BF16) for _ in range(2)]
            T["wut_s"] = [AR.alloc([128, KD, 128], BF16) for _ in range(2)]
            T["wdt_s"] = [AR.alloc([128, HM, 128], BF16) for _ in range(2)]
            return T

        if True:
            AR.push()
            T = alloc_ffn_scope()
            prev = {}
            x1sp = [None, None]
            xs_vc = [None, None]
            xs_sc = [None, None]
            for j in range(NJ):
                par = j % 2
                jt = slice(TC * j, TC * (j + 1))
                xs = T["xs"][par]
                if j >= 2:
                    P.wait("sync", VC, xs_vc[par])
                    P.wait("sync", SC, xs_sc[par])
                    P.wait("sync", LD, x1sp[par])
                lx = P.dma("sync", lambda e, jt=jt, xs=xs: e.dma_start(out=xs[:], in_=x_fm_r[:, :, jt]))
                P.wait("scalar", LD, lx)
                P.wait("vector", LD, lx)
                s_r, _ = ln_stats(T, xs)
                P.wait("vector", SC, s_r)
                vx = ln_apply_plain(T, xs, T["xnb"])
                P.wait("tensor", VC, vx)
                prev = ffn(T, par, prev)
                xv = prev["xv"]
                P.wait("sync", VC, xv)
                x1sp[par] = P.dma("sync", lambda e, jt=jt, xs=xs: e.dma_start(
                    out=d_x1_r[:, :, jt], in_=xs[:]))
                s_r, c2m = ln_stats(T, xs, wait_vc=xv)
                xs_sc[par] = c2m
                P.wait("vector", SC, s_r)
                xs_vc[par] = P.op("vector", lambda e, xs=xs: e.tensor_tensor(
                    T["ln2t"][:], xs[:], bc_mid(mu_t[:], KD), ALU.subtract), VC)
                v2 = P.op("vector", lambda e: e.tensor_tensor(
                    T["ln2t"][:], T["ln2t"][:], bc_mid(rstd_t[:], KD), ALU.mult), VC)
                P.wait("sync", VC, v2)
                P.dma("sync", lambda e, jt=jt: e.dma_start(out=d_ln2_r[:, :, jt], in_=T["ln2t"][:]))
                s_r, _ = ln_stats(T, T["ln2t"], wait_vc=v2)
                P.wait("vector", SC, s_r)
                P.op("vector", lambda e: e.tensor_tensor(
                    T["cen"][:], T["ln2t"][:], bc_mid(mu_t[:], KD), ALU.subtract))
                P.op("vector", lambda e: e.tensor_tensor(
                    T["cen"][:], T["cen"][:], bc_mid(rstd_t[:], KD), ALU.mult))
                v3 = None
                for kb in range(KD):
                    v3 = P.op("vector", lambda e, kb=kb: e.tensor_scalar(
                        T["xnb"][:, kb, :], T["cen"][:, kb, :],
                        lngbt[:, 2 * kb:2 * kb + 1], lngbt[:, 2 * kb + 1:2 * kb + 2],
                        ALU.mult, ALU.add), VC)
                P.wait("sync", VC, v3)
                P.dma("sync", lambda e, j=j: e.dma_start(
                    out=d_xn.ap()[D * j:D * (j + 1), :].rearrange("(k p) t -> p k t", p=128),
                    in_=T["xnb"][:]), ctr=XN)
                if debug and "p1" in dbg_parts:
                    P.dma("sync", lambda e, jt=jt: e.dma_start(
                        out=dbg["dbg_xn"].rearrange("(k p) t -> p k t", p=128)[:, :, jt], in_=T["xnb"][:]))
                    P.dma("sync", lambda e, jt=jt, xs=xs: e.dma_start(
                        out=dbg["dbg_x1"].rearrange("(k p) t -> p k t", p=128)[:, :, jt], in_=xs[:]))
            P.barrier()
            AR.pop()
        ag_done = [None] * NJ
        for j in range(NJ):
            ag_done[j] = P.coll(
                "AllGather", ALU.bypass,
                [d_xn.ap()[D * j:D * (j + 1), :].opt()],
                [d_xng.ap()[4 * D * j:4 * D * (j + 1), :].opt()])

        if True:
            AR.push()
            Winb = AR.alloc([128, KD, 1024], BF16)
            xng = [AR.alloc([128, KD, TC], BF16) for _ in range(2)]
            xhh = AR.alloc([128, NMI, TC + 8], F32)
            zst = AR.alloc([128, NMI, TC], F32)
            xct2 = AR.alloc([128, NMI, TC], F32)
            xcb2 = AR.alloc([128, NMI, TC], BF16)
            cvac = AR.alloc([128, NMI, TC], F32)
            Wxpb = AR.alloc([128, NMI, 33], BF16)
            xpjt = AR.alloc([33, TC], F32)
            lw = P.dma("sync", lambda e: e.dma_start(
                out=Winb[:], in_=d_winb.ap().rearrange("(k p) m -> p k m", p=128)))
            for mi in range(NMI):
                tw = P.dma("sync", lambda e, mi=mi: e.dma_start(
                    out=var_t[:, 0:33], in_=wxpc[128 * mi:128 * (mi + 1), :]))
                P.wait("scalar", LD, tw)
                sw = P.op("scalar", lambda e, mi=mi: e.activation(Wxpb[:, mi, :], var_t[:, 0:33], AF.Copy), SC)
                P.wait("sync", SC, sw)
            P.wait("tensor", LD, lw)
            P.op("gpsimd", lambda e: e.memset(xhh[:, :, 0:3], 0.0), PC)
            P.wait("scalar", PC, PC.n)
            conv_done = None
            sil_done = None
            cast_done = None
            tpj_prev = None
            spj_prev = None
            zs_spill = None
            xc_spill = None
            pj_spill = None
            bank_ev = {}
            mm_last = [None, None]
            for k in range(NCH):
                bk = k % 2
                r0 = 4 * D * (k // 4) + D * (k % 4)
                c0 = TC * k
                if mm_last[bk] is not None:
                    P.wait("sync", TM, mm_last[bk])
                P.wait("sync", CD, ag_done[k // 4])
                lx = P.dma("sync", lambda e, r0=r0, bk=bk: e.dma_start(
                    out=xng[bk][:],
                    in_=d_xng.ap()[r0:r0 + D, :].rearrange("(k p) t -> p k t", p=128)))
                if k > 0:
                    P.wait("scalar", VC, conv_done)
                    P.op("scalar", lambda e: e.activation(xhh[:, :, 0:3], xhh[:, :, TC:TC + 3], AF.Copy))
                P.wait("tensor", LD, lx)
                ev = []
                for mi in range(KD):
                    psn = mi % 4
                    if bank_ev.get(psn) is not None:
                        P.wait("tensor", SC, bank_ev[psn])
                    tmm = None
                    for kb in range(KD):
                        tmm = P.op("tensor", lambda e, kb=kb, mi=mi, bk=bk, psn=psn: e.matmul(
                            PS[psn][:], Winb[:, kb, 128 * mi:128 * (mi + 1)], xng[bk][:, kb, :],
                            start=(kb == 0), stop=(kb == KD - 1)), TM if kb == KD - 1 else None)
                    if mi == KD - 1:
                        mm_last[bk] = tmm
                    P.wait("scalar", TM, tmm)
                    if mi < NMI:
                        evv = P.op("scalar", lambda e, mi=mi, psn=psn: e.activation(
                            xhh[:, mi, 3:3 + TC], PS[psn][:], AF.Copy), SC)
                    else:
                        if zs_spill is not None and mi == NMI:
                            P.wait("scalar", LD, zs_spill)
                        evv = P.op("scalar", lambda e, mi=mi, psn=psn: e.activation(
                            zst[:, mi - NMI, :], PS[psn][:], AF.Silu), SC)
                    ev.append(evv)
                    bank_ev[psn] = evv
                P.wait("sync", SC, ev[-1])
                zs_spill = P.dma("sync", lambda e, c0=c0: e.dma_start(
                    out=d_z_r[:, :, c0:c0 + TC], in_=zst[:]))
                P.wait("vector", SC, ev[NMI - 1])
                if sil_done is not None:
                    P.wait("vector", SC, sil_done)
                cv = None
                for mi in range(NMI):
                    P.op("vector", lambda e, mi=mi: e.tensor_scalar(
                        cvac[:, mi, :], xhh[:, mi, 0:TC],
                        cwt[:, 4 * mi + 0:4 * mi + 1], vect[:, 4 * mi + 0:4 * mi + 1],
                        ALU.mult, ALU.add))
                    for kk in range(1, 4):
                        cv = P.op("vector", lambda e, mi=mi, kk=kk: e.scalar_tensor_tensor(
                            cvac[:, mi, :], xhh[:, mi, kk:kk + TC],
                            cwt[:, 4 * mi + kk:4 * mi + kk + 1], cvac[:, mi, :],
                            ALU.mult, ALU.add), VC if (mi == NMI - 1 and kk == 3) else None)
                conv_done = cv
                P.wait("scalar", VC, cv)
                if xc_spill is not None:
                    P.wait("scalar", LD, xc_spill)
                sil_done = P.op("scalar", lambda e: e.activation(
                    xct2[:].rearrange("p m t -> p (m t)"), cvac[:].rearrange("p m t -> p (m t)"),
                    AF.Silu), SC)
                if tpj_prev is not None:
                    P.wait("scalar", TM, tpj_prev)
                cast_done = P.op("scalar", lambda e: e.activation(
                    xcb2[:].rearrange("p m t -> p (m t)"), xct2[:].rearrange("p m t -> p (m t)"),
                    AF.Copy), SC)
                P.wait("sync", SC, sil_done)
                xc_spill = P.dma("sync", lambda e, c0=c0: e.dma_start(
                    out=d_xc_r[:, :, c0:c0 + TC], in_=xct2[:]))
                P.wait("tensor", SC, cast_done)
                if spj_prev is not None:
                    P.wait("tensor", SC, spj_prev)
                tpj = None
                for mi in range(NMI):
                    tpj = P.op("tensor", lambda e, mi=mi: e.matmul(
                        PS[4][0:33, :], Wxpb[:, mi, :], xcb2[:, mi, :],
                        start=(mi == 0), stop=(mi == NMI - 1)), TM if mi == NMI - 1 else None)
                tpj_prev = tpj
                P.wait("scalar", TM, tpj)
                if pj_spill is not None:
                    P.wait("scalar", LD, pj_spill)
                spj = P.op("scalar", lambda e: e.activation(xpjt[:], PS[4][0:33, :], AF.Copy), SC)
                spj_prev = spj
                P.wait("sync", SC, spj)
                pj_spill = P.dma("sync", lambda e, c0=c0: e.dma_start(
                    out=d_xpj.ap()[:, c0:c0 + TC], in_=xpjt[:]))
            P.barrier()
            AR.pop()

        art = P.coll("AllReduce", ALU.add, [d_xpj.ap().opt()], [d_xpjr.ap().opt()],
                     wait_ld=P.LD.n)
        P.barrier()

        if True:
            AR.push()
            Wob = AR.alloc([128, NMI, 1024], BF16)
            xct3 = [AR.alloc([128, NMI, TC], F32) for _ in range(2)]
            zsl = [AR.alloc([128, NMI, TC], F32) for _ in range(2)]
            dtt = [AR.alloc([128, NMI, TC], F32) for _ in range(2)]
            ut = [AR.alloc([128, NMI, TC], BF16) for _ in range(2)]
            Bxt = [AR.alloc([16, TC], F32) for _ in range(2)]
            Cxt = [AR.alloc([16, TC], F32) for _ in range(2)]
            dtr = [AR.alloc([1, TC], F32) for _ in range(2)]
            Bxb = [AR.alloc([16, TC], BF16) for _ in range(2)]
            Cxb = [AR.alloc([16, TC], BF16) for _ in range(2)]
            Cb = [AR.alloc([128, TC], BF16) for _ in range(2)]
            Bbb = [AR.alloc([128, TC], BF16) for _ in range(2)]
            ab = [AR.alloc([128, NMI * TC], F32) for _ in range(2)]
            bb3 = AR.alloc([128, NMI * TC], BF16)
            hta = [AR.alloc([128, NMI, TC], F32) for _ in range(2)]
            tmpb = [AR.alloc([128, NMI * TC], BF16) for _ in range(2)]
            yt = [AR.alloc([128, NMI, TC], BF16) for _ in range(2)]
            gt = [AR.alloc([128, NMI, TC], BF16) for _ in range(2)]
            eb2b = AR.alloc([128, NMI * TC], F32)
            rb2b = AR.alloc([128, NMI * TC], F32)
            wot = [AR.alloc([128, TC], F32) for _ in range(2)]

            lw = P.dma("sync", lambda e: e.dma_start(
                out=Wob[:], in_=d_wob.ap().rearrange("(k p) m -> p k m", p=128)))
            P.op("vector", lambda e: e.memset(st[:], 0.0), VC)
            P.wait("tensor", LD, lw)

            roll = {"a3": {}, "dve": {}, "psB_mm": {}, "psC_ev": {}, "tmp": {}, "tmp_rd": {}}
            S = {"l1": {}, "l2": {}, "vsp": {}, "u": {}, "bxb": {}, "tdt": {},
                 "b3l": {}, "vg": {}, "wol": {}}
            wo_bank_ev = {0: None, 1: None}

            def head(k):
                pk = k % 2
                kt = slice(TC * k, TC * (k + 1))
                if S["vg"].get(k - 2) is not None:
                    P.wait("sync", VC, S["vg"][k - 2])
                    P.wait("sync", PC, S["u"][k - 2])
                S["l1"][k] = P.dma("sync", lambda e, kt=kt, pk=pk: e.dma_start(
                    out=xct3[pk][:], in_=d_xc_r[:, :, kt]))
                S["l2"][k] = P.dma("sync", lambda e, kt=kt, pk=pk: e.dma_start(
                    out=zsl[pk][:], in_=d_z_r[:, :, kt]))
                if S["bxb"].get(k - 2) is not None:
                    P.wait("sync", SC, S["bxb"][k - 2])
                if S["tdt"].get(k - 2) is not None:
                    P.wait("sync", TM, S["tdt"][k - 2])
                lB = P.dma("sync", lambda e, kt=kt, pk=pk: e.dma_start(
                    out=Bxt[pk][:], in_=d_xpjr.ap()[0:16, kt]))
                lC = P.dma("sync", lambda e, kt=kt, pk=pk: e.dma_start(
                    out=Cxt[pk][:], in_=d_xpjr.ap()[16:32, kt]))
                lD = P.dma("sync", lambda e, kt=kt, pk=pk: e.dma_start(
                    out=dtr[pk][:], in_=d_xpjr.ap()[32:33, kt]))
                P.wait("tensor", LD, lD)
                if S["vsp"].get(k - 1) is not None:
                    P.wait("tensor", VC, S["vsp"][k - 1])
                tdt = P.op("tensor", lambda e, pk=pk: e.matmul(
                    PS[6 + pk][:], ones1t[:], dtr[pk][:], start=True, stop=True), TM)
                S["tdt"][k] = tdt
                P.wait("vector", TM, tdt)
                if k >= 2:
                    P.wait("vector", SC, roll["a3"][16 * (k - 2) + 15])
                    P.wait("vector", PC, S["u"][k - 2])
                for mi in range(NMI):
                    P.op("vector", lambda e, mi=mi, pk=pk: e.tensor_scalar(
                        dtt[pk][:, mi, :], PS[6 + pk][:],
                        vect[:, 4 * mi + 2:4 * mi + 3], vect[:, 4 * mi + 1:4 * mi + 2],
                        ALU.mult, ALU.add))
                vrl = P.op("vector", lambda e, pk=pk: e.tensor_scalar(
                    rb2b[:], dtt[pk][:].rearrange("p m t -> p (m t)"), 0.0, None, ALU.max), VC)
                P.wait("scalar", VC, vrl)
                P.op("scalar", lambda e, pk=pk: e.activation(
                    eb2b[:], dtt[pk][:].rearrange("p m t -> p (m t)"), AF.Abs))
                P.op("scalar", lambda e: e.activation(eb2b[:], eb2b[:], AF.Exp, scale=-1.0))
                ssp = P.op("scalar", lambda e: e.activation(eb2b[:], eb2b[:], AF.Ln, bias=one_t[:, 0:1]), SC)
                P.wait("vector", SC, ssp)
                vsp = P.op("vector", lambda e, pk=pk: e.tensor_tensor(
                    dtt[pk][:].rearrange("p m t -> p (m t)"), rb2b[:], eb2b[:], ALU.add), VC)
                S["vsp"][k] = vsp
                P.wait("gpsimd", VC, vsp)
                P.wait("gpsimd", LD, S["l1"][k])
                if S["b3l"].get(k - 2) is not None:
                    P.wait("gpsimd", VC, S["b3l"][k - 2])
                S["u"][k] = P.op("gpsimd", lambda e, pk=pk: e.tensor_tensor(
                    ut[pk][:].rearrange("p m t -> p (m t)"), dtt[pk][:].rearrange("p m t -> p (m t)"),
                    xct3[pk][:].rearrange("p m t -> p (m t)"), ALU.mult), PC)
                P.wait("scalar", LD, lC)
                if k >= 2:
                    P.wait("scalar", TM, roll["psB_mm"][16 * (k - 2) + 15])
                P.op("scalar", lambda e, pk=pk: e.activation(Bxb[pk][:], Bxt[pk][:], AF.Copy))
                S["bxb"][k] = P.op("scalar", lambda e, pk=pk: e.activation(Cxb[pk][:], Cxt[pk][:], AF.Copy), SC)

            head(0)
            for k in range(NCH):
                pk = k % 2
                if debug and "dty" in dbg_parts and k == 0:
                    P.wait("sync", VC, S["vsp"][0])
                    P.dma("sync", lambda e: e.dma_start(out=dbg["dbg_dt"], in_=dtt[0][:].rearrange("p m t -> p (m t)")))
                def ensure_bcast(g2, kk):
                    if g2 in roll["psB_mm"] or g2 >= 16 * NCH:
                        return
                    n2 = g2 % 16
                    pp2 = g2 % 2
                    psB2 = PS[0 + 2 * pp2]
                    psC2 = PS[1 + 2 * pp2]
                    P.wait("tensor", SC, S["bxb"][kk])
                    if roll["psC_ev"].get(g2 - 2) is not None:
                        P.wait("tensor", SC, roll["psC_ev"][g2 - 2])
                    P.op("tensor", lambda e, n2=n2, psB2=psB2, kk=kk: e.matmul(
                        psB2[:], selb[:, 128 * n2:128 * (n2 + 1)], Bxb[kk % 2][:], start=True, stop=True))
                    tbc = P.op("tensor", lambda e, n2=n2, psC2=psC2, kk=kk: e.matmul(
                        psC2[:], selb[:, 128 * n2:128 * (n2 + 1)], Cxb[kk % 2][:], start=True, stop=True), TM)
                    roll["psB_mm"][g2] = tbc
                    P.wait("scalar", TM, tbc)
                    if roll["tmp"].get(g2 - 2) is not None:
                        P.wait("scalar", VC, roll["tmp"][g2 - 2])
                    if roll["dve"].get(g2 - 2) is not None:
                        P.wait("scalar", VC, roll["dve"][g2 - 2])
                    P.op("scalar", lambda e, pp2=pp2, psB2=psB2: e.activation(Bbb[pp2][:], psB2[:], AF.Copy))
                    cev = P.op("scalar", lambda e, pp2=pp2, psC2=psC2: e.activation(Cb[pp2][:], psC2[:], AF.Copy), SC)
                    roll["psC_ev"][g2] = cev

                def emit_yadd(g2):
                    pp2 = g2 % 2
                    pk2 = (g2 // 16) % 2
                    if g2 % 16 == 0:
                        v = P.op("vector", lambda e, pp2=pp2, pk2=pk2: e.tensor_scalar(
                            yt[pk2][:].rearrange("p m t -> p (m t)"),
                            tmpb[pp2][:], 0.0, None, ALU.add), VC)
                    else:
                        v = P.op("vector", lambda e, pp2=pp2, pk2=pk2: e.tensor_tensor(
                            yt[pk2][:].rearrange("p m t -> p (m t)"),
                            yt[pk2][:].rearrange("p m t -> p (m t)"),
                            tmpb[pp2][:], ALU.add), VC)
                    roll["tmp_rd"][g2] = v

                for n in range(NS):
                    g = 16 * k + n
                    pp = g % 2
                    ensure_bcast(g, k)
                    cev = roll["psC_ev"][g]
                    if n == 0:
                        P.wait("scalar", VC, S["vsp"][k])
                    sa = None
                    for mi in range(NMI):
                        sa = P.op("scalar", lambda e, mi=mi, n=n, pp=pp, pk=pk: e.activation(
                            ab[pp][:, TC * mi:TC * (mi + 1)], dtt[pk][:, mi, :], AF.Exp,
                            scale=Aneg[:, 16 * mi + n:16 * mi + n + 1]),
                            SC if mi == NMI - 1 else None)
                    roll["a3"][g] = sa
                    if n == 0:
                        P.wait("vector", PC, S["u"][k])
                    P.wait("vector", SC, cev)
                    b3m = P.op("vector", lambda e, pp=pp, pk=pk: e.tensor_tensor(
                        bb3[:].rearrange("p (m t) -> p m t", m=NMI),
                        ut[pk][:], bc_mid(Bbb[pp][:], NMI), ALU.mult), VC)
                    if n == NS - 1:
                        S["b3l"][k] = b3m
                    P.wait("vector", SC, roll["a3"][g])
                    for mi in range(NMI):
                        P.op("vector", lambda e, mi=mi, n=n, pp=pp: e.tensor_tensor_scan(
                            hta[pp][:, mi, :],
                            ab[pp][:, TC * mi:TC * (mi + 1)],
                            bb3[:, TC * mi:TC * (mi + 1)],
                            st[:, mi, n:n + 1], ALU.mult, ALU.add))
                    vsc = P.op("vector", lambda e, n=n, pp=pp: e.tensor_copy(
                        st[:, :, n:n + 1], hta[pp][:, :, TC - 1:TC]), VC)
                    roll["dve"][g] = vsc
                    tm_ = P.op("vector", lambda e, pp=pp: e.tensor_tensor(
                        tmpb[pp][:].rearrange("p (m t) -> p m t", m=NMI),
                        hta[pp][:], bc_mid(Cb[pp][:], NMI), ALU.mult), VC)
                    roll["tmp"][g] = tm_
                    if g > 0 and (g - 1) not in roll["tmp_rd"]:
                        emit_yadd(g - 1)
                    ensure_bcast(g + 1, k if n + 1 < NS else k + 1)
                    if n == 8 and k + 1 < NCH:
                        head(k + 1)
                emit_yadd(16 * k + 15)
                P.wait("vector", LD, S["l1"][k])
                for mi in range(NMI):
                    P.op("vector", lambda e, mi=mi, pk=pk: e.scalar_tensor_tensor(
                        yt[pk][:, mi, :], xct3[pk][:, mi, :],
                        vect[:, 4 * mi + 3:4 * mi + 4], yt[pk][:, mi, :], ALU.mult, ALU.add))
                P.wait("vector", LD, S["l2"][k])
                if S["wol"].get(k - 2) is not None:
                    P.wait("vector", TM, S["wol"][k - 2])
                vg = P.op("vector", lambda e, pk=pk: e.tensor_tensor(
                    gt[pk][:].rearrange("p m t -> p (m t)"), yt[pk][:].rearrange("p m t -> p (m t)"),
                    zsl[pk][:].rearrange("p m t -> p (m t)"), ALU.mult), VC)
                S["vg"][k] = vg
                if debug and "dty" in dbg_parts and k == 0:
                    P.wait("sync", VC, vg)
                    P.dma("sync", lambda e: e.dma_start(out=dbg["dbg_y"], in_=yt[0][:].rearrange("p m t -> p (m t)")))
                P.wait("tensor", VC, vg)
                wo_dma = {}
                for mo in range(KD):
                    pw = mo % 2
                    psW = PS[4 + pw]
                    if wo_bank_ev[pw] is not None:
                        P.wait("tensor", SC, wo_bank_ev[pw])
                    two = None
                    for kd in range(NMI):
                        two = P.op("tensor", lambda e, kd=kd, mo=mo, pk=pk, psW=psW: e.matmul(
                            psW[:], Wob[:, kd, 128 * mo:128 * (mo + 1)], gt[pk][:, kd, :],
                            start=(kd == 0), stop=(kd == NMI - 1)), TM if kd == NMI - 1 else None)
                    if mo == KD - 1:
                        S["wol"][k] = two
                    P.wait("scalar", TM, two)
                    if wo_dma.get(mo - 2) is not None:
                        P.wait("scalar", WO, wo_dma[mo - 2])
                    swo = P.op("scalar", lambda e, pw=pw, psW=psW: e.activation(wot[pw][:], psW[:], AF.Copy), SC)
                    wo_bank_ev[pw] = swo
                    P.wait("sync", SC, swo)
                    wo_dma[mo] = P.dma("sync", lambda e, k=k, mo=mo, pw=pw: e.dma_start(
                        out=d_wop_j[k // 4].ap()[1024 * (k % 4) + 128 * mo:1024 * (k % 4) + 128 * (mo + 1), :],
                        in_=wot[pw][:]), ctr=WO)
                if k % 4 == 3:
                    P.wait("gpsimd", WO, 16 * KD * (k + 1))
                    CC.n += 1
                    P.op("gpsimd", lambda e, j=k // 4: e.collective_compute(
                        "ReduceScatter", ALU.add, GROUPS,
                        [d_wop_j[j].ap().opt()], [d_att_j[j].ap().opt()]).then_inc(CC.h, 1))
                if debug and "wop" in dbg_parts and k == 3:
                    P.wait("sync", LD, P.LD.n)
                    P.dma("sync", lambda e: e.dma_start(out=dbg["dbg_wop"], in_=d_wop_j[0].ap()))
            P.wait("gpsimd", CC, CC.n)
            P.op("gpsimd", lambda e: e.drain(fusable=False))
            P.barrier()
            AR.pop()

        if True:
            AR.push()
            T = alloc_ffn_scope()
            prev = {}
            oysp = [None, None]
            xs_vc = [None, None]
            xs_sc = [None, None]
            ln2_rd = None
            cen_rd = None
            for j in range(NJ):
                par = j % 2
                jt = slice(TC * j, TC * (j + 1))
                xs = T["xs"][par]
                if j >= 2:
                    P.wait("sync", VC, xs_vc[par])
                    P.wait("sync", SC, xs_sc[par])
                    P.wait("sync", LD, oysp[par])
                l1 = P.dma("sync", lambda e, jt=jt, xs=xs: e.dma_start(out=xs[:], in_=d_x1_r[:, :, jt]))
                if ln2_rd is not None:
                    P.wait("sync", VC, ln2_rd)
                l2 = P.dma("sync", lambda e, jt=jt: e.dma_start(out=T["ln2t"][:], in_=d_ln2_r[:, :, jt]))
                if cen_rd is not None:
                    P.wait("sync", VC, cen_rd)
                l3 = P.dma("sync", lambda e, j=j: e.dma_start(
                    out=T["cen"][:],
                    in_=d_att_j[j].ap().rearrange("(k p) t -> p k t", p=128)))
                P.wait("vector", LD, l3)
                ln2_rd = P.op("vector", lambda e, xs=xs: e.tensor_tensor(
                    xs[:].rearrange("p k t -> p (k t)"), xs[:].rearrange("p k t -> p (k t)"),
                    T["ln2t"][:].rearrange("p k t -> p (k t)"), ALU.add), VC)
                vx2 = P.op("vector", lambda e, xs=xs: e.tensor_tensor(
                    xs[:].rearrange("p k t -> p (k t)"), xs[:].rearrange("p k t -> p (k t)"),
                    T["cen"][:].rearrange("p k t -> p (k t)"), ALU.add), VC)
                if debug and "p3" in dbg_parts:
                    P.wait("sync", LD, l3)
                    P.dma("sync", lambda e, j=j: e.dma_start(
                        out=dbg["dbg_att"][D * j:D * (j + 1), :].rearrange("(k p) t -> p k t", p=128),
                        in_=T["cen"][:]))
                s_r, c2m = ln_stats(T, xs, wait_vc=vx2)
                xs_sc[par] = c2m
                P.wait("vector", SC, s_r)
                vx = ln_apply_plain(T, xs, T["xnb"])
                cen_rd = vx
                xs_vc[par] = vx
                P.wait("tensor", VC, vx)
                prev = ffn(T, par, prev)
                xv = prev["xv"]
                P.wait("sync", VC, xv)
                oysp[par] = P.dma("sync", lambda e, jt=jt, xs=xs: e.dma_start(
                    out=o_y_r[:, :, jt], in_=xs[:]))

        P.wait("sync", LD, P.LD.n)

        block = E(nc.Block())
        P.emit(block)

    return nc


_DEBUG = False
_NC_CACHE = {}


_DBG_PARTS = ("p3",)


def _get_nc():
    key = (bool(_DEBUG), tuple(_DBG_PARTS))
    if key not in _NC_CACHE:
        _NC_CACHE[key] = build_program(debug=key[0], dbg_parts=key[1])
    return _NC_CACHE[key]


def kernel(L0_attn_x, L0_attn_ln_g, L0_attn_ln_b, L0_attn_W_in, L0_attn_conv_w,
           L0_attn_conv_b, L0_attn_W_xp, L0_attn_W_dt, L0_attn_b_dt, L0_attn_A_log,
           L0_attn_D, L0_attn_W_out, L0_ffn_w_gate, L0_ffn_w_up, L0_ffn_w_down):
    f32 = np.float32
    x = np.asarray(L0_attn_x, f32)
    W_in = np.asarray(L0_attn_W_in, f32)
    W_out = np.asarray(L0_attn_W_out, f32)
    W_xp = np.asarray(L0_attn_W_xp, f32)
    conv_w = np.asarray(L0_attn_conv_w, f32)
    conv_b = np.asarray(L0_attn_conv_b, f32)
    W_dt = np.asarray(L0_attn_W_dt, f32)
    b_dt = np.asarray(L0_attn_b_dt, f32)
    A_log = np.asarray(L0_attn_A_log, f32)
    Dvec = np.asarray(L0_attn_D, f32)
    ln_g = np.asarray(L0_attn_ln_g, f32)
    ln_b = np.asarray(L0_attn_ln_b, f32)
    wg = np.ascontiguousarray(np.asarray(L0_ffn_w_gate, np.int32).T)
    wu = np.ascontiguousarray(np.asarray(L0_ffn_w_up, np.int32).T)
    wd = np.ascontiguousarray(np.asarray(L0_ffn_w_down, np.int32).T)

    sel = np.zeros((16, 2048), f32)
    for n in range(16):
        sel[n, 128 * n:128 * (n + 1)] = 1.0
    ones128 = np.ones((128, 128), f32)
    ones1 = np.ones((1, 128), f32)
    lngb = np.zeros((128, 16), f32)
    for kb in range(KD):
        lngb[:, 2 * kb] = ln_g[128 * kb:128 * (kb + 1)]
        lngb[:, 2 * kb + 1] = ln_b[128 * kb:128 * (kb + 1)]

    in_maps = []
    for c in range(8):
        b, r = divmod(c, 4)
        sl = slice(512 * r, 512 * (r + 1))
        xc = np.concatenate(
            [x[b, TC * (4 * j + r):TC * (4 * j + r + 1), :].T for j in range(NJ)], axis=1)
        winc_c = np.concatenate([W_in[:, sl], W_in[:, 2048 + sl.start:2048 + sl.stop]], axis=1)
        cwc = conv_w[sl].reshape(4, 128, 4).transpose(1, 0, 2).reshape(128, 16)
        avals = A_log[sl].reshape(4, 128, 16).transpose(1, 0, 2).reshape(128, 64)
        vv = np.stack([conv_b[sl], b_dt[sl], W_dt[0, sl], Dvec[sl]], axis=-1)
        vv = vv.reshape(4, 128, 4).transpose(1, 0, 2).reshape(128, 16)
        in_maps.append(dict(
            x_fm=np.ascontiguousarray(xc),
            winc=np.ascontiguousarray(winc_c),
            woc=np.ascontiguousarray(W_out[sl]),
            wxpc=np.ascontiguousarray(W_xp[sl]),
            wgT=wg, wuT=wu, wdT=wd,
            cwc=np.ascontiguousarray(cwc),
            avals=np.ascontiguousarray(avals),
            vecs=np.ascontiguousarray(vv),
            lngb=lngb, sel=sel, ones128=ones128, ones1=ones1,
        ))

    nc = _get_nc()
    res = run_bass_kernel_spmd(nc, in_maps, core_ids=list(range(8)))

    out = np.empty((B, L, D), f32)
    for c in range(8):
        b, r = divmod(c, 4)
        oy = np.asarray(res.results[c]["o_y"], f32)
        for j in range(NJ):
            out[b, TC * (4 * j + r):TC * (4 * j + r + 1), :] = oy[:, TC * j:TC * (j + 1)].T
    if _DEBUG:
        kernel._dbg = res.results
    return out


# version 28
# speedup vs baseline: 1.1534x; 1.0230x over previous
import dataclasses
from contextlib import ExitStack

import numpy as np

import concourse.bass as bass
import concourse.mybir as mybir
from concourse.bass_utils import run_bass_kernel_spmd

F32 = mybir.dt.float32
BF16 = mybir.dt.bfloat16
I32 = mybir.dt.int32
AF = mybir.ActivationFunctionType
ALU = mybir.AluOpType

B, L, D, DIN, H, NS = 2, 8192, 1024, 2048, 3072, 16
TC = 512
NCH = 16
NJ = 4
KD = 8
NMI = 4
HM = H // 128
EPS = 1e-5
GROUPS = [[0, 1, 2, 3], [4, 5, 6, 7]]
ENGINES = ("sync", "vector", "scalar", "tensor", "gpsimd")


def bc_mid(ap, rep):
    assert len(ap.ap) == 2
    return dataclasses.replace(ap, ap=[ap.ap[0], [0, rep], ap.ap[1]])


class Ctr:
    def __init__(self, h):
        self.h = h
        self.n = 0


class Bld:
    def __init__(self, nc, ctx):
        self.nc = nc
        self.ops = {e: [] for e in ENGINES}
        E = ctx.enter_context
        self.LD = Ctr(E(nc.semaphore("LD")))
        self.CC = Ctr(E(nc.semaphore("CC")))
        self.VC = Ctr(E(nc.semaphore("VC")))
        self.SC = Ctr(E(nc.semaphore("SC")))
        self.TM = Ctr(E(nc.semaphore("TM")))
        self.PC = Ctr(E(nc.semaphore("PC")))
        self.CD = Ctr(E(nc.semaphore("CD")))
        self.WQ = Ctr(E(nc.semaphore("WQ")))
        self.WO = Ctr(E(nc.semaphore("WO")))
        self.XN = Ctr(E(nc.semaphore("XN")))
        self.BAR = Ctr(E(nc.semaphore("BAR")))

    def op(self, eng, build, ctr=None):
        if ctr is None:
            self.ops[eng].append(build)
            return None
        ctr.n += 1
        tgt = ctr.n
        self.ops[eng].append(lambda e, b=build, c=ctr: b(e).then_inc(c.h, 1))
        return tgt

    def dma(self, eng, build, ctr=None):
        ctr = ctr or self.LD
        ctr.n += 16
        tgt = ctr.n
        self.ops[eng].append(lambda e, b=build, c=ctr: b(e).then_inc(c.h, 16))
        return tgt

    def wait(self, eng, ctr, val):
        if val is None:
            return
        self.ops[eng].append(lambda e, c=ctr, v=val: e.wait_ge(c.h, v))

    def barrier(self):
        self.wait("sync", self.LD, self.LD.n)
        self.wait("sync", self.WQ, self.WQ.n)
        self.wait("sync", self.WO, self.WO.n)
        self.wait("sync", self.XN, self.XN.n)
        tgt = self.BAR.n + len(ENGINES)
        for e in ENGINES:
            self.ops[e].append(lambda E: E.sem_inc(self.BAR.h, 1))
        self.BAR.n = tgt
        for e in ENGINES:
            self.ops[e].append(lambda E, v=tgt: E.wait_ge(self.BAR.h, v))

    def coll(self, kind, op_, ins, outs, wait_ld=None):
        if wait_ld is not None:
            self.wait("gpsimd", self.LD, wait_ld)
        self.CC.n += 1
        tgt = self.CC.n
        self.ops["gpsimd"].append(lambda e, k=kind, o=op_, i=ins, u=outs: e.collective_compute(
            k, o, GROUPS, i, u).then_inc(self.CC.h, 1))
        self.wait("gpsimd", self.CC, tgt)
        self.ops["gpsimd"].append(lambda e: e.drain(fusable=False))
        self.CD.n += 1
        cdt = self.CD.n
        self.ops["gpsimd"].append(lambda e, v=cdt: e.sem_inc(self.CD.h, 1))
        return cdt

    def emit(self, block):
        b = self
        @block.sync
        def _(eng):
            for f in b.ops["sync"]:
                f(eng)
        @block.vector
        def _(eng):
            for f in b.ops["vector"]:
                f(eng)
        @block.scalar
        def _(eng):
            for f in b.ops["scalar"]:
                f(eng)
        @block.tensor
        def _(eng):
            for f in b.ops["tensor"]:
                f(eng)
        @block.gpsimd
        def _(eng):
            for f in b.ops["gpsimd"]:
                f(eng)


class Arena:
    ESZ = {F32: 4, BF16: 2, I32: 4}

    def __init__(self, nc, ctx, name, words):
        self.h = ctx.enter_context(nc.sbuf_tensor(name, [128, words], F32))
        self.hb = self.h.bitcast(BF16)
        self.hi = self.h.bitcast(I32)
        self.cap = words * 4
        self.off = 0
        self.stack = []

    def push(self):
        self.stack.append(self.off)

    def pop(self):
        self.off = self.stack.pop()

    def alloc(self, shape, dt=F32):
        n = 1
        for s in shape[1:]:
            n *= s
        esz = self.ESZ[dt]
        self.off = (self.off + 63) & ~63
        start = self.off
        self.off += n * esz
        assert self.off <= self.cap, f"arena overflow: {self.off} > {self.cap}"
        if dt == F32:
            base = self.h[:, start // 4: start // 4 + n]
        elif dt == I32:
            base = self.hi[:, start // 4: start // 4 + n]
        else:
            base = self.hb[:, start // 2: start // 2 + n]
        if shape[0] != 128:
            base = base[0:shape[0], :]
        if len(shape) == 3:
            base = base.rearrange("p (a b) -> p a b", a=shape[1])
        return base


def build_program(debug=False, dbg_parts=("p1","scan","wop","rs","p3","dty")):
    nc = bass.Bass("TRN2", target_bir_lowering=False, detect_race_conditions=False)

    def din(name, shape, dt=F32):
        return nc.dram_tensor(name, shape, dt, kind="ExternalInput").ap()

    x_fm = din("x_fm", [D, NJ * TC])
    winc = din("winc", [D, 1024])
    woc = din("woc", [512, D])
    wxpc = din("wxpc", [512, 33])
    wgT = din("wgT", [D, H], I32)
    wuT = din("wuT", [D, H], I32)
    wdT = din("wdT", [H, D], I32)
    cwc = din("cwc", [128, 16])
    avals = din("avals", [128, 64])
    vecs = din("vecs", [128, 16])
    lngb = din("lngb", [128, 16])
    sel_i = din("sel", [16, 2048])
    ones128 = din("ones128", [128, 128])
    ones1 = din("ones1", [1, 128])

    o_y = nc.dram_tensor("o_y", [D, NJ * TC], F32, kind="ExternalOutput").ap()

    def dint(name, shape, dt=F32):
        return nc.dram_tensor(name, shape, dt, kind="Internal")

    d_wg = dint("d_wg", [D, H], BF16)
    d_wu = dint("d_wu", [D, H], BF16)
    d_wd = dint("d_wd", [H, D], BF16)
    d_winb = dint("d_winb", [D, 1024], BF16)
    d_wob = dint("d_wob", [512, D], BF16)
    d_x1 = dint("d_x1", [D, NJ * TC], F32)
    d_ln2 = dint("d_ln2", [D, NJ * TC], F32)
    d_xn = dint("d_xn", [NJ * D, TC], BF16)
    d_xng = dint("d_xng", [NJ * 4 * D, TC], BF16)
    d_xc = dint("d_xc", [512, L], F32)
    d_z = dint("d_z", [512, L], F32)
    d_xpj = dint("d_xpj", [33, L], F32)
    d_xpjr = dint("d_xpjr", [33, L], F32)
    d_wop_j = [dint(f"d_wop{j}", [4 * D, TC], F32) for j in range(NJ)]
    d_att_j = [dint(f"d_att{j}", [D, TC], F32) for j in range(NJ)]

    dbg = {}
    if debug:
        for name, shape, dt in [
            ("dbg_xn", [D, NJ * TC], BF16), ("dbg_x1", [D, NJ * TC], F32),
            ("dbg_dt", [128, NMI * TC], F32),
            ("dbg_y", [128, NMI * TC], F32), ("dbg_att", [NJ * D, TC], F32),
            ("dbg_u", [128, NMI * TC], F32), ("dbg_a0", [128, NMI * TC], F32),
            ("dbg_b0", [128, NMI * TC], F32), ("dbg_h0", [128, NMI * TC], F32),
            ("dbg_y0", [128, NMI * TC], F32), ("dbg_wop", [4 * D, TC], F32),
            ("dbg_wop2", [4 * D, TC], F32),
        ]:
            dbg[name] = nc.dram_tensor(name, shape, dt, kind="ExternalOutput").ap()

    x_fm_r = x_fm.rearrange("(k p) t -> p k t", p=128)
    d_x1_r = d_x1.ap().rearrange("(k p) t -> p k t", p=128)
    d_ln2_r = d_ln2.ap().rearrange("(k p) t -> p k t", p=128)
    d_xc_r = d_xc.ap().rearrange("(m p) t -> p m t", p=128)
    d_z_r = d_z.ap().rearrange("(m p) t -> p m t", p=128)
    o_y_r = o_y.rearrange("(k p) t -> p k t", p=128)

    with ExitStack() as ctx:
        E = ctx.enter_context
        P = Bld(nc, ctx)
        LD, CC, VC, SC, TM, PC, CD, WQ, WO, XN = P.LD, P.CC, P.VC, P.SC, P.TM, P.PC, P.CD, P.WQ, P.WO, P.XN

        def sb(name, shape, dt=F32):
            return E(nc.sbuf_tensor(name, shape, dt))

        mu_t = sb("mu_t", [128, TC])
        ms_t = sb("ms_t", [128, TC])
        var_t = sb("var_t", [128, TC])
        rstd_t = sb("rstd_t", [128, TC])
        f1buf = sb("f1buf", [128, TC])
        cwt = sb("cwt", [128, 16])
        Aneg = sb("Aneg", [128, 64])
        vect = sb("vect", [128, 16])
        lngbt = sb("lngbt", [128, 16])
        onesb = sb("onesb", [128, 128], BF16)
        ones1t = sb("ones1t", [1, 128])
        eps_t = sb("eps_t", [128, 1])
        one_t = sb("one_t", [128, 1])
        selb = sb("selb", [16, 2048], BF16)
        st = sb("st", [128, NMI, NS])

        PS = [E(nc.psum_tensor(f"ps{i}", [128, TC], F32)) for i in range(8)]
        AR = Arena(nc, ctx, "arena", 43 * 1024)

        if True:
            AR.push()
            deqi = [AR.alloc([128, H], I32) for _ in range(2)]
            deqb = [AR.alloc([128, H], BF16) for _ in range(2)]
            wfl = AR.alloc([128, 1024], F32)
            p0sel = AR.alloc([16, 2048], F32)

            P.op("vector", lambda e: e.memset(eps_t[:], float(EPS)))
            vcn = P.op("vector", lambda e: e.memset(one_t[:], 1.0), VC)
            P.wait("scalar", VC, vcn)
            P.dma("sync", lambda e: e.dma_start(out=cwt[:], in_=cwc))
            P.dma("sync", lambda e: e.dma_start(out=vect[:], in_=vecs))
            P.dma("sync", lambda e: e.dma_start(out=lngbt[:], in_=lngb))
            P.dma("sync", lambda e: e.dma_start(out=ones1t[:], in_=ones1))
            t5 = P.dma("sync", lambda e: e.dma_start(out=mu_t[:, 0:64], in_=avals))
            P.wait("scalar", LD, t5)
            s1 = P.op("scalar", lambda e: e.activation(ms_t[:, 0:64], mu_t[:, 0:64], AF.Exp), SC)
            P.wait("vector", SC, s1)
            P.op("vector", lambda e: e.tensor_scalar(Aneg[:], ms_t[:, 0:64], -1.0, None, ALU.mult), VC)
            t6 = P.dma("sync", lambda e: e.dma_start(out=rstd_t[:, 0:128], in_=ones128))
            P.wait("scalar", LD, t6)
            P.op("scalar", lambda e: e.activation(onesb[:], rstd_t[:, 0:128], AF.Copy), SC)
            t7 = P.dma("sync", lambda e: e.dma_start(out=p0sel[:], in_=sel_i))
            P.wait("scalar", LD, t7)
            P.op("scalar", lambda e: e.activation(selb[:], p0sel[:], AF.Copy), SC)

            def dequant(src_ap, dst_t, nblocks, width):
                vvs = [None] * nblocks
                dmas = [None] * nblocks
                for i in range(nblocks):
                    bb = i % 2
                    if i >= 2:
                        P.wait("sync", VC, vvs[i - 2])
                    tl = P.dma("sync", lambda e, i=i, bb=bb, s=src_ap, w=width: e.dma_start(
                        out=deqi[bb][:, 0:w], in_=s[128 * i:128 * (i + 1), 0:w]))
                    P.wait("vector", LD, tl)
                    if i >= 2:
                        P.wait("vector", LD, dmas[i - 2])
                    vvs[i] = P.op("vector", lambda e, bb=bb, w=width: e.tensor_scalar(
                        deqb[bb][:, 0:w], deqi[bb][:, 0:w], 1.0 / 127.0, None, ALU.mult), VC)
                    P.wait("sync", VC, vvs[i])
                    dmas[i] = P.dma("sync", lambda e, i=i, bb=bb, d=dst_t, w=width: e.dma_start(
                        out=d.ap()[128 * i:128 * (i + 1), 0:w], in_=deqb[bb][:, 0:w]))

            dequant(wgT, d_wg, KD, H)
            dequant(wuT, d_wu, KD, H)
            dequant(wdT, d_wd, HM, D)

            def stage(src, dst, nblocks):
                prev = {"sv": None, "dma": None}
                for i in range(nblocks):
                    if prev["sv"] is not None:
                        P.wait("sync", SC, prev["sv"])
                    tl = P.dma("sync", lambda e, i=i, s=src: e.dma_start(
                        out=wfl[:], in_=s[128 * i:128 * (i + 1), :]))
                    P.wait("scalar", LD, tl)
                    if prev["dma"] is not None:
                        P.wait("scalar", LD, prev["dma"])
                    sv = P.op("scalar", lambda e: e.activation(deqb[0][:, 0:1024], wfl[:], AF.Copy), SC)
                    prev["sv"] = sv
                    P.wait("sync", SC, sv)
                    prev["dma"] = P.dma("sync", lambda e, i=i, d=dst: e.dma_start(
                        out=d.ap()[128 * i:128 * (i + 1), :], in_=deqb[0][:, 0:1024]))
            stage(winc, d_winb, KD)
            stage(woc, d_wob, NMI)
            P.barrier()
            AR.pop()

        def ln_stats(T, src, wait_vc=None):
            if wait_vc is not None:
                P.wait("scalar", VC, wait_vc)
            P.op("scalar", lambda e: e.activation(T["xbb"][:], src[:], AF.Copy))
            c2 = P.op("scalar", lambda e: e.activation(T["xnb"][:], src[:], AF.Square), SC)
            P.wait("tensor", SC, c2)
            for kb in range(KD):
                P.op("tensor", lambda e, kb=kb: e.matmul(
                    PS[4][:], onesb[:], T["xbb"][:, kb, :], start=(kb == 0), stop=(kb == KD - 1)),
                    TM if kb == KD - 1 else None)
            tm2 = None
            for kb in range(KD):
                tm2 = P.op("tensor", lambda e, kb=kb: e.matmul(
                    PS[5][:], onesb[:], T["xnb"][:, kb, :], start=(kb == 0), stop=(kb == KD - 1)),
                    TM if kb == KD - 1 else None)
            P.wait("vector", TM, tm2)
            P.op("vector", lambda e: e.tensor_scalar(mu_t[:], PS[4][:], 1.0 / D, None, ALU.mult))
            P.op("vector", lambda e: e.tensor_scalar(ms_t[:], PS[5][:], 1.0 / D, None, ALU.mult))
            P.op("vector", lambda e: e.tensor_tensor(var_t[:], mu_t[:], mu_t[:], ALU.mult))
            v1 = P.op("vector", lambda e: e.tensor_tensor(var_t[:], ms_t[:], var_t[:], ALU.subtract), VC)
            P.wait("scalar", VC, v1)
            P.op("scalar", lambda e: e.activation(rstd_t[:], var_t[:], AF.Ln, bias=eps_t[:, 0:1]))
            s2 = P.op("scalar", lambda e: e.activation(rstd_t[:], rstd_t[:], AF.Exp, scale=-0.5), SC)
            return s2, c2

        def ffn(T, par, prev):
            xs, xnb, fbuf = T["xs"][par], T["xnb"], T["fbuf"]
            wgt_s, wut_s, wdt_s = T["wgt_s"], T["wut_s"], T["wdt_s"]
            if prev.get("tds7") is not None:
                P.wait("sync", TM, prev["tds7"])
            tus = [None] * HM
            fvs = [None] * HM
            f1s = [None] * HM
            for mh in range(HM):
                bw = mh % 2
                pg = PS[0 + 2 * bw]
                pu = PS[1 + 2 * bw]
                if mh >= 2:
                    P.wait("sync", TM, tus[mh - 2])
                P.dma("sync", lambda e, mh=mh, bw=bw: e.dma_start(
                    out=wgt_s[bw][:],
                    in_=d_wg.ap()[:, 128 * mh:128 * (mh + 1)].rearrange("(k p) m -> p k m", p=128)), ctr=WQ)
                lw2 = P.dma("sync", lambda e, mh=mh, bw=bw: e.dma_start(
                    out=wut_s[bw][:],
                    in_=d_wu.ap()[:, 128 * mh:128 * (mh + 1)].rearrange("(k p) m -> p k m", p=128)), ctr=WQ)
                P.wait("tensor", WQ, lw2)
                if mh >= 2:
                    P.wait("tensor", SC, f1s[mh - 2])
                    P.wait("tensor", VC, fvs[mh - 2])
                elif prev.get("f1s7") is not None:
                    P.wait("tensor", SC, prev["f1s7"])
                    P.wait("tensor", VC, prev["fvs7"])
                tg = None
                for kb in range(KD):
                    tg = P.op("tensor", lambda e, kb=kb, bw=bw, pg=pg: e.matmul(
                        pg[:], wgt_s[bw][:, kb, :], xnb[:, kb, :],
                        start=(kb == 0), stop=(kb == KD - 1)), TM if kb == KD - 1 else None)
                for kb in range(KD):
                    tus[mh] = P.op("tensor", lambda e, kb=kb, bw=bw, pu=pu: e.matmul(
                        pu[:], wut_s[bw][:, kb, :], xnb[:, kb, :],
                        start=(kb == 0), stop=(kb == KD - 1)), TM if kb == KD - 1 else None)
                P.wait("scalar", TM, tg)
                if mh >= 1:
                    P.wait("scalar", VC, fvs[mh - 1])
                elif prev.get("fvs7") is not None:
                    P.wait("scalar", VC, prev["fvs7"])
                f1s[mh] = P.op("scalar", lambda e, pg=pg: e.activation(f1buf[:], pg[:], AF.Silu), SC)
                P.wait("vector", SC, f1s[mh])
                P.wait("vector", TM, tus[mh])
                if mh == 0 and prev.get("tds7") is not None:
                    P.wait("vector", TM, prev["tds7"])
                fvs[mh] = P.op("vector", lambda e, mh=mh, pu=pu: e.tensor_tensor(
                    fbuf[:, mh, :], f1buf[:], pu[:], ALU.mult), VC)
            xv = None
            tds = [None] * KD
            for mo in range(KD):
                bw = mo % 2
                pd = PS[6 + bw]
                if mo >= 2:
                    P.wait("sync", TM, tds[mo - 2])
                lw = P.dma("sync", lambda e, mo=mo, bw=bw: e.dma_start(
                    out=wdt_s[bw][:],
                    in_=d_wd.ap()[:, 128 * mo:128 * (mo + 1)].rearrange("(k p) m -> p k m", p=128)), ctr=WQ)
                P.wait("tensor", WQ, lw)
                if mo == 0:
                    P.wait("tensor", VC, fvs[-1])
                if mo >= 2:
                    P.wait("tensor", VC, xv)
                elif prev.get("xv7") is not None:
                    P.wait("tensor", VC, prev["xv7"])
                for kh in range(HM):
                    tds[mo] = P.op("tensor", lambda e, kh=kh, bw=bw, pd=pd: e.matmul(
                        pd[:], wdt_s[bw][:, kh, :], fbuf[:, kh, :],
                        start=(kh == 0), stop=(kh == HM - 1)), TM if kh == HM - 1 else None)
                P.wait("vector", TM, tds[mo])
                xv = P.op("vector", lambda e, mo=mo, pd=pd, xs=xs: e.scalar_tensor_tensor(
                    xs[:, mo, :], pd[:], 0.5, xs[:, mo, :], ALU.mult, ALU.add), VC)
            return dict(xv=xv, tds7=tds[-1], f1s7=f1s[-1], fvs7=fvs[-1], xv7=xv)

        def ln_apply_plain(T, src, dst_bf):
            P.op("vector", lambda e, src=src: e.tensor_tensor(
                T["cen"][:], src[:], bc_mid(mu_t[:], KD), ALU.subtract))
            return P.op("vector", lambda e, dst_bf=dst_bf: e.tensor_tensor(
                dst_bf[:], T["cen"][:], bc_mid(rstd_t[:], KD), ALU.mult), VC)

        def alloc_ffn_scope():
            T = {}
            T["xs"] = [AR.alloc([128, KD, TC], F32) for _ in range(2)]
            T["ln2t"] = AR.alloc([128, KD, TC], F32)
            T["cen"] = AR.alloc([128, KD, TC], F32)
            T["xbb"] = AR.alloc([128, KD, TC], BF16)
            T["xnb"] = AR.alloc([128, KD, TC], BF16)
            T["fbuf"] = AR.alloc([128, HM, TC], BF16)
            T["wgt_s"] = [AR.alloc([128, KD, 128], BF16) for _ in range(2)]
            T["wut_s"] = [AR.alloc([128, KD, 128], BF16) for _ in range(2)]
            T["wdt_s"] = [AR.alloc([128, HM, 128], BF16) for _ in range(2)]
            return T

        if True:
            AR.push()
            T = alloc_ffn_scope()
            prev = {}
            x1sp = [None, None]
            xs_vc = [None, None]
            xs_sc = [None, None]
            for j in range(NJ):
                par = j % 2
                jt = slice(TC * j, TC * (j + 1))
                xs = T["xs"][par]
                if j >= 2:
                    P.wait("sync", VC, xs_vc[par])
                    P.wait("sync", SC, xs_sc[par])
                    P.wait("sync", LD, x1sp[par])
                lx = P.dma("sync", lambda e, jt=jt, xs=xs: e.dma_start(out=xs[:], in_=x_fm_r[:, :, jt]))
                P.wait("scalar", LD, lx)
                P.wait("vector", LD, lx)
                s_r, _ = ln_stats(T, xs)
                P.wait("vector", SC, s_r)
                vx = ln_apply_plain(T, xs, T["xnb"])
                P.wait("tensor", VC, vx)
                prev = ffn(T, par, prev)
                xv = prev["xv"]
                P.wait("sync", VC, xv)
                x1sp[par] = P.dma("sync", lambda e, jt=jt, xs=xs: e.dma_start(
                    out=d_x1_r[:, :, jt], in_=xs[:]))
                s_r, c2m = ln_stats(T, xs, wait_vc=xv)
                xs_sc[par] = c2m
                P.wait("vector", SC, s_r)
                xs_vc[par] = P.op("vector", lambda e, xs=xs: e.tensor_tensor(
                    T["ln2t"][:], xs[:], bc_mid(mu_t[:], KD), ALU.subtract), VC)
                v2 = P.op("vector", lambda e: e.tensor_tensor(
                    T["ln2t"][:], T["ln2t"][:], bc_mid(rstd_t[:], KD), ALU.mult), VC)
                P.wait("sync", VC, v2)
                P.dma("sync", lambda e, jt=jt: e.dma_start(out=d_ln2_r[:, :, jt], in_=T["ln2t"][:]))
                s_r, _ = ln_stats(T, T["ln2t"], wait_vc=v2)
                P.wait("vector", SC, s_r)
                P.op("vector", lambda e: e.tensor_tensor(
                    T["cen"][:], T["ln2t"][:], bc_mid(mu_t[:], KD), ALU.subtract))
                P.op("vector", lambda e: e.tensor_tensor(
                    T["cen"][:], T["cen"][:], bc_mid(rstd_t[:], KD), ALU.mult))
                v3 = None
                for kb in range(KD):
                    v3 = P.op("vector", lambda e, kb=kb: e.tensor_scalar(
                        T["xnb"][:, kb, :], T["cen"][:, kb, :],
                        lngbt[:, 2 * kb:2 * kb + 1], lngbt[:, 2 * kb + 1:2 * kb + 2],
                        ALU.mult, ALU.add), VC)
                P.wait("sync", VC, v3)
                P.dma("sync", lambda e, j=j: e.dma_start(
                    out=d_xn.ap()[D * j:D * (j + 1), :].rearrange("(k p) t -> p k t", p=128),
                    in_=T["xnb"][:]), ctr=XN)
                if debug and "p1" in dbg_parts:
                    P.dma("sync", lambda e, jt=jt: e.dma_start(
                        out=dbg["dbg_xn"].rearrange("(k p) t -> p k t", p=128)[:, :, jt], in_=T["xnb"][:]))
                    P.dma("sync", lambda e, jt=jt, xs=xs: e.dma_start(
                        out=dbg["dbg_x1"].rearrange("(k p) t -> p k t", p=128)[:, :, jt], in_=xs[:]))
            P.barrier()
            AR.pop()
        ag_done = [None] * NJ
        for j in range(NJ):
            ag_done[j] = P.coll(
                "AllGather", ALU.bypass,
                [d_xn.ap()[D * j:D * (j + 1), :].opt()],
                [d_xng.ap()[4 * D * j:4 * D * (j + 1), :].opt()])

        if True:
            AR.push()
            Winb = AR.alloc([128, KD, 1024], BF16)
            xng = [AR.alloc([128, KD, TC], BF16) for _ in range(2)]
            xhh = AR.alloc([128, NMI, TC + 8], F32)
            zst = AR.alloc([128, NMI, TC], F32)
            xct2 = AR.alloc([128, NMI, TC], F32)
            xcb2 = AR.alloc([128, NMI, TC], BF16)
            cvac = AR.alloc([128, NMI, TC], F32)
            Wxpb = AR.alloc([128, NMI, 33], BF16)
            xpjt = AR.alloc([33, TC], F32)
            lw = P.dma("sync", lambda e: e.dma_start(
                out=Winb[:], in_=d_winb.ap().rearrange("(k p) m -> p k m", p=128)))
            for mi in range(NMI):
                tw = P.dma("sync", lambda e, mi=mi: e.dma_start(
                    out=var_t[:, 0:33], in_=wxpc[128 * mi:128 * (mi + 1), :]))
                P.wait("scalar", LD, tw)
                sw = P.op("scalar", lambda e, mi=mi: e.activation(Wxpb[:, mi, :], var_t[:, 0:33], AF.Copy), SC)
                P.wait("sync", SC, sw)
            P.wait("tensor", LD, lw)
            P.op("gpsimd", lambda e: e.memset(xhh[:, :, 0:3], 0.0), PC)
            P.wait("scalar", PC, PC.n)
            conv_done = None
            sil_done = None
            cast_done = None
            tpj_prev = None
            spj_prev = None
            zs_spill = None
            xc_spill = None
            pj_spill = None
            bank_ev = {}
            mm_last = [None, None]
            for k in range(NCH):
                bk = k % 2
                r0 = 4 * D * (k // 4) + D * (k % 4)
                c0 = TC * k
                if mm_last[bk] is not None:
                    P.wait("sync", TM, mm_last[bk])
                P.wait("sync", CD, ag_done[k // 4])
                lx = P.dma("sync", lambda e, r0=r0, bk=bk: e.dma_start(
                    out=xng[bk][:],
                    in_=d_xng.ap()[r0:r0 + D, :].rearrange("(k p) t -> p k t", p=128)))
                if k > 0:
                    P.wait("scalar", VC, conv_done)
                    P.op("scalar", lambda e: e.activation(xhh[:, :, 0:3], xhh[:, :, TC:TC + 3], AF.Copy))
                P.wait("tensor", LD, lx)
                ev = []
                for mi in range(KD):
                    psn = mi % 4
                    if bank_ev.get(psn) is not None:
                        P.wait("tensor", SC, bank_ev[psn])
                    tmm = None
                    for kb in range(KD):
                        tmm = P.op("tensor", lambda e, kb=kb, mi=mi, bk=bk, psn=psn: e.matmul(
                            PS[psn][:], Winb[:, kb, 128 * mi:128 * (mi + 1)], xng[bk][:, kb, :],
                            start=(kb == 0), stop=(kb == KD - 1)), TM if kb == KD - 1 else None)
                    if mi == KD - 1:
                        mm_last[bk] = tmm
                    P.wait("scalar", TM, tmm)
                    if mi < NMI:
                        evv = P.op("scalar", lambda e, mi=mi, psn=psn: e.activation(
                            xhh[:, mi, 3:3 + TC], PS[psn][:], AF.Copy), SC)
                    else:
                        if zs_spill is not None and mi == NMI:
                            P.wait("scalar", LD, zs_spill)
                        evv = P.op("scalar", lambda e, mi=mi, psn=psn: e.activation(
                            zst[:, mi - NMI, :], PS[psn][:], AF.Silu), SC)
                    ev.append(evv)
                    bank_ev[psn] = evv
                P.wait("sync", SC, ev[-1])
                zs_spill = P.dma("sync", lambda e, c0=c0: e.dma_start(
                    out=d_z_r[:, :, c0:c0 + TC], in_=zst[:]))
                P.wait("vector", SC, ev[NMI - 1])
                if sil_done is not None:
                    P.wait("vector", SC, sil_done)
                cv = None
                for mi in range(NMI):
                    P.op("vector", lambda e, mi=mi: e.tensor_scalar(
                        cvac[:, mi, :], xhh[:, mi, 0:TC],
                        cwt[:, 4 * mi + 0:4 * mi + 1], vect[:, 4 * mi + 0:4 * mi + 1],
                        ALU.mult, ALU.add))
                    for kk in range(1, 4):
                        cv = P.op("vector", lambda e, mi=mi, kk=kk: e.scalar_tensor_tensor(
                            cvac[:, mi, :], xhh[:, mi, kk:kk + TC],
                            cwt[:, 4 * mi + kk:4 * mi + kk + 1], cvac[:, mi, :],
                            ALU.mult, ALU.add), VC if (mi == NMI - 1 and kk == 3) else None)
                conv_done = cv
                P.wait("scalar", VC, cv)
                if xc_spill is not None:
                    P.wait("scalar", LD, xc_spill)
                sil_done = P.op("scalar", lambda e: e.activation(
                    xct2[:].rearrange("p m t -> p (m t)"), cvac[:].rearrange("p m t -> p (m t)"),
                    AF.Silu), SC)
                if tpj_prev is not None:
                    P.wait("scalar", TM, tpj_prev)
                cast_done = P.op("scalar", lambda e: e.activation(
                    xcb2[:].rearrange("p m t -> p (m t)"), xct2[:].rearrange("p m t -> p (m t)"),
                    AF.Copy), SC)
                P.wait("sync", SC, sil_done)
                xc_spill = P.dma("sync", lambda e, c0=c0: e.dma_start(
                    out=d_xc_r[:, :, c0:c0 + TC], in_=xct2[:]))
                P.wait("tensor", SC, cast_done)
                if spj_prev is not None:
                    P.wait("tensor", SC, spj_prev)
                tpj = None
                for mi in range(NMI):
                    tpj = P.op("tensor", lambda e, mi=mi: e.matmul(
                        PS[4][0:33, :], Wxpb[:, mi, :], xcb2[:, mi, :],
                        start=(mi == 0), stop=(mi == NMI - 1)), TM if mi == NMI - 1 else None)
                tpj_prev = tpj
                P.wait("scalar", TM, tpj)
                if pj_spill is not None:
                    P.wait("scalar", LD, pj_spill)
                spj = P.op("scalar", lambda e: e.activation(xpjt[:], PS[4][0:33, :], AF.Copy), SC)
                spj_prev = spj
                P.wait("sync", SC, spj)
                pj_spill = P.dma("sync", lambda e, c0=c0: e.dma_start(
                    out=d_xpj.ap()[:, c0:c0 + TC], in_=xpjt[:]))
            P.barrier()
            AR.pop()

        art = P.coll("AllReduce", ALU.add, [d_xpj.ap().opt()], [d_xpjr.ap().opt()],
                     wait_ld=P.LD.n)
        P.barrier()

        if True:
            AR.push()
            Wob = AR.alloc([128, NMI, 1024], BF16)
            xct3 = [AR.alloc([128, NMI, TC], F32) for _ in range(2)]
            zsl = [AR.alloc([128, NMI, TC], F32) for _ in range(2)]
            dtt = [AR.alloc([128, NMI, TC], F32) for _ in range(2)]
            ut = [AR.alloc([128, NMI, TC], BF16) for _ in range(2)]
            Bxt = [AR.alloc([16, TC], F32) for _ in range(2)]
            Cxt = [AR.alloc([16, TC], F32) for _ in range(2)]
            dtr = [AR.alloc([1, TC], F32) for _ in range(2)]
            Bxb = [AR.alloc([16, TC], BF16) for _ in range(2)]
            Cxb = [AR.alloc([16, TC], BF16) for _ in range(2)]
            Cb = [AR.alloc([128, TC], BF16) for _ in range(2)]
            Bbb = [AR.alloc([128, TC], BF16) for _ in range(2)]
            ab = [AR.alloc([128, NMI * TC], F32) for _ in range(2)]
            bb3 = AR.alloc([128, NMI * TC], BF16)
            hta = [AR.alloc([128, NMI, TC], F32) for _ in range(2)]
            tmpb = [AR.alloc([128, NMI * TC], BF16) for _ in range(2)]
            yt = [AR.alloc([128, NMI, TC], BF16) for _ in range(2)]
            gt = [AR.alloc([128, NMI, TC], BF16) for _ in range(2)]
            eb2b = AR.alloc([128, NMI * TC], F32)
            rb2b = AR.alloc([128, NMI * TC], F32)
            wot = [AR.alloc([128, TC], F32) for _ in range(2)]

            lw = P.dma("sync", lambda e: e.dma_start(
                out=Wob[:], in_=d_wob.ap().rearrange("(k p) m -> p k m", p=128)))
            P.op("vector", lambda e: e.memset(st[:], 0.0), VC)
            P.wait("tensor", LD, lw)

            roll = {"a3": {}, "dve": {}, "psB_mm": {}, "psC_ev": {}, "tmp": {}, "tmp_rd": {}}
            S = {"l1": {}, "l2": {}, "vsp": {}, "u": {}, "bxb": {}, "tdt": {},
                 "b3l": {}, "vg": {}, "wol": {}}
            wo_bank_ev = {0: None, 1: None}

            def head(k):
                pk = k % 2
                kt = slice(TC * k, TC * (k + 1))
                if S["vg"].get(k - 2) is not None:
                    P.wait("sync", VC, S["vg"][k - 2])
                    P.wait("sync", PC, S["u"][k - 2])
                S["l1"][k] = P.dma("sync", lambda e, kt=kt, pk=pk: e.dma_start(
                    out=xct3[pk][:], in_=d_xc_r[:, :, kt]))
                S["l2"][k] = P.dma("sync", lambda e, kt=kt, pk=pk: e.dma_start(
                    out=zsl[pk][:], in_=d_z_r[:, :, kt]))
                if S["bxb"].get(k - 2) is not None:
                    P.wait("sync", SC, S["bxb"][k - 2])
                if S["tdt"].get(k - 2) is not None:
                    P.wait("sync", TM, S["tdt"][k - 2])
                lB = P.dma("sync", lambda e, kt=kt, pk=pk: e.dma_start(
                    out=Bxt[pk][:], in_=d_xpjr.ap()[0:16, kt]))
                lC = P.dma("sync", lambda e, kt=kt, pk=pk: e.dma_start(
                    out=Cxt[pk][:], in_=d_xpjr.ap()[16:32, kt]))
                lD = P.dma("sync", lambda e, kt=kt, pk=pk: e.dma_start(
                    out=dtr[pk][:], in_=d_xpjr.ap()[32:33, kt]))
                P.wait("tensor", LD, lD)
                if S["vsp"].get(k - 1) is not None:
                    P.wait("tensor", VC, S["vsp"][k - 1])
                tdt = P.op("tensor", lambda e, pk=pk: e.matmul(
                    PS[6 + pk][:], ones1t[:], dtr[pk][:], start=True, stop=True), TM)
                S["tdt"][k] = tdt
                P.wait("vector", TM, tdt)
                if k >= 2:
                    P.wait("vector", SC, roll["a3"][16 * (k - 2) + 15])
                    P.wait("vector", PC, S["u"][k - 2])
                for mi in range(NMI):
                    P.op("vector", lambda e, mi=mi, pk=pk: e.tensor_scalar(
                        dtt[pk][:, mi, :], PS[6 + pk][:],
                        vect[:, 4 * mi + 2:4 * mi + 3], vect[:, 4 * mi + 1:4 * mi + 2],
                        ALU.mult, ALU.add))
                vrl = P.op("vector", lambda e, pk=pk: e.tensor_scalar(
                    rb2b[:], dtt[pk][:].rearrange("p m t -> p (m t)"), 0.0, None, ALU.max), VC)
                P.wait("scalar", VC, vrl)
                P.op("scalar", lambda e, pk=pk: e.activation(
                    eb2b[:], dtt[pk][:].rearrange("p m t -> p (m t)"), AF.Abs))
                P.op("scalar", lambda e: e.activation(eb2b[:], eb2b[:], AF.Exp, scale=-1.0))
                ssp = P.op("scalar", lambda e: e.activation(eb2b[:], eb2b[:], AF.Ln, bias=one_t[:, 0:1]), SC)
                P.wait("vector", SC, ssp)
                vsp = P.op("vector", lambda e, pk=pk: e.tensor_tensor(
                    dtt[pk][:].rearrange("p m t -> p (m t)"), rb2b[:], eb2b[:], ALU.add), VC)
                S["vsp"][k] = vsp
                P.wait("gpsimd", VC, vsp)
                P.wait("gpsimd", LD, S["l1"][k])
                if S["b3l"].get(k - 2) is not None:
                    P.wait("gpsimd", VC, S["b3l"][k - 2])
                S["u"][k] = P.op("gpsimd", lambda e, pk=pk: e.tensor_tensor(
                    ut[pk][:].rearrange("p m t -> p (m t)"), dtt[pk][:].rearrange("p m t -> p (m t)"),
                    xct3[pk][:].rearrange("p m t -> p (m t)"), ALU.mult), PC)
                P.wait("scalar", LD, lC)
                if k >= 2:
                    P.wait("scalar", TM, roll["psB_mm"][16 * (k - 2) + 15])
                P.op("scalar", lambda e, pk=pk: e.activation(Bxb[pk][:], Bxt[pk][:], AF.Copy))
                S["bxb"][k] = P.op("scalar", lambda e, pk=pk: e.activation(Cxb[pk][:], Cxt[pk][:], AF.Copy), SC)

            head(0)
            for k in range(NCH):
                pk = k % 2
                if debug and "dty" in dbg_parts and k == 0:
                    P.wait("sync", VC, S["vsp"][0])
                    P.dma("sync", lambda e: e.dma_start(out=dbg["dbg_dt"], in_=dtt[0][:].rearrange("p m t -> p (m t)")))
                def ensure_bcast(g2, kk):
                    if g2 in roll["psB_mm"] or g2 >= 16 * NCH:
                        return
                    n2 = g2 % 16
                    pp2 = g2 % 2
                    psB2 = PS[0 + 2 * pp2]
                    psC2 = PS[1 + 2 * pp2]
                    P.wait("tensor", SC, S["bxb"][kk])
                    if roll["psC_ev"].get(g2 - 2) is not None:
                        P.wait("tensor", SC, roll["psC_ev"][g2 - 2])
                    P.op("tensor", lambda e, n2=n2, psB2=psB2, kk=kk: e.matmul(
                        psB2[:], selb[:, 128 * n2:128 * (n2 + 1)], Bxb[kk % 2][:], start=True, stop=True))
                    tbc = P.op("tensor", lambda e, n2=n2, psC2=psC2, kk=kk: e.matmul(
                        psC2[:], selb[:, 128 * n2:128 * (n2 + 1)], Cxb[kk % 2][:], start=True, stop=True), TM)
                    roll["psB_mm"][g2] = tbc
                    P.wait("scalar", TM, tbc)
                    if roll["tmp"].get(g2 - 2) is not None:
                        P.wait("scalar", VC, roll["tmp"][g2 - 2])
                    if roll["dve"].get(g2 - 2) is not None:
                        P.wait("scalar", VC, roll["dve"][g2 - 2])
                    P.op("scalar", lambda e, pp2=pp2, psB2=psB2: e.activation(Bbb[pp2][:], psB2[:], AF.Copy))
                    cev = P.op("scalar", lambda e, pp2=pp2, psC2=psC2: e.activation(Cb[pp2][:], psC2[:], AF.Copy), SC)
                    roll["psC_ev"][g2] = cev

                def emit_yadd(g2):
                    pp2 = g2 % 2
                    pk2 = (g2 // 16) % 2
                    if g2 % 16 == 0:
                        v = P.op("vector", lambda e, pp2=pp2, pk2=pk2: e.tensor_scalar(
                            yt[pk2][:].rearrange("p m t -> p (m t)"),
                            tmpb[pp2][:], 0.0, None, ALU.add), VC)
                    else:
                        v = P.op("vector", lambda e, pp2=pp2, pk2=pk2: e.tensor_tensor(
                            yt[pk2][:].rearrange("p m t -> p (m t)"),
                            yt[pk2][:].rearrange("p m t -> p (m t)"),
                            tmpb[pp2][:], ALU.add), VC)
                    roll["tmp_rd"][g2] = v

                def emit_a(g2, kk):
                    if g2 in roll["a3"] or g2 >= 16 * NCH:
                        return
                    n2 = g2 % 16
                    pp2 = g2 % 2
                    pk2 = kk % 2
                    if n2 == 0:
                        P.wait("scalar", VC, S["vsp"][kk])
                    if roll["dve"].get(g2 - 2) is not None:
                        P.wait("scalar", VC, roll["dve"][g2 - 2])
                    sa = None
                    for mi in range(NMI):
                        sa = P.op("scalar", lambda e, mi=mi, n2=n2, pp2=pp2, pk2=pk2: e.activation(
                            ab[pp2][:, TC * mi:TC * (mi + 1)], dtt[pk2][:, mi, :], AF.Exp,
                            scale=Aneg[:, 16 * mi + n2:16 * mi + n2 + 1]),
                            SC if mi == NMI - 1 else None)
                    roll["a3"][g2] = sa

                for n in range(NS):
                    g = 16 * k + n
                    pp = g % 2
                    ensure_bcast(g, k)
                    cev = roll["psC_ev"][g]
                    emit_a(g, k)
                    if n == 0:
                        P.wait("vector", PC, S["u"][k])
                    P.wait("vector", SC, cev)
                    b3m = P.op("vector", lambda e, pp=pp, pk=pk: e.tensor_tensor(
                        bb3[:].rearrange("p (m t) -> p m t", m=NMI),
                        ut[pk][:], bc_mid(Bbb[pp][:], NMI), ALU.mult), VC)
                    if n == NS - 1:
                        S["b3l"][k] = b3m
                    P.wait("vector", SC, roll["a3"][g])
                    for mi in range(NMI):
                        P.op("vector", lambda e, mi=mi, n=n, pp=pp: e.tensor_tensor_scan(
                            hta[pp][:, mi, :],
                            ab[pp][:, TC * mi:TC * (mi + 1)],
                            bb3[:, TC * mi:TC * (mi + 1)],
                            st[:, mi, n:n + 1], ALU.mult, ALU.add))
                    vsc = P.op("vector", lambda e, n=n, pp=pp: e.tensor_copy(
                        st[:, :, n:n + 1], hta[pp][:, :, TC - 1:TC]), VC)
                    roll["dve"][g] = vsc
                    tm_ = P.op("vector", lambda e, pp=pp: e.tensor_tensor(
                        tmpb[pp][:].rearrange("p (m t) -> p m t", m=NMI),
                        hta[pp][:], bc_mid(Cb[pp][:], NMI), ALU.mult), VC)
                    roll["tmp"][g] = tm_
                    if g > 0 and (g - 1) not in roll["tmp_rd"]:
                        emit_yadd(g - 1)
                    ensure_bcast(g + 1, k if n + 1 < NS else k + 1)
                    if n == 8 and k + 1 < NCH:
                        head(k + 1)
                    if n >= 14 and k + 1 < NCH:
                        ensure_bcast(g + 2, k + 1)
                        emit_a(g + 2, k + 1)
                emit_yadd(16 * k + 15)
                P.wait("vector", LD, S["l1"][k])
                for mi in range(NMI):
                    P.op("vector", lambda e, mi=mi, pk=pk: e.scalar_tensor_tensor(
                        yt[pk][:, mi, :], xct3[pk][:, mi, :],
                        vect[:, 4 * mi + 3:4 * mi + 4], yt[pk][:, mi, :], ALU.mult, ALU.add))
                P.wait("vector", LD, S["l2"][k])
                if S["wol"].get(k - 2) is not None:
                    P.wait("vector", TM, S["wol"][k - 2])
                vg = P.op("vector", lambda e, pk=pk: e.tensor_tensor(
                    gt[pk][:].rearrange("p m t -> p (m t)"), yt[pk][:].rearrange("p m t -> p (m t)"),
                    zsl[pk][:].rearrange("p m t -> p (m t)"), ALU.mult), VC)
                S["vg"][k] = vg
                if debug and "dty" in dbg_parts and k == 0:
                    P.wait("sync", VC, vg)
                    P.dma("sync", lambda e: e.dma_start(out=dbg["dbg_y"], in_=yt[0][:].rearrange("p m t -> p (m t)")))
                P.wait("tensor", VC, vg)
                wo_dma = {}
                for mo in range(KD):
                    pw = mo % 2
                    psW = PS[4 + pw]
                    if wo_bank_ev[pw] is not None:
                        P.wait("tensor", SC, wo_bank_ev[pw])
                    two = None
                    for kd in range(NMI):
                        two = P.op("tensor", lambda e, kd=kd, mo=mo, pk=pk, psW=psW: e.matmul(
                            psW[:], Wob[:, kd, 128 * mo:128 * (mo + 1)], gt[pk][:, kd, :],
                            start=(kd == 0), stop=(kd == NMI - 1)), TM if kd == NMI - 1 else None)
                    if mo == KD - 1:
                        S["wol"][k] = two
                    P.wait("scalar", TM, two)
                    if wo_dma.get(mo - 2) is not None:
                        P.wait("scalar", WO, wo_dma[mo - 2])
                    swo = P.op("scalar", lambda e, pw=pw, psW=psW: e.activation(wot[pw][:], psW[:], AF.Copy), SC)
                    wo_bank_ev[pw] = swo
                    P.wait("sync", SC, swo)
                    wo_dma[mo] = P.dma("sync", lambda e, k=k, mo=mo, pw=pw: e.dma_start(
                        out=d_wop_j[k // 4].ap()[1024 * (k % 4) + 128 * mo:1024 * (k % 4) + 128 * (mo + 1), :],
                        in_=wot[pw][:]), ctr=WO)
                if k % 4 == 3:
                    P.wait("gpsimd", WO, 16 * KD * (k + 1))
                    CC.n += 1
                    P.op("gpsimd", lambda e, j=k // 4: e.collective_compute(
                        "ReduceScatter", ALU.add, GROUPS,
                        [d_wop_j[j].ap().opt()], [d_att_j[j].ap().opt()]).then_inc(CC.h, 1))
                if debug and "wop" in dbg_parts and k == 3:
                    P.wait("sync", LD, P.LD.n)
                    P.dma("sync", lambda e: e.dma_start(out=dbg["dbg_wop"], in_=d_wop_j[0].ap()))
            P.wait("gpsimd", CC, CC.n)
            P.op("gpsimd", lambda e: e.drain(fusable=False))
            P.barrier()
            AR.pop()

        if True:
            AR.push()
            T = alloc_ffn_scope()
            prev = {}
            oysp = [None, None]
            xs_vc = [None, None]
            xs_sc = [None, None]
            ln2_rd = None
            cen_rd = None
            for j in range(NJ):
                par = j % 2
                jt = slice(TC * j, TC * (j + 1))
                xs = T["xs"][par]
                if j >= 2:
                    P.wait("sync", VC, xs_vc[par])
                    P.wait("sync", SC, xs_sc[par])
                    P.wait("sync", LD, oysp[par])
                l1 = P.dma("sync", lambda e, jt=jt, xs=xs: e.dma_start(out=xs[:], in_=d_x1_r[:, :, jt]))
                if ln2_rd is not None:
                    P.wait("sync", VC, ln2_rd)
                l2 = P.dma("sync", lambda e, jt=jt: e.dma_start(out=T["ln2t"][:], in_=d_ln2_r[:, :, jt]))
                if cen_rd is not None:
                    P.wait("sync", VC, cen_rd)
                l3 = P.dma("sync", lambda e, j=j: e.dma_start(
                    out=T["cen"][:],
                    in_=d_att_j[j].ap().rearrange("(k p) t -> p k t", p=128)))
                P.wait("vector", LD, l3)
                ln2_rd = P.op("vector", lambda e, xs=xs: e.tensor_tensor(
                    xs[:].rearrange("p k t -> p (k t)"), xs[:].rearrange("p k t -> p (k t)"),
                    T["ln2t"][:].rearrange("p k t -> p (k t)"), ALU.add), VC)
                vx2 = P.op("vector", lambda e, xs=xs: e.tensor_tensor(
                    xs[:].rearrange("p k t -> p (k t)"), xs[:].rearrange("p k t -> p (k t)"),
                    T["cen"][:].rearrange("p k t -> p (k t)"), ALU.add), VC)
                if debug and "p3" in dbg_parts:
                    P.wait("sync", LD, l3)
                    P.dma("sync", lambda e, j=j: e.dma_start(
                        out=dbg["dbg_att"][D * j:D * (j + 1), :].rearrange("(k p) t -> p k t", p=128),
                        in_=T["cen"][:]))
                s_r, c2m = ln_stats(T, xs, wait_vc=vx2)
                xs_sc[par] = c2m
                P.wait("vector", SC, s_r)
                vx = ln_apply_plain(T, xs, T["xnb"])
                cen_rd = vx
                xs_vc[par] = vx
                P.wait("tensor", VC, vx)
                prev = ffn(T, par, prev)
                xv = prev["xv"]
                P.wait("sync", VC, xv)
                oysp[par] = P.dma("sync", lambda e, jt=jt, xs=xs: e.dma_start(
                    out=o_y_r[:, :, jt], in_=xs[:]))

        P.wait("sync", LD, P.LD.n)

        block = E(nc.Block())
        P.emit(block)

    return nc


_DEBUG = False
_NC_CACHE = {}


_DBG_PARTS = ("p3",)


def _get_nc():
    key = (bool(_DEBUG), tuple(_DBG_PARTS))
    if key not in _NC_CACHE:
        _NC_CACHE[key] = build_program(debug=key[0], dbg_parts=key[1])
    return _NC_CACHE[key]


def kernel(L0_attn_x, L0_attn_ln_g, L0_attn_ln_b, L0_attn_W_in, L0_attn_conv_w,
           L0_attn_conv_b, L0_attn_W_xp, L0_attn_W_dt, L0_attn_b_dt, L0_attn_A_log,
           L0_attn_D, L0_attn_W_out, L0_ffn_w_gate, L0_ffn_w_up, L0_ffn_w_down):
    f32 = np.float32
    x = np.asarray(L0_attn_x, f32)
    W_in = np.asarray(L0_attn_W_in, f32)
    W_out = np.asarray(L0_attn_W_out, f32)
    W_xp = np.asarray(L0_attn_W_xp, f32)
    conv_w = np.asarray(L0_attn_conv_w, f32)
    conv_b = np.asarray(L0_attn_conv_b, f32)
    W_dt = np.asarray(L0_attn_W_dt, f32)
    b_dt = np.asarray(L0_attn_b_dt, f32)
    A_log = np.asarray(L0_attn_A_log, f32)
    Dvec = np.asarray(L0_attn_D, f32)
    ln_g = np.asarray(L0_attn_ln_g, f32)
    ln_b = np.asarray(L0_attn_ln_b, f32)
    wg = np.ascontiguousarray(np.asarray(L0_ffn_w_gate, np.int32).T)
    wu = np.ascontiguousarray(np.asarray(L0_ffn_w_up, np.int32).T)
    wd = np.ascontiguousarray(np.asarray(L0_ffn_w_down, np.int32).T)

    sel = np.zeros((16, 2048), f32)
    for n in range(16):
        sel[n, 128 * n:128 * (n + 1)] = 1.0
    ones128 = np.ones((128, 128), f32)
    ones1 = np.ones((1, 128), f32)
    lngb = np.zeros((128, 16), f32)
    for kb in range(KD):
        lngb[:, 2 * kb] = ln_g[128 * kb:128 * (kb + 1)]
        lngb[:, 2 * kb + 1] = ln_b[128 * kb:128 * (kb + 1)]

    in_maps = []
    for c in range(8):
        b, r = divmod(c, 4)
        sl = slice(512 * r, 512 * (r + 1))
        xc = np.concatenate(
            [x[b, TC * (4 * j + r):TC * (4 * j + r + 1), :].T for j in range(NJ)], axis=1)
        winc_c = np.concatenate([W_in[:, sl], W_in[:, 2048 + sl.start:2048 + sl.stop]], axis=1)
        cwc = conv_w[sl].reshape(4, 128, 4).transpose(1, 0, 2).reshape(128, 16)
        avals = A_log[sl].reshape(4, 128, 16).transpose(1, 0, 2).reshape(128, 64)
        vv = np.stack([conv_b[sl], b_dt[sl], W_dt[0, sl], Dvec[sl]], axis=-1)
        vv = vv.reshape(4, 128, 4).transpose(1, 0, 2).reshape(128, 16)
        in_maps.append(dict(
            x_fm=np.ascontiguousarray(xc),
            winc=np.ascontiguousarray(winc_c),
            woc=np.ascontiguousarray(W_out[sl]),
            wxpc=np.ascontiguousarray(W_xp[sl]),
            wgT=wg, wuT=wu, wdT=wd,
            cwc=np.ascontiguousarray(cwc),
            avals=np.ascontiguousarray(avals),
            vecs=np.ascontiguousarray(vv),
            lngb=lngb, sel=sel, ones128=ones128, ones1=ones1,
        ))

    nc = _get_nc()
    res = run_bass_kernel_spmd(nc, in_maps, core_ids=list(range(8)))

    out = np.empty((B, L, D), f32)
    for c in range(8):
        b, r = divmod(c, 4)
        oy = np.asarray(res.results[c]["o_y"], f32)
        for j in range(NJ):
            out[b, TC * (4 * j + r):TC * (4 * j + r + 1), :] = oy[:, TC * j:TC * (j + 1)].T
    if _DEBUG:
        kernel._dbg = res.results
    return out


# version 32
# speedup vs baseline: 1.1538x; 1.0003x over previous
import dataclasses
from contextlib import ExitStack

import numpy as np

import concourse.bass as bass
import concourse.mybir as mybir
from concourse.bass_utils import run_bass_kernel_spmd

F32 = mybir.dt.float32
BF16 = mybir.dt.bfloat16
I32 = mybir.dt.int32
AF = mybir.ActivationFunctionType
ALU = mybir.AluOpType

B, L, D, DIN, H, NS = 2, 8192, 1024, 2048, 3072, 16
TC = 512
NCH = 16
NJ = 4
KD = 8
NMI = 4
HM = H // 128
EPS = 1e-5
GROUPS = [[0, 1, 2, 3], [4, 5, 6, 7]]
ENGINES = ("sync", "vector", "scalar", "tensor", "gpsimd")


def bc_mid(ap, rep):
    assert len(ap.ap) == 2
    return dataclasses.replace(ap, ap=[ap.ap[0], [0, rep], ap.ap[1]])


class Ctr:
    def __init__(self, h):
        self.h = h
        self.n = 0


class Bld:
    def __init__(self, nc, ctx):
        self.nc = nc
        self.ops = {e: [] for e in ENGINES}
        E = ctx.enter_context
        self.LD = Ctr(E(nc.semaphore("LD")))
        self.CC = Ctr(E(nc.semaphore("CC")))
        self.VC = Ctr(E(nc.semaphore("VC")))
        self.SC = Ctr(E(nc.semaphore("SC")))
        self.TM = Ctr(E(nc.semaphore("TM")))
        self.PC = Ctr(E(nc.semaphore("PC")))
        self.CD = Ctr(E(nc.semaphore("CD")))
        self.WQ = Ctr(E(nc.semaphore("WQ")))
        self.WO = Ctr(E(nc.semaphore("WO")))
        self.XN = Ctr(E(nc.semaphore("XN")))
        self.BAR = Ctr(E(nc.semaphore("BAR")))

    def op(self, eng, build, ctr=None):
        if ctr is None:
            self.ops[eng].append(build)
            return None
        ctr.n += 1
        tgt = ctr.n
        self.ops[eng].append(lambda e, b=build, c=ctr: b(e).then_inc(c.h, 1))
        return tgt

    def dma(self, eng, build, ctr=None):
        ctr = ctr or self.LD
        ctr.n += 16
        tgt = ctr.n
        self.ops[eng].append(lambda e, b=build, c=ctr: b(e).then_inc(c.h, 16))
        return tgt

    def wait(self, eng, ctr, val):
        if val is None:
            return
        self.ops[eng].append(lambda e, c=ctr, v=val: e.wait_ge(c.h, v))

    def barrier(self):
        self.wait("sync", self.LD, self.LD.n)
        self.wait("sync", self.WQ, self.WQ.n)
        self.wait("sync", self.WO, self.WO.n)
        self.wait("sync", self.XN, self.XN.n)
        tgt = self.BAR.n + len(ENGINES)
        for e in ENGINES:
            self.ops[e].append(lambda E: E.sem_inc(self.BAR.h, 1))
        self.BAR.n = tgt
        for e in ENGINES:
            self.ops[e].append(lambda E, v=tgt: E.wait_ge(self.BAR.h, v))

    def coll(self, kind, op_, ins, outs, wait_ld=None):
        if wait_ld is not None:
            self.wait("gpsimd", self.LD, wait_ld)
        self.CC.n += 1
        tgt = self.CC.n
        self.ops["gpsimd"].append(lambda e, k=kind, o=op_, i=ins, u=outs: e.collective_compute(
            k, o, GROUPS, i, u).then_inc(self.CC.h, 1))
        self.wait("gpsimd", self.CC, tgt)
        self.ops["gpsimd"].append(lambda e: e.drain(fusable=False))
        self.CD.n += 1
        cdt = self.CD.n
        self.ops["gpsimd"].append(lambda e, v=cdt: e.sem_inc(self.CD.h, 1))
        return cdt

    def emit(self, block):
        b = self
        @block.sync
        def _(eng):
            for f in b.ops["sync"]:
                f(eng)
        @block.vector
        def _(eng):
            for f in b.ops["vector"]:
                f(eng)
        @block.scalar
        def _(eng):
            for f in b.ops["scalar"]:
                f(eng)
        @block.tensor
        def _(eng):
            for f in b.ops["tensor"]:
                f(eng)
        @block.gpsimd
        def _(eng):
            for f in b.ops["gpsimd"]:
                f(eng)


class Arena:
    ESZ = {F32: 4, BF16: 2, I32: 4}

    def __init__(self, nc, ctx, name, words):
        self.h = ctx.enter_context(nc.sbuf_tensor(name, [128, words], F32))
        self.hb = self.h.bitcast(BF16)
        self.hi = self.h.bitcast(I32)
        self.cap = words * 4
        self.off = 0
        self.stack = []

    def push(self):
        self.stack.append(self.off)

    def pop(self):
        self.off = self.stack.pop()

    def alloc(self, shape, dt=F32):
        n = 1
        for s in shape[1:]:
            n *= s
        esz = self.ESZ[dt]
        self.off = (self.off + 63) & ~63
        start = self.off
        self.off += n * esz
        assert self.off <= self.cap, f"arena overflow: {self.off} > {self.cap}"
        if dt == F32:
            base = self.h[:, start // 4: start // 4 + n]
        elif dt == I32:
            base = self.hi[:, start // 4: start // 4 + n]
        else:
            base = self.hb[:, start // 2: start // 2 + n]
        if shape[0] != 128:
            base = base[0:shape[0], :]
        if len(shape) == 3:
            base = base.rearrange("p (a b) -> p a b", a=shape[1])
        return base


def build_program(debug=False, dbg_parts=("p1","scan","wop","rs","p3","dty")):
    nc = bass.Bass("TRN2", target_bir_lowering=False, detect_race_conditions=False)

    def din(name, shape, dt=F32):
        return nc.dram_tensor(name, shape, dt, kind="ExternalInput").ap()

    x_fm = din("x_fm", [D, NJ * TC])
    winc = din("winc", [D, 1024])
    woc = din("woc", [512, D])
    wxpc = din("wxpc", [512, 33])
    wgT = din("wgT", [D, H], I32)
    wuT = din("wuT", [D, H], I32)
    wdT = din("wdT", [H, D], I32)
    cwc = din("cwc", [128, 16])
    avals = din("avals", [128, 64])
    vecs = din("vecs", [128, 16])
    lngb = din("lngb", [128, 16])
    sel_i = din("sel", [16, 2048])
    ones128 = din("ones128", [128, 128])
    ones1 = din("ones1", [1, 128])

    o_y = nc.dram_tensor("o_y", [D, NJ * TC], F32, kind="ExternalOutput").ap()

    def dint(name, shape, dt=F32):
        return nc.dram_tensor(name, shape, dt, kind="Internal")

    d_wg = dint("d_wg", [D, H], BF16)
    d_wu = dint("d_wu", [D, H], BF16)
    d_wd = dint("d_wd", [H, D], BF16)
    d_winb = dint("d_winb", [D, 1024], BF16)
    d_wob = dint("d_wob", [512, D], BF16)
    d_x1 = dint("d_x1", [D, NJ * TC], F32)
    d_ln2 = dint("d_ln2", [D, NJ * TC], F32)
    d_xn = dint("d_xn", [NJ * D, TC], BF16)
    d_xng = dint("d_xng", [NJ * 4 * D, TC], BF16)
    d_fin = dint("d_fin", [1, 128], F32)
    d_fout = dint("d_fout", [1, 128], F32)
    d_xc = dint("d_xc", [512, L], F32)
    d_z = dint("d_z", [512, L], F32)
    d_xpj = dint("d_xpj", [33, L], F32)
    d_xpjr = dint("d_xpjr", [33, L], F32)
    d_wop_j = [dint(f"d_wop{j}", [4 * D, TC], F32) for j in range(NJ)]
    d_att_j = [dint(f"d_att{j}", [D, TC], F32) for j in range(NJ)]

    dbg = {}
    if debug:
        for name, shape, dt in [
            ("dbg_xn", [D, NJ * TC], BF16), ("dbg_x1", [D, NJ * TC], F32),
            ("dbg_dt", [128, NMI * TC], F32),
            ("dbg_y", [128, NMI * TC], F32), ("dbg_att", [NJ * D, TC], F32),
            ("dbg_u", [128, NMI * TC], F32), ("dbg_a0", [128, NMI * TC], F32),
            ("dbg_b0", [128, NMI * TC], F32), ("dbg_h0", [128, NMI * TC], F32),
            ("dbg_y0", [128, NMI * TC], F32), ("dbg_wop", [4 * D, TC], F32),
            ("dbg_wop2", [4 * D, TC], F32),
        ]:
            dbg[name] = nc.dram_tensor(name, shape, dt, kind="ExternalOutput").ap()

    x_fm_r = x_fm.rearrange("(k p) t -> p k t", p=128)
    d_x1_r = d_x1.ap().rearrange("(k p) t -> p k t", p=128)
    d_ln2_r = d_ln2.ap().rearrange("(k p) t -> p k t", p=128)
    d_xc_r = d_xc.ap().rearrange("(m p) t -> p m t", p=128)
    d_z_r = d_z.ap().rearrange("(m p) t -> p m t", p=128)
    o_y_r = o_y.rearrange("(k p) t -> p k t", p=128)

    with ExitStack() as ctx:
        E = ctx.enter_context
        P = Bld(nc, ctx)
        LD, CC, VC, SC, TM, PC, CD, WQ, WO, XN = P.LD, P.CC, P.VC, P.SC, P.TM, P.PC, P.CD, P.WQ, P.WO, P.XN

        def sb(name, shape, dt=F32):
            return E(nc.sbuf_tensor(name, shape, dt))

        mu_t = sb("mu_t", [128, TC])
        ms_t = sb("ms_t", [128, TC])
        var_t = sb("var_t", [128, TC])
        rstd_t = sb("rstd_t", [128, TC])
        f1buf = sb("f1buf", [128, TC])
        cwt = sb("cwt", [128, 16])
        Aneg = sb("Aneg", [128, 64])
        vect = sb("vect", [128, 16])
        lngbt = sb("lngbt", [128, 16])
        onesb = sb("onesb", [128, 128], BF16)
        ones1t = sb("ones1t", [1, 128])
        eps_t = sb("eps_t", [128, 1])
        one_t = sb("one_t", [128, 1])
        selb = sb("selb", [16, 2048], BF16)
        st = sb("st", [128, NMI, NS])

        PS = [E(nc.psum_tensor(f"ps{i}", [128, TC], F32)) for i in range(8)]
        AR = Arena(nc, ctx, "arena", 43 * 1024)

        if True:
            AR.push()
            deqi = [AR.alloc([128, H], I32) for _ in range(2)]
            deqb = [AR.alloc([128, H], BF16) for _ in range(2)]
            wfl = AR.alloc([128, 1024], F32)
            p0sel = AR.alloc([16, 2048], F32)

            P.op("vector", lambda e: e.memset(eps_t[:], float(EPS)))
            vcn = P.op("vector", lambda e: e.memset(one_t[:], 1.0), VC)
            P.wait("scalar", VC, vcn)
            P.dma("sync", lambda e: e.dma_start(out=cwt[:], in_=cwc))
            P.dma("sync", lambda e: e.dma_start(out=vect[:], in_=vecs))
            P.dma("sync", lambda e: e.dma_start(out=lngbt[:], in_=lngb))
            P.dma("sync", lambda e: e.dma_start(out=ones1t[:], in_=ones1))
            t5 = P.dma("sync", lambda e: e.dma_start(out=mu_t[:, 0:64], in_=avals))
            P.wait("scalar", LD, t5)
            s1 = P.op("scalar", lambda e: e.activation(ms_t[:, 0:64], mu_t[:, 0:64], AF.Exp), SC)
            P.wait("vector", SC, s1)
            P.op("vector", lambda e: e.tensor_scalar(Aneg[:], ms_t[:, 0:64], -1.0, None, ALU.mult), VC)
            t6 = P.dma("sync", lambda e: e.dma_start(out=rstd_t[:, 0:128], in_=ones128))
            P.wait("scalar", LD, t6)
            P.op("scalar", lambda e: e.activation(onesb[:], rstd_t[:, 0:128], AF.Copy), SC)
            t7 = P.dma("sync", lambda e: e.dma_start(out=p0sel[:], in_=sel_i))
            P.wait("scalar", LD, t7)
            P.op("scalar", lambda e: e.activation(selb[:], p0sel[:], AF.Copy), SC)

            def dequant(src_ap, dst_t, nblocks, width):
                vvs = [None] * nblocks
                dmas = [None] * nblocks
                for i in range(nblocks):
                    bb = i % 2
                    if i >= 2:
                        P.wait("sync", VC, vvs[i - 2])
                    tl = P.dma("sync", lambda e, i=i, bb=bb, s=src_ap, w=width: e.dma_start(
                        out=deqi[bb][:, 0:w], in_=s[128 * i:128 * (i + 1), 0:w]))
                    P.wait("vector", LD, tl)
                    if i >= 2:
                        P.wait("vector", LD, dmas[i - 2])
                    vvs[i] = P.op("vector", lambda e, bb=bb, w=width: e.tensor_scalar(
                        deqb[bb][:, 0:w], deqi[bb][:, 0:w], 1.0 / 127.0, None, ALU.mult), VC)
                    P.wait("sync", VC, vvs[i])
                    dmas[i] = P.dma("sync", lambda e, i=i, bb=bb, d=dst_t, w=width: e.dma_start(
                        out=d.ap()[128 * i:128 * (i + 1), 0:w], in_=deqb[bb][:, 0:w]))

            dequant(wgT, d_wg, KD, H)
            dequant(wuT, d_wu, KD, H)
            dequant(wdT, d_wd, HM, D)

            def stage(src, dst, nblocks):
                prev = {"sv": None, "dma": None}
                for i in range(nblocks):
                    if prev["sv"] is not None:
                        P.wait("sync", SC, prev["sv"])
                    tl = P.dma("sync", lambda e, i=i, s=src: e.dma_start(
                        out=wfl[:], in_=s[128 * i:128 * (i + 1), :]))
                    P.wait("scalar", LD, tl)
                    if prev["dma"] is not None:
                        P.wait("scalar", LD, prev["dma"])
                    sv = P.op("scalar", lambda e: e.activation(deqb[0][:, 0:1024], wfl[:], AF.Copy), SC)
                    prev["sv"] = sv
                    P.wait("sync", SC, sv)
                    prev["dma"] = P.dma("sync", lambda e, i=i, d=dst: e.dma_start(
                        out=d.ap()[128 * i:128 * (i + 1), :], in_=deqb[0][:, 0:1024]))
            stage(winc, d_winb, KD)
            stage(woc, d_wob, NMI)
            P.barrier()
            AR.pop()

        def ln_stats(T, src, wait_vc=None):
            if wait_vc is not None:
                P.wait("scalar", VC, wait_vc)
            P.op("scalar", lambda e: e.activation(T["xbb"][:], src[:], AF.Copy))
            c2 = P.op("scalar", lambda e: e.activation(T["xnb"][:], src[:], AF.Square), SC)
            P.wait("tensor", SC, c2)
            for kb in range(KD):
                P.op("tensor", lambda e, kb=kb: e.matmul(
                    PS[4][:], onesb[:], T["xbb"][:, kb, :], start=(kb == 0), stop=(kb == KD - 1)),
                    TM if kb == KD - 1 else None)
            tm2 = None
            for kb in range(KD):
                tm2 = P.op("tensor", lambda e, kb=kb: e.matmul(
                    PS[5][:], onesb[:], T["xnb"][:, kb, :], start=(kb == 0), stop=(kb == KD - 1)),
                    TM if kb == KD - 1 else None)
            P.wait("vector", TM, tm2)
            P.op("vector", lambda e: e.tensor_scalar(mu_t[:], PS[4][:], 1.0 / D, None, ALU.mult))
            P.op("vector", lambda e: e.tensor_scalar(ms_t[:], PS[5][:], 1.0 / D, None, ALU.mult))
            P.op("vector", lambda e: e.tensor_tensor(var_t[:], mu_t[:], mu_t[:], ALU.mult))
            v1 = P.op("vector", lambda e: e.tensor_tensor(var_t[:], ms_t[:], var_t[:], ALU.subtract), VC)
            P.wait("scalar", VC, v1)
            P.op("scalar", lambda e: e.activation(rstd_t[:], var_t[:], AF.Ln, bias=eps_t[:, 0:1]))
            s2 = P.op("scalar", lambda e: e.activation(rstd_t[:], rstd_t[:], AF.Exp, scale=-0.5), SC)
            return s2, c2

        def ffn(T, par, prev):
            xs, xnb, fbuf = T["xs"][par], T["xnb"], T["fbuf"]
            wgt_s, wut_s, wdt_s = T["wgt_s"], T["wut_s"], T["wdt_s"]
            if prev.get("tds7") is not None:
                P.wait("sync", TM, prev["tds7"])
            tus = [None] * HM
            fvs = [None] * HM
            f1s = [None] * HM
            for mh in range(HM):
                bw = mh % 2
                pg = PS[0 + 2 * bw]
                pu = PS[1 + 2 * bw]
                if mh >= 2:
                    P.wait("sync", TM, tus[mh - 2])
                P.dma("sync", lambda e, mh=mh, bw=bw: e.dma_start(
                    out=wgt_s[bw][:],
                    in_=d_wg.ap()[:, 128 * mh:128 * (mh + 1)].rearrange("(k p) m -> p k m", p=128)), ctr=WQ)
                lw2 = P.dma("sync", lambda e, mh=mh, bw=bw: e.dma_start(
                    out=wut_s[bw][:],
                    in_=d_wu.ap()[:, 128 * mh:128 * (mh + 1)].rearrange("(k p) m -> p k m", p=128)), ctr=WQ)
                P.wait("tensor", WQ, lw2)
                if mh >= 2:
                    P.wait("tensor", SC, f1s[mh - 2])
                    P.wait("tensor", VC, fvs[mh - 2])
                elif prev.get("f1s7") is not None:
                    P.wait("tensor", SC, prev["f1s7"])
                    P.wait("tensor", VC, prev["fvs7"])
                tg = None
                for kb in range(KD):
                    tg = P.op("tensor", lambda e, kb=kb, bw=bw, pg=pg: e.matmul(
                        pg[:], wgt_s[bw][:, kb, :], xnb[:, kb, :],
                        start=(kb == 0), stop=(kb == KD - 1)), TM if kb == KD - 1 else None)
                for kb in range(KD):
                    tus[mh] = P.op("tensor", lambda e, kb=kb, bw=bw, pu=pu: e.matmul(
                        pu[:], wut_s[bw][:, kb, :], xnb[:, kb, :],
                        start=(kb == 0), stop=(kb == KD - 1)), TM if kb == KD - 1 else None)
                P.wait("scalar", TM, tg)
                if mh >= 1:
                    P.wait("scalar", VC, fvs[mh - 1])
                elif prev.get("fvs7") is not None:
                    P.wait("scalar", VC, prev["fvs7"])
                f1s[mh] = P.op("scalar", lambda e, pg=pg: e.activation(f1buf[:], pg[:], AF.Silu), SC)
                P.wait("vector", SC, f1s[mh])
                P.wait("vector", TM, tus[mh])
                if mh == 0 and prev.get("tds7") is not None:
                    P.wait("vector", TM, prev["tds7"])
                fvs[mh] = P.op("vector", lambda e, mh=mh, pu=pu: e.tensor_tensor(
                    fbuf[:, mh, :], f1buf[:], pu[:], ALU.mult), VC)
            xv = None
            tds = [None] * KD
            for mo in range(KD):
                bw = mo % 2
                pd = PS[6 + bw]
                if mo >= 2:
                    P.wait("sync", TM, tds[mo - 2])
                lw = P.dma("sync", lambda e, mo=mo, bw=bw: e.dma_start(
                    out=wdt_s[bw][:],
                    in_=d_wd.ap()[:, 128 * mo:128 * (mo + 1)].rearrange("(k p) m -> p k m", p=128)), ctr=WQ)
                P.wait("tensor", WQ, lw)
                if mo == 0:
                    P.wait("tensor", VC, fvs[-1])
                if mo >= 2:
                    P.wait("tensor", VC, xv)
                elif prev.get("xv7") is not None:
                    P.wait("tensor", VC, prev["xv7"])
                for kh in range(HM):
                    tds[mo] = P.op("tensor", lambda e, kh=kh, bw=bw, pd=pd: e.matmul(
                        pd[:], wdt_s[bw][:, kh, :], fbuf[:, kh, :],
                        start=(kh == 0), stop=(kh == HM - 1)), TM if kh == HM - 1 else None)
                P.wait("vector", TM, tds[mo])
                xv = P.op("vector", lambda e, mo=mo, pd=pd, xs=xs: e.scalar_tensor_tensor(
                    xs[:, mo, :], pd[:], 0.5, xs[:, mo, :], ALU.mult, ALU.add), VC)
            return dict(xv=xv, tds7=tds[-1], f1s7=f1s[-1], fvs7=fvs[-1], xv7=xv)

        def ln_apply_plain(T, src, dst_bf):
            P.op("vector", lambda e, src=src: e.tensor_tensor(
                T["cen"][:], src[:], bc_mid(mu_t[:], KD), ALU.subtract))
            return P.op("vector", lambda e, dst_bf=dst_bf: e.tensor_tensor(
                dst_bf[:], T["cen"][:], bc_mid(rstd_t[:], KD), ALU.mult), VC)

        def alloc_ffn_scope():
            T = {}
            T["xs"] = [AR.alloc([128, KD, TC], F32) for _ in range(2)]
            T["ln2t"] = AR.alloc([128, KD, TC], F32)
            T["cen"] = AR.alloc([128, KD, TC], F32)
            T["xbb"] = AR.alloc([128, KD, TC], BF16)
            T["xnb"] = AR.alloc([128, KD, TC], BF16)
            T["fbuf"] = AR.alloc([128, HM, TC], BF16)
            T["wgt_s"] = [AR.alloc([128, KD, 128], BF16) for _ in range(2)]
            T["wut_s"] = [AR.alloc([128, KD, 128], BF16) for _ in range(2)]
            T["wdt_s"] = [AR.alloc([128, HM, 128], BF16) for _ in range(2)]
            return T

        ag_done = [None] * NJ
        if True:
            AR.push()
            T = alloc_ffn_scope()
            prev = {}
            x1sp = [None, None]
            xs_vc = [None, None]
            xs_sc = [None, None]
            for j in range(NJ):
                par = j % 2
                jt = slice(TC * j, TC * (j + 1))
                xs = T["xs"][par]
                if j >= 2:
                    P.wait("sync", VC, xs_vc[par])
                    P.wait("sync", SC, xs_sc[par])
                    P.wait("sync", LD, x1sp[par])
                lx = P.dma("sync", lambda e, jt=jt, xs=xs: e.dma_start(out=xs[:], in_=x_fm_r[:, :, jt]))
                P.wait("scalar", LD, lx)
                P.wait("vector", LD, lx)
                s_r, _ = ln_stats(T, xs)
                P.wait("vector", SC, s_r)
                vx = ln_apply_plain(T, xs, T["xnb"])
                P.wait("tensor", VC, vx)
                prev = ffn(T, par, prev)
                xv = prev["xv"]
                P.wait("sync", VC, xv)
                x1sp[par] = P.dma("sync", lambda e, jt=jt, xs=xs: e.dma_start(
                    out=d_x1_r[:, :, jt], in_=xs[:]))
                s_r, c2m = ln_stats(T, xs, wait_vc=xv)
                xs_sc[par] = c2m
                P.wait("vector", SC, s_r)
                xs_vc[par] = P.op("vector", lambda e, xs=xs: e.tensor_tensor(
                    T["ln2t"][:], xs[:], bc_mid(mu_t[:], KD), ALU.subtract), VC)
                v2 = P.op("vector", lambda e: e.tensor_tensor(
                    T["ln2t"][:], T["ln2t"][:], bc_mid(rstd_t[:], KD), ALU.mult), VC)
                P.wait("sync", VC, v2)
                P.dma("sync", lambda e, jt=jt: e.dma_start(out=d_ln2_r[:, :, jt], in_=T["ln2t"][:]))
                s_r, _ = ln_stats(T, T["ln2t"], wait_vc=v2)
                P.wait("vector", SC, s_r)
                P.op("vector", lambda e: e.tensor_tensor(
                    T["cen"][:], T["ln2t"][:], bc_mid(mu_t[:], KD), ALU.subtract))
                P.op("vector", lambda e: e.tensor_tensor(
                    T["cen"][:], T["cen"][:], bc_mid(rstd_t[:], KD), ALU.mult))
                v3 = None
                for kb in range(KD):
                    v3 = P.op("vector", lambda e, kb=kb: e.tensor_scalar(
                        T["xnb"][:, kb, :], T["cen"][:, kb, :],
                        lngbt[:, 2 * kb:2 * kb + 1], lngbt[:, 2 * kb + 1:2 * kb + 2],
                        ALU.mult, ALU.add), VC)
                P.wait("sync", VC, v3)
                P.dma("sync", lambda e, j=j: e.dma_start(
                    out=d_xn.ap()[D * j:D * (j + 1), :].rearrange("(k p) t -> p k t", p=128),
                    in_=T["xnb"][:]), ctr=XN)
                if debug and "p1" in dbg_parts:
                    P.dma("sync", lambda e, jt=jt: e.dma_start(
                        out=dbg["dbg_xn"].rearrange("(k p) t -> p k t", p=128)[:, :, jt], in_=T["xnb"][:]))
                    P.dma("sync", lambda e, jt=jt, xs=xs: e.dma_start(
                        out=dbg["dbg_x1"].rearrange("(k p) t -> p k t", p=128)[:, :, jt], in_=xs[:]))
            P.barrier()
            AR.pop()
        for j in range(NJ):
            CC.n += 1
            P.op("gpsimd", lambda e, j=j: e.collective_compute(
                "AllGather", ALU.bypass, GROUPS,
                [d_xn.ap()[D * j:D * (j + 1), :].opt()],
                [d_xng.ap()[4 * D * j:4 * D * (j + 1), :].opt()]).then_inc(CC.h, 1))
        P.wait("gpsimd", CC, CC.n)
        P.op("gpsimd", lambda e: e.drain(fusable=False))
        CD.n += 1
        P.op("gpsimd", lambda e: e.sem_inc(CD.h, 1))
        ag_all = CD.n

        if True:
            AR.push()
            Winb = AR.alloc([128, KD, 1024], BF16)
            xng = [AR.alloc([128, KD, TC], BF16) for _ in range(2)]
            xhh = AR.alloc([128, NMI, TC + 8], F32)
            zst = AR.alloc([128, NMI, TC], F32)
            xct2 = AR.alloc([128, NMI, TC], F32)
            xcb2 = AR.alloc([128, NMI, TC], BF16)
            cvac = AR.alloc([128, NMI, TC], F32)
            Wxpb = AR.alloc([128, NMI, 33], BF16)
            xpjt = AR.alloc([33, TC], F32)
            lw = P.dma("sync", lambda e: e.dma_start(
                out=Winb[:], in_=d_winb.ap().rearrange("(k p) m -> p k m", p=128)))
            for mi in range(NMI):
                tw = P.dma("sync", lambda e, mi=mi: e.dma_start(
                    out=var_t[:, 0:33], in_=wxpc[128 * mi:128 * (mi + 1), :]))
                P.wait("scalar", LD, tw)
                sw = P.op("scalar", lambda e, mi=mi: e.activation(Wxpb[:, mi, :], var_t[:, 0:33], AF.Copy), SC)
                P.wait("sync", SC, sw)
            P.wait("tensor", LD, lw)
            P.op("gpsimd", lambda e: e.memset(xhh[:, :, 0:3], 0.0), PC)
            P.wait("scalar", PC, PC.n)
            conv_done = None
            sil_done = None
            cast_done = None
            tpj_prev = None
            spj_prev = None
            zs_spill = None
            xc_spill = None
            pj_spill = None
            bank_ev = {}
            mm_last = [None, None]
            for k in range(NCH):
                bk = k % 2
                r0 = 4 * D * (k // 4) + D * (k % 4)
                c0 = TC * k
                if mm_last[bk] is not None:
                    P.wait("sync", TM, mm_last[bk])
                P.wait("sync", CD, ag_all)
                lx = P.dma("sync", lambda e, r0=r0, bk=bk: e.dma_start(
                    out=xng[bk][:],
                    in_=d_xng.ap()[r0:r0 + D, :].rearrange("(k p) t -> p k t", p=128)))
                if k > 0:
                    P.wait("scalar", VC, conv_done)
                    P.op("scalar", lambda e: e.activation(xhh[:, :, 0:3], xhh[:, :, TC:TC + 3], AF.Copy))
                P.wait("tensor", LD, lx)
                ev = []
                for mi in range(KD):
                    psn = mi % 4
                    if bank_ev.get(psn) is not None:
                        P.wait("tensor", SC, bank_ev[psn])
                    tmm = None
                    for kb in range(KD):
                        tmm = P.op("tensor", lambda e, kb=kb, mi=mi, bk=bk, psn=psn: e.matmul(
                            PS[psn][:], Winb[:, kb, 128 * mi:128 * (mi + 1)], xng[bk][:, kb, :],
                            start=(kb == 0), stop=(kb == KD - 1)), TM if kb == KD - 1 else None)
                    if mi == KD - 1:
                        mm_last[bk] = tmm
                    P.wait("scalar", TM, tmm)
                    if mi < NMI:
                        evv = P.op("scalar", lambda e, mi=mi, psn=psn: e.activation(
                            xhh[:, mi, 3:3 + TC], PS[psn][:], AF.Copy), SC)
                    else:
                        if zs_spill is not None and mi == NMI:
                            P.wait("scalar", LD, zs_spill)
                        evv = P.op("scalar", lambda e, mi=mi, psn=psn: e.activation(
                            zst[:, mi - NMI, :], PS[psn][:], AF.Silu), SC)
                    ev.append(evv)
                    bank_ev[psn] = evv
                P.wait("sync", SC, ev[-1])
                zs_spill = P.dma("sync", lambda e, c0=c0: e.dma_start(
                    out=d_z_r[:, :, c0:c0 + TC], in_=zst[:]))
                P.wait("vector", SC, ev[NMI - 1])
                if sil_done is not None:
                    P.wait("vector", SC, sil_done)
                cv = None
                for mi in range(NMI):
                    P.op("vector", lambda e, mi=mi: e.tensor_scalar(
                        cvac[:, mi, :], xhh[:, mi, 0:TC],
                        cwt[:, 4 * mi + 0:4 * mi + 1], vect[:, 4 * mi + 0:4 * mi + 1],
                        ALU.mult, ALU.add))
                    for kk in range(1, 4):
                        cv = P.op("vector", lambda e, mi=mi, kk=kk: e.scalar_tensor_tensor(
                            cvac[:, mi, :], xhh[:, mi, kk:kk + TC],
                            cwt[:, 4 * mi + kk:4 * mi + kk + 1], cvac[:, mi, :],
                            ALU.mult, ALU.add), VC if (mi == NMI - 1 and kk == 3) else None)
                conv_done = cv
                P.wait("scalar", VC, cv)
                if xc_spill is not None:
                    P.wait("scalar", LD, xc_spill)
                sil_done = P.op("scalar", lambda e: e.activation(
                    xct2[:].rearrange("p m t -> p (m t)"), cvac[:].rearrange("p m t -> p (m t)"),
                    AF.Silu), SC)
                if tpj_prev is not None:
                    P.wait("scalar", TM, tpj_prev)
                cast_done = P.op("scalar", lambda e: e.activation(
                    xcb2[:].rearrange("p m t -> p (m t)"), xct2[:].rearrange("p m t -> p (m t)"),
                    AF.Copy), SC)
                P.wait("sync", SC, sil_done)
                xc_spill = P.dma("sync", lambda e, c0=c0: e.dma_start(
                    out=d_xc_r[:, :, c0:c0 + TC], in_=xct2[:]))
                P.wait("tensor", SC, cast_done)
                if spj_prev is not None:
                    P.wait("tensor", SC, spj_prev)
                tpj = None
                for mi in range(NMI):
                    tpj = P.op("tensor", lambda e, mi=mi: e.matmul(
                        PS[4][0:33, :], Wxpb[:, mi, :], xcb2[:, mi, :],
                        start=(mi == 0), stop=(mi == NMI - 1)), TM if mi == NMI - 1 else None)
                tpj_prev = tpj
                P.wait("scalar", TM, tpj)
                if pj_spill is not None:
                    P.wait("scalar", LD, pj_spill)
                spj = P.op("scalar", lambda e: e.activation(xpjt[:], PS[4][0:33, :], AF.Copy), SC)
                spj_prev = spj
                P.wait("sync", SC, spj)
                pj_spill = P.dma("sync", lambda e, c0=c0: e.dma_start(
                    out=d_xpj.ap()[:, c0:c0 + TC], in_=xpjt[:]))
            P.barrier()
            AR.pop()

        art = P.coll("AllReduce", ALU.add, [d_xpj.ap().opt()], [d_xpjr.ap().opt()],
                     wait_ld=P.LD.n)
        P.barrier()

        if True:
            AR.push()
            Wob = AR.alloc([128, NMI, 1024], BF16)
            xct3 = [AR.alloc([128, NMI, TC], F32) for _ in range(2)]
            zsl = [AR.alloc([128, NMI, TC], F32) for _ in range(2)]
            dtt = [AR.alloc([128, NMI, TC], F32) for _ in range(2)]
            ut = [AR.alloc([128, NMI, TC], BF16) for _ in range(2)]
            Bxt = [AR.alloc([16, TC], F32) for _ in range(2)]
            Cxt = [AR.alloc([16, TC], F32) for _ in range(2)]
            dtr = [AR.alloc([1, TC], F32) for _ in range(2)]
            Bxb = [AR.alloc([16, TC], BF16) for _ in range(2)]
            Cxb = [AR.alloc([16, TC], BF16) for _ in range(2)]
            Cb = [AR.alloc([128, TC], BF16) for _ in range(2)]
            Bbb = [AR.alloc([128, TC], BF16) for _ in range(2)]
            ab = [AR.alloc([128, NMI * TC], F32) for _ in range(2)]
            bb3 = AR.alloc([128, NMI * TC], BF16)
            hta = [AR.alloc([128, NMI, TC], F32) for _ in range(2)]
            tmpb = [AR.alloc([128, NMI * TC], BF16) for _ in range(2)]
            yt = [AR.alloc([128, NMI, TC], BF16) for _ in range(2)]
            gt = [AR.alloc([128, NMI, TC], BF16) for _ in range(2)]
            eb2b = AR.alloc([128, NMI * TC], F32)
            rb2b = AR.alloc([128, NMI * TC], F32)
            wot = [AR.alloc([128, TC], F32) for _ in range(2)]

            lw = P.dma("sync", lambda e: e.dma_start(
                out=Wob[:], in_=d_wob.ap().rearrange("(k p) m -> p k m", p=128)))
            P.op("vector", lambda e: e.memset(st[:], 0.0), VC)
            P.wait("tensor", LD, lw)

            roll = {"a3": {}, "dve": {}, "psB_mm": {}, "psC_ev": {}, "tmp": {}, "tmp_rd": {}}
            S = {"l1": {}, "l2": {}, "vsp": {}, "u": {}, "bxb": {}, "tdt": {},
                 "b3l": {}, "vg": {}, "wol": {}}
            wo_bank_ev = {0: None, 1: None}

            def head(k):
                pk = k % 2
                kt = slice(TC * k, TC * (k + 1))
                if S["vg"].get(k - 2) is not None:
                    P.wait("sync", VC, S["vg"][k - 2])
                    P.wait("sync", PC, S["u"][k - 2])
                S["l1"][k] = P.dma("sync", lambda e, kt=kt, pk=pk: e.dma_start(
                    out=xct3[pk][:], in_=d_xc_r[:, :, kt]))
                S["l2"][k] = P.dma("sync", lambda e, kt=kt, pk=pk: e.dma_start(
                    out=zsl[pk][:], in_=d_z_r[:, :, kt]))
                if S["bxb"].get(k - 2) is not None:
                    P.wait("sync", SC, S["bxb"][k - 2])
                if S["tdt"].get(k - 2) is not None:
                    P.wait("sync", TM, S["tdt"][k - 2])
                lB = P.dma("sync", lambda e, kt=kt, pk=pk: e.dma_start(
                    out=Bxt[pk][:], in_=d_xpjr.ap()[0:16, kt]))
                lC = P.dma("sync", lambda e, kt=kt, pk=pk: e.dma_start(
                    out=Cxt[pk][:], in_=d_xpjr.ap()[16:32, kt]))
                lD = P.dma("sync", lambda e, kt=kt, pk=pk: e.dma_start(
                    out=dtr[pk][:], in_=d_xpjr.ap()[32:33, kt]))
                P.wait("tensor", LD, lD)
                if S["vsp"].get(k - 1) is not None:
                    P.wait("tensor", VC, S["vsp"][k - 1])
                tdt = P.op("tensor", lambda e, pk=pk: e.matmul(
                    PS[6 + pk][:], ones1t[:], dtr[pk][:], start=True, stop=True), TM)
                S["tdt"][k] = tdt
                P.wait("vector", TM, tdt)
                if k >= 2:
                    P.wait("vector", SC, roll["a3"][16 * (k - 2) + 15])
                    P.wait("vector", PC, S["u"][k - 2])
                for mi in range(NMI):
                    P.op("vector", lambda e, mi=mi, pk=pk: e.tensor_scalar(
                        dtt[pk][:, mi, :], PS[6 + pk][:],
                        vect[:, 4 * mi + 2:4 * mi + 3], vect[:, 4 * mi + 1:4 * mi + 2],
                        ALU.mult, ALU.add))
                vrl = P.op("vector", lambda e, pk=pk: e.tensor_scalar(
                    rb2b[:], dtt[pk][:].rearrange("p m t -> p (m t)"), 0.0, None, ALU.max), VC)
                P.wait("scalar", VC, vrl)
                P.op("scalar", lambda e, pk=pk: e.activation(
                    eb2b[:], dtt[pk][:].rearrange("p m t -> p (m t)"), AF.Abs))
                P.op("scalar", lambda e: e.activation(eb2b[:], eb2b[:], AF.Exp, scale=-1.0))
                ssp = P.op("scalar", lambda e: e.activation(eb2b[:], eb2b[:], AF.Ln, bias=one_t[:, 0:1]), SC)
                P.wait("vector", SC, ssp)
                vsp = P.op("vector", lambda e, pk=pk: e.tensor_tensor(
                    dtt[pk][:].rearrange("p m t -> p (m t)"), rb2b[:], eb2b[:], ALU.add), VC)
                S["vsp"][k] = vsp
                P.wait("gpsimd", VC, vsp)
                P.wait("gpsimd", LD, S["l1"][k])
                if S["b3l"].get(k - 2) is not None:
                    P.wait("gpsimd", VC, S["b3l"][k - 2])
                S["u"][k] = P.op("gpsimd", lambda e, pk=pk: e.tensor_tensor(
                    ut[pk][:].rearrange("p m t -> p (m t)"), dtt[pk][:].rearrange("p m t -> p (m t)"),
                    xct3[pk][:].rearrange("p m t -> p (m t)"), ALU.mult), PC)
                P.wait("scalar", LD, lC)
                if k >= 2:
                    P.wait("scalar", TM, roll["psB_mm"][16 * (k - 2) + 15])
                P.op("scalar", lambda e, pk=pk: e.activation(Bxb[pk][:], Bxt[pk][:], AF.Copy))
                S["bxb"][k] = P.op("scalar", lambda e, pk=pk: e.activation(Cxb[pk][:], Cxt[pk][:], AF.Copy), SC)

            head(0)
            for k in range(NCH):
                pk = k % 2
                if debug and "dty" in dbg_parts and k == 0:
                    P.wait("sync", VC, S["vsp"][0])
                    P.dma("sync", lambda e: e.dma_start(out=dbg["dbg_dt"], in_=dtt[0][:].rearrange("p m t -> p (m t)")))
                def ensure_bcast(g2, kk):
                    if g2 in roll["psB_mm"] or g2 >= 16 * NCH:
                        return
                    n2 = g2 % 16
                    pp2 = g2 % 2
                    psB2 = PS[0 + 2 * pp2]
                    psC2 = PS[1 + 2 * pp2]
                    P.wait("tensor", SC, S["bxb"][kk])
                    if roll["psC_ev"].get(g2 - 2) is not None:
                        P.wait("tensor", SC, roll["psC_ev"][g2 - 2])
                    P.op("tensor", lambda e, n2=n2, psB2=psB2, kk=kk: e.matmul(
                        psB2[:], selb[:, 128 * n2:128 * (n2 + 1)], Bxb[kk % 2][:], start=True, stop=True))
                    tbc = P.op("tensor", lambda e, n2=n2, psC2=psC2, kk=kk: e.matmul(
                        psC2[:], selb[:, 128 * n2:128 * (n2 + 1)], Cxb[kk % 2][:], start=True, stop=True), TM)
                    roll["psB_mm"][g2] = tbc
                    P.wait("scalar", TM, tbc)
                    if roll["tmp"].get(g2 - 2) is not None:
                        P.wait("scalar", VC, roll["tmp"][g2 - 2])
                    if roll["dve"].get(g2 - 2) is not None:
                        P.wait("scalar", VC, roll["dve"][g2 - 2])
                    P.op("scalar", lambda e, pp2=pp2, psB2=psB2: e.activation(Bbb[pp2][:], psB2[:], AF.Copy))
                    cev = P.op("scalar", lambda e, pp2=pp2, psC2=psC2: e.activation(Cb[pp2][:], psC2[:], AF.Copy), SC)
                    roll["psC_ev"][g2] = cev

                def emit_yadd(g2):
                    pp2 = g2 % 2
                    pk2 = (g2 // 16) % 2
                    if g2 % 16 == 0:
                        v = P.op("vector", lambda e, pp2=pp2, pk2=pk2: e.tensor_scalar(
                            yt[pk2][:].rearrange("p m t -> p (m t)"),
                            tmpb[pp2][:], 0.0, None, ALU.add), VC)
                    else:
                        v = P.op("vector", lambda e, pp2=pp2, pk2=pk2: e.tensor_tensor(
                            yt[pk2][:].rearrange("p m t -> p (m t)"),
                            yt[pk2][:].rearrange("p m t -> p (m t)"),
                            tmpb[pp2][:], ALU.add), VC)
                    roll["tmp_rd"][g2] = v

                def emit_a(g2, kk):
                    if g2 in roll["a3"] or g2 >= 16 * NCH:
                        return
                    n2 = g2 % 16
                    pp2 = g2 % 2
                    pk2 = kk % 2
                    if n2 == 0:
                        P.wait("scalar", VC, S["vsp"][kk])
                    if roll["dve"].get(g2 - 2) is not None:
                        P.wait("scalar", VC, roll["dve"][g2 - 2])
                    sa = None
                    for mi in range(NMI):
                        sa = P.op("scalar", lambda e, mi=mi, n2=n2, pp2=pp2, pk2=pk2: e.activation(
                            ab[pp2][:, TC * mi:TC * (mi + 1)], dtt[pk2][:, mi, :], AF.Exp,
                            scale=Aneg[:, 16 * mi + n2:16 * mi + n2 + 1]),
                            SC if mi == NMI - 1 else None)
                    roll["a3"][g2] = sa

                for n in range(NS):
                    g = 16 * k + n
                    pp = g % 2
                    ensure_bcast(g, k)
                    cev = roll["psC_ev"][g]
                    emit_a(g, k)
                    if n == 0:
                        P.wait("vector", PC, S["u"][k])
                    P.wait("vector", SC, cev)
                    b3m = P.op("vector", lambda e, pp=pp, pk=pk: e.tensor_tensor(
                        bb3[:].rearrange("p (m t) -> p m t", m=NMI),
                        ut[pk][:], bc_mid(Bbb[pp][:], NMI), ALU.mult), VC)
                    if n == NS - 1:
                        S["b3l"][k] = b3m
                    P.wait("vector", SC, roll["a3"][g])
                    for mi in range(NMI):
                        P.op("vector", lambda e, mi=mi, n=n, pp=pp: e.tensor_tensor_scan(
                            hta[pp][:, mi, :],
                            ab[pp][:, TC * mi:TC * (mi + 1)],
                            bb3[:, TC * mi:TC * (mi + 1)],
                            st[:, mi, n:n + 1], ALU.mult, ALU.add))
                    vsc = P.op("vector", lambda e, n=n, pp=pp: e.tensor_copy(
                        st[:, :, n:n + 1], hta[pp][:, :, TC - 1:TC]), VC)
                    roll["dve"][g] = vsc
                    tm_ = P.op("vector", lambda e, pp=pp: e.tensor_tensor(
                        tmpb[pp][:].rearrange("p (m t) -> p m t", m=NMI),
                        hta[pp][:], bc_mid(Cb[pp][:], NMI), ALU.mult), VC)
                    roll["tmp"][g] = tm_
                    if g > 0 and (g - 1) not in roll["tmp_rd"]:
                        emit_yadd(g - 1)
                    ensure_bcast(g + 1, k if n + 1 < NS else k + 1)
                    if n == 8 and k + 1 < NCH:
                        head(k + 1)
                    if n >= 14 and k + 1 < NCH:
                        ensure_bcast(g + 2, k + 1)
                        emit_a(g + 2, k + 1)
                emit_yadd(16 * k + 15)
                P.wait("vector", LD, S["l1"][k])
                for mi in range(NMI):
                    P.op("vector", lambda e, mi=mi, pk=pk: e.scalar_tensor_tensor(
                        yt[pk][:, mi, :], xct3[pk][:, mi, :],
                        vect[:, 4 * mi + 3:4 * mi + 4], yt[pk][:, mi, :], ALU.mult, ALU.add))
                P.wait("vector", LD, S["l2"][k])
                if S["wol"].get(k - 2) is not None:
                    P.wait("vector", TM, S["wol"][k - 2])
                vg = P.op("vector", lambda e, pk=pk: e.tensor_tensor(
                    gt[pk][:].rearrange("p m t -> p (m t)"), yt[pk][:].rearrange("p m t -> p (m t)"),
                    zsl[pk][:].rearrange("p m t -> p (m t)"), ALU.mult), VC)
                S["vg"][k] = vg
                if debug and "dty" in dbg_parts and k == 0:
                    P.wait("gpsimd", VC, vg)
                    P.dma("gpsimd", lambda e: e.dma_start(out=dbg["dbg_y"], in_=yt[0][:].rearrange("p m t -> p (m t)")))
                P.wait("tensor", VC, vg)
                wo_dma = {}
                for mo in range(KD):
                    pw = mo % 2
                    psW = PS[4 + pw]
                    if wo_bank_ev[pw] is not None:
                        P.wait("tensor", SC, wo_bank_ev[pw])
                    two = None
                    for kd in range(NMI):
                        two = P.op("tensor", lambda e, kd=kd, mo=mo, pk=pk, psW=psW: e.matmul(
                            psW[:], Wob[:, kd, 128 * mo:128 * (mo + 1)], gt[pk][:, kd, :],
                            start=(kd == 0), stop=(kd == NMI - 1)), TM if kd == NMI - 1 else None)
                    if mo == KD - 1:
                        S["wol"][k] = two
                    P.wait("scalar", TM, two)
                    if wo_dma.get(mo - 2) is not None:
                        P.wait("scalar", WO, wo_dma[mo - 2])
                    swo = P.op("scalar", lambda e, pw=pw, psW=psW: e.activation(wot[pw][:], psW[:], AF.Copy), SC)
                    wo_bank_ev[pw] = swo
                    P.wait("sync", SC, swo)
                    wo_dma[mo] = P.dma("sync", lambda e, k=k, mo=mo, pw=pw: e.dma_start(
                        out=d_wop_j[k // 4].ap()[1024 * (k % 4) + 128 * mo:1024 * (k % 4) + 128 * (mo + 1), :],
                        in_=wot[pw][:]), ctr=WO)
                if k % 4 == 3:
                    P.wait("gpsimd", WO, 16 * KD * (k + 1))
                    CC.n += 1
                    P.op("gpsimd", lambda e, j=k // 4: e.collective_compute(
                        "ReduceScatter", ALU.add, GROUPS,
                        [d_wop_j[j].ap().opt()], [d_att_j[j].ap().opt()]).then_inc(CC.h, 1))
                if debug and "wop" in dbg_parts and k == 3:
                    P.wait("sync", LD, P.LD.n)
                    P.dma("sync", lambda e: e.dma_start(out=dbg["dbg_wop"], in_=d_wop_j[0].ap()))
            P.wait("gpsimd", CC, CC.n)
            P.op("gpsimd", lambda e: e.drain(fusable=False))
            P.barrier()
            AR.pop()

        if True:
            AR.push()
            T = alloc_ffn_scope()
            prev = {}
            oysp = [None, None]
            xs_vc = [None, None]
            xs_sc = [None, None]
            ln2_rd = None
            cen_rd = None
            for j in range(NJ):
                par = j % 2
                jt = slice(TC * j, TC * (j + 1))
                xs = T["xs"][par]
                if j >= 2:
                    P.wait("sync", VC, xs_vc[par])
                    P.wait("sync", SC, xs_sc[par])
                    P.wait("sync", LD, oysp[par])
                l1 = P.dma("sync", lambda e, jt=jt, xs=xs: e.dma_start(out=xs[:], in_=d_x1_r[:, :, jt]))
                if ln2_rd is not None:
                    P.wait("sync", VC, ln2_rd)
                l2 = P.dma("sync", lambda e, jt=jt: e.dma_start(out=T["ln2t"][:], in_=d_ln2_r[:, :, jt]))
                if cen_rd is not None:
                    P.wait("sync", VC, cen_rd)
                l3 = P.dma("sync", lambda e, j=j: e.dma_start(
                    out=T["cen"][:],
                    in_=d_att_j[j].ap().rearrange("(k p) t -> p k t", p=128)))
                P.wait("vector", LD, l3)
                ln2_rd = P.op("vector", lambda e, xs=xs: e.tensor_tensor(
                    xs[:].rearrange("p k t -> p (k t)"), xs[:].rearrange("p k t -> p (k t)"),
                    T["ln2t"][:].rearrange("p k t -> p (k t)"), ALU.add), VC)
                vx2 = P.op("vector", lambda e, xs=xs: e.tensor_tensor(
                    xs[:].rearrange("p k t -> p (k t)"), xs[:].rearrange("p k t -> p (k t)"),
                    T["cen"][:].rearrange("p k t -> p (k t)"), ALU.add), VC)
                if debug and "p3" in dbg_parts:
                    P.wait("sync", LD, l3)
                    P.dma("sync", lambda e, j=j: e.dma_start(
                        out=dbg["dbg_att"][D * j:D * (j + 1), :].rearrange("(k p) t -> p k t", p=128),
                        in_=T["cen"][:]))
                s_r, c2m = ln_stats(T, xs, wait_vc=vx2)
                xs_sc[par] = c2m
                P.wait("vector", SC, s_r)
                vx = ln_apply_plain(T, xs, T["xnb"])
                cen_rd = vx
                xs_vc[par] = vx
                P.wait("tensor", VC, vx)
                prev = ffn(T, par, prev)
                xv = prev["xv"]
                P.wait("sync", VC, xv)
                oysp[par] = P.dma("sync", lambda e, jt=jt, xs=xs: e.dma_start(
                    out=o_y_r[:, :, jt], in_=xs[:]))

        P.wait("sync", LD, P.LD.n)

        block = E(nc.Block())
        P.emit(block)

    return nc


_DEBUG = False
_NC_CACHE = {}


_DBG_PARTS = ("p3",)


def _get_nc():
    key = (bool(_DEBUG), tuple(_DBG_PARTS))
    if key not in _NC_CACHE:
        _NC_CACHE[key] = build_program(debug=key[0], dbg_parts=key[1])
    return _NC_CACHE[key]


def kernel(L0_attn_x, L0_attn_ln_g, L0_attn_ln_b, L0_attn_W_in, L0_attn_conv_w,
           L0_attn_conv_b, L0_attn_W_xp, L0_attn_W_dt, L0_attn_b_dt, L0_attn_A_log,
           L0_attn_D, L0_attn_W_out, L0_ffn_w_gate, L0_ffn_w_up, L0_ffn_w_down):
    f32 = np.float32
    x = np.asarray(L0_attn_x, f32)
    W_in = np.asarray(L0_attn_W_in, f32)
    W_out = np.asarray(L0_attn_W_out, f32)
    W_xp = np.asarray(L0_attn_W_xp, f32)
    conv_w = np.asarray(L0_attn_conv_w, f32)
    conv_b = np.asarray(L0_attn_conv_b, f32)
    W_dt = np.asarray(L0_attn_W_dt, f32)
    b_dt = np.asarray(L0_attn_b_dt, f32)
    A_log = np.asarray(L0_attn_A_log, f32)
    Dvec = np.asarray(L0_attn_D, f32)
    ln_g = np.asarray(L0_attn_ln_g, f32)
    ln_b = np.asarray(L0_attn_ln_b, f32)
    wg = np.ascontiguousarray(np.asarray(L0_ffn_w_gate, np.int32).T)
    wu = np.ascontiguousarray(np.asarray(L0_ffn_w_up, np.int32).T)
    wd = np.ascontiguousarray(np.asarray(L0_ffn_w_down, np.int32).T)

    sel = np.zeros((16, 2048), f32)
    for n in range(16):
        sel[n, 128 * n:128 * (n + 1)] = 1.0
    ones128 = np.ones((128, 128), f32)
    ones1 = np.ones((1, 128), f32)
    lngb = np.zeros((128, 16), f32)
    for kb in range(KD):
        lngb[:, 2 * kb] = ln_g[128 * kb:128 * (kb + 1)]
        lngb[:, 2 * kb + 1] = ln_b[128 * kb:128 * (kb + 1)]

    in_maps = []
    for c in range(8):
        b, r = divmod(c, 4)
        sl = slice(512 * r, 512 * (r + 1))
        xc = np.concatenate(
            [x[b, TC * (4 * j + r):TC * (4 * j + r + 1), :].T for j in range(NJ)], axis=1)
        winc_c = np.concatenate([W_in[:, sl], W_in[:, 2048 + sl.start:2048 + sl.stop]], axis=1)
        cwc = conv_w[sl].reshape(4, 128, 4).transpose(1, 0, 2).reshape(128, 16)
        avals = A_log[sl].reshape(4, 128, 16).transpose(1, 0, 2).reshape(128, 64)
        vv = np.stack([conv_b[sl], b_dt[sl], W_dt[0, sl], Dvec[sl]], axis=-1)
        vv = vv.reshape(4, 128, 4).transpose(1, 0, 2).reshape(128, 16)
        in_maps.append(dict(
            x_fm=np.ascontiguousarray(xc),
            winc=np.ascontiguousarray(winc_c),
            woc=np.ascontiguousarray(W_out[sl]),
            wxpc=np.ascontiguousarray(W_xp[sl]),
            wgT=wg, wuT=wu, wdT=wd,
            cwc=np.ascontiguousarray(cwc),
            avals=np.ascontiguousarray(avals),
            vecs=np.ascontiguousarray(vv),
            lngb=lngb, sel=sel, ones128=ones128, ones1=ones1,
        ))

    nc = _get_nc()
    res = run_bass_kernel_spmd(nc, in_maps, core_ids=list(range(8)))

    out = np.empty((B, L, D), f32)
    for c in range(8):
        b, r = divmod(c, 4)
        oy = np.asarray(res.results[c]["o_y"], f32)
        for j in range(NJ):
            out[b, TC * (4 * j + r):TC * (4 * j + r + 1), :] = oy[:, TC * j:TC * (j + 1)].T
    if _DEBUG:
        kernel._dbg = res.results
    return out


# version 33
# speedup vs baseline: 1.1757x; 1.0190x over previous
import dataclasses
from contextlib import ExitStack

import numpy as np

import concourse.bass as bass
import concourse.mybir as mybir
from concourse.bass_utils import run_bass_kernel_spmd

F32 = mybir.dt.float32
BF16 = mybir.dt.bfloat16
I32 = mybir.dt.int32
AF = mybir.ActivationFunctionType
ALU = mybir.AluOpType

B, L, D, DIN, H, NS = 2, 8192, 1024, 2048, 3072, 16
TC = 512
NCH = 16
NJ = 4
KD = 8
NMI = 4
HM = H // 128
EPS = 1e-5
GROUPS = [[0, 1, 2, 3], [4, 5, 6, 7]]
ENGINES = ("sync", "vector", "scalar", "tensor", "gpsimd")


def bc_mid(ap, rep):
    assert len(ap.ap) == 2
    return dataclasses.replace(ap, ap=[ap.ap[0], [0, rep], ap.ap[1]])


class Ctr:
    def __init__(self, h):
        self.h = h
        self.n = 0


class Bld:
    def __init__(self, nc, ctx):
        self.nc = nc
        self.ops = {e: [] for e in ENGINES}
        E = ctx.enter_context
        self.LD = Ctr(E(nc.semaphore("LD")))
        self.CC = Ctr(E(nc.semaphore("CC")))
        self.VC = Ctr(E(nc.semaphore("VC")))
        self.SC = Ctr(E(nc.semaphore("SC")))
        self.TM = Ctr(E(nc.semaphore("TM")))
        self.PC = Ctr(E(nc.semaphore("PC")))
        self.CD = Ctr(E(nc.semaphore("CD")))
        self.WQ = Ctr(E(nc.semaphore("WQ")))
        self.WO = Ctr(E(nc.semaphore("WO")))
        self.XN = Ctr(E(nc.semaphore("XN")))
        self.BAR = Ctr(E(nc.semaphore("BAR")))

    def op(self, eng, build, ctr=None):
        if ctr is None:
            self.ops[eng].append(build)
            return None
        ctr.n += 1
        tgt = ctr.n
        self.ops[eng].append(lambda e, b=build, c=ctr: b(e).then_inc(c.h, 1))
        return tgt

    def dma(self, eng, build, ctr=None):
        ctr = ctr or self.LD
        ctr.n += 16
        tgt = ctr.n
        self.ops[eng].append(lambda e, b=build, c=ctr: b(e).then_inc(c.h, 16))
        return tgt

    def wait(self, eng, ctr, val):
        if val is None:
            return
        self.ops[eng].append(lambda e, c=ctr, v=val: e.wait_ge(c.h, v))

    def barrier(self):
        self.wait("sync", self.LD, self.LD.n)
        self.wait("sync", self.WQ, self.WQ.n)
        self.wait("sync", self.WO, self.WO.n)
        self.wait("sync", self.XN, self.XN.n)
        tgt = self.BAR.n + len(ENGINES)
        for e in ENGINES:
            self.ops[e].append(lambda E: E.sem_inc(self.BAR.h, 1))
        self.BAR.n = tgt
        for e in ENGINES:
            self.ops[e].append(lambda E, v=tgt: E.wait_ge(self.BAR.h, v))

    def coll(self, kind, op_, ins, outs, wait_ld=None):
        if wait_ld is not None:
            self.wait("gpsimd", self.LD, wait_ld)
        self.CC.n += 1
        tgt = self.CC.n
        self.ops["gpsimd"].append(lambda e, k=kind, o=op_, i=ins, u=outs: e.collective_compute(
            k, o, GROUPS, i, u).then_inc(self.CC.h, 1))
        self.wait("gpsimd", self.CC, tgt)
        self.ops["gpsimd"].append(lambda e: e.drain(fusable=False))
        self.CD.n += 1
        cdt = self.CD.n
        self.ops["gpsimd"].append(lambda e, v=cdt: e.sem_inc(self.CD.h, 1))
        return cdt

    def emit(self, block):
        b = self
        @block.sync
        def _(eng):
            for f in b.ops["sync"]:
                f(eng)
        @block.vector
        def _(eng):
            for f in b.ops["vector"]:
                f(eng)
        @block.scalar
        def _(eng):
            for f in b.ops["scalar"]:
                f(eng)
        @block.tensor
        def _(eng):
            for f in b.ops["tensor"]:
                f(eng)
        @block.gpsimd
        def _(eng):
            for f in b.ops["gpsimd"]:
                f(eng)


class Arena:
    ESZ = {F32: 4, BF16: 2, I32: 4}

    def __init__(self, nc, ctx, name, words):
        self.h = ctx.enter_context(nc.sbuf_tensor(name, [128, words], F32))
        self.hb = self.h.bitcast(BF16)
        self.hi = self.h.bitcast(I32)
        self.cap = words * 4
        self.off = 0
        self.stack = []

    def push(self):
        self.stack.append(self.off)

    def pop(self):
        self.off = self.stack.pop()

    def alloc(self, shape, dt=F32):
        n = 1
        for s in shape[1:]:
            n *= s
        esz = self.ESZ[dt]
        self.off = (self.off + 63) & ~63
        start = self.off
        self.off += n * esz
        assert self.off <= self.cap, f"arena overflow: {self.off} > {self.cap}"
        if dt == F32:
            base = self.h[:, start // 4: start // 4 + n]
        elif dt == I32:
            base = self.hi[:, start // 4: start // 4 + n]
        else:
            base = self.hb[:, start // 2: start // 2 + n]
        if shape[0] != 128:
            base = base[0:shape[0], :]
        if len(shape) == 3:
            base = base.rearrange("p (a b) -> p a b", a=shape[1])
        return base


def build_program(debug=False, dbg_parts=("p1","scan","wop","rs","p3","dty")):
    nc = bass.Bass("TRN2", target_bir_lowering=False, detect_race_conditions=False)

    def din(name, shape, dt=F32):
        return nc.dram_tensor(name, shape, dt, kind="ExternalInput").ap()

    x_fm = din("x_fm", [D, NJ * TC])
    winc = din("winc", [D, 1024])
    woc = din("woc", [512, D])
    wxpc = din("wxpc", [512, 33])
    wgT = din("wgT", [D, H], I32)
    wuT = din("wuT", [D, H], I32)
    wdT = din("wdT", [H, D], I32)
    cwc = din("cwc", [128, 16])
    avals = din("avals", [128, 64])
    vecs = din("vecs", [128, 16])
    lngb = din("lngb", [128, 16])
    sel_i = din("sel", [16, 2048])
    ones128 = din("ones128", [128, 128])
    ones1 = din("ones1", [1, 128])

    o_y = nc.dram_tensor("o_y", [D, NJ * TC], F32, kind="ExternalOutput").ap()

    def dint(name, shape, dt=F32):
        return nc.dram_tensor(name, shape, dt, kind="Internal")

    d_wg = dint("d_wg", [D, H], BF16)
    d_wu = dint("d_wu", [D, H], BF16)
    d_wd = dint("d_wd", [H, D], BF16)
    d_winb = dint("d_winb", [D, 1024], BF16)
    d_wob = dint("d_wob", [512, D], BF16)
    d_x1 = dint("d_x1", [D, NJ * TC], F32)
    d_ln2 = dint("d_ln2", [D, NJ * TC], F32)
    d_xn = dint("d_xn", [NJ * D, TC], BF16)
    d_xng = dint("d_xng", [NJ * 4 * D, TC], BF16)
    d_fin = dint("d_fin", [1, 128], F32)
    d_fout = dint("d_fout", [1, 128], F32)
    d_xc = dint("d_xc", [512, L], F32)
    d_z = dint("d_z", [512, L], F32)
    d_xpj = dint("d_xpj", [33, L], F32)
    d_xpjr = dint("d_xpjr", [33, L], F32)
    d_wop_j = [dint(f"d_wop{j}", [4 * D, TC], F32) for j in range(NJ)]
    d_att_j = [dint(f"d_att{j}", [D, TC], F32) for j in range(NJ)]

    dbg = {}
    if debug:
        for name, shape, dt in [
            ("dbg_xn", [D, NJ * TC], BF16), ("dbg_x1", [D, NJ * TC], F32),
            ("dbg_dt", [128, NMI * TC], F32),
            ("dbg_y", [128, NMI * TC], F32), ("dbg_att", [NJ * D, TC], F32),
            ("dbg_u", [128, NMI * TC], F32), ("dbg_a0", [128, NMI * TC], F32),
            ("dbg_b0", [128, NMI * TC], F32), ("dbg_h0", [128, NMI * TC], F32),
            ("dbg_y0", [128, NMI * TC], F32), ("dbg_wop", [4 * D, TC], F32),
            ("dbg_wop2", [4 * D, TC], F32),
        ]:
            dbg[name] = nc.dram_tensor(name, shape, dt, kind="ExternalOutput").ap()

    x_fm_r = x_fm.rearrange("(k p) t -> p k t", p=128)
    d_x1_r = d_x1.ap().rearrange("(k p) t -> p k t", p=128)
    d_ln2_r = d_ln2.ap().rearrange("(k p) t -> p k t", p=128)
    d_xc_r = d_xc.ap().rearrange("(m p) t -> p m t", p=128)
    d_z_r = d_z.ap().rearrange("(m p) t -> p m t", p=128)
    o_y_r = o_y.rearrange("(k p) t -> p k t", p=128)

    with ExitStack() as ctx:
        E = ctx.enter_context
        P = Bld(nc, ctx)
        LD, CC, VC, SC, TM, PC, CD, WQ, WO, XN = P.LD, P.CC, P.VC, P.SC, P.TM, P.PC, P.CD, P.WQ, P.WO, P.XN

        def sb(name, shape, dt=F32):
            return E(nc.sbuf_tensor(name, shape, dt))

        mu_t = sb("mu_t", [128, TC])
        ms_t = sb("ms_t", [128, TC])
        var_t = sb("var_t", [128, TC])
        rstd_t = sb("rstd_t", [128, TC])
        f1buf = sb("f1buf", [128, TC])
        cwt = sb("cwt", [128, 16])
        Aneg = sb("Aneg", [128, 64])
        vect = sb("vect", [128, 16])
        lngbt = sb("lngbt", [128, 16])
        onesb = sb("onesb", [128, 128], BF16)
        ones1t = sb("ones1t", [1, 128])
        eps_t = sb("eps_t", [128, 1])
        one_t = sb("one_t", [128, 1])
        selb = sb("selb", [16, 2048], BF16)
        st = sb("st", [128, NMI, NS])

        PS = [E(nc.psum_tensor(f"ps{i}", [128, TC], F32)) for i in range(8)]
        AR = Arena(nc, ctx, "arena", 43 * 1024)

        if True:
            AR.push()
            deqi = [AR.alloc([128, H], I32) for _ in range(2)]
            deqb = [AR.alloc([128, H], BF16) for _ in range(2)]
            wfl = AR.alloc([128, 1024], F32)
            p0sel = AR.alloc([16, 2048], F32)

            P.op("vector", lambda e: e.memset(eps_t[:], float(EPS)))
            vcn = P.op("vector", lambda e: e.memset(one_t[:], 1.0), VC)
            P.wait("scalar", VC, vcn)
            P.dma("sync", lambda e: e.dma_start(out=cwt[:], in_=cwc))
            P.dma("sync", lambda e: e.dma_start(out=vect[:], in_=vecs))
            P.dma("sync", lambda e: e.dma_start(out=lngbt[:], in_=lngb))
            P.dma("sync", lambda e: e.dma_start(out=ones1t[:], in_=ones1))
            t5 = P.dma("sync", lambda e: e.dma_start(out=mu_t[:, 0:64], in_=avals))
            P.wait("scalar", LD, t5)
            s1 = P.op("scalar", lambda e: e.activation(ms_t[:, 0:64], mu_t[:, 0:64], AF.Exp), SC)
            P.wait("vector", SC, s1)
            P.op("vector", lambda e: e.tensor_scalar(Aneg[:], ms_t[:, 0:64], -1.0, None, ALU.mult), VC)
            t6 = P.dma("sync", lambda e: e.dma_start(out=rstd_t[:, 0:128], in_=ones128))
            P.wait("scalar", LD, t6)
            P.op("scalar", lambda e: e.activation(onesb[:], rstd_t[:, 0:128], AF.Copy), SC)
            t7 = P.dma("sync", lambda e: e.dma_start(out=p0sel[:], in_=sel_i))
            P.wait("scalar", LD, t7)
            P.op("scalar", lambda e: e.activation(selb[:], p0sel[:], AF.Copy), SC)

            def dequant(src_ap, dst_t, nblocks, width):
                vvs = [None] * nblocks
                dmas = [None] * nblocks
                for i in range(nblocks):
                    bb = i % 2
                    if i >= 2:
                        P.wait("sync", VC, vvs[i - 2])
                    tl = P.dma("sync", lambda e, i=i, bb=bb, s=src_ap, w=width: e.dma_start(
                        out=deqi[bb][:, 0:w], in_=s[128 * i:128 * (i + 1), 0:w]))
                    P.wait("vector", LD, tl)
                    if i >= 2:
                        P.wait("vector", LD, dmas[i - 2])
                    vvs[i] = P.op("vector", lambda e, bb=bb, w=width: e.tensor_scalar(
                        deqb[bb][:, 0:w], deqi[bb][:, 0:w], 1.0 / 127.0, None, ALU.mult), VC)
                    P.wait("sync", VC, vvs[i])
                    dmas[i] = P.dma("sync", lambda e, i=i, bb=bb, d=dst_t, w=width: e.dma_start(
                        out=d.ap()[128 * i:128 * (i + 1), 0:w], in_=deqb[bb][:, 0:w]))

            dequant(wgT, d_wg, KD, H)
            dequant(wuT, d_wu, KD, H)
            dequant(wdT, d_wd, HM, D)

            def stage(src, dst, nblocks):
                prev = {"sv": None, "dma": None}
                for i in range(nblocks):
                    if prev["sv"] is not None:
                        P.wait("sync", SC, prev["sv"])
                    tl = P.dma("sync", lambda e, i=i, s=src: e.dma_start(
                        out=wfl[:], in_=s[128 * i:128 * (i + 1), :]))
                    P.wait("scalar", LD, tl)
                    if prev["dma"] is not None:
                        P.wait("scalar", LD, prev["dma"])
                    sv = P.op("scalar", lambda e: e.activation(deqb[0][:, 0:1024], wfl[:], AF.Copy), SC)
                    prev["sv"] = sv
                    P.wait("sync", SC, sv)
                    prev["dma"] = P.dma("sync", lambda e, i=i, d=dst: e.dma_start(
                        out=d.ap()[128 * i:128 * (i + 1), :], in_=deqb[0][:, 0:1024]))
            stage(winc, d_winb, KD)
            stage(woc, d_wob, NMI)
            P.barrier()
            AR.pop()

        def ln_stats(T, src, wait_vc=None):
            if wait_vc is not None:
                P.wait("scalar", VC, wait_vc)
            P.op("scalar", lambda e: e.activation(T["xbb"][:], src[:], AF.Copy))
            c2 = P.op("scalar", lambda e: e.activation(T["xnb"][:], src[:], AF.Square), SC)
            P.wait("tensor", SC, c2)
            for kb in range(KD):
                P.op("tensor", lambda e, kb=kb: e.matmul(
                    PS[4][:], onesb[:], T["xbb"][:, kb, :], start=(kb == 0), stop=(kb == KD - 1)),
                    TM if kb == KD - 1 else None)
            tm2 = None
            for kb in range(KD):
                tm2 = P.op("tensor", lambda e, kb=kb: e.matmul(
                    PS[5][:], onesb[:], T["xnb"][:, kb, :], start=(kb == 0), stop=(kb == KD - 1)),
                    TM if kb == KD - 1 else None)
            P.wait("vector", TM, tm2)
            P.op("vector", lambda e: e.tensor_scalar(mu_t[:], PS[4][:], 1.0 / D, None, ALU.mult))
            P.op("vector", lambda e: e.tensor_scalar(ms_t[:], PS[5][:], 1.0 / D, None, ALU.mult))
            P.op("vector", lambda e: e.tensor_tensor(var_t[:], mu_t[:], mu_t[:], ALU.mult))
            v1 = P.op("vector", lambda e: e.tensor_tensor(var_t[:], ms_t[:], var_t[:], ALU.subtract), VC)
            P.wait("scalar", VC, v1)
            P.op("scalar", lambda e: e.activation(rstd_t[:], var_t[:], AF.Ln, bias=eps_t[:, 0:1]))
            s2 = P.op("scalar", lambda e: e.activation(rstd_t[:], rstd_t[:], AF.Exp, scale=-0.5), SC)
            return s2, c2

        def ffn(T, par, prev, xnb=None, gu_hook=None, mid_tc=None):
            xs, fbuf = T["xs"][par], T["fbuf"]
            if xnb is None:
                xnb = T["xnb"]
            wgt_s, wut_s, wdt_s = T["wgt_s"], T["wut_s"], T["wdt_s"]
            if prev.get("tds7") is not None:
                P.wait("sync", TM, prev["tds7"])
            tus = [None] * HM
            fvs = [None] * HM
            f1s = [None] * HM
            for mh in range(HM):
                bw = mh % 2
                pg = PS[0 + 2 * bw]
                pu = PS[1 + 2 * bw]
                if mh >= 2:
                    P.wait("sync", TM, tus[mh - 2])
                P.dma("sync", lambda e, mh=mh, bw=bw: e.dma_start(
                    out=wgt_s[bw][:],
                    in_=d_wg.ap()[:, 128 * mh:128 * (mh + 1)].rearrange("(k p) m -> p k m", p=128)), ctr=WQ)
                lw2 = P.dma("sync", lambda e, mh=mh, bw=bw: e.dma_start(
                    out=wut_s[bw][:],
                    in_=d_wu.ap()[:, 128 * mh:128 * (mh + 1)].rearrange("(k p) m -> p k m", p=128)), ctr=WQ)
                P.wait("tensor", WQ, lw2)
                if mh >= 2:
                    P.wait("tensor", SC, f1s[mh - 2])
                    P.wait("tensor", VC, fvs[mh - 2])
                elif prev.get("f1s7") is not None:
                    P.wait("tensor", SC, prev["f1s7"])
                    P.wait("tensor", VC, prev["fvs7"])
                tg = None
                for kb in range(KD):
                    tg = P.op("tensor", lambda e, kb=kb, bw=bw, pg=pg: e.matmul(
                        pg[:], wgt_s[bw][:, kb, :], xnb[:, kb, :],
                        start=(kb == 0), stop=(kb == KD - 1)), TM if kb == KD - 1 else None)
                for kb in range(KD):
                    tus[mh] = P.op("tensor", lambda e, kb=kb, bw=bw, pu=pu: e.matmul(
                        pu[:], wut_s[bw][:, kb, :], xnb[:, kb, :],
                        start=(kb == 0), stop=(kb == KD - 1)), TM if kb == KD - 1 else None)
                P.wait("scalar", TM, tg)
                if mh >= 1:
                    P.wait("scalar", VC, fvs[mh - 1])
                elif prev.get("fvs7") is not None:
                    P.wait("scalar", VC, prev["fvs7"])
                f1s[mh] = P.op("scalar", lambda e, pg=pg: e.activation(f1buf[:], pg[:], AF.Silu), SC)
                P.wait("vector", SC, f1s[mh])
                P.wait("vector", TM, tus[mh])
                if mh == 0 and prev.get("tds7") is not None:
                    P.wait("vector", TM, prev["tds7"])
                fvs[mh] = P.op("vector", lambda e, mh=mh, pu=pu: e.tensor_tensor(
                    fbuf[:, mh, :], f1buf[:], pu[:], ALU.mult), VC)
                if gu_hook is not None:
                    gu_hook(mh)
            if mid_tc is not None:
                mid_tc()
            xv = None
            tds = [None] * KD
            for mo in range(KD):
                bw = mo % 2
                pd = PS[6 + bw]
                if mo >= 2:
                    P.wait("sync", TM, tds[mo - 2])
                lw = P.dma("sync", lambda e, mo=mo, bw=bw: e.dma_start(
                    out=wdt_s[bw][:],
                    in_=d_wd.ap()[:, 128 * mo:128 * (mo + 1)].rearrange("(k p) m -> p k m", p=128)), ctr=WQ)
                P.wait("tensor", WQ, lw)
                if mo == 0:
                    P.wait("tensor", VC, fvs[-1])
                if mo >= 2:
                    P.wait("tensor", VC, xv)
                elif prev.get("xv7") is not None:
                    P.wait("tensor", VC, prev["xv7"])
                for kh in range(HM):
                    tds[mo] = P.op("tensor", lambda e, kh=kh, bw=bw, pd=pd: e.matmul(
                        pd[:], wdt_s[bw][:, kh, :], fbuf[:, kh, :],
                        start=(kh == 0), stop=(kh == HM - 1)), TM if kh == HM - 1 else None)
                P.wait("vector", TM, tds[mo])
                xv = P.op("vector", lambda e, mo=mo, pd=pd, xs=xs: e.scalar_tensor_tensor(
                    xs[:, mo, :], pd[:], 0.5, xs[:, mo, :], ALU.mult, ALU.add), VC)
            return dict(xv=xv, tds7=tds[-1], f1s7=f1s[-1], fvs7=fvs[-1], xv7=xv)

        def ln_apply_plain(T, src, dst_bf):
            P.op("vector", lambda e, src=src: e.tensor_tensor(
                T["cen"][:], src[:], bc_mid(mu_t[:], KD), ALU.subtract))
            return P.op("vector", lambda e, dst_bf=dst_bf: e.tensor_tensor(
                dst_bf[:], T["cen"][:], bc_mid(rstd_t[:], KD), ALU.mult), VC)

        def alloc_ffn_scope():
            T = {}
            T["xs"] = [AR.alloc([128, KD, TC], F32) for _ in range(2)]
            T["ln2t"] = AR.alloc([128, KD, TC], F32)
            T["cen"] = AR.alloc([128, KD, TC], F32)
            T["xbb"] = AR.alloc([128, KD, TC], BF16)
            T["xnb"] = AR.alloc([128, KD, TC], BF16)
            T["fbuf"] = AR.alloc([128, HM, TC], BF16)
            T["wgt_s"] = [AR.alloc([128, KD, 128], BF16) for _ in range(2)]
            T["wut_s"] = [AR.alloc([128, KD, 128], BF16) for _ in range(2)]
            T["wdt_s"] = [AR.alloc([128, HM, 128], BF16) for _ in range(2)]
            return T

        ag_done = [None] * NJ
        if True:
            AR.push()
            T = alloc_ffn_scope()
            prev = {}
            x1sp = [None, None]
            xs_vc = [None, None]
            xs_sc = [None, None]
            for j in range(NJ):
                par = j % 2
                jt = slice(TC * j, TC * (j + 1))
                xs = T["xs"][par]
                if j >= 2:
                    P.wait("sync", VC, xs_vc[par])
                    P.wait("sync", SC, xs_sc[par])
                    P.wait("sync", LD, x1sp[par])
                lx = P.dma("sync", lambda e, jt=jt, xs=xs: e.dma_start(out=xs[:], in_=x_fm_r[:, :, jt]))
                P.wait("scalar", LD, lx)
                P.wait("vector", LD, lx)
                s_r, _ = ln_stats(T, xs)
                P.wait("vector", SC, s_r)
                vx = ln_apply_plain(T, xs, T["xnb"])
                P.wait("tensor", VC, vx)
                prev = ffn(T, par, prev)
                xv = prev["xv"]
                P.wait("sync", VC, xv)
                x1sp[par] = P.dma("sync", lambda e, jt=jt, xs=xs: e.dma_start(
                    out=d_x1_r[:, :, jt], in_=xs[:]))
                s_r, c2m = ln_stats(T, xs, wait_vc=xv)
                xs_sc[par] = c2m
                P.wait("vector", SC, s_r)
                xs_vc[par] = P.op("vector", lambda e, xs=xs: e.tensor_tensor(
                    T["ln2t"][:], xs[:], bc_mid(mu_t[:], KD), ALU.subtract), VC)
                v2 = P.op("vector", lambda e: e.tensor_tensor(
                    T["ln2t"][:], T["ln2t"][:], bc_mid(rstd_t[:], KD), ALU.mult), VC)
                P.wait("sync", VC, v2)
                P.dma("sync", lambda e, jt=jt: e.dma_start(out=d_ln2_r[:, :, jt], in_=T["ln2t"][:]))
                s_r, _ = ln_stats(T, T["ln2t"], wait_vc=v2)
                P.wait("vector", SC, s_r)
                P.op("vector", lambda e: e.tensor_tensor(
                    T["cen"][:], T["ln2t"][:], bc_mid(mu_t[:], KD), ALU.subtract))
                P.op("vector", lambda e: e.tensor_tensor(
                    T["cen"][:], T["cen"][:], bc_mid(rstd_t[:], KD), ALU.mult))
                v3 = None
                for kb in range(KD):
                    v3 = P.op("vector", lambda e, kb=kb: e.tensor_scalar(
                        T["xnb"][:, kb, :], T["cen"][:, kb, :],
                        lngbt[:, 2 * kb:2 * kb + 1], lngbt[:, 2 * kb + 1:2 * kb + 2],
                        ALU.mult, ALU.add), VC)
                P.wait("sync", VC, v3)
                P.dma("sync", lambda e, j=j: e.dma_start(
                    out=d_xn.ap()[D * j:D * (j + 1), :].rearrange("(k p) t -> p k t", p=128),
                    in_=T["xnb"][:]), ctr=XN)
                if debug and "p1" in dbg_parts:
                    P.dma("sync", lambda e, jt=jt: e.dma_start(
                        out=dbg["dbg_xn"].rearrange("(k p) t -> p k t", p=128)[:, :, jt], in_=T["xnb"][:]))
                    P.dma("sync", lambda e, jt=jt, xs=xs: e.dma_start(
                        out=dbg["dbg_x1"].rearrange("(k p) t -> p k t", p=128)[:, :, jt], in_=xs[:]))
            P.barrier()
            AR.pop()
        for j in range(NJ):
            CC.n += 1
            P.op("gpsimd", lambda e, j=j: e.collective_compute(
                "AllGather", ALU.bypass, GROUPS,
                [d_xn.ap()[D * j:D * (j + 1), :].opt()],
                [d_xng.ap()[4 * D * j:4 * D * (j + 1), :].opt()]).then_inc(CC.h, 1))
        P.wait("gpsimd", CC, CC.n)
        P.op("gpsimd", lambda e: e.drain(fusable=False))
        CD.n += 1
        P.op("gpsimd", lambda e: e.sem_inc(CD.h, 1))
        ag_all = CD.n

        if True:
            AR.push()
            Winb = AR.alloc([128, KD, 1024], BF16)
            xng = [AR.alloc([128, KD, TC], BF16) for _ in range(2)]
            xhh = AR.alloc([128, NMI, TC + 8], F32)
            zst = AR.alloc([128, NMI, TC], F32)
            xct2 = AR.alloc([128, NMI, TC], F32)
            xcb2 = AR.alloc([128, NMI, TC], BF16)
            cvac = AR.alloc([128, NMI, TC], F32)
            Wxpb = AR.alloc([128, NMI, 33], BF16)
            xpjt = AR.alloc([33, TC], F32)
            lw = P.dma("sync", lambda e: e.dma_start(
                out=Winb[:], in_=d_winb.ap().rearrange("(k p) m -> p k m", p=128)))
            for mi in range(NMI):
                tw = P.dma("sync", lambda e, mi=mi: e.dma_start(
                    out=var_t[:, 0:33], in_=wxpc[128 * mi:128 * (mi + 1), :]))
                P.wait("scalar", LD, tw)
                sw = P.op("scalar", lambda e, mi=mi: e.activation(Wxpb[:, mi, :], var_t[:, 0:33], AF.Copy), SC)
                P.wait("sync", SC, sw)
            P.wait("tensor", LD, lw)
            P.op("gpsimd", lambda e: e.memset(xhh[:, :, 0:3], 0.0), PC)
            P.wait("scalar", PC, PC.n)
            conv_done = None
            sil_done = None
            cast_done = None
            tpj_prev = None
            spj_prev = None
            zs_spill = None
            xc_spill = None
            pj_spill = None
            bank_ev = {}
            mm_last = [None, None]
            for k in range(NCH):
                bk = k % 2
                r0 = 4 * D * (k // 4) + D * (k % 4)
                c0 = TC * k
                if mm_last[bk] is not None:
                    P.wait("sync", TM, mm_last[bk])
                P.wait("sync", CD, ag_all)
                lx = P.dma("sync", lambda e, r0=r0, bk=bk: e.dma_start(
                    out=xng[bk][:],
                    in_=d_xng.ap()[r0:r0 + D, :].rearrange("(k p) t -> p k t", p=128)))
                if k > 0:
                    P.wait("scalar", VC, conv_done)
                    P.op("scalar", lambda e: e.activation(xhh[:, :, 0:3], xhh[:, :, TC:TC + 3], AF.Copy))
                P.wait("tensor", LD, lx)
                ev = []
                for mi in range(KD):
                    psn = mi % 4
                    if bank_ev.get(psn) is not None:
                        P.wait("tensor", SC, bank_ev[psn])
                    tmm = None
                    for kb in range(KD):
                        tmm = P.op("tensor", lambda e, kb=kb, mi=mi, bk=bk, psn=psn: e.matmul(
                            PS[psn][:], Winb[:, kb, 128 * mi:128 * (mi + 1)], xng[bk][:, kb, :],
                            start=(kb == 0), stop=(kb == KD - 1)), TM if kb == KD - 1 else None)
                    if mi == KD - 1:
                        mm_last[bk] = tmm
                    P.wait("scalar", TM, tmm)
                    if mi < NMI:
                        evv = P.op("scalar", lambda e, mi=mi, psn=psn: e.activation(
                            xhh[:, mi, 3:3 + TC], PS[psn][:], AF.Copy), SC)
                    else:
                        if zs_spill is not None and mi == NMI:
                            P.wait("scalar", LD, zs_spill)
                        evv = P.op("scalar", lambda e, mi=mi, psn=psn: e.activation(
                            zst[:, mi - NMI, :], PS[psn][:], AF.Silu), SC)
                    ev.append(evv)
                    bank_ev[psn] = evv
                P.wait("sync", SC, ev[-1])
                zs_spill = P.dma("sync", lambda e, c0=c0: e.dma_start(
                    out=d_z_r[:, :, c0:c0 + TC], in_=zst[:]))
                P.wait("vector", SC, ev[NMI - 1])
                if sil_done is not None:
                    P.wait("vector", SC, sil_done)
                cv = None
                for mi in range(NMI):
                    P.op("vector", lambda e, mi=mi: e.tensor_scalar(
                        cvac[:, mi, :], xhh[:, mi, 0:TC],
                        cwt[:, 4 * mi + 0:4 * mi + 1], vect[:, 4 * mi + 0:4 * mi + 1],
                        ALU.mult, ALU.add))
                    for kk in range(1, 4):
                        cv = P.op("vector", lambda e, mi=mi, kk=kk: e.scalar_tensor_tensor(
                            cvac[:, mi, :], xhh[:, mi, kk:kk + TC],
                            cwt[:, 4 * mi + kk:4 * mi + kk + 1], cvac[:, mi, :],
                            ALU.mult, ALU.add), VC if (mi == NMI - 1 and kk == 3) else None)
                conv_done = cv
                P.wait("scalar", VC, cv)
                if xc_spill is not None:
                    P.wait("scalar", LD, xc_spill)
                sil_done = P.op("scalar", lambda e: e.activation(
                    xct2[:].rearrange("p m t -> p (m t)"), cvac[:].rearrange("p m t -> p (m t)"),
                    AF.Silu), SC)
                if tpj_prev is not None:
                    P.wait("scalar", TM, tpj_prev)
                cast_done = P.op("scalar", lambda e: e.activation(
                    xcb2[:].rearrange("p m t -> p (m t)"), xct2[:].rearrange("p m t -> p (m t)"),
                    AF.Copy), SC)
                P.wait("sync", SC, sil_done)
                xc_spill = P.dma("sync", lambda e, c0=c0: e.dma_start(
                    out=d_xc_r[:, :, c0:c0 + TC], in_=xct2[:]))
                P.wait("tensor", SC, cast_done)
                if spj_prev is not None:
                    P.wait("tensor", SC, spj_prev)
                tpj = None
                for mi in range(NMI):
                    tpj = P.op("tensor", lambda e, mi=mi: e.matmul(
                        PS[4][0:33, :], Wxpb[:, mi, :], xcb2[:, mi, :],
                        start=(mi == 0), stop=(mi == NMI - 1)), TM if mi == NMI - 1 else None)
                tpj_prev = tpj
                P.wait("scalar", TM, tpj)
                if pj_spill is not None:
                    P.wait("scalar", LD, pj_spill)
                spj = P.op("scalar", lambda e: e.activation(xpjt[:], PS[4][0:33, :], AF.Copy), SC)
                spj_prev = spj
                P.wait("sync", SC, spj)
                pj_spill = P.dma("sync", lambda e, c0=c0: e.dma_start(
                    out=d_xpj.ap()[:, c0:c0 + TC], in_=xpjt[:]))
            P.barrier()
            AR.pop()

        art = P.coll("AllReduce", ALU.add, [d_xpj.ap().opt()], [d_xpjr.ap().opt()],
                     wait_ld=P.LD.n)
        P.barrier()

        if True:
            AR.push()
            Wob = AR.alloc([128, NMI, 1024], BF16)
            xct3 = [AR.alloc([128, NMI, TC], F32) for _ in range(2)]
            zsl = [AR.alloc([128, NMI, TC], F32) for _ in range(2)]
            dtt = [AR.alloc([128, NMI, TC], F32) for _ in range(2)]
            ut = [AR.alloc([128, NMI, TC], BF16) for _ in range(2)]
            Bxt = [AR.alloc([16, TC], F32) for _ in range(2)]
            Cxt = [AR.alloc([16, TC], F32) for _ in range(2)]
            dtr = [AR.alloc([1, TC], F32) for _ in range(2)]
            Bxb = [AR.alloc([16, TC], BF16) for _ in range(2)]
            Cxb = [AR.alloc([16, TC], BF16) for _ in range(2)]
            Cb = [AR.alloc([128, TC], BF16) for _ in range(2)]
            Bbb = [AR.alloc([128, TC], BF16) for _ in range(2)]
            ab = [AR.alloc([128, NMI * TC], F32) for _ in range(2)]
            bb3 = AR.alloc([128, NMI * TC], BF16)
            hta = [AR.alloc([128, NMI, TC], F32) for _ in range(2)]
            tmpb = [AR.alloc([128, NMI * TC], BF16) for _ in range(2)]
            yt = [AR.alloc([128, NMI, TC], BF16) for _ in range(2)]
            gt = [AR.alloc([128, NMI, TC], BF16) for _ in range(2)]
            eb2b = AR.alloc([128, NMI * TC], F32)
            rb2b = AR.alloc([128, NMI * TC], F32)
            wot = [AR.alloc([128, TC], F32) for _ in range(2)]

            lw = P.dma("sync", lambda e: e.dma_start(
                out=Wob[:], in_=d_wob.ap().rearrange("(k p) m -> p k m", p=128)))
            P.op("vector", lambda e: e.memset(st[:], 0.0), VC)
            P.wait("tensor", LD, lw)

            roll = {"a3": {}, "dve": {}, "psB_mm": {}, "psC_ev": {}, "tmp": {}, "tmp_rd": {}}
            S = {"l1": {}, "l2": {}, "vsp": {}, "u": {}, "bxb": {}, "tdt": {},
                 "b3l": {}, "vg": {}, "wol": {}}
            wo_bank_ev = {0: None, 1: None}

            def head(k):
                pk = k % 2
                kt = slice(TC * k, TC * (k + 1))
                if S["vg"].get(k - 2) is not None:
                    P.wait("sync", VC, S["vg"][k - 2])
                    P.wait("sync", PC, S["u"][k - 2])
                S["l1"][k] = P.dma("sync", lambda e, kt=kt, pk=pk: e.dma_start(
                    out=xct3[pk][:], in_=d_xc_r[:, :, kt]))
                S["l2"][k] = P.dma("sync", lambda e, kt=kt, pk=pk: e.dma_start(
                    out=zsl[pk][:], in_=d_z_r[:, :, kt]))
                if S["bxb"].get(k - 2) is not None:
                    P.wait("sync", SC, S["bxb"][k - 2])
                if S["tdt"].get(k - 2) is not None:
                    P.wait("sync", TM, S["tdt"][k - 2])
                lB = P.dma("sync", lambda e, kt=kt, pk=pk: e.dma_start(
                    out=Bxt[pk][:], in_=d_xpjr.ap()[0:16, kt]))
                lC = P.dma("sync", lambda e, kt=kt, pk=pk: e.dma_start(
                    out=Cxt[pk][:], in_=d_xpjr.ap()[16:32, kt]))
                lD = P.dma("sync", lambda e, kt=kt, pk=pk: e.dma_start(
                    out=dtr[pk][:], in_=d_xpjr.ap()[32:33, kt]))
                P.wait("tensor", LD, lD)
                if S["vsp"].get(k - 1) is not None:
                    P.wait("tensor", VC, S["vsp"][k - 1])
                tdt = P.op("tensor", lambda e, pk=pk: e.matmul(
                    PS[6 + pk][:], ones1t[:], dtr[pk][:], start=True, stop=True), TM)
                S["tdt"][k] = tdt
                P.wait("vector", TM, tdt)
                if k >= 2:
                    P.wait("vector", SC, roll["a3"][16 * (k - 2) + 15])
                    P.wait("vector", PC, S["u"][k - 2])
                for mi in range(NMI):
                    P.op("vector", lambda e, mi=mi, pk=pk: e.tensor_scalar(
                        dtt[pk][:, mi, :], PS[6 + pk][:],
                        vect[:, 4 * mi + 2:4 * mi + 3], vect[:, 4 * mi + 1:4 * mi + 2],
                        ALU.mult, ALU.add))
                vrl = P.op("vector", lambda e, pk=pk: e.tensor_scalar(
                    rb2b[:], dtt[pk][:].rearrange("p m t -> p (m t)"), 0.0, None, ALU.max), VC)
                P.wait("scalar", VC, vrl)
                P.op("scalar", lambda e, pk=pk: e.activation(
                    eb2b[:], dtt[pk][:].rearrange("p m t -> p (m t)"), AF.Abs))
                P.op("scalar", lambda e: e.activation(eb2b[:], eb2b[:], AF.Exp, scale=-1.0))
                ssp = P.op("scalar", lambda e: e.activation(eb2b[:], eb2b[:], AF.Ln, bias=one_t[:, 0:1]), SC)
                P.wait("vector", SC, ssp)
                vsp = P.op("vector", lambda e, pk=pk: e.tensor_tensor(
                    dtt[pk][:].rearrange("p m t -> p (m t)"), rb2b[:], eb2b[:], ALU.add), VC)
                S["vsp"][k] = vsp
                P.wait("gpsimd", VC, vsp)
                P.wait("gpsimd", LD, S["l1"][k])
                if S["b3l"].get(k - 2) is not None:
                    P.wait("gpsimd", VC, S["b3l"][k - 2])
                S["u"][k] = P.op("gpsimd", lambda e, pk=pk: e.tensor_tensor(
                    ut[pk][:].rearrange("p m t -> p (m t)"), dtt[pk][:].rearrange("p m t -> p (m t)"),
                    xct3[pk][:].rearrange("p m t -> p (m t)"), ALU.mult), PC)
                P.wait("scalar", LD, lC)
                if k >= 2:
                    P.wait("scalar", TM, roll["psB_mm"][16 * (k - 2) + 15])
                P.op("scalar", lambda e, pk=pk: e.activation(Bxb[pk][:], Bxt[pk][:], AF.Copy))
                S["bxb"][k] = P.op("scalar", lambda e, pk=pk: e.activation(Cxb[pk][:], Cxt[pk][:], AF.Copy), SC)

            head(0)
            for k in range(NCH):
                pk = k % 2
                if debug and "dty" in dbg_parts and k == 0:
                    P.wait("sync", VC, S["vsp"][0])
                    P.dma("sync", lambda e: e.dma_start(out=dbg["dbg_dt"], in_=dtt[0][:].rearrange("p m t -> p (m t)")))
                def ensure_bcast(g2, kk):
                    if g2 in roll["psB_mm"] or g2 >= 16 * NCH:
                        return
                    n2 = g2 % 16
                    pp2 = g2 % 2
                    psB2 = PS[0 + 2 * pp2]
                    psC2 = PS[1 + 2 * pp2]
                    P.wait("tensor", SC, S["bxb"][kk])
                    if roll["psC_ev"].get(g2 - 2) is not None:
                        P.wait("tensor", SC, roll["psC_ev"][g2 - 2])
                    P.op("tensor", lambda e, n2=n2, psB2=psB2, kk=kk: e.matmul(
                        psB2[:], selb[:, 128 * n2:128 * (n2 + 1)], Bxb[kk % 2][:], start=True, stop=True))
                    tbc = P.op("tensor", lambda e, n2=n2, psC2=psC2, kk=kk: e.matmul(
                        psC2[:], selb[:, 128 * n2:128 * (n2 + 1)], Cxb[kk % 2][:], start=True, stop=True), TM)
                    roll["psB_mm"][g2] = tbc
                    P.wait("scalar", TM, tbc)
                    if roll["tmp"].get(g2 - 2) is not None:
                        P.wait("scalar", VC, roll["tmp"][g2 - 2])
                    if roll["dve"].get(g2 - 2) is not None:
                        P.wait("scalar", VC, roll["dve"][g2 - 2])
                    P.op("scalar", lambda e, pp2=pp2, psB2=psB2: e.activation(Bbb[pp2][:], psB2[:], AF.Copy))
                    cev = P.op("scalar", lambda e, pp2=pp2, psC2=psC2: e.activation(Cb[pp2][:], psC2[:], AF.Copy), SC)
                    roll["psC_ev"][g2] = cev

                def emit_yadd(g2):
                    pp2 = g2 % 2
                    pk2 = (g2 // 16) % 2
                    if g2 % 16 == 0:
                        v = P.op("vector", lambda e, pp2=pp2, pk2=pk2: e.tensor_scalar(
                            yt[pk2][:].rearrange("p m t -> p (m t)"),
                            tmpb[pp2][:], 0.0, None, ALU.add), VC)
                    else:
                        v = P.op("vector", lambda e, pp2=pp2, pk2=pk2: e.tensor_tensor(
                            yt[pk2][:].rearrange("p m t -> p (m t)"),
                            yt[pk2][:].rearrange("p m t -> p (m t)"),
                            tmpb[pp2][:], ALU.add), VC)
                    roll["tmp_rd"][g2] = v

                def emit_a(g2, kk):
                    if g2 in roll["a3"] or g2 >= 16 * NCH:
                        return
                    n2 = g2 % 16
                    pp2 = g2 % 2
                    pk2 = kk % 2
                    if n2 == 0:
                        P.wait("scalar", VC, S["vsp"][kk])
                    if roll["dve"].get(g2 - 2) is not None:
                        P.wait("scalar", VC, roll["dve"][g2 - 2])
                    sa = None
                    for mi in range(NMI):
                        sa = P.op("scalar", lambda e, mi=mi, n2=n2, pp2=pp2, pk2=pk2: e.activation(
                            ab[pp2][:, TC * mi:TC * (mi + 1)], dtt[pk2][:, mi, :], AF.Exp,
                            scale=Aneg[:, 16 * mi + n2:16 * mi + n2 + 1]),
                            SC if mi == NMI - 1 else None)
                    roll["a3"][g2] = sa

                for n in range(NS):
                    g = 16 * k + n
                    pp = g % 2
                    ensure_bcast(g, k)
                    cev = roll["psC_ev"][g]
                    emit_a(g, k)
                    if n == 0:
                        P.wait("vector", PC, S["u"][k])
                    P.wait("vector", SC, cev)
                    b3m = P.op("vector", lambda e, pp=pp, pk=pk: e.tensor_tensor(
                        bb3[:].rearrange("p (m t) -> p m t", m=NMI),
                        ut[pk][:], bc_mid(Bbb[pp][:], NMI), ALU.mult), VC)
                    if n == NS - 1:
                        S["b3l"][k] = b3m
                    P.wait("vector", SC, roll["a3"][g])
                    for mi in range(NMI):
                        P.op("vector", lambda e, mi=mi, n=n, pp=pp: e.tensor_tensor_scan(
                            hta[pp][:, mi, :],
                            ab[pp][:, TC * mi:TC * (mi + 1)],
                            bb3[:, TC * mi:TC * (mi + 1)],
                            st[:, mi, n:n + 1], ALU.mult, ALU.add))
                    vsc = P.op("vector", lambda e, n=n, pp=pp: e.tensor_copy(
                        st[:, :, n:n + 1], hta[pp][:, :, TC - 1:TC]), VC)
                    roll["dve"][g] = vsc
                    tm_ = P.op("vector", lambda e, pp=pp: e.tensor_tensor(
                        tmpb[pp][:].rearrange("p (m t) -> p m t", m=NMI),
                        hta[pp][:], bc_mid(Cb[pp][:], NMI), ALU.mult), VC)
                    roll["tmp"][g] = tm_
                    if g > 0 and (g - 1) not in roll["tmp_rd"]:
                        emit_yadd(g - 1)
                    ensure_bcast(g + 1, k if n + 1 < NS else k + 1)
                    if n == 8 and k + 1 < NCH:
                        head(k + 1)
                    if n >= 14 and k + 1 < NCH:
                        ensure_bcast(g + 2, k + 1)
                        emit_a(g + 2, k + 1)
                emit_yadd(16 * k + 15)
                P.wait("vector", LD, S["l1"][k])
                for mi in range(NMI):
                    P.op("vector", lambda e, mi=mi, pk=pk: e.scalar_tensor_tensor(
                        yt[pk][:, mi, :], xct3[pk][:, mi, :],
                        vect[:, 4 * mi + 3:4 * mi + 4], yt[pk][:, mi, :], ALU.mult, ALU.add))
                P.wait("vector", LD, S["l2"][k])
                if S["wol"].get(k - 2) is not None:
                    P.wait("vector", TM, S["wol"][k - 2])
                vg = P.op("vector", lambda e, pk=pk: e.tensor_tensor(
                    gt[pk][:].rearrange("p m t -> p (m t)"), yt[pk][:].rearrange("p m t -> p (m t)"),
                    zsl[pk][:].rearrange("p m t -> p (m t)"), ALU.mult), VC)
                S["vg"][k] = vg
                if debug and "dty" in dbg_parts and k == 0:
                    P.wait("gpsimd", VC, vg)
                    P.dma("gpsimd", lambda e: e.dma_start(out=dbg["dbg_y"], in_=yt[0][:].rearrange("p m t -> p (m t)")))
                P.wait("tensor", VC, vg)
                wo_dma = {}
                for mo in range(KD):
                    pw = mo % 2
                    psW = PS[4 + pw]
                    if wo_bank_ev[pw] is not None:
                        P.wait("tensor", SC, wo_bank_ev[pw])
                    two = None
                    for kd in range(NMI):
                        two = P.op("tensor", lambda e, kd=kd, mo=mo, pk=pk, psW=psW: e.matmul(
                            psW[:], Wob[:, kd, 128 * mo:128 * (mo + 1)], gt[pk][:, kd, :],
                            start=(kd == 0), stop=(kd == NMI - 1)), TM if kd == NMI - 1 else None)
                    if mo == KD - 1:
                        S["wol"][k] = two
                    P.wait("scalar", TM, two)
                    if wo_dma.get(mo - 2) is not None:
                        P.wait("scalar", WO, wo_dma[mo - 2])
                    swo = P.op("scalar", lambda e, pw=pw, psW=psW: e.activation(wot[pw][:], psW[:], AF.Copy), SC)
                    wo_bank_ev[pw] = swo
                    P.wait("sync", SC, swo)
                    wo_dma[mo] = P.dma("sync", lambda e, k=k, mo=mo, pw=pw: e.dma_start(
                        out=d_wop_j[k // 4].ap()[1024 * (k % 4) + 128 * mo:1024 * (k % 4) + 128 * (mo + 1), :],
                        in_=wot[pw][:]), ctr=WO)
                if k % 4 == 3:
                    P.wait("gpsimd", WO, 16 * KD * (k + 1))
                    CC.n += 1
                    P.op("gpsimd", lambda e, j=k // 4: e.collective_compute(
                        "ReduceScatter", ALU.add, GROUPS,
                        [d_wop_j[j].ap().opt()], [d_att_j[j].ap().opt()]).then_inc(CC.h, 1))
                if debug and "wop" in dbg_parts and k == 3:
                    P.wait("sync", LD, P.LD.n)
                    P.dma("sync", lambda e: e.dma_start(out=dbg["dbg_wop"], in_=d_wop_j[0].ap()))
            P.wait("gpsimd", CC, CC.n)
            P.op("gpsimd", lambda e: e.drain(fusable=False))
            P.barrier()
            AR.pop()

        if True:
            AR.push()
            T = alloc_ffn_scope()
            xnb2 = [AR.alloc([128, KD, TC], BF16) for _ in range(2)]
            prev = {}
            oysp = [None, None]
            xs_vc = [None, None]
            xs_sc = [None, None]
            ST = {"v1": None, "app": None, "scr": None, "vx2": [None, None]}
            PR = {}

            def p3_pre(j):
                par = j % 2
                jt = slice(TC * j, TC * (j + 1))
                xs = T["xs"][par]
                if xs_vc[par] is not None:
                    P.wait("sync", VC, xs_vc[par])
                if xs_sc[par] is not None:
                    P.wait("sync", SC, xs_sc[par])
                if oysp[par] is not None:
                    P.wait("sync", LD, oysp[par])
                P.dma("sync", lambda e, jt=jt, xs=xs: e.dma_start(out=xs[:], in_=d_x1_r[:, :, jt]))
                if ST["scr"] is not None:
                    P.wait("sync", VC, ST["scr"])
                P.dma("sync", lambda e, jt=jt: e.dma_start(out=T["ln2t"][:], in_=d_ln2_r[:, :, jt]))
                if ST["vx2"][1 - par] is not None:
                    P.wait("sync", VC, ST["vx2"][1 - par])
                l3 = P.dma("sync", lambda e, j=j: e.dma_start(
                    out=T["cen"][:],
                    in_=d_att_j[j].ap().rearrange("(k p) t -> p k t", p=128)))
                P.wait("vector", LD, l3)
                P.op("vector", lambda e, xs=xs: e.tensor_tensor(
                    xs[:].rearrange("p k t -> p (k t)"), xs[:].rearrange("p k t -> p (k t)"),
                    T["ln2t"][:].rearrange("p k t -> p (k t)"), ALU.add))
                vx2 = P.op("vector", lambda e, xs=xs: e.tensor_tensor(
                    xs[:].rearrange("p k t -> p (k t)"), xs[:].rearrange("p k t -> p (k t)"),
                    T["cen"][:].rearrange("p k t -> p (k t)"), ALU.add), VC)
                ST["vx2"][par] = vx2
                if debug and "p3" in dbg_parts:
                    P.wait("sync", LD, l3)
                    P.dma("sync", lambda e, j=j: e.dma_start(
                        out=dbg["dbg_att"][D * j:D * (j + 1), :].rearrange("(k p) t -> p k t", p=128),
                        in_=T["cen"][:]))
                PR[j] = vx2

            def p3_casts(j):
                par = j % 2
                xs = T["xs"][par]
                P.wait("scalar", VC, PR[j])
                P.op("scalar", lambda e, xs=xs: e.activation(T["xbb"][:], xs[:], AF.Copy))
                c2 = P.op("scalar", lambda e, xs=xs, par=par: e.activation(
                    xnb2[par][:], xs[:], AF.Square), SC)
                xs_sc[par] = c2
                PR[("c", j)] = c2

            def p3_lnmm(j):
                par = j % 2
                xs = T["xs"][par]
                P.wait("tensor", SC, PR[("c", j)])
                if ST["v1"] is not None:
                    P.wait("tensor", VC, ST["v1"])
                for kb in range(KD):
                    P.op("tensor", lambda e, kb=kb: e.matmul(
                        PS[4][:], onesb[:], T["xbb"][:, kb, :], start=(kb == 0), stop=(kb == KD - 1)),
                        TM if kb == KD - 1 else None)
                tm2 = None
                for kb in range(KD):
                    tm2 = P.op("tensor", lambda e, kb=kb, par=par: e.matmul(
                        PS[5][:], onesb[:], xnb2[par][:, kb, :], start=(kb == 0), stop=(kb == KD - 1)),
                        TM if kb == KD - 1 else None)
                P.wait("vector", TM, tm2)
                P.op("vector", lambda e: e.tensor_scalar(mu_t[:], PS[4][:], 1.0 / D, None, ALU.mult))
                P.op("vector", lambda e: e.tensor_scalar(ms_t[:], PS[5][:], 1.0 / D, None, ALU.mult))
                P.op("vector", lambda e: e.tensor_tensor(var_t[:], mu_t[:], mu_t[:], ALU.mult))
                v1 = P.op("vector", lambda e: e.tensor_tensor(var_t[:], ms_t[:], var_t[:], ALU.subtract), VC)
                ST["v1"] = v1
                P.wait("scalar", VC, v1)
                if ST["app"] is not None:
                    P.wait("scalar", VC, ST["app"])
                P.op("scalar", lambda e: e.activation(rstd_t[:], var_t[:], AF.Ln, bias=eps_t[:, 0:1]))
                s2 = P.op("scalar", lambda e: e.activation(rstd_t[:], rstd_t[:], AF.Exp, scale=-0.5), SC)
                P.wait("vector", SC, s2)
                sub = P.op("vector", lambda e, xs=xs: e.tensor_tensor(
                    T["ln2t"][:], xs[:], bc_mid(mu_t[:], KD), ALU.subtract), VC)
                xs_vc[par] = sub
                vx = P.op("vector", lambda e, par=par: e.tensor_tensor(
                    xnb2[par][:], T["ln2t"][:], bc_mid(rstd_t[:], KD), ALU.mult), VC)
                ST["scr"] = vx
                ST["app"] = vx
                PR[("a", j)] = vx

            p3_pre(0)
            p3_casts(0)
            p3_lnmm(0)
            for j in range(NJ):
                par = j % 2
                jt = slice(TC * j, TC * (j + 1))
                xs = T["xs"][par]
                if j + 1 < NJ:
                    p3_pre(j + 1)
                    gu = (lambda mh, j1=j + 1: p3_casts(j1) if mh == 4 else None)
                    mid = (lambda j1=j + 1: p3_lnmm(j1))
                else:
                    gu = None
                    mid = None
                P.wait("tensor", VC, PR[("a", j)])
                prev = ffn(T, par, prev, xnb=xnb2[par], gu_hook=gu, mid_tc=mid)
                xv = prev["xv"]
                P.wait("sync", VC, xv)
                oysp[par] = P.dma("sync", lambda e, jt=jt, xs=xs: e.dma_start(
                    out=o_y_r[:, :, jt], in_=xs[:]))

        P.wait("sync", LD, P.LD.n)

        block = E(nc.Block())
        P.emit(block)

    return nc


_DEBUG = False
_NC_CACHE = {}


_DBG_PARTS = ("p3",)


def _get_nc():
    key = (bool(_DEBUG), tuple(_DBG_PARTS))
    if key not in _NC_CACHE:
        _NC_CACHE[key] = build_program(debug=key[0], dbg_parts=key[1])
    return _NC_CACHE[key]


def kernel(L0_attn_x, L0_attn_ln_g, L0_attn_ln_b, L0_attn_W_in, L0_attn_conv_w,
           L0_attn_conv_b, L0_attn_W_xp, L0_attn_W_dt, L0_attn_b_dt, L0_attn_A_log,
           L0_attn_D, L0_attn_W_out, L0_ffn_w_gate, L0_ffn_w_up, L0_ffn_w_down):
    f32 = np.float32
    x = np.asarray(L0_attn_x, f32)
    W_in = np.asarray(L0_attn_W_in, f32)
    W_out = np.asarray(L0_attn_W_out, f32)
    W_xp = np.asarray(L0_attn_W_xp, f32)
    conv_w = np.asarray(L0_attn_conv_w, f32)
    conv_b = np.asarray(L0_attn_conv_b, f32)
    W_dt = np.asarray(L0_attn_W_dt, f32)
    b_dt = np.asarray(L0_attn_b_dt, f32)
    A_log = np.asarray(L0_attn_A_log, f32)
    Dvec = np.asarray(L0_attn_D, f32)
    ln_g = np.asarray(L0_attn_ln_g, f32)
    ln_b = np.asarray(L0_attn_ln_b, f32)
    wg = np.ascontiguousarray(np.asarray(L0_ffn_w_gate, np.int32).T)
    wu = np.ascontiguousarray(np.asarray(L0_ffn_w_up, np.int32).T)
    wd = np.ascontiguousarray(np.asarray(L0_ffn_w_down, np.int32).T)

    sel = np.zeros((16, 2048), f32)
    for n in range(16):
        sel[n, 128 * n:128 * (n + 1)] = 1.0
    ones128 = np.ones((128, 128), f32)
    ones1 = np.ones((1, 128), f32)
    lngb = np.zeros((128, 16), f32)
    for kb in range(KD):
        lngb[:, 2 * kb] = ln_g[128 * kb:128 * (kb + 1)]
        lngb[:, 2 * kb + 1] = ln_b[128 * kb:128 * (kb + 1)]

    in_maps = []
    for c in range(8):
        b, r = divmod(c, 4)
        sl = slice(512 * r, 512 * (r + 1))
        xc = np.concatenate(
            [x[b, TC * (4 * j + r):TC * (4 * j + r + 1), :].T for j in range(NJ)], axis=1)
        winc_c = np.concatenate([W_in[:, sl], W_in[:, 2048 + sl.start:2048 + sl.stop]], axis=1)
        cwc = conv_w[sl].reshape(4, 128, 4).transpose(1, 0, 2).reshape(128, 16)
        avals = A_log[sl].reshape(4, 128, 16).transpose(1, 0, 2).reshape(128, 64)
        vv = np.stack([conv_b[sl], b_dt[sl], W_dt[0, sl], Dvec[sl]], axis=-1)
        vv = vv.reshape(4, 128, 4).transpose(1, 0, 2).reshape(128, 16)
        in_maps.append(dict(
            x_fm=np.ascontiguousarray(xc),
            winc=np.ascontiguousarray(winc_c),
            woc=np.ascontiguousarray(W_out[sl]),
            wxpc=np.ascontiguousarray(W_xp[sl]),
            wgT=wg, wuT=wu, wdT=wd,
            cwc=np.ascontiguousarray(cwc),
            avals=np.ascontiguousarray(avals),
            vecs=np.ascontiguousarray(vv),
            lngb=lngb, sel=sel, ones128=ones128, ones1=ones1,
        ))

    nc = _get_nc()
    res = run_bass_kernel_spmd(nc, in_maps, core_ids=list(range(8)))

    out = np.empty((B, L, D), f32)
    for c in range(8):
        b, r = divmod(c, 4)
        oy = np.asarray(res.results[c]["o_y"], f32)
        for j in range(NJ):
            out[b, TC * (4 * j + r):TC * (4 * j + r + 1), :] = oy[:, TC * j:TC * (j + 1)].T
    if _DEBUG:
        kernel._dbg = res.results
    return out


# version 37
# speedup vs baseline: 1.2225x; 1.0398x over previous
import dataclasses
from contextlib import ExitStack

import numpy as np

import concourse.bass as bass
import concourse.mybir as mybir
from concourse.bass_utils import run_bass_kernel_spmd

F32 = mybir.dt.float32
BF16 = mybir.dt.bfloat16
I32 = mybir.dt.int32
AF = mybir.ActivationFunctionType
ALU = mybir.AluOpType

B, L, D, DIN, H, NS = 2, 8192, 1024, 2048, 3072, 16
TC = 512
NCH = 16
NJ = 4
KD = 8
NMI = 4
HM = H // 128
EPS = 1e-5
GROUPS = [[0, 1, 2, 3], [4, 5, 6, 7]]
ENGINES = ("sync", "vector", "scalar", "tensor", "gpsimd")


def bc_mid(ap, rep):
    assert len(ap.ap) == 2
    return dataclasses.replace(ap, ap=[ap.ap[0], [0, rep], ap.ap[1]])


class Ctr:
    def __init__(self, h):
        self.h = h
        self.n = 0


class Bld:
    def __init__(self, nc, ctx):
        self.nc = nc
        self.ops = {e: [] for e in ENGINES}
        E = ctx.enter_context
        self.LD = Ctr(E(nc.semaphore("LD")))
        self.CC = Ctr(E(nc.semaphore("CC")))
        self.VC = Ctr(E(nc.semaphore("VC")))
        self.SC = Ctr(E(nc.semaphore("SC")))
        self.TM = Ctr(E(nc.semaphore("TM")))
        self.PC = Ctr(E(nc.semaphore("PC")))
        self.CD = Ctr(E(nc.semaphore("CD")))
        self.WQ = Ctr(E(nc.semaphore("WQ")))
        self.WO = Ctr(E(nc.semaphore("WO")))
        self.XN = Ctr(E(nc.semaphore("XN")))
        self.L2 = Ctr(E(nc.semaphore("L2")))
        self.L3 = Ctr(E(nc.semaphore("L3")))
        self.L4 = Ctr(E(nc.semaphore("L4")))
        self.BAR = Ctr(E(nc.semaphore("BAR")))

    def op(self, eng, build, ctr=None):
        if ctr is None:
            self.ops[eng].append(build)
            return None
        ctr.n += 1
        tgt = ctr.n
        self.ops[eng].append(lambda e, b=build, c=ctr: b(e).then_inc(c.h, 1))
        return tgt

    def dma(self, eng, build, ctr=None):
        ctr = ctr or self.LD
        ctr.n += 16
        tgt = ctr.n
        self.ops[eng].append(lambda e, b=build, c=ctr: b(e).then_inc(c.h, 16))
        return tgt

    def wait(self, eng, ctr, val):
        if val is None:
            return
        self.ops[eng].append(lambda e, c=ctr, v=val: e.wait_ge(c.h, v))

    def barrier(self):
        self.wait("sync", self.LD, self.LD.n)
        self.wait("sync", self.WQ, self.WQ.n)
        self.wait("sync", self.WO, self.WO.n)
        self.wait("sync", self.XN, self.XN.n)
        self.wait("sync", self.L2, self.L2.n)
        self.wait("sync", self.L3, self.L3.n)
        self.wait("sync", self.L4, self.L4.n)
        tgt = self.BAR.n + len(ENGINES)
        for e in ENGINES:
            self.ops[e].append(lambda E: E.sem_inc(self.BAR.h, 1))
        self.BAR.n = tgt
        for e in ENGINES:
            self.ops[e].append(lambda E, v=tgt: E.wait_ge(self.BAR.h, v))

    def coll(self, kind, op_, ins, outs, wait_ld=None):
        if wait_ld is not None:
            self.wait("gpsimd", self.LD, wait_ld)
        self.CC.n += 1
        tgt = self.CC.n
        self.ops["gpsimd"].append(lambda e, k=kind, o=op_, i=ins, u=outs: e.collective_compute(
            k, o, GROUPS, i, u).then_inc(self.CC.h, 1))
        self.wait("gpsimd", self.CC, tgt)
        self.ops["gpsimd"].append(lambda e: e.drain(fusable=False))
        self.CD.n += 1
        cdt = self.CD.n
        self.ops["gpsimd"].append(lambda e, v=cdt: e.sem_inc(self.CD.h, 1))
        return cdt

    def emit(self, block):
        b = self
        @block.sync
        def _(eng):
            for f in b.ops["sync"]:
                f(eng)
        @block.vector
        def _(eng):
            for f in b.ops["vector"]:
                f(eng)
        @block.scalar
        def _(eng):
            for f in b.ops["scalar"]:
                f(eng)
        @block.tensor
        def _(eng):
            for f in b.ops["tensor"]:
                f(eng)
        @block.gpsimd
        def _(eng):
            for f in b.ops["gpsimd"]:
                f(eng)


class Arena:
    ESZ = {F32: 4, BF16: 2, I32: 4}

    def __init__(self, nc, ctx, name, words):
        self.h = ctx.enter_context(nc.sbuf_tensor(name, [128, words], F32))
        self.hb = self.h.bitcast(BF16)
        self.hi = self.h.bitcast(I32)
        self.cap = words * 4
        self.off = 0
        self.stack = []

    def push(self):
        self.stack.append(self.off)

    def pop(self):
        self.off = self.stack.pop()

    def alloc(self, shape, dt=F32):
        n = 1
        for s in shape[1:]:
            n *= s
        esz = self.ESZ[dt]
        self.off = (self.off + 63) & ~63
        start = self.off
        self.off += n * esz
        assert self.off <= self.cap, f"arena overflow: {self.off} > {self.cap}"
        if dt == F32:
            base = self.h[:, start // 4: start // 4 + n]
        elif dt == I32:
            base = self.hi[:, start // 4: start // 4 + n]
        else:
            base = self.hb[:, start // 2: start // 2 + n]
        if shape[0] != 128:
            base = base[0:shape[0], :]
        if len(shape) == 3:
            base = base.rearrange("p (a b) -> p a b", a=shape[1])
        return base


def build_program(debug=False, dbg_parts=("p1","scan","wop","rs","p3","dty")):
    nc = bass.Bass("TRN2", target_bir_lowering=False, detect_race_conditions=False)

    def din(name, shape, dt=F32):
        return nc.dram_tensor(name, shape, dt, kind="ExternalInput").ap()

    x_fm = din("x_fm", [D, NJ * TC])
    winc = din("winc", [D, 1024])
    woc = din("woc", [512, D])
    wxpc = din("wxpc", [512, 33])
    wgT = din("wgT", [D, H], I32)
    wuT = din("wuT", [D, H], I32)
    wdT = din("wdT", [H, D], I32)
    cwc = din("cwc", [128, 16])
    avals = din("avals", [128, 64])
    vecs = din("vecs", [128, 16])
    lngb = din("lngb", [128, 16])
    sel_i = din("sel", [16, 2048])
    ones128 = din("ones128", [128, 128])
    ones1 = din("ones1", [1, 128])

    o_y = nc.dram_tensor("o_y", [D, NJ * TC], F32, kind="ExternalOutput").ap()

    def dint(name, shape, dt=F32):
        return nc.dram_tensor(name, shape, dt, kind="Internal")

    d_wg = dint("d_wg", [D, H], BF16)
    d_wu = dint("d_wu", [D, H], BF16)
    d_wd = dint("d_wd", [H, D], BF16)
    d_winb = dint("d_winb", [D, 1024], BF16)
    d_wob = dint("d_wob", [512, D], BF16)
    d_x1 = dint("d_x1", [D, NJ * TC], F32)
    d_ln2 = dint("d_ln2", [D, NJ * TC], F32)
    d_xn = dint("d_xn", [NJ * D, TC], BF16)
    d_xng = dint("d_xng", [NJ * 4 * D, TC], BF16)
    d_fin = dint("d_fin", [1, 128], F32)
    d_fout = dint("d_fout", [1, 128], F32)
    d_xc = dint("d_xc", [512, L], F32)
    d_z = dint("d_z", [512, L], F32)
    d_xpj_h = [dint("d_xpj0", [33, L // 2], F32), dint("d_xpj1", [33, L // 2], F32)]
    d_xpjr_h = [dint("d_xpjr0", [33, L // 2], F32), dint("d_xpjr1", [33, L // 2], F32)]
    d_wop_j = [dint(f"d_wop{j}", [4 * D, TC], F32) for j in range(NJ)]
    d_att_j = [dint(f"d_att{j}", [D, TC], F32) for j in range(NJ)]

    dbg = {}
    if debug:
        for name, shape, dt in [
            ("dbg_xn", [D, NJ * TC], BF16), ("dbg_x1", [D, NJ * TC], F32),
            ("dbg_dt", [128, NMI * TC], F32),
            ("dbg_y", [128, NMI * TC], F32), ("dbg_att", [NJ * D, TC], F32),
            ("dbg_u", [128, NMI * TC], F32), ("dbg_a0", [128, NMI * TC], F32),
            ("dbg_b0", [128, NMI * TC], F32), ("dbg_h0", [128, NMI * TC], F32),
            ("dbg_y0", [128, NMI * TC], F32), ("dbg_wop", [4 * D, TC], F32),
            ("dbg_wop2", [4 * D, TC], F32),
        ]:
            dbg[name] = nc.dram_tensor(name, shape, dt, kind="ExternalOutput").ap()

    x_fm_r = x_fm.rearrange("(k p) t -> p k t", p=128)
    d_x1_r = d_x1.ap().rearrange("(k p) t -> p k t", p=128)
    d_ln2_r = d_ln2.ap().rearrange("(k p) t -> p k t", p=128)
    d_xc_r = d_xc.ap().rearrange("(m p) t -> p m t", p=128)
    d_z_r = d_z.ap().rearrange("(m p) t -> p m t", p=128)
    o_y_r = o_y.rearrange("(k p) t -> p k t", p=128)

    with ExitStack() as ctx:
        E = ctx.enter_context
        P = Bld(nc, ctx)
        LD, CC, VC, SC, TM, PC, CD, WQ, WO, XN = P.LD, P.CC, P.VC, P.SC, P.TM, P.PC, P.CD, P.WQ, P.WO, P.XN
        L2, L3, L4 = P.L2, P.L3, P.L4

        def sb(name, shape, dt=F32):
            return E(nc.sbuf_tensor(name, shape, dt))

        mu_t = sb("mu_t", [128, TC])
        ms_t = sb("ms_t", [128, TC])
        var_t = sb("var_t", [128, TC])
        rstd_t = sb("rstd_t", [128, TC])
        f1buf = sb("f1buf", [128, TC])
        cwt = sb("cwt", [128, 16])
        Aneg = sb("Aneg", [128, 64])
        vect = sb("vect", [128, 16])
        lngbt = sb("lngbt", [128, 16])
        onesb = sb("onesb", [128, 128], BF16)
        ones1t = sb("ones1t", [1, 128])
        eps_t = sb("eps_t", [128, 1])
        one_t = sb("one_t", [128, 1])
        selb = sb("selb", [16, 2048], BF16)
        st = sb("st", [128, NMI, NS])

        PS = [E(nc.psum_tensor(f"ps{i}", [128, TC], F32)) for i in range(8)]
        AR = Arena(nc, ctx, "arena", 43 * 1024)

        if True:
            AR.push()
            deqi = [AR.alloc([128, H], I32) for _ in range(2)]
            deqb = [AR.alloc([128, H], BF16) for _ in range(2)]
            wfl = AR.alloc([128, 1024], F32)
            p0sel = AR.alloc([16, 2048], F32)

            P.op("vector", lambda e: e.memset(eps_t[:], float(EPS)))
            vcn = P.op("vector", lambda e: e.memset(one_t[:], 1.0), VC)
            P.wait("scalar", VC, vcn)
            P.dma("sync", lambda e: e.dma_start(out=cwt[:], in_=cwc))
            P.dma("sync", lambda e: e.dma_start(out=vect[:], in_=vecs))
            P.dma("sync", lambda e: e.dma_start(out=lngbt[:], in_=lngb))
            P.dma("sync", lambda e: e.dma_start(out=ones1t[:], in_=ones1))
            t5 = P.dma("sync", lambda e: e.dma_start(out=mu_t[:, 0:64], in_=avals))
            P.wait("scalar", LD, t5)
            s1 = P.op("scalar", lambda e: e.activation(ms_t[:, 0:64], mu_t[:, 0:64], AF.Exp), SC)
            P.wait("vector", SC, s1)
            P.op("vector", lambda e: e.tensor_scalar(Aneg[:], ms_t[:, 0:64], -1.0, None, ALU.mult), VC)
            t6 = P.dma("sync", lambda e: e.dma_start(out=rstd_t[:, 0:128], in_=ones128))
            P.wait("scalar", LD, t6)
            P.op("scalar", lambda e: e.activation(onesb[:], rstd_t[:, 0:128], AF.Copy), SC)
            t7 = P.dma("sync", lambda e: e.dma_start(out=p0sel[:], in_=sel_i))
            P.wait("scalar", LD, t7)
            P.op("scalar", lambda e: e.activation(selb[:], p0sel[:], AF.Copy), SC)

            ld_q = [("sync", LD), ("scalar", L2)]
            st_q = [("gpsimd", L4), ("gpsimd", L4)]
            dq = {"g": 0, "vv": {}, "dm": {}}

            def dequant(src_ap, dst_t, nblocks, width):
                for i in range(nblocks):
                    g = dq["g"]
                    bb = g % 2
                    le, lc = ld_q[bb]
                    se, sc = st_q[bb]
                    if dq["vv"].get(g - 2) is not None:
                        P.wait(le, VC, dq["vv"][g - 2])
                    tl = P.dma(le, lambda e, i=i, bb=bb, s=src_ap, w=width: e.dma_start(
                        out=deqi[bb][:, 0:w], in_=s[128 * i:128 * (i + 1), 0:w]), ctr=lc)
                    P.wait("vector", lc, tl)
                    if dq["dm"].get(g - 2) is not None:
                        P.wait("vector", sc, dq["dm"][g - 2])
                    vv = P.op("vector", lambda e, bb=bb, w=width: e.tensor_scalar(
                        deqb[bb][:, 0:w], deqi[bb][:, 0:w], 1.0 / 127.0, None, ALU.mult), VC)
                    dq["vv"][g] = vv
                    P.wait(se, VC, vv)
                    dq["dm"][g] = P.dma(se, lambda e, i=i, bb=bb, d=dst_t, w=width: e.dma_start(
                        out=d.ap()[128 * i:128 * (i + 1), 0:w], in_=deqb[bb][:, 0:w]), ctr=sc)
                    dq["g"] += 1

            dequant(wgT, d_wg, KD, H)
            dequant(wuT, d_wu, KD, H)
            dequant(wdT, d_wd, HM, D)

            stg = {"sv": None, "dma": None, "first": True}

            def stage(src, dst, nblocks):
                for i in range(nblocks):
                    if stg["sv"] is not None:
                        P.wait("sync", SC, stg["sv"])
                    tl = P.dma("sync", lambda e, i=i, s=src: e.dma_start(
                        out=wfl[:], in_=s[128 * i:128 * (i + 1), :]))
                    P.wait("scalar", LD, tl)
                    if stg["first"]:
                        P.wait("scalar", L4, max(dq["dm"].values()))
                        stg["first"] = False
                    if stg["dma"] is not None:
                        P.wait("scalar", L4, stg["dma"])
                    sv = P.op("scalar", lambda e: e.activation(deqb[0][:, 0:1024], wfl[:], AF.Copy), SC)
                    stg["sv"] = sv
                    P.wait("gpsimd", SC, sv)
                    stg["dma"] = P.dma("gpsimd", lambda e, i=i, d=dst: e.dma_start(
                        out=d.ap()[128 * i:128 * (i + 1), :], in_=deqb[0][:, 0:1024]), ctr=L4)
            stage(winc, d_winb, KD)
            stage(woc, d_wob, NMI)
            P.barrier()
            AR.pop()

        def ln_stats(T, src, wait_vc=None):
            if wait_vc is not None:
                P.wait("scalar", VC, wait_vc)
            P.op("scalar", lambda e: e.activation(T["xbb"][:], src[:], AF.Copy))
            c2 = P.op("scalar", lambda e: e.activation(T["xnb"][:], src[:], AF.Square), SC)
            P.wait("tensor", SC, c2)
            for kb in range(KD):
                P.op("tensor", lambda e, kb=kb: e.matmul(
                    PS[4][:], onesb[:], T["xbb"][:, kb, :], start=(kb == 0), stop=(kb == KD - 1)),
                    TM if kb == KD - 1 else None)
            tm2 = None
            for kb in range(KD):
                tm2 = P.op("tensor", lambda e, kb=kb: e.matmul(
                    PS[5][:], onesb[:], T["xnb"][:, kb, :], start=(kb == 0), stop=(kb == KD - 1)),
                    TM if kb == KD - 1 else None)
            P.wait("vector", TM, tm2)
            P.op("vector", lambda e: e.tensor_scalar(mu_t[:], PS[4][:], 1.0 / D, None, ALU.mult))
            P.op("vector", lambda e: e.tensor_scalar(ms_t[:], PS[5][:], 1.0 / D, None, ALU.mult))
            P.op("vector", lambda e: e.tensor_tensor(var_t[:], mu_t[:], mu_t[:], ALU.mult))
            v1 = P.op("vector", lambda e: e.tensor_tensor(var_t[:], ms_t[:], var_t[:], ALU.subtract), VC)
            P.wait("scalar", VC, v1)
            P.op("scalar", lambda e: e.activation(rstd_t[:], var_t[:], AF.Ln, bias=eps_t[:, 0:1]))
            s2 = P.op("scalar", lambda e: e.activation(rstd_t[:], rstd_t[:], AF.Exp, scale=-0.5), SC)
            return s2, c2

        def ffn(T, par, prev, xnb=None, gu_hook=None, mid_tc=None):
            xs, fbuf = T["xs"][par], T["fbuf"]
            if xnb is None:
                xnb = T["xnb"]
            wgt_s, wut_s, wdt_s = T["wgt_s"], T["wut_s"], T["wdt_s"]
            if prev.get("tds7") is not None:
                P.wait("sync", TM, prev["tds7"])
            tus = [None] * HM
            fvs = [None] * HM
            f1s = [None] * HM
            for mh in range(HM):
                bw = mh % 2
                pg = PS[0 + 2 * bw]
                pu = PS[1 + 2 * bw]
                if mh >= 2:
                    P.wait("sync", TM, tus[mh - 2])
                P.dma("sync", lambda e, mh=mh, bw=bw: e.dma_start(
                    out=wgt_s[bw][:],
                    in_=d_wg.ap()[:, 128 * mh:128 * (mh + 1)].rearrange("(k p) m -> p k m", p=128)), ctr=WQ)
                lw2 = P.dma("sync", lambda e, mh=mh, bw=bw: e.dma_start(
                    out=wut_s[bw][:],
                    in_=d_wu.ap()[:, 128 * mh:128 * (mh + 1)].rearrange("(k p) m -> p k m", p=128)), ctr=WQ)
                P.wait("tensor", WQ, lw2)
                if mh >= 2:
                    P.wait("tensor", SC, f1s[mh - 2])
                    P.wait("tensor", VC, fvs[mh - 2])
                elif prev.get("f1s7") is not None:
                    P.wait("tensor", SC, prev["f1s7"])
                    P.wait("tensor", VC, prev["fvs7"])
                tg = None
                for kb in range(KD):
                    tg = P.op("tensor", lambda e, kb=kb, bw=bw, pg=pg: e.matmul(
                        pg[:], wgt_s[bw][:, kb, :], xnb[:, kb, :],
                        start=(kb == 0), stop=(kb == KD - 1)), TM if kb == KD - 1 else None)
                for kb in range(KD):
                    tus[mh] = P.op("tensor", lambda e, kb=kb, bw=bw, pu=pu: e.matmul(
                        pu[:], wut_s[bw][:, kb, :], xnb[:, kb, :],
                        start=(kb == 0), stop=(kb == KD - 1)), TM if kb == KD - 1 else None)
                P.wait("scalar", TM, tg)
                if mh >= 1:
                    P.wait("scalar", VC, fvs[mh - 1])
                elif prev.get("fvs7") is not None:
                    P.wait("scalar", VC, prev["fvs7"])
                f1s[mh] = P.op("scalar", lambda e, pg=pg: e.activation(f1buf[:], pg[:], AF.Silu), SC)
                P.wait("vector", SC, f1s[mh])
                P.wait("vector", TM, tus[mh])
                if mh == 0 and prev.get("tds7") is not None:
                    P.wait("vector", TM, prev["tds7"])
                fvs[mh] = P.op("vector", lambda e, mh=mh, pu=pu: e.tensor_tensor(
                    fbuf[:, mh, :], f1buf[:], pu[:], ALU.mult), VC)
                if gu_hook is not None:
                    gu_hook(mh)
            if mid_tc is not None:
                mid_tc()
            xv = None
            tds = [None] * KD
            for mo in range(KD):
                bw = mo % 2
                pd = PS[6 + bw]
                if mo >= 2:
                    P.wait("sync", TM, tds[mo - 2])
                lw = P.dma("sync", lambda e, mo=mo, bw=bw: e.dma_start(
                    out=wdt_s[bw][:],
                    in_=d_wd.ap()[:, 128 * mo:128 * (mo + 1)].rearrange("(k p) m -> p k m", p=128)), ctr=WQ)
                P.wait("tensor", WQ, lw)
                if mo == 0:
                    P.wait("tensor", VC, fvs[-1])
                if mo >= 2:
                    P.wait("tensor", VC, xv)
                elif prev.get("xv7") is not None:
                    P.wait("tensor", VC, prev["xv7"])
                for kh in range(HM):
                    tds[mo] = P.op("tensor", lambda e, kh=kh, bw=bw, pd=pd: e.matmul(
                        pd[:], wdt_s[bw][:, kh, :], fbuf[:, kh, :],
                        start=(kh == 0), stop=(kh == HM - 1)), TM if kh == HM - 1 else None)
                P.wait("vector", TM, tds[mo])
                xv = P.op("vector", lambda e, mo=mo, pd=pd, xs=xs: e.scalar_tensor_tensor(
                    xs[:, mo, :], pd[:], 0.5, xs[:, mo, :], ALU.mult, ALU.add), VC)
            return dict(xv=xv, tds7=tds[-1], f1s7=f1s[-1], fvs7=fvs[-1], xv7=xv)

        def ln_apply_plain(T, src, dst_bf):
            P.op("vector", lambda e, src=src: e.tensor_tensor(
                T["cen"][:], src[:], bc_mid(mu_t[:], KD), ALU.subtract))
            return P.op("vector", lambda e, dst_bf=dst_bf: e.tensor_tensor(
                dst_bf[:], T["cen"][:], bc_mid(rstd_t[:], KD), ALU.mult), VC)

        def alloc_ffn_scope():
            T = {}
            T["xs"] = [AR.alloc([128, KD, TC], F32) for _ in range(2)]
            T["ln2t"] = AR.alloc([128, KD, TC], F32)
            T["cen"] = AR.alloc([128, KD, TC], F32)
            T["xbb"] = AR.alloc([128, KD, TC], BF16)
            T["xnb"] = AR.alloc([128, KD, TC], BF16)
            T["fbuf"] = AR.alloc([128, HM, TC], BF16)
            T["wgt_s"] = [AR.alloc([128, KD, 128], BF16) for _ in range(2)]
            T["wut_s"] = [AR.alloc([128, KD, 128], BF16) for _ in range(2)]
            T["wdt_s"] = [AR.alloc([128, HM, 128], BF16) for _ in range(2)]
            return T

        ag_done = [None] * NJ
        if True:
            AR.push()
            T = alloc_ffn_scope()
            prev = {}
            x1sp = [None, None]
            xs_vc = [None, None]
            xs_sc = [None, None]
            for j in range(NJ):
                par = j % 2
                jt = slice(TC * j, TC * (j + 1))
                xs = T["xs"][par]
                if j >= 2:
                    P.wait("sync", VC, xs_vc[par])
                    P.wait("sync", SC, xs_sc[par])
                    P.wait("sync", LD, x1sp[par])
                lx = P.dma("sync", lambda e, jt=jt, xs=xs: e.dma_start(out=xs[:], in_=x_fm_r[:, :, jt]))
                P.wait("scalar", LD, lx)
                P.wait("vector", LD, lx)
                s_r, _ = ln_stats(T, xs)
                P.wait("vector", SC, s_r)
                vx = ln_apply_plain(T, xs, T["xnb"])
                P.wait("tensor", VC, vx)
                prev = ffn(T, par, prev)
                xv = prev["xv"]
                P.wait("sync", VC, xv)
                x1sp[par] = P.dma("sync", lambda e, jt=jt, xs=xs: e.dma_start(
                    out=d_x1_r[:, :, jt], in_=xs[:]))
                s_r, c2m = ln_stats(T, xs, wait_vc=xv)
                xs_sc[par] = c2m
                P.wait("vector", SC, s_r)
                xs_vc[par] = P.op("vector", lambda e, xs=xs: e.tensor_tensor(
                    T["ln2t"][:], xs[:], bc_mid(mu_t[:], KD), ALU.subtract), VC)
                v2 = P.op("vector", lambda e: e.tensor_tensor(
                    T["ln2t"][:], T["ln2t"][:], bc_mid(rstd_t[:], KD), ALU.mult), VC)
                P.wait("sync", VC, v2)
                P.dma("sync", lambda e, jt=jt: e.dma_start(out=d_ln2_r[:, :, jt], in_=T["ln2t"][:]))
                s_r, _ = ln_stats(T, T["ln2t"], wait_vc=v2)
                P.wait("vector", SC, s_r)
                P.op("vector", lambda e: e.tensor_tensor(
                    T["cen"][:], T["ln2t"][:], bc_mid(mu_t[:], KD), ALU.subtract))
                P.op("vector", lambda e: e.tensor_tensor(
                    T["cen"][:], T["cen"][:], bc_mid(rstd_t[:], KD), ALU.mult))
                v3 = None
                for kb in range(KD):
                    v3 = P.op("vector", lambda e, kb=kb: e.tensor_scalar(
                        T["xnb"][:, kb, :], T["cen"][:, kb, :],
                        lngbt[:, 2 * kb:2 * kb + 1], lngbt[:, 2 * kb + 1:2 * kb + 2],
                        ALU.mult, ALU.add), VC)
                P.wait("sync", VC, v3)
                P.dma("sync", lambda e, j=j: e.dma_start(
                    out=d_xn.ap()[D * j:D * (j + 1), :].rearrange("(k p) t -> p k t", p=128),
                    in_=T["xnb"][:]), ctr=XN)
                if debug and "p1" in dbg_parts:
                    P.dma("sync", lambda e, jt=jt: e.dma_start(
                        out=dbg["dbg_xn"].rearrange("(k p) t -> p k t", p=128)[:, :, jt], in_=T["xnb"][:]))
                    P.dma("sync", lambda e, jt=jt, xs=xs: e.dma_start(
                        out=dbg["dbg_x1"].rearrange("(k p) t -> p k t", p=128)[:, :, jt], in_=xs[:]))
            P.barrier()
            AR.pop()
        def fire_ag(js):
            for j in js:
                CC.n += 1
                P.op("gpsimd", lambda e, j=j: e.collective_compute(
                    "AllGather", ALU.bypass, GROUPS,
                    [d_xn.ap()[D * j:D * (j + 1), :].opt()],
                    [d_xng.ap()[4 * D * j:4 * D * (j + 1), :].opt()]).then_inc(CC.h, 1))
            P.wait("gpsimd", CC, CC.n)
            P.op("gpsimd", lambda e: e.drain(fusable=False))
            CD.n += 1
            P.op("gpsimd", lambda e: e.sem_inc(CD.h, 1))
            return CD.n

        if True:
            AR.push()
            Winb = AR.alloc([128, KD, 1024], BF16)
            xng = [AR.alloc([128, KD, TC], BF16) for _ in range(2)]
            xhh = AR.alloc([128, NMI, TC + 8], F32)
            zst = AR.alloc([128, NMI, TC], F32)
            xct2 = AR.alloc([128, NMI, TC], F32)
            xcb2 = AR.alloc([128, NMI, TC], BF16)
            cvac = AR.alloc([128, NMI, TC], F32)
            Wxpb = AR.alloc([128, NMI, 33], BF16)
            xpjt = AR.alloc([33, TC], F32)
            lw = P.dma("sync", lambda e: e.dma_start(
                out=Winb[:], in_=d_winb.ap().rearrange("(k p) m -> p k m", p=128)))
            for mi in range(NMI):
                tw = P.dma("sync", lambda e, mi=mi: e.dma_start(
                    out=var_t[:, 0:33], in_=wxpc[128 * mi:128 * (mi + 1), :]))
                P.wait("scalar", LD, tw)
                sw = P.op("scalar", lambda e, mi=mi: e.activation(Wxpb[:, mi, :], var_t[:, 0:33], AF.Copy), SC)
                P.wait("sync", SC, sw)
            P.wait("tensor", LD, lw)
            P.op("gpsimd", lambda e: e.memset(xhh[:, :, 0:3], 0.0), PC)
            P.wait("scalar", PC, PC.n)
            ag_first = fire_ag([0])
            ag_rest = fire_ag([1, 2, 3])
            conv_done = None
            sil_done = None
            cast_done = None
            tpj_prev = None
            spj_prev = None
            zs_spill = None
            xc_spill = None
            pj_spill = None
            bank_ev = {}
            mm_last = [None, None]
            for k in range(NCH):
                bk = k % 2
                r0 = 4 * D * (k // 4) + D * (k % 4)
                c0 = TC * k
                if mm_last[bk] is not None:
                    P.wait("sync", TM, mm_last[bk])
                P.wait("sync", CD, ag_first if k < 4 else ag_rest)
                lx = P.dma("sync", lambda e, r0=r0, bk=bk: e.dma_start(
                    out=xng[bk][:],
                    in_=d_xng.ap()[r0:r0 + D, :].rearrange("(k p) t -> p k t", p=128)))
                if k > 0:
                    P.wait("scalar", VC, conv_done)
                    P.op("scalar", lambda e: e.activation(xhh[:, :, 0:3], xhh[:, :, TC:TC + 3], AF.Copy))
                P.wait("tensor", LD, lx)
                ev = []
                for mi in range(KD):
                    psn = mi % 4
                    if bank_ev.get(psn) is not None:
                        P.wait("tensor", SC, bank_ev[psn])
                    tmm = None
                    for kb in range(KD):
                        tmm = P.op("tensor", lambda e, kb=kb, mi=mi, bk=bk, psn=psn: e.matmul(
                            PS[psn][:], Winb[:, kb, 128 * mi:128 * (mi + 1)], xng[bk][:, kb, :],
                            start=(kb == 0), stop=(kb == KD - 1)), TM if kb == KD - 1 else None)
                    if mi == KD - 1:
                        mm_last[bk] = tmm
                    P.wait("scalar", TM, tmm)
                    if mi < NMI:
                        evv = P.op("scalar", lambda e, mi=mi, psn=psn: e.activation(
                            xhh[:, mi, 3:3 + TC], PS[psn][:], AF.Copy), SC)
                    else:
                        if zs_spill is not None and mi == NMI:
                            P.wait("scalar", LD, zs_spill)
                        evv = P.op("scalar", lambda e, mi=mi, psn=psn: e.activation(
                            zst[:, mi - NMI, :], PS[psn][:], AF.Silu), SC)
                    ev.append(evv)
                    bank_ev[psn] = evv
                P.wait("sync", SC, ev[-1])
                zs_spill = P.dma("sync", lambda e, c0=c0: e.dma_start(
                    out=d_z_r[:, :, c0:c0 + TC], in_=zst[:]))
                P.wait("vector", SC, ev[NMI - 1])
                if sil_done is not None:
                    P.wait("vector", SC, sil_done)
                cv = None
                for mi in range(NMI):
                    P.op("vector", lambda e, mi=mi: e.tensor_scalar(
                        cvac[:, mi, :], xhh[:, mi, 0:TC],
                        cwt[:, 4 * mi + 0:4 * mi + 1], vect[:, 4 * mi + 0:4 * mi + 1],
                        ALU.mult, ALU.add))
                    for kk in range(1, 4):
                        cv = P.op("vector", lambda e, mi=mi, kk=kk: e.scalar_tensor_tensor(
                            cvac[:, mi, :], xhh[:, mi, kk:kk + TC],
                            cwt[:, 4 * mi + kk:4 * mi + kk + 1], cvac[:, mi, :],
                            ALU.mult, ALU.add), VC if (mi == NMI - 1 and kk == 3) else None)
                conv_done = cv
                P.wait("scalar", VC, cv)
                if xc_spill is not None:
                    P.wait("scalar", LD, xc_spill)
                sil_done = P.op("scalar", lambda e: e.activation(
                    xct2[:].rearrange("p m t -> p (m t)"), cvac[:].rearrange("p m t -> p (m t)"),
                    AF.Silu), SC)
                if tpj_prev is not None:
                    P.wait("scalar", TM, tpj_prev)
                cast_done = P.op("scalar", lambda e: e.activation(
                    xcb2[:].rearrange("p m t -> p (m t)"), xct2[:].rearrange("p m t -> p (m t)"),
                    AF.Copy), SC)
                P.wait("sync", SC, sil_done)
                xc_spill = P.dma("sync", lambda e, c0=c0: e.dma_start(
                    out=d_xc_r[:, :, c0:c0 + TC], in_=xct2[:]))
                P.wait("tensor", SC, cast_done)
                if spj_prev is not None:
                    P.wait("tensor", SC, spj_prev)
                tpj = None
                for mi in range(NMI):
                    tpj = P.op("tensor", lambda e, mi=mi: e.matmul(
                        PS[4][0:33, :], Wxpb[:, mi, :], xcb2[:, mi, :],
                        start=(mi == 0), stop=(mi == NMI - 1)), TM if mi == NMI - 1 else None)
                tpj_prev = tpj
                P.wait("scalar", TM, tpj)
                if pj_spill is not None:
                    P.wait("scalar", LD, pj_spill)
                spj = P.op("scalar", lambda e: e.activation(xpjt[:], PS[4][0:33, :], AF.Copy), SC)
                spj_prev = spj
                P.wait("sync", SC, spj)
                pj_spill = P.dma("sync", lambda e, k=k, c0=c0: e.dma_start(
                    out=d_xpj_h[k // 8].ap()[:, c0 - (k // 8) * (L // 2):
                                             c0 - (k // 8) * (L // 2) + TC], in_=xpjt[:]))
                if k == 7:
                    P.coll("AllReduce", ALU.add, [d_xpj_h[0].ap().opt()],
                           [d_xpjr_h[0].ap().opt()], wait_ld=pj_spill)
            P.barrier()
            AR.pop()

        art = P.coll("AllReduce", ALU.add, [d_xpj_h[1].ap().opt()], [d_xpjr_h[1].ap().opt()],
                     wait_ld=P.LD.n)
        P.barrier()

        if True:
            AR.push()
            Wob = AR.alloc([128, NMI, 1024], BF16)
            xct3 = [AR.alloc([128, NMI, TC], F32) for _ in range(2)]
            zsl = [AR.alloc([128, NMI, TC], F32) for _ in range(2)]
            dtt = [AR.alloc([128, NMI, TC], F32) for _ in range(2)]
            ut = [AR.alloc([128, NMI, TC], BF16) for _ in range(2)]
            Bxt = [AR.alloc([16, TC], F32) for _ in range(2)]
            Cxt = [AR.alloc([16, TC], F32) for _ in range(2)]
            dtr = [AR.alloc([1, TC], F32) for _ in range(2)]
            Bxb = [AR.alloc([16, TC], BF16) for _ in range(2)]
            Cxb = [AR.alloc([16, TC], BF16) for _ in range(2)]
            Cb = [AR.alloc([128, TC], BF16) for _ in range(2)]
            Bbb = [AR.alloc([128, TC], BF16) for _ in range(2)]
            ab = [AR.alloc([128, NMI * TC], F32) for _ in range(2)]
            bb3 = AR.alloc([128, NMI * TC], BF16)
            hta = [AR.alloc([128, NMI, TC], F32) for _ in range(2)]
            tmpb = [AR.alloc([128, NMI * TC], BF16) for _ in range(2)]
            yt = [AR.alloc([128, NMI, TC], BF16) for _ in range(2)]
            gt = [AR.alloc([128, NMI, TC], BF16) for _ in range(2)]
            eb2b = AR.alloc([128, NMI * TC], F32)
            rb2b = AR.alloc([128, NMI * TC], F32)
            wot = [AR.alloc([128, TC], F32) for _ in range(2)]

            lw = P.dma("sync", lambda e: e.dma_start(
                out=Wob[:], in_=d_wob.ap().rearrange("(k p) m -> p k m", p=128)))
            P.op("vector", lambda e: e.memset(st[:], 0.0), VC)
            P.wait("tensor", LD, lw)

            roll = {"a3": {}, "dve": {}, "psB_mm": {}, "psC_ev": {}, "tmp": {}, "tmp_rd": {}}
            S = {"l1": {}, "l2": {}, "vsp": {}, "u": {}, "bxb": {}, "tdt": {},
                 "b3l": {}, "vg": {}, "wol": {}}
            wo_bank_ev = {0: None, 1: None}

            def head(k):
                pk = k % 2
                kt = slice(TC * k, TC * (k + 1))
                if S["vg"].get(k - 2) is not None:
                    P.wait("sync", VC, S["vg"][k - 2])
                    P.wait("sync", PC, S["u"][k - 2])
                S["l1"][k] = P.dma("sync", lambda e, kt=kt, pk=pk: e.dma_start(
                    out=xct3[pk][:], in_=d_xc_r[:, :, kt]))
                S["l2"][k] = P.dma("sync", lambda e, kt=kt, pk=pk: e.dma_start(
                    out=zsl[pk][:], in_=d_z_r[:, :, kt]))
                if S["bxb"].get(k - 2) is not None:
                    P.wait("sync", SC, S["bxb"][k - 2])
                if S["tdt"].get(k - 2) is not None:
                    P.wait("sync", TM, S["tdt"][k - 2])
                hh = k // 8
                kth = slice(TC * k - hh * (L // 2), TC * (k + 1) - hh * (L // 2))
                lB = P.dma("sync", lambda e, hh=hh, kth=kth, pk=pk: e.dma_start(
                    out=Bxt[pk][:], in_=d_xpjr_h[hh].ap()[0:16, kth]))
                lC = P.dma("sync", lambda e, hh=hh, kth=kth, pk=pk: e.dma_start(
                    out=Cxt[pk][:], in_=d_xpjr_h[hh].ap()[16:32, kth]))
                lD = P.dma("sync", lambda e, hh=hh, kth=kth, pk=pk: e.dma_start(
                    out=dtr[pk][:], in_=d_xpjr_h[hh].ap()[32:33, kth]))
                P.wait("tensor", LD, lD)
                if S["vsp"].get(k - 1) is not None:
                    P.wait("tensor", VC, S["vsp"][k - 1])
                tdt = P.op("tensor", lambda e, pk=pk: e.matmul(
                    PS[6 + pk][:], ones1t[:], dtr[pk][:], start=True, stop=True), TM)
                S["tdt"][k] = tdt
                P.wait("vector", TM, tdt)
                if k >= 2:
                    P.wait("vector", SC, roll["a3"][16 * (k - 2) + 15])
                    P.wait("vector", PC, S["u"][k - 2])
                for mi in range(NMI):
                    P.op("vector", lambda e, mi=mi, pk=pk: e.tensor_scalar(
                        dtt[pk][:, mi, :], PS[6 + pk][:],
                        vect[:, 4 * mi + 2:4 * mi + 3], vect[:, 4 * mi + 1:4 * mi + 2],
                        ALU.mult, ALU.add))
                vrl = P.op("vector", lambda e, pk=pk: e.tensor_scalar(
                    rb2b[:], dtt[pk][:].rearrange("p m t -> p (m t)"), 0.0, None, ALU.max), VC)
                P.wait("scalar", VC, vrl)
                P.op("scalar", lambda e, pk=pk: e.activation(
                    eb2b[:], dtt[pk][:].rearrange("p m t -> p (m t)"), AF.Abs))
                P.op("scalar", lambda e: e.activation(eb2b[:], eb2b[:], AF.Exp, scale=-1.0))
                ssp = P.op("scalar", lambda e: e.activation(eb2b[:], eb2b[:], AF.Ln, bias=one_t[:, 0:1]), SC)
                P.wait("vector", SC, ssp)
                vsp = P.op("vector", lambda e, pk=pk: e.tensor_tensor(
                    dtt[pk][:].rearrange("p m t -> p (m t)"), rb2b[:], eb2b[:], ALU.add), VC)
                S["vsp"][k] = vsp
                P.wait("gpsimd", VC, vsp)
                P.wait("gpsimd", LD, S["l1"][k])
                if S["b3l"].get(k - 2) is not None:
                    P.wait("gpsimd", VC, S["b3l"][k - 2])
                S["u"][k] = P.op("gpsimd", lambda e, pk=pk: e.tensor_tensor(
                    ut[pk][:].rearrange("p m t -> p (m t)"), dtt[pk][:].rearrange("p m t -> p (m t)"),
                    xct3[pk][:].rearrange("p m t -> p (m t)"), ALU.mult), PC)
                P.wait("scalar", LD, lC)
                if k >= 2:
                    P.wait("scalar", TM, roll["psB_mm"][16 * (k - 2) + 15])
                P.op("scalar", lambda e, pk=pk: e.activation(Bxb[pk][:], Bxt[pk][:], AF.Copy))
                S["bxb"][k] = P.op("scalar", lambda e, pk=pk: e.activation(Cxb[pk][:], Cxt[pk][:], AF.Copy), SC)

            head(0)
            for k in range(NCH):
                pk = k % 2
                if debug and "dty" in dbg_parts and k == 0:
                    P.wait("sync", VC, S["vsp"][0])
                    P.dma("sync", lambda e: e.dma_start(out=dbg["dbg_dt"], in_=dtt[0][:].rearrange("p m t -> p (m t)")))
                def ensure_bcast(g2, kk):
                    if g2 in roll["psB_mm"] or g2 >= 16 * NCH:
                        return
                    n2 = g2 % 16
                    pp2 = g2 % 2
                    psB2 = PS[0 + 2 * pp2]
                    psC2 = PS[1 + 2 * pp2]
                    P.wait("tensor", SC, S["bxb"][kk])
                    if roll["psC_ev"].get(g2 - 2) is not None:
                        P.wait("tensor", SC, roll["psC_ev"][g2 - 2])
                    P.op("tensor", lambda e, n2=n2, psB2=psB2, kk=kk: e.matmul(
                        psB2[:], selb[:, 128 * n2:128 * (n2 + 1)], Bxb[kk % 2][:], start=True, stop=True))
                    tbc = P.op("tensor", lambda e, n2=n2, psC2=psC2, kk=kk: e.matmul(
                        psC2[:], selb[:, 128 * n2:128 * (n2 + 1)], Cxb[kk % 2][:], start=True, stop=True), TM)
                    roll["psB_mm"][g2] = tbc
                    P.wait("scalar", TM, tbc)
                    if roll["tmp"].get(g2 - 2) is not None:
                        P.wait("scalar", VC, roll["tmp"][g2 - 2])
                    if roll["dve"].get(g2 - 2) is not None:
                        P.wait("scalar", VC, roll["dve"][g2 - 2])
                    P.op("scalar", lambda e, pp2=pp2, psB2=psB2: e.activation(Bbb[pp2][:], psB2[:], AF.Copy))
                    cev = P.op("scalar", lambda e, pp2=pp2, psC2=psC2: e.activation(Cb[pp2][:], psC2[:], AF.Copy), SC)
                    roll["psC_ev"][g2] = cev

                def emit_yadd(g2):
                    pp2 = g2 % 2
                    pk2 = (g2 // 16) % 2
                    if g2 % 16 == 0:
                        v = P.op("vector", lambda e, pp2=pp2, pk2=pk2: e.tensor_scalar(
                            yt[pk2][:].rearrange("p m t -> p (m t)"),
                            tmpb[pp2][:], 0.0, None, ALU.add), VC)
                    else:
                        v = P.op("vector", lambda e, pp2=pp2, pk2=pk2: e.tensor_tensor(
                            yt[pk2][:].rearrange("p m t -> p (m t)"),
                            yt[pk2][:].rearrange("p m t -> p (m t)"),
                            tmpb[pp2][:], ALU.add), VC)
                    roll["tmp_rd"][g2] = v

                def emit_a(g2, kk):
                    if g2 in roll["a3"] or g2 >= 16 * NCH:
                        return
                    n2 = g2 % 16
                    pp2 = g2 % 2
                    pk2 = kk % 2
                    if n2 == 0:
                        P.wait("scalar", VC, S["vsp"][kk])
                    if roll["dve"].get(g2 - 2) is not None:
                        P.wait("scalar", VC, roll["dve"][g2 - 2])
                    sa = None
                    for mi in range(NMI):
                        sa = P.op("scalar", lambda e, mi=mi, n2=n2, pp2=pp2, pk2=pk2: e.activation(
                            ab[pp2][:, TC * mi:TC * (mi + 1)], dtt[pk2][:, mi, :], AF.Exp,
                            scale=Aneg[:, 16 * mi + n2:16 * mi + n2 + 1]),
                            SC if mi == NMI - 1 else None)
                    roll["a3"][g2] = sa

                for n in range(NS):
                    g = 16 * k + n
                    pp = g % 2
                    ensure_bcast(g, k)
                    cev = roll["psC_ev"][g]
                    emit_a(g, k)
                    if n == 0:
                        P.wait("vector", PC, S["u"][k])
                    P.wait("vector", SC, cev)
                    b3m = P.op("vector", lambda e, pp=pp, pk=pk: e.tensor_tensor(
                        bb3[:].rearrange("p (m t) -> p m t", m=NMI),
                        ut[pk][:], bc_mid(Bbb[pp][:], NMI), ALU.mult), VC)
                    if n == NS - 1:
                        S["b3l"][k] = b3m
                    P.wait("vector", SC, roll["a3"][g])
                    for mi in range(NMI):
                        P.op("vector", lambda e, mi=mi, n=n, pp=pp: e.tensor_tensor_scan(
                            hta[pp][:, mi, :],
                            ab[pp][:, TC * mi:TC * (mi + 1)],
                            bb3[:, TC * mi:TC * (mi + 1)],
                            st[:, mi, n:n + 1], ALU.mult, ALU.add))
                    vsc = P.op("vector", lambda e, n=n, pp=pp: e.tensor_copy(
                        st[:, :, n:n + 1], hta[pp][:, :, TC - 1:TC]), VC)
                    roll["dve"][g] = vsc
                    tm_ = P.op("vector", lambda e, pp=pp: e.tensor_tensor(
                        tmpb[pp][:].rearrange("p (m t) -> p m t", m=NMI),
                        hta[pp][:], bc_mid(Cb[pp][:], NMI), ALU.mult), VC)
                    roll["tmp"][g] = tm_
                    if g > 0 and (g - 1) not in roll["tmp_rd"]:
                        emit_yadd(g - 1)
                    ensure_bcast(g + 1, k if n + 1 < NS else k + 1)
                    if n == 8 and k + 1 < NCH:
                        head(k + 1)
                    if n >= 14 and k + 1 < NCH:
                        ensure_bcast(g + 2, k + 1)
                        emit_a(g + 2, k + 1)
                emit_yadd(16 * k + 15)
                P.wait("vector", LD, S["l1"][k])
                for mi in range(NMI):
                    P.op("vector", lambda e, mi=mi, pk=pk: e.scalar_tensor_tensor(
                        yt[pk][:, mi, :], xct3[pk][:, mi, :],
                        vect[:, 4 * mi + 3:4 * mi + 4], yt[pk][:, mi, :], ALU.mult, ALU.add))
                P.wait("vector", LD, S["l2"][k])
                if S["wol"].get(k - 2) is not None:
                    P.wait("vector", TM, S["wol"][k - 2])
                vg = P.op("vector", lambda e, pk=pk: e.tensor_tensor(
                    gt[pk][:].rearrange("p m t -> p (m t)"), yt[pk][:].rearrange("p m t -> p (m t)"),
                    zsl[pk][:].rearrange("p m t -> p (m t)"), ALU.mult), VC)
                S["vg"][k] = vg
                if debug and "dty" in dbg_parts and k == 0:
                    P.wait("gpsimd", VC, vg)
                    P.dma("gpsimd", lambda e: e.dma_start(out=dbg["dbg_y"], in_=yt[0][:].rearrange("p m t -> p (m t)")))
                P.wait("tensor", VC, vg)
                wo_dma = {}
                for mo in range(KD):
                    pw = mo % 2
                    psW = PS[4 + pw]
                    if wo_bank_ev[pw] is not None:
                        P.wait("tensor", SC, wo_bank_ev[pw])
                    two = None
                    for kd in range(NMI):
                        two = P.op("tensor", lambda e, kd=kd, mo=mo, pk=pk, psW=psW: e.matmul(
                            psW[:], Wob[:, kd, 128 * mo:128 * (mo + 1)], gt[pk][:, kd, :],
                            start=(kd == 0), stop=(kd == NMI - 1)), TM if kd == NMI - 1 else None)
                    if mo == KD - 1:
                        S["wol"][k] = two
                    P.wait("scalar", TM, two)
                    if wo_dma.get(mo - 2) is not None:
                        P.wait("scalar", WO, wo_dma[mo - 2])
                    swo = P.op("scalar", lambda e, pw=pw, psW=psW: e.activation(wot[pw][:], psW[:], AF.Copy), SC)
                    wo_bank_ev[pw] = swo
                    P.wait("sync", SC, swo)
                    wo_dma[mo] = P.dma("sync", lambda e, k=k, mo=mo, pw=pw: e.dma_start(
                        out=d_wop_j[k // 4].ap()[1024 * (k % 4) + 128 * mo:1024 * (k % 4) + 128 * (mo + 1), :],
                        in_=wot[pw][:]), ctr=WO)
                if k % 4 == 3 and k < 12:
                    P.wait("gpsimd", WO, 16 * KD * (k + 1))
                    CC.n += 1
                    P.op("gpsimd", lambda e, j=k // 4: e.collective_compute(
                        "ReduceScatter", ALU.add, GROUPS,
                        [d_wop_j[j].ap().opt()], [d_att_j[j].ap().opt()]).then_inc(CC.h, 1))
                if debug and "wop" in dbg_parts and k == 3:
                    P.wait("sync", LD, P.LD.n)
                    P.dma("sync", lambda e: e.dma_start(out=dbg["dbg_wop"], in_=d_wop_j[0].ap()))
            P.wait("gpsimd", CC, CC.n)
            P.op("gpsimd", lambda e: e.drain(fusable=False))
            CD.n += 1
            P.op("gpsimd", lambda e: e.sem_inc(CD.h, 1))
            cd_rs012 = CD.n
            P.wait("gpsimd", WO, 16 * KD * NCH)
            CC.n += 1
            P.op("gpsimd", lambda e: e.collective_compute(
                "ReduceScatter", ALU.add, GROUPS,
                [d_wop_j[3].ap().opt()], [d_att_j[3].ap().opt()]).then_inc(CC.h, 1))
            rs3_cc = CC.n
            P.barrier()
            AR.pop()
        P.wait("gpsimd", CC, rs3_cc)
        P.op("gpsimd", lambda e: e.drain(fusable=False))
        CD.n += 1
        P.op("gpsimd", lambda e: e.sem_inc(CD.h, 1))
        cd_rs3 = CD.n

        if True:
            AR.push()
            T = alloc_ffn_scope()
            xnb2 = [AR.alloc([128, KD, TC], BF16) for _ in range(2)]
            prev = {}
            oysp = [None, None]
            xs_vc = [None, None]
            xs_sc = [None, None]
            ST = {"v1": None, "app": None, "scr": None, "vx2": [None, None]}
            PR = {}

            def p3_pre(j):
                par = j % 2
                jt = slice(TC * j, TC * (j + 1))
                xs = T["xs"][par]
                if xs_vc[par] is not None:
                    P.wait("sync", VC, xs_vc[par])
                if xs_sc[par] is not None:
                    P.wait("sync", SC, xs_sc[par])
                if oysp[par] is not None:
                    P.wait("sync", LD, oysp[par])
                P.dma("sync", lambda e, jt=jt, xs=xs: e.dma_start(out=xs[:], in_=d_x1_r[:, :, jt]))
                if ST["scr"] is not None:
                    P.wait("sync", VC, ST["scr"])
                P.dma("sync", lambda e, jt=jt: e.dma_start(out=T["ln2t"][:], in_=d_ln2_r[:, :, jt]))
                if ST["vx2"][1 - par] is not None:
                    P.wait("sync", VC, ST["vx2"][1 - par])
                P.wait("sync", CD, cd_rs012 if j < 3 else cd_rs3)
                l3 = P.dma("sync", lambda e, j=j: e.dma_start(
                    out=T["cen"][:],
                    in_=d_att_j[j].ap().rearrange("(k p) t -> p k t", p=128)))
                P.wait("vector", LD, l3)
                P.op("vector", lambda e, xs=xs: e.tensor_tensor(
                    xs[:].rearrange("p k t -> p (k t)"), xs[:].rearrange("p k t -> p (k t)"),
                    T["ln2t"][:].rearrange("p k t -> p (k t)"), ALU.add))
                vx2 = P.op("vector", lambda e, xs=xs: e.tensor_tensor(
                    xs[:].rearrange("p k t -> p (k t)"), xs[:].rearrange("p k t -> p (k t)"),
                    T["cen"][:].rearrange("p k t -> p (k t)"), ALU.add), VC)
                ST["vx2"][par] = vx2
                if debug and "p3" in dbg_parts:
                    P.wait("sync", LD, l3)
                    P.dma("sync", lambda e, j=j: e.dma_start(
                        out=dbg["dbg_att"][D * j:D * (j + 1), :].rearrange("(k p) t -> p k t", p=128),
                        in_=T["cen"][:]))
                PR[j] = vx2

            def p3_casts(j):
                par = j % 2
                xs = T["xs"][par]
                P.wait("scalar", VC, PR[j])
                P.op("scalar", lambda e, xs=xs: e.activation(T["xbb"][:], xs[:], AF.Copy))
                c2 = P.op("scalar", lambda e, xs=xs, par=par: e.activation(
                    xnb2[par][:], xs[:], AF.Square), SC)
                xs_sc[par] = c2
                PR[("c", j)] = c2

            def p3_lnmm(j):
                par = j % 2
                xs = T["xs"][par]
                P.wait("tensor", SC, PR[("c", j)])
                if ST["v1"] is not None:
                    P.wait("tensor", VC, ST["v1"])
                for kb in range(KD):
                    P.op("tensor", lambda e, kb=kb: e.matmul(
                        PS[4][:], onesb[:], T["xbb"][:, kb, :], start=(kb == 0), stop=(kb == KD - 1)),
                        TM if kb == KD - 1 else None)
                tm2 = None
                for kb in range(KD):
                    tm2 = P.op("tensor", lambda e, kb=kb, par=par: e.matmul(
                        PS[5][:], onesb[:], xnb2[par][:, kb, :], start=(kb == 0), stop=(kb == KD - 1)),
                        TM if kb == KD - 1 else None)
                P.wait("vector", TM, tm2)
                P.op("vector", lambda e: e.tensor_scalar(mu_t[:], PS[4][:], 1.0 / D, None, ALU.mult))
                P.op("vector", lambda e: e.tensor_scalar(ms_t[:], PS[5][:], 1.0 / D, None, ALU.mult))
                P.op("vector", lambda e: e.tensor_tensor(var_t[:], mu_t[:], mu_t[:], ALU.mult))
                v1 = P.op("vector", lambda e: e.tensor_tensor(var_t[:], ms_t[:], var_t[:], ALU.subtract), VC)
                ST["v1"] = v1
                P.wait("scalar", VC, v1)
                if ST["app"] is not None:
                    P.wait("scalar", VC, ST["app"])
                P.op("scalar", lambda e: e.activation(rstd_t[:], var_t[:], AF.Ln, bias=eps_t[:, 0:1]))
                s2 = P.op("scalar", lambda e: e.activation(rstd_t[:], rstd_t[:], AF.Exp, scale=-0.5), SC)
                P.wait("vector", SC, s2)
                sub = P.op("vector", lambda e, xs=xs: e.tensor_tensor(
                    T["ln2t"][:], xs[:], bc_mid(mu_t[:], KD), ALU.subtract), VC)
                xs_vc[par] = sub
                vx = P.op("vector", lambda e, par=par: e.tensor_tensor(
                    xnb2[par][:], T["ln2t"][:], bc_mid(rstd_t[:], KD), ALU.mult), VC)
                ST["scr"] = vx
                ST["app"] = vx
                PR[("a", j)] = vx

            p3_pre(0)
            p3_casts(0)
            p3_lnmm(0)
            for j in range(NJ):
                par = j % 2
                jt = slice(TC * j, TC * (j + 1))
                xs = T["xs"][par]
                if j + 1 < NJ:
                    p3_pre(j + 1)
                    gu = (lambda mh, j1=j + 1: p3_casts(j1) if mh == 4 else None)
                    mid = (lambda j1=j + 1: p3_lnmm(j1))
                else:
                    gu = None
                    mid = None
                P.wait("tensor", VC, PR[("a", j)])
                prev = ffn(T, par, prev, xnb=xnb2[par], gu_hook=gu, mid_tc=mid)
                xv = prev["xv"]
                P.wait("sync", VC, xv)
                oysp[par] = P.dma("sync", lambda e, jt=jt, xs=xs: e.dma_start(
                    out=o_y_r[:, :, jt], in_=xs[:]))

        P.wait("sync", LD, P.LD.n)

        block = E(nc.Block())
        P.emit(block)

    return nc


_DEBUG = False
_NC_CACHE = {}


_DBG_PARTS = ("p3",)


def _get_nc():
    key = (bool(_DEBUG), tuple(_DBG_PARTS))
    if key not in _NC_CACHE:
        _NC_CACHE[key] = build_program(debug=key[0], dbg_parts=key[1])
    return _NC_CACHE[key]


def kernel(L0_attn_x, L0_attn_ln_g, L0_attn_ln_b, L0_attn_W_in, L0_attn_conv_w,
           L0_attn_conv_b, L0_attn_W_xp, L0_attn_W_dt, L0_attn_b_dt, L0_attn_A_log,
           L0_attn_D, L0_attn_W_out, L0_ffn_w_gate, L0_ffn_w_up, L0_ffn_w_down):
    f32 = np.float32
    x = np.asarray(L0_attn_x, f32)
    W_in = np.asarray(L0_attn_W_in, f32)
    W_out = np.asarray(L0_attn_W_out, f32)
    W_xp = np.asarray(L0_attn_W_xp, f32)
    conv_w = np.asarray(L0_attn_conv_w, f32)
    conv_b = np.asarray(L0_attn_conv_b, f32)
    W_dt = np.asarray(L0_attn_W_dt, f32)
    b_dt = np.asarray(L0_attn_b_dt, f32)
    A_log = np.asarray(L0_attn_A_log, f32)
    Dvec = np.asarray(L0_attn_D, f32)
    ln_g = np.asarray(L0_attn_ln_g, f32)
    ln_b = np.asarray(L0_attn_ln_b, f32)
    wg = np.ascontiguousarray(np.asarray(L0_ffn_w_gate, np.int32).T)
    wu = np.ascontiguousarray(np.asarray(L0_ffn_w_up, np.int32).T)
    wd = np.ascontiguousarray(np.asarray(L0_ffn_w_down, np.int32).T)

    sel = np.zeros((16, 2048), f32)
    for n in range(16):
        sel[n, 128 * n:128 * (n + 1)] = 1.0
    ones128 = np.ones((128, 128), f32)
    ones1 = np.ones((1, 128), f32)
    lngb = np.zeros((128, 16), f32)
    for kb in range(KD):
        lngb[:, 2 * kb] = ln_g[128 * kb:128 * (kb + 1)]
        lngb[:, 2 * kb + 1] = ln_b[128 * kb:128 * (kb + 1)]

    in_maps = []
    for c in range(8):
        b, r = divmod(c, 4)
        sl = slice(512 * r, 512 * (r + 1))
        xc = np.concatenate(
            [x[b, TC * (4 * j + r):TC * (4 * j + r + 1), :].T for j in range(NJ)], axis=1)
        winc_c = np.concatenate([W_in[:, sl], W_in[:, 2048 + sl.start:2048 + sl.stop]], axis=1)
        cwc = conv_w[sl].reshape(4, 128, 4).transpose(1, 0, 2).reshape(128, 16)
        avals = A_log[sl].reshape(4, 128, 16).transpose(1, 0, 2).reshape(128, 64)
        vv = np.stack([conv_b[sl], b_dt[sl], W_dt[0, sl], Dvec[sl]], axis=-1)
        vv = vv.reshape(4, 128, 4).transpose(1, 0, 2).reshape(128, 16)
        in_maps.append(dict(
            x_fm=np.ascontiguousarray(xc),
            winc=np.ascontiguousarray(winc_c),
            woc=np.ascontiguousarray(W_out[sl]),
            wxpc=np.ascontiguousarray(W_xp[sl]),
            wgT=wg, wuT=wu, wdT=wd,
            cwc=np.ascontiguousarray(cwc),
            avals=np.ascontiguousarray(avals),
            vecs=np.ascontiguousarray(vv),
            lngb=lngb, sel=sel, ones128=ones128, ones1=ones1,
        ))

    nc = _get_nc()
    res = run_bass_kernel_spmd(nc, in_maps, core_ids=list(range(8)))

    out = np.empty((B, L, D), f32)
    for c in range(8):
        b, r = divmod(c, 4)
        oy = np.asarray(res.results[c]["o_y"], f32)
        for j in range(NJ):
            out[b, TC * (4 * j + r):TC * (4 * j + r + 1), :] = oy[:, TC * j:TC * (j + 1)].T
    if _DEBUG:
        kernel._dbg = res.results
    return out
